# version 04
# baseline (speedup 1.0000x reference)
import contextlib
import math
import os

import numpy as np

os.environ.setdefault("GAUGE_NTFF_IGNORE_DMA", "1")

import concourse.bass as bass
import concourse.mybir as mybir
from concourse.bass_utils import run_bass_kernel_spmd

F32 = mybir.dt.float32
BF16 = mybir.dt.bfloat16
U8 = mybir.dt.uint8
AF = mybir.ActivationFunctionType
OP = mybir.AluOpType

NCORES = 8
SEQ_PER_CORE = 2
T = 2048
NTOK = SEQ_PER_CORE * T
D = 1024


class Sem:
    __slots__ = ("name", "count", "h")

    def __init__(self, name):
        self.name = name
        self.count = 0
        self.h = None


class _Op:
    __slots__ = ("fn", "waits", "inc", "inc_k", "pure")

    def __init__(self, fn, waits, inc, inc_k, pure):
        self.fn = fn
        self.waits = waits
        self.inc = inc
        self.inc_k = inc_k
        self.pure = pure


class Prog:
    ENGINES = ("tensor", "vector", "scalar", "gpsimd", "sync")

    def __init__(self, nc):
        self.nc = nc
        self.ops = {e: [] for e in self.ENGINES}
        self.sems = []
        self._pending_waits = {e: [] for e in self.ENGINES}

    def sem(self, name):
        s = Sem(name)
        self.sems.append(s)
        return s

    def on(self, eng, fn, waits=(), inc=None, inc_k=1, pure=False):
        wl = [(s, int(v)) for (s, v) in waits]
        if self._pending_waits[eng]:
            wl = self._pending_waits[eng] + wl
            self._pending_waits[eng] = []
        self.ops[eng].append(_Op(fn, wl, inc, inc_k, pure))
        if inc is not None:
            inc.count += inc_k
            return inc.count
        return None

    def comp(self, eng, fn, waits=(), inc=None):
        return self.on(eng, fn, waits, inc, 1)

    def dma(self, eng, fn, waits=(), inc=None):
        return self.on(eng, fn, waits, inc, 16)

    def wait(self, eng, pairs):
        self.on(eng, lambda e: None, waits=pairs, pure=True)

    def barrier(self, name):
        if not hasattr(self, "_bar_sem"):
            self._bar_sem = self.sem("p_barrier")
        done = self._bar_sem
        pairs = []
        for e in self.ENGINES:
            lst = self.ops[e]
            idx = None
            for i in range(len(lst) - 1, -1, -1):
                if not lst[i].pure:
                    idx = i
                    break
            if idx is None:
                continue
            op = lst[idx]
            if op.inc is None:
                op.inc = done
                op.inc_k = 1
                done.count += 1
                pairs.append((done, done.count))
            else:
                pairs.append((op.inc, op.inc.count))
        best = {}
        for s, v in pairs:
            best[s] = max(best.get(s, 0), v)
        pairs = [(s, v) for s, v in best.items()]
        for e in self.ENGINES:
            self._pending_waits[e] = self._pending_waits[e] + [(s, int(v)) for s, v in pairs]
        return pairs

    def build(self):
        nc = self.nc
        with contextlib.ExitStack() as es:
            for s in self.sems:
                s.h = es.enter_context(nc.semaphore(s.name))
            block = es.enter_context(nc.Block())

            def mk(eng_name):
                def emit(engine):
                    for op in self.ops[eng_name]:
                        for (s, v) in op.waits:
                            engine.wait_ge(s.h, v)
                        instr = op.fn(engine)
                        if op.inc is not None:
                            assert instr is not None, f"op on {eng_name} with inc returned no instr"
                            instr.then_inc(op.inc.h, op.inc_k)
                return emit

            for e in self.ENGINES:
                if self.ops[e]:
                    getattr(block, e)(mk(e))


class Consts:
    def __init__(self, p, nc, es):
        self.ones_col = es.enter_context(nc.sbuf_tensor("c_ones_col", [128, 1], F32))
        self.ones_row = es.enter_context(nc.sbuf_tensor("c_ones_row", [1, 128], F32))
        self.ident_bf = es.enter_context(nc.sbuf_tensor("c_ident_bf", [128, 128], BF16))
        self.anti128 = es.enter_context(nc.sbuf_tensor("c_anti128", [128, 128], BF16))
        self.anti127 = es.enter_context(nc.sbuf_tensor("c_anti127", [128, 128], BF16))
        sem = p.sem("c_const")

        def emit(g):
            g.memset(self.ones_col[:, :], 1.0)
            g.memset(self.ones_row[:1, :], 1.0)
            g.memset(self.ident_bf[:, :], 0.0)
            g.memset(self.anti128[:, :], 0.0)
            g.memset(self.anti127[:, :], 0.0)
            g.affine_select(
                out=self.anti128[:, :], in_=self.anti128[:, :],
                compare_op=OP.not_equal, fill=1.0, base=-127,
                pattern=[[1, 128]], channel_multiplier=1)
            g.affine_select(
                out=self.anti127[:, :], in_=self.anti127[:, :],
                compare_op=OP.not_equal, fill=1.0, base=-126,
                pattern=[[1, 128]], channel_multiplier=1)
            return g.affine_select(
                out=self.ident_bf[:, :], in_=self.ident_bf[:, :],
                compare_op=OP.not_equal, fill=1.0, base=0,
                pattern=[[-1, 128]], channel_multiplier=1)

        def emit2(g):
            g.tensor_copy(self.ones_col[:, :], self.ones_col[:, :])
            return g.tensor_copy(self.ones_row[:1, :], self.ones_row[:1, :])

        p.comp("gpsimd", emit)
        self.ready = p.comp("gpsimd", emit2, inc=sem)
        self.sem = sem


def make_ln_ctx(p, nc, es, consts, Tg, tag, share=None):
    if share is not None:
        psums = {k: share[k] for k in ("ps_s", "ps_q", "ps_A", "ps_B")}
    else:
        psums = {
            "ps_s": es.enter_context(nc.psum_tensor(f"ln_ps_s{tag}", [1, Tg], F32)),
            "ps_q": es.enter_context(nc.psum_tensor(f"ln_ps_q{tag}", [1, Tg], F32)),
            "ps_A": es.enter_context(nc.psum_tensor(f"ln_ps_A{tag}", [128, Tg], F32)),
            "ps_B": es.enter_context(nc.psum_tensor(f"ln_ps_B{tag}", [128, Tg], F32)),
        }
    ctx = {
        "consts": consts,
        "s_xsq": es.enter_context(nc.sbuf_tensor(f"ln_xsq{tag}", [128, Tg], F32)),
        "s_t1": es.enter_context(nc.sbuf_tensor(f"ln_t1{tag}", [128, Tg], F32)),
        "s_rows": [es.enter_context(nc.sbuf_tensor(f"ln_row{i}{tag}", [1, Tg], F32)) for i in range(4)],
        **psums,
        "sem_sq": p.sem(f"ln_sq{tag}"),
        "sem_mm": p.sem(f"ln_mm{tag}"),
        "sem_dv": p.sem(f"ln_dv{tag}"),
        "sem_ac": p.sem(f"ln_ac{tag}"),
        "Tg": Tg,
    }
    return ctx


def emit_layernorm(p, ctx, s_x3, s_xn3, xin_ready, write_waits=(), n_chunks=8,
                   eps=1e-5, affine=None, resid3=None):
    consts = ctx["consts"]
    s_xsq = ctx["s_xsq"]
    r_m, r_rstd, r_b, r_t = (r[:1, :] for r in ctx["s_rows"])
    ps_s, ps_q, ps_A, ps_B = ctx["ps_s"], ctx["ps_q"], ctx["ps_A"], ctx["ps_B"]
    sq, mm, dv, ac = ctx["sem_sq"], ctx["sem_mm"], ctx["sem_dv"], ctx["sem_ac"]

    stats_guard = [(dv, dv.count)]

    for c in range(n_chunks):
        w = list(xin_ready) if c == 0 else [(mm, mm.count)]
        sv = p.comp("scalar", lambda s, c=c: s.activation(s_xsq[:, :], s_x3[:, c, :], AF.Square),
                    waits=w, inc=sq)
        p.comp("tensor", lambda t, c=c, n=n_chunks: t.matmul(
            ps_s[:1, :], consts.ones_col[:, :], s_x3[:, c, :],
            start=(c == 0), stop=(c == n - 1), skip_group_check=True),
            waits=(list(xin_ready) + stats_guard) if c == 0 else [])
        p.comp("tensor", lambda t, c=c, n=n_chunks: t.matmul(
            ps_q[:1, :], consts.ones_col[:, :], s_xsq[:, :],
            start=(c == 0), stop=(c == n - 1), skip_group_check=True),
            waits=[(sq, sv)], inc=mm)
    stats_done = mm.count

    nd = float(n_chunks * 128)
    p.comp("vector", lambda v: v.tensor_scalar(r_m, ps_s[:1, :], 1.0 / nd, None, OP.mult),
           waits=[(mm, stats_done)])
    p.comp("vector", lambda v: v.tensor_scalar(r_t, ps_q[:1, :], 1.0 / nd, None, OP.mult))
    p.comp("vector", lambda v: v.tensor_tensor(r_b, r_m, r_m, OP.mult))
    dvv = p.comp("vector", lambda v: v.scalar_tensor_tensor(r_t, r_t, eps, r_b,
                                                            OP.add, OP.subtract), inc=dv)
    p.comp("scalar", lambda s: s.activation(r_t, r_t, AF.Ln), waits=[(dv, dvv)])
    acv = p.comp("scalar", lambda s: s.activation(r_rstd, r_t, AF.Exp, scale=-0.5), inc=ac)
    p.comp("vector", lambda v: v.tensor_tensor(r_b, r_m, r_rstd, OP.mult), waits=[(ac, acv)])
    dvv = p.comp("vector", lambda v: v.tensor_scalar(r_b, r_b, -1.0, None, OP.mult), inc=dv)
    p.comp("tensor", lambda t: t.matmul(ps_A[:, :], consts.ones_row[:1, :], r_rstd,
                                        start=True, stop=True), waits=[(dv, dvv)])
    mv = p.comp("tensor", lambda t: t.matmul(ps_B[:, :], consts.ones_row[:1, :], r_b,
                                             start=True, stop=True), inc=mm)
    t1 = ctx["s_t1"]
    chunk_done = []
    for c in range(n_chunks):
        w = ([(mm, mv)] + list(write_waits)) if c == 0 else []
        p.comp("vector", lambda v, c=c: v.tensor_tensor(t1[:, :], s_x3[:, c, :], ps_A[:, :], OP.mult),
               waits=w)
        if affine is None and resid3 is None:
            cv = p.comp("vector", lambda v, c=c: v.tensor_tensor(s_xn3[:, c, :], t1[:, :], ps_B[:, :], OP.add),
                        inc=dv)
        else:
            p.comp("vector", lambda v, c=c: v.tensor_tensor(t1[:, :], t1[:, :], ps_B[:, :], OP.add))
            if affine is not None:
                gam, bet = affine
                if resid3 is None:
                    cv = p.comp("vector", lambda v, c=c, gam=gam, bet=bet: v.tensor_scalar(
                        s_xn3[:, c, :], t1[:, :], gam[:, c:c + 1], bet[:, c:c + 1],
                        OP.mult, OP.add), inc=dv)
                else:
                    p.comp("vector", lambda v, c=c, gam=gam, bet=bet: v.tensor_scalar(
                        t1[:, :], t1[:, :], gam[:, c:c + 1], bet[:, c:c + 1], OP.mult, OP.add))
                    cv = p.comp("vector", lambda v, c=c: v.tensor_tensor(
                        s_xn3[:, c, :], t1[:, :], resid3[:, c, :], OP.add), inc=dv)
            else:
                cv = p.comp("vector", lambda v, c=c: v.tensor_tensor(
                    s_xn3[:, c, :], t1[:, :], resid3[:, c, :], OP.add), inc=dv)
        chunk_done.append(cv)
    return dv, chunk_done[-1]


def emit_wprep(p, nc, es, consts, jobs, tag):
    s_src = [es.enter_context(nc.sbuf_tensor(f"wp_src{tag}{i}", [128, 128], BF16)) for i in range(2)]
    maxO = max(src.shape[0] for src, _ in jobs)
    s_strip = es.enter_context(nc.sbuf_tensor(f"wp_strip{tag}", [128, maxO], BF16))
    ps = [es.enter_context(nc.psum_tensor(f"wp_ps{tag}{i}", [128, 128], BF16)) for i in range(2)]
    s_ld = p.sem(f"wp_ld{tag}")
    s_mm = p.sem(f"wp_mm{tag}")
    s_ev = p.sem(f"wp_ev{tag}")
    s_st = p.sem(f"wp_st{tag}")

    k = 0
    mm_at = {}
    for src, dst in jobs:
        O, I = src.shape
        n_ob = O // 128
        for ic in range(I // 128):
            st_guard = s_st.count
            for ob in range(n_ob):
                buf = k % 2
                w = [(consts.sem, consts.ready)] if k < 2 else [(s_mm, mm_at[k - 2])]
                lv = p.dma("gpsimd",
                           lambda g, src=src, ob=ob, ic=ic, buf=buf: g.dma_start(
                               out=s_src[buf][:, :],
                               in_=src.ap()[ob * 128:(ob + 1) * 128, ic * 128:(ic + 1) * 128]),
                           waits=w, inc=s_ld)
                w = [(s_ld, lv)]
                if k >= 2:
                    w.append((s_ev, s_ev.count - 1))
                mv = p.comp("tensor",
                            lambda t, buf=buf: t.transpose(ps[buf][:, :], s_src[buf][:, :],
                                                           consts.ident_bf[:, :]),
                            waits=w, inc=s_mm)
                mm_at[k] = mv
                ev_w = [(s_mm, mv)]
                if ob == 0:
                    ev_w.append((s_st, st_guard))
                p.comp("vector",
                       lambda v, buf=buf, ob=ob: v.tensor_copy(
                           s_strip[:, ob * 128:(ob + 1) * 128], ps[buf][:, :]),
                       waits=ev_w, inc=s_ev)
                k += 1
            p.dma("sync",
                  lambda sy, dst=dst, ic=ic, O=O: sy.dma_start(
                      out=dst.ap()[ic * 128:(ic + 1) * 128, :],
                      in_=s_strip[:, 0:O]),
                  waits=[(s_ev, s_ev.count)], inc=s_st)
    p.wait("sync", [(s_st, s_st.count)])
    return s_st


def emit_l0ffn(p, nc, es, consts, x_in, y_out, W, tag):
    Tg = 512
    NG = NTOK // Tg

    s_x = es.enter_context(nc.sbuf_tensor(f"ff_x{tag}", [128, 8, Tg], F32))
    s_xn = es.enter_context(nc.sbuf_tensor(f"ff_xn{tag}", [128, 8, Tg], BF16))
    s_ha = es.enter_context(nc.sbuf_tensor(f"ff_ha{tag}", [128, 32, Tg], BF16))
    s_hb = es.enter_context(nc.sbuf_tensor(f"ff_hb{tag}", [128, 32, Tg], BF16))
    s_u = es.enter_context(nc.sbuf_tensor(f"ff_u{tag}", [128, Tg], BF16))
    s_v = es.enter_context(nc.sbuf_tensor(f"ff_v{tag}", [128, Tg], BF16))
    s_y = es.enter_context(nc.sbuf_tensor(f"ff_y{tag}", [128, 8, Tg], F32))
    s_wsl = [es.enter_context(nc.sbuf_tensor(f"ff_w{tag}{i}", [128, 32, 128], BF16))
             for i in range(4)]
    s_s1 = es.enter_context(nc.sbuf_tensor(f"ff_s1{tag}", [128, 32], F32))
    s_s2 = es.enter_context(nc.sbuf_tensor(f"ff_s2{tag}", [128, 32], F32))
    s_sh1 = es.enter_context(nc.sbuf_tensor(f"ff_sh1{tag}", [128, 32], F32))
    s_sh2 = es.enter_context(nc.sbuf_tensor(f"ff_sh2{tag}", [128, 32], F32))
    s_s3h = es.enter_context(nc.sbuf_tensor(f"ff_s3h{tag}", [128, 8], F32))
    ps_mm = [es.enter_context(nc.psum_tensor(f"ff_ps{tag}{i}", [128, Tg], F32)) for i in range(4)]

    ln = make_ln_ctx(p, nc, es, consts, Tg, "f" + tag)

    s_ld = p.sem(f"ff_ld{tag}")
    s_wld = p.sem(f"ff_wld{tag}")
    s_mm = p.sem(f"ff_mm{tag}")
    s_ev = p.sem(f"ff_ev{tag}")
    s_dv = p.sem(f"ff_dv{tag}")
    s_st = p.sem(f"ff_st{tag}")

    for dst, key in ((s_s1, "s1"), (s_s2, "s2"), (s_sh1, "sh1"), (s_sh2, "sh2")):
        p.dma("gpsimd", lambda g, dst=dst, key=key: g.dma_start(out=dst[:, :], in_=W[key].ap()),
              inc=s_ld)
    sc_v = p.dma("gpsimd", lambda g: g.dma_start(out=s_s3h[:, :], in_=W["s3"].ap()), inc=s_ld)
    p.comp("vector", lambda v: v.tensor_scalar(s_s3h[:, :], s_s3h[:, :], 0.5, None, OP.mult),
           waits=[(s_ld, sc_v)], inc=s_dv)

    slab_jobs = []
    for g in range(NG):
        for o in range(32):
            slab_jobs.append(("w1", W["w1T"], 8, o))
            slab_jobs.append(("w2", W["w2T"], 8, o))
        for o in range(32):
            slab_jobs.append(("h1", W["h1T"], 32, o))
        for o in range(32):
            slab_jobs.append(("h2", W["h2T"], 32, o))
        for o in range(8):
            slab_jobs.append(("w3", W["w3T"], 32, o))
    NSLAB_BUF = 4
    slab_free = [None] * len(slab_jobs)
    slab_ld = [None] * len(slab_jobs)
    next_slab = [0]

    def issue_slab(consts_gate=False):
        k = next_slab[0]
        next_slab[0] += 1
        layer, wt, nch, o = slab_jobs[k]
        buf = k % NSLAB_BUF
        waits = []
        if k >= NSLAB_BUF:
            waits.append(slab_free[k - NSLAB_BUF])
        elif consts_gate:
            waits.append((consts.sem, consts.ready))
        Ocols = wt.shape[1]

        def fn(sy, wt=wt, nch=nch, o=o, buf=buf, Ocols=Ocols):
            src = bass.AP(wt, o * 128, [[Ocols, 128], [128 * Ocols, nch], [1, 128]])
            return sy.dma_start(out=s_wsl[buf][:, 0:nch, :], in_=src)
        slab_ld[k] = p.dma("sync", fn, waits=waits, inc=s_wld)
        return k

    for _ in range(NSLAB_BUF):
        issue_slab(consts_gate=True)

    bank_free = {}

    def stage_matmuls(bank_i, nch, src3, slab_k, extra_waits):
        buf = slab_k % NSLAB_BUF
        v = None
        for c in range(nch):
            w = []
            if c == 0:
                w.append((s_wld, slab_ld[slab_k]))
                if bank_i in bank_free:
                    w.append(bank_free[bank_i])
                w += list(extra_waits)
            v = p.comp("tensor",
                       lambda t, c=c, bank_i=bank_i, buf=buf, nch=nch, src3=src3:
                           t.matmul(ps_mm[bank_i][:, :], s_wsl[buf][:, c, :], src3[:, c, :],
                                    start=(c == 0), stop=(c == nch - 1), skip_group_check=True),
                       waits=w, inc=(s_mm if c == nch - 1 else None))
        return v

    xn3 = s_xn.ap()
    x3 = s_x.ap()
    ha3 = s_ha.ap()
    hb3 = s_hb.ap()
    y3 = s_y.ap()

    store_val = None
    resid_done = None
    for g in range(NG):
        xwaits = [] if resid_done is None else [(s_dv, resid_done)]
        xv = p.dma("gpsimd", lambda gg, g=g: gg.dma_start(
            out=s_x[:, :, :],
            in_=bass.AP(x_in, g * Tg, [[NTOK, 128], [128 * NTOK, 8], [1, Tg]])),
            waits=xwaits, inc=s_ld)

        ln_in = [(s_ld, xv)]
        wr = [(s_mm, s_mm.count)] if g > 0 else []
        lnsem, lnval = emit_layernorm(p, ln, x3, xn3, ln_in, write_waits=wr)

        uv_free = {}
        for o in range(32):
            k1 = next_slab[0] - NSLAB_BUF
            k_w1 = g * 136 + 2 * o
            k_w2 = k_w1 + 1
            b1 = (2 * o) % 4
            b2 = (2 * o + 1) % 4
            mv1 = stage_matmuls(b1, 8, xn3, k_w1, [(lnsem, lnval)] if o == 0 else [])
            mv2 = stage_matmuls(b2, 8, xn3, k_w2, [])
            slab_free[k_w1] = (s_mm, mv1)
            slab_free[k_w2] = (s_mm, mv2)
            if next_slab[0] < len(slab_jobs):
                issue_slab()
            if next_slab[0] < len(slab_jobs):
                issue_slab()
            ew1 = [(s_mm, mv1)]
            if o >= 1:
                ew1.append(uv_free[0])
            e1 = p.comp("scalar", lambda s, o=o, b1=b1: s.activation(
                s_u[:, :], ps_mm[b1][:, :], AF.Silu, scale=s_s1[:, o:o + 1]),
                waits=ew1, inc=s_ev)
            ew2 = [(s_mm, mv2)]
            if o >= 1:
                ew2.append(uv_free[1])
            e2 = p.comp("scalar", lambda s, o=o, b2=b2: s.activation(
                s_v[:, :], ps_mm[b2][:, :], AF.Copy, scale=s_s2[:, o:o + 1]),
                waits=ew2, inc=s_ev)
            bank_free[b1] = (s_ev, e1)
            bank_free[b2] = (s_ev, e2)
            dv = p.comp("vector", lambda v, o=o: v.tensor_tensor(
                ha3[:, o, :], s_u[:, :], s_v[:, :], OP.mult),
                waits=[(s_ev, e2)], inc=s_dv)
            uv_free = {0: (s_dv, dv), 1: (s_dv, dv)}
        stage1_done = s_dv.count

        for o in range(32):
            k_h1 = g * 136 + 64 + o
            b = o % 4
            mv = stage_matmuls(b, 32, ha3, k_h1,
                               [(s_dv, stage1_done)] if o == 0 else [])
            slab_free[k_h1] = (s_mm, mv)
            if next_slab[0] < len(slab_jobs):
                issue_slab()
            e = p.comp("scalar", lambda s, o=o, b=b: s.activation(
                hb3[:, o, :], ps_mm[b][:, :], AF.Gelu, scale=s_sh1[:, o:o + 1]),
                waits=[(s_mm, mv)], inc=s_ev)
            bank_free[b] = (s_ev, e)
        h1_done = s_ev.count

        for o in range(32):
            k_h2 = g * 136 + 96 + o
            b = o % 4
            mv = stage_matmuls(b, 32, hb3, k_h2,
                               [(s_ev, h1_done)] if o == 0 else [])
            slab_free[k_h2] = (s_mm, mv)
            if next_slab[0] < len(slab_jobs):
                issue_slab()
            e = p.comp("scalar", lambda s, o=o, b=b: s.activation(
                ha3[:, o, :], ps_mm[b][:, :], AF.Gelu, scale=s_sh2[:, o:o + 1]),
                waits=[(s_mm, mv)], inc=s_ev)
            bank_free[b] = (s_ev, e)
        h2_done = s_ev.count

        for o in range(8):
            k_w3 = g * 136 + 128 + o
            b = o % 4
            mv = stage_matmuls(b, 32, ha3, k_w3,
                               [(s_ev, h2_done)] if o == 0 else [])
            slab_free[k_w3] = (s_mm, mv)
            if next_slab[0] < len(slab_jobs):
                issue_slab()
            w = [(s_mm, mv)]
            if g > 0 and o == 0:
                w.append((s_st, store_val))
            dv = p.comp("vector", lambda v, o=o, b=b: v.scalar_tensor_tensor(
                y3[:, o, :], ps_mm[b][:, :], s_s3h[:, o:o + 1], x3[:, o, :],
                OP.mult, OP.add),
                waits=w, inc=s_dv)
            bank_free[b] = (s_dv, dv)
        resid_done = s_dv.count

        store_val = p.dma("gpsimd", lambda gg, g=g: gg.dma_start(
            out=bass.AP(y_out, g * Tg, [[NTOK, 128], [128 * NTOK, 8], [1, Tg]]),
            in_=s_y[:, :, :]),
            waits=[(s_dv, resid_done)], inc=s_st)

    p.wait("gpsimd", [(s_st, s_st.count)])
    return s_st


def _featmajor(x):
    B, Tt, Dd = x.shape
    return np.ascontiguousarray(x.reshape(B * Tt, Dd).T)


def _scale_cols(s):
    return np.ascontiguousarray(s.reshape(-1, 128).T)


def build_prep():
    nc = bass.Bass("TRN2", target_bir_lowering=False)
    ins = {}
    for nm, shape in (("w1", (4096, 1024)), ("w2", (4096, 1024)), ("h1", (4096, 4096)),
                      ("h2", (4096, 4096)), ("w3", (1024, 4096))):
        ins[nm] = nc.dram_tensor(nm, list(shape), F32, kind="ExternalInput")
    outs = {}
    for nm, shape in (("w1T", (1024, 4096)), ("w2T", (1024, 4096)), ("h1T", (4096, 4096)),
                      ("h2T", (4096, 4096)), ("w3T", (4096, 1024))):
        outs[nm] = nc.dram_tensor(nm, list(shape), BF16, kind="ExternalOutput")
    for nm, shape in (("wg", (2816, 1024)), ("wu", (2816, 1024)), ("wd", (1024, 2816))):
        ins[nm] = nc.dram_tensor(nm, list(shape), F32, kind="ExternalInput")
    for nm, shape in (("wgT", (1024, 2816)), ("wuT", (1024, 2816)), ("wdT", (2816, 1024))):
        outs[nm] = nc.dram_tensor(nm, list(shape), BF16, kind="ExternalOutput")
    p = Prog(nc)
    with contextlib.ExitStack() as es:
        consts = Consts(p, nc, es)
        emit_wprep(p, nc, es, consts,
                   [(ins[a], outs[a + "T"]) for a in ("w1", "w2", "h1", "h2", "w3")], "p")
        p.barrier("prep_bar")
        emit_l1wprep(p, nc, es, consts,
                     [(ins[a], outs[a + "T"]) for a in ("wg", "wu", "wd")], "p")
        p.build()
    return nc


def build_l0ffn():
    nc = bass.Bass("TRN2", target_bir_lowering=False)
    x_in = nc.dram_tensor("x", [1024 * NTOK], F32, kind="ExternalInput")
    y_out = nc.dram_tensor("y", [1024 * NTOK], F32, kind="ExternalOutput")
    W = {}
    for nm, shape, dt in (("w1T", (1024, 4096), BF16), ("w2T", (1024, 4096), BF16),
                          ("h1T", (4096, 4096), BF16), ("h2T", (4096, 4096), BF16),
                          ("w3T", (4096, 1024), BF16), ("s1", (128, 32), F32),
                          ("s2", (128, 32), F32), ("sh1", (128, 32), F32),
                          ("sh2", (128, 32), F32), ("s3", (128, 8), F32)):
        W[nm] = nc.dram_tensor(nm, list(shape), dt, kind="ExternalInput")
    p = Prog(nc)
    with contextlib.ExitStack() as es:
        consts = Consts(p, nc, es)
        emit_l0ffn(p, nc, es, consts, x_in, y_out, W, "a")
        p.build()
    return nc


def build_l0attn():
    nc = bass.Bass("TRN2", target_bir_lowering=False)
    x_in = nc.dram_tensor("x", [1024 * NTOK], F32, kind="ExternalInput")
    y_out = nc.dram_tensor("y", [1024 * NTOK], F32, kind="ExternalOutput")
    W = {}
    for nm, shape in (("Wq", (1024, 1024)), ("Wk", (1024, 1024)), ("Wv", (1024, 2048)),
                      ("Wg", (1024, 2048)), ("Wab", (1024, 8)), ("Wo", (2048, 1024)),
                      ("W2", (1024, 1024))):
        W[nm] = nc.dram_tensor(nm, list(shape), F32, kind="ExternalInput")
    for nm, shape in (("ln1g", (128, 8)), ("ln1b", (128, 8)), ("ln2g", (128, 8)),
                      ("ln2b", (128, 8)), ("rmsw", (128, 8)), ("b2", (128, 8)),
                      ("onw", (128, 4)), ("cq", (128, 32)), ("ck", (128, 32)),
                      ("cv", (128, 64)), ("dtb", (4, 1)), ("alog", (4, 1))):
        W[nm] = nc.dram_tensor(nm, list(shape), F32, kind="ExternalInput")
    scr = {}
    for nm, size, dt in (("xsub", 1024 * NTOK, F32), ("qraw", 1024 * NTOK, BF16),
                         ("kraw", 1024 * NTOK, BF16), ("vraw", 2048 * NTOK, BF16),
                         ("gate", 2048 * NTOK, BF16), ("qn", 1024 * NTOK, BF16),
                         ("kn", 1024 * NTOK, BF16), ("vv", 2048 * NTOK, BF16),
                         ("ab", 8 * NTOK, F32), ("Bsc", 4 * 2 * (T + 1), F32),
                         ("beta", 4 * NTOK, F32), ("osc", 2048 * NTOK, BF16)):
        scr[nm] = nc.dram_tensor("scr_" + nm, [size], dt, kind="Internal")
    p = Prog(nc)
    with contextlib.ExitStack() as es:
        consts = Consts(p, nc, es)
        emit_l0attn(p, nc, es, consts, x_in, y_out, W, scr, "a")
        p.build()
    return nc


def build_l1ffn():
    nc = bass.Bass("TRN2", target_bir_lowering=False)
    x_in = nc.dram_tensor("x", [1024 * NTOK], F32, kind="ExternalInput")
    y_out = nc.dram_tensor("y", [1024 * NTOK], F32, kind="ExternalOutput")
    W = {
        "wgT": nc.dram_tensor("wgT", [1024, 2816], BF16, kind="ExternalInput"),
        "wuT": nc.dram_tensor("wuT", [1024, 2816], BF16, kind="ExternalInput"),
        "wdT": nc.dram_tensor("wdT", [2816, 1024], BF16, kind="ExternalInput"),
    }
    p = Prog(nc)
    with contextlib.ExitStack() as es:
        consts = Consts(p, nc, es)
        emit_l1ffn(p, nc, es, consts, x_in, y_out, W, "a")
        p.build()
    return nc


def build_l1attn():
    nc = bass.Bass("TRN2", target_bir_lowering=False)
    x_in = nc.dram_tensor("x", [1024 * NTOK], F32, kind="ExternalInput")
    y_out = nc.dram_tensor("y", [1024 * NTOK], F32, kind="ExternalOutput")
    W = {}
    for nm in ("Wq", "Wk", "Wv", "Wo"):
        W[nm] = nc.dram_tensor(nm, [1024, 1024], F32, kind="ExternalInput")
    W["dist"] = nc.dram_tensor("dist", [2047 * 64], F32, kind="ExternalInput")
    for nm in ("bq", "bk", "bv", "bo", "gam", "bet"):
        W[nm] = nc.dram_tensor(nm, [128, 8], F32, kind="ExternalInput")
    scr = {
        "xs": nc.dram_tensor("scr_xs", [1024 * NTOK], F32),
        "qf": nc.dram_tensor("scr_qf", [1024 * NTOK], BF16),
        "kf": nc.dram_tensor("scr_kf", [1024 * NTOK], BF16),
        "vt": nc.dram_tensor("scr_vt", [NTOK * 1024], BF16),
        "cf": nc.dram_tensor("scr_cf", [1024 * NTOK], BF16),
        "qe": nc.dram_tensor("scr_qe", [16 * 128 * 2176], BF16),
        "ke": nc.dram_tensor("scr_ke", [16 * 128 * 2176], BF16),
    }
    p = Prog(nc)
    with contextlib.ExitStack() as es:
        consts = Consts(p, nc, es)
        emit_l1attn(p, nc, es, consts, x_in, y_out, W, scr, "a")
        p.build()
    return nc


def build_fused():
    nc = bass.Bass("TRN2", target_bir_lowering=False)
    x_in = nc.dram_tensor("x", [1024 * NTOK], F32, kind="ExternalInput")
    y_out = nc.dram_tensor("y", [1024 * NTOK], F32, kind="ExternalOutput")
    raw = {}
    for nm, shape in (("w1", (4096, 1024)), ("w2", (4096, 1024)), ("h1", (4096, 4096)),
                      ("h2", (4096, 4096)), ("w3", (1024, 4096)),
                      ("wg", (2816, 1024)), ("wu", (2816, 1024)), ("wd", (1024, 2816))):
        raw[nm] = nc.dram_tensor(nm, list(shape), F32, kind="ExternalInput")
    cache = {}
    for nm, shape in (("w1T", (1024, 4096)), ("w2T", (1024, 4096)), ("h1T", (4096, 4096)),
                      ("h2T", (4096, 4096)), ("w3T", (4096, 1024)),
                      ("wgT", (1024, 2816)), ("wuT", (1024, 2816)), ("wdT", (2816, 1024))):
        cache[nm] = nc.dram_tensor(nm, list(shape), BF16, kind="Internal")
    W0 = {nm: cache[nm] for nm in ("w1T", "w2T", "h1T", "h2T", "w3T")}
    for nm, shape in (("s1", (128, 32)), ("s2", (128, 32)), ("sh1", (128, 32)),
                      ("sh2", (128, 32)), ("s3", (128, 8))):
        W0[nm] = nc.dram_tensor(nm, list(shape), F32, kind="ExternalInput")
    W1 = {nm: cache[nm] for nm in ("wgT", "wuT", "wdT")}
    WA = {}
    for nm, shape in (("Wq", (1024, 1024)), ("Wk", (1024, 1024)), ("Wv", (1024, 2048)),
                      ("Wg", (1024, 2048)), ("Wab", (1024, 8)), ("Wo", (2048, 1024)),
                      ("W2", (1024, 1024))):
        WA[nm] = nc.dram_tensor("a" + nm, list(shape), F32, kind="ExternalInput")
    for nm, shape in (("ln1g", (128, 8)), ("ln1b", (128, 8)), ("ln2g", (128, 8)),
                      ("ln2b", (128, 8)), ("rmsw", (128, 8)), ("b2", (128, 8)),
                      ("onw", (128, 4)), ("cq", (128, 32)), ("ck", (128, 32)),
                      ("cv", (128, 64)), ("dtb", (4, 1)), ("alog", (4, 1))):
        WA[nm] = nc.dram_tensor("a" + nm, list(shape), F32, kind="ExternalInput")
    scrA = {}
    for nm, size, dt in (("xsub", 1024 * NTOK, F32), ("qraw", 1024 * NTOK, BF16),
                         ("kraw", 1024 * NTOK, BF16), ("vraw", 2048 * NTOK, BF16),
                         ("gate", 2048 * NTOK, BF16), ("qn", 1024 * NTOK, BF16),
                         ("kn", 1024 * NTOK, BF16), ("vv", 2048 * NTOK, BF16),
                         ("ab", 8 * NTOK, F32), ("Bsc", 4 * 2 * (T + 1), F32),
                         ("beta", 4 * NTOK, F32), ("osc", 2048 * NTOK, BF16)):
        scrA[nm] = nc.dram_tensor("ascr_" + nm, [size], dt, kind="Internal")
    WB = {}
    for nm in ("Wq", "Wk", "Wv", "Wo"):
        WB[nm] = nc.dram_tensor("b" + nm, [1024, 1024], F32, kind="ExternalInput")
    WB["dist"] = nc.dram_tensor("bdist", [2047 * 64], F32, kind="ExternalInput")
    for nm in ("bq", "bk", "bv", "bo", "gam", "bet"):
        WB[nm] = nc.dram_tensor("b" + nm, [128, 8], F32, kind="ExternalInput")
    scrB = {
        "xs": nc.dram_tensor("bscr_xs", [1024 * NTOK], F32),
        "qf": nc.dram_tensor("bscr_qf", [1024 * NTOK], BF16),
        "kf": nc.dram_tensor("bscr_kf", [1024 * NTOK], BF16),
        "vt": nc.dram_tensor("bscr_vt", [NTOK * 1024], BF16),
        "cf": nc.dram_tensor("bscr_cf", [1024 * NTOK], BF16),
        "qe": nc.dram_tensor("bscr_qe", [16 * 128 * 2176], BF16),
        "ke": nc.dram_tensor("bscr_ke", [16 * 128 * 2176], BF16),
    }
    r = [nc.dram_tensor(f"r{i}", [1024 * NTOK], F32, kind="Internal") for i in range(5)]

    p = Prog(nc)
    with contextlib.ExitStack() as esC:
        consts = Consts(p, nc, esC)
        with contextlib.ExitStack() as es:
            emit_wprep(p, nc, es, consts,
                       [(raw[a], cache[a + "T"]) for a in ("w1", "w2", "h1", "h2", "w3")], "p")
            p.barrier("fz0")
            emit_l1wprep(p, nc, es, consts,
                         [(raw[a], cache[a + "T"]) for a in ("wg", "wu", "wd")], "p")
        p.barrier("fz1")
        with contextlib.ExitStack() as es:
            emit_l0ffn(p, nc, es, consts, x_in, r[0], W0, "f0")
        p.barrier("fz2")
        with contextlib.ExitStack() as es:
            emit_l0attn(p, nc, es, consts, r[0], r[1], WA, scrA, "fa")
        p.barrier("fz3")
        with contextlib.ExitStack() as es:
            emit_l0ffn(p, nc, es, consts, r[1], r[2], W0, "f1")
        p.barrier("fz4")
        with contextlib.ExitStack() as es:
            emit_l1ffn(p, nc, es, consts, r[2], r[3], W1, "g0")
        p.barrier("fz5")
        with contextlib.ExitStack() as es:
            emit_l1attn(p, nc, es, consts, r[3], r[4], WB, scrB, "ga")
        p.barrier("fz6")
        with contextlib.ExitStack() as es:
            emit_l1ffn(p, nc, es, consts, r[4], y_out, W1, "g1")
        p.barrier("fz7")
        p.wait("vector", [])
        p.build()
    return nc


def _scale_cols32(s):
    return np.ascontiguousarray(np.asarray(s, np.float32).reshape(-1, 128).T)


def _conv_cols(w):
    w = np.asarray(w, np.float32)
    nch = w.shape[0] // 128
    return np.ascontiguousarray(w.reshape(nch, 128, 4).transpose(1, 0, 2).reshape(128, nch * 4))


def kernel(**inputs):
    N = NCORES
    g = lambda k: np.asarray(inputs[k], dtype=np.float32)

    x = g("L0_attn_x")
    xsh = [np.ascontiguousarray(x[2 * c:2 * c + 2].reshape(NTOK, D).T).reshape(-1)
           for c in range(N)]

    nc = build_fused()
    common = {
        "w1": g("L0_ffn_w1_w"), "w2": g("L0_ffn_w2_w"), "h1": g("L0_ffn_h1_w"),
        "h2": g("L0_ffn_h2_w"), "w3": g("L0_ffn_w3_w"),
        "wg": g("L1_ffn_w_gate"), "wu": g("L1_ffn_w_up"), "wd": g("L1_ffn_w_down"),
        "s1": _scale_cols32(g("L0_ffn_w1_s")), "s2": _scale_cols32(g("L0_ffn_w2_s")),
        "sh1": _scale_cols32(g("L0_ffn_h1_s")), "sh2": _scale_cols32(g("L0_ffn_h2_s")),
        "s3": _scale_cols32(g("L0_ffn_w3_s")),
        "aWq": g("L0_attn_Wq"), "aWk": g("L0_attn_Wk"), "aWv": g("L0_attn_Wv"),
        "aWg": g("L0_attn_Wg"),
        "aWab": np.ascontiguousarray(np.concatenate(
            [g("L0_attn_Wa"), g("L0_attn_Wb")], axis=1)),
        "aWo": g("L0_attn_Wo"), "aW2": g("L0_attn_W2"),
        "aln1g": _scale_cols32(g("L0_attn_ln1_g")), "aln1b": _scale_cols32(g("L0_attn_ln1_b")),
        "aln2g": _scale_cols32(g("L0_attn_ln2_g")), "aln2b": _scale_cols32(g("L0_attn_ln2_b")),
        "armsw": _scale_cols32(g("L0_attn_rms_in_w")), "ab2": _scale_cols32(g("L0_attn_b2")),
        "aonw": _scale_cols32(g("L0_attn_o_norm_w")),
        "acq": _conv_cols(g("L0_attn_conv_q_w")), "ack": _conv_cols(g("L0_attn_conv_k_w")),
        "acv": _conv_cols(g("L0_attn_conv_v_w")),
        "adtb": np.ascontiguousarray(g("L0_attn_dt_bias").reshape(4, 1)),
        "aalog": np.ascontiguousarray(g("L0_attn_A_log").reshape(4, 1)),
        "bWq": g("L1_attn_Wq"), "bWk": g("L1_attn_Wk"), "bWv": g("L1_attn_Wv"),
        "bWo": g("L1_attn_Wo"), "bdist": g("L1_attn_dist_emb").reshape(-1),
        "bbq": _scale_cols32(g("L1_attn_bq")), "bbk": _scale_cols32(g("L1_attn_bk")),
        "bbv": _scale_cols32(g("L1_attn_bv")), "bbo": _scale_cols32(g("L1_attn_bo")),
        "bgam": _scale_cols32(g("L1_attn_ln_gamma")),
        "bbet": _scale_cols32(g("L1_attn_ln_beta")),
    }
    maps = [dict(common, x=xsh[c]) for c in range(N)]
    r = run_bass_kernel_spmd(nc, maps, core_ids=list(range(N))).results
    out = np.empty((16, 2048, 1024), dtype=np.float32)
    for c in range(N):
        out[2 * c:2 * c + 2] = r[c]["y"].reshape(1024, NTOK).T.reshape(2, 2048, 1024)
    return out


def kernel_multi(**inputs):
    N = NCORES
    g = lambda k: np.asarray(inputs[k], dtype=np.float32)

    x = g("L0_attn_x")
    xsh = [np.ascontiguousarray(x[2 * c:2 * c + 2].reshape(NTOK, D).T).reshape(-1)
           for c in range(N)]

    ncA = build_prep()
    inA = {
        "w1": g("L0_ffn_w1_w"), "w2": g("L0_ffn_w2_w"), "h1": g("L0_ffn_h1_w"),
        "h2": g("L0_ffn_h2_w"), "w3": g("L0_ffn_w3_w"),
        "wg": g("L1_ffn_w_gate"), "wu": g("L1_ffn_w_up"), "wd": g("L1_ffn_w_down"),
    }
    resA = run_bass_kernel_spmd(ncA, [inA] * N, core_ids=list(range(N))).results

    ncB = build_l0ffn()
    l0ffn_w = {
        "s1": _scale_cols32(g("L0_ffn_w1_s")), "s2": _scale_cols32(g("L0_ffn_w2_s")),
        "sh1": _scale_cols32(g("L0_ffn_h1_s")), "sh2": _scale_cols32(g("L0_ffn_h2_s")),
        "s3": _scale_cols32(g("L0_ffn_w3_s")),
    }

    def run_l0ffn(xs):
        maps = []
        for c in range(N):
            m = dict(l0ffn_w)
            for nm in ("w1T", "w2T", "h1T", "h2T", "w3T"):
                m[nm] = resA[c][nm]
            m["x"] = xs[c]
            maps.append(m)
        r = run_bass_kernel_spmd(ncB, maps, core_ids=list(range(N))).results
        return [r[c]["y"] for c in range(N)]

    y = run_l0ffn(xsh)

    ncC = build_l0attn()
    inC = {
        "Wq": g("L0_attn_Wq"), "Wk": g("L0_attn_Wk"), "Wv": g("L0_attn_Wv"),
        "Wg": g("L0_attn_Wg"),
        "Wab": np.ascontiguousarray(np.concatenate([g("L0_attn_Wa"), g("L0_attn_Wb")], axis=1)),
        "Wo": g("L0_attn_Wo"), "W2": g("L0_attn_W2"),
        "ln1g": _scale_cols32(g("L0_attn_ln1_g")), "ln1b": _scale_cols32(g("L0_attn_ln1_b")),
        "ln2g": _scale_cols32(g("L0_attn_ln2_g")), "ln2b": _scale_cols32(g("L0_attn_ln2_b")),
        "rmsw": _scale_cols32(g("L0_attn_rms_in_w")), "b2": _scale_cols32(g("L0_attn_b2")),
        "onw": _scale_cols32(g("L0_attn_o_norm_w")),
        "cq": _conv_cols(g("L0_attn_conv_q_w")), "ck": _conv_cols(g("L0_attn_conv_k_w")),
        "cv": _conv_cols(g("L0_attn_conv_v_w")),
        "dtb": np.ascontiguousarray(g("L0_attn_dt_bias").reshape(4, 1)),
        "alog": np.ascontiguousarray(g("L0_attn_A_log").reshape(4, 1)),
    }
    maps = [dict(inC, x=y[c]) for c in range(N)]
    r = run_bass_kernel_spmd(ncC, maps, core_ids=list(range(N))).results
    y = [r[c]["y"] for c in range(N)]

    y = run_l0ffn(y)

    ncE = build_l1ffn()

    def run_l1ffn(xs):
        maps = []
        for c in range(N):
            m = {"wgT": resA[c]["wgT"], "wuT": resA[c]["wuT"], "wdT": resA[c]["wdT"],
                 "x": xs[c]}
            maps.append(m)
        r = run_bass_kernel_spmd(ncE, maps, core_ids=list(range(N))).results
        return [r[c]["y"] for c in range(N)]

    y = run_l1ffn(y)

    ncF = build_l1attn()
    inF = {
        "Wq": g("L1_attn_Wq"), "Wk": g("L1_attn_Wk"), "Wv": g("L1_attn_Wv"),
        "Wo": g("L1_attn_Wo"), "dist": g("L1_attn_dist_emb").reshape(-1),
        "bq": _scale_cols32(g("L1_attn_bq")), "bk": _scale_cols32(g("L1_attn_bk")),
        "bv": _scale_cols32(g("L1_attn_bv")), "bo": _scale_cols32(g("L1_attn_bo")),
        "gam": _scale_cols32(g("L1_attn_ln_gamma")), "bet": _scale_cols32(g("L1_attn_ln_beta")),
    }
    maps = [dict(inF, x=y[c]) for c in range(N)]
    r = run_bass_kernel_spmd(ncF, maps, core_ids=list(range(N))).results
    y = [r[c]["y"] for c in range(N)]

    y = run_l1ffn(y)

    out = np.empty((16, 2048, 1024), dtype=np.float32)
    for c in range(N):
        out[2 * c:2 * c + 2] = y[c].reshape(1024, NTOK).T.reshape(2, 2048, 1024)
    return out


_RNE_C = 12582912.0


def make_quant_scratch(p, nc, es, Fmax, tag):
    qs = {
        "red": es.enter_context(nc.sbuf_tensor(f"q_red{tag}", [128, Fmax // 16], F32)),
        "inv": es.enter_context(nc.sbuf_tensor(f"q_inv{tag}", [128, Fmax // 16], F32)),
        "u": es.enter_context(nc.sbuf_tensor(f"q_u{tag}", [128, Fmax], F32)),
        "sgn": es.enter_context(nc.sbuf_tensor(f"q_sgn{tag}", [128, Fmax], F32)),
        "r1": es.enter_context(nc.sbuf_tensor(f"q_r1{tag}", [128, Fmax], F32)),
        "r2": es.enter_context(nc.sbuf_tensor(f"q_r2{tag}", [128, Fmax], F32)),
        "m1": es.enter_context(nc.sbuf_tensor(f"q_m1{tag}", [128, Fmax], U8)),
        "m2": es.enter_context(nc.sbuf_tensor(f"q_m2{tag}", [128, Fmax], U8)),
        "dv": p.sem(f"q_dv{tag}"),
        "gp": p.sem(f"q_gp{tag}"),
        "ac": p.sem(f"q_ac{tag}"),
        "Fmax": Fmax,
    }
    return qs


def emit_quant(p, qs, x_ap, out_ap, F, in_ready, out_waits=()):
    nb = F // 16
    dv, gp, ac = qs["dv"], qs["gp"], qs["ac"]
    red = qs["red"].ap()[:, 0:nb]
    inv = qs["inv"].ap()[:, 0:nb]
    u = qs["u"].ap()[:, 0:F]
    sgn = qs["sgn"].ap()[:, 0:F]
    r1 = qs["r1"].ap()[:, 0:F]
    r2 = qs["r2"].ap()[:, 0:F]
    m1 = qs["m1"].ap()[:, 0:F]
    m2 = qs["m2"].ap()[:, 0:F]
    x3 = x_ap.rearrange("p (b k) -> p b k", k=16)

    v0 = p.comp("vector", lambda v: v.reduce_max(red.unsqueeze(2), x3, mybir.AxisListType.X,
                                                 apply_absolute_value=True),
                waits=in_ready, inc=dv)
    v1 = p.comp("vector", lambda v: v.tensor_scalar(red, red, 1.0 / 6.0, 1e-12, OP.mult, OP.max),
                waits=[(dv, v0)], inc=dv)
    dv_inv = p.comp("vector", lambda v: v.reciprocal(inv, red), waits=[(dv, v1)], inc=dv)
    dv_xn = p.comp("vector", lambda v: v.tensor_tensor(
        x3, x3, inv.unsqueeze(2).broadcast_to([128, nb, 16]), OP.mult),
        waits=[(dv, dv_inv)], inc=dv)
    ac_u = p.comp("scalar", lambda s: s.activation(u, x_ap, AF.Abs), waits=[(dv, dv_xn)], inc=ac)
    ac_s = p.comp("scalar", lambda s: s.activation(sgn, x_ap, AF.Sign), inc=ac)
    p.comp("gpsimd", lambda g: g.tensor_scalar(r1, x_ap, 2.0, _RNE_C, OP.mult, OP.add),
           waits=[(dv, dv_xn)])
    p.comp("gpsimd", lambda g: g.tensor_scalar(r1, r1, _RNE_C, 4.0, OP.subtract, OP.min))
    gp_r1 = p.comp("gpsimd", lambda g: g.tensor_scalar(r1, r1, -4.0, 0.5, OP.max, OP.mult), inc=gp)
    p.comp("gpsimd", lambda g: g.tensor_scalar(r2, x_ap, _RNE_C, None, OP.add))
    p.comp("gpsimd", lambda g: g.tensor_scalar(r2, r2, _RNE_C, 4.0, OP.subtract, OP.min))
    gp_r2 = p.comp("gpsimd", lambda g: g.tensor_scalar(r2, r2, -4.0, None, OP.max), inc=gp)
    p.comp("vector", lambda v: v.tensor_scalar(m1, u, 2.5, None, OP.is_gt), waits=[(ac, ac_u)])
    p.comp("vector", lambda v: v.tensor_scalar(m2, u, 5.0, None, OP.is_gt))
    p.comp("vector", lambda v: v.select(r1, m1, r2, r1), waits=[(gp, gp_r2)])
    p.comp("vector", lambda v: v.tensor_scalar(sgn, sgn, 6.0, None, OP.mult),
           waits=[(ac, ac_s)])
    p.comp("vector", lambda v: v.select(r1, m2, sgn, r1))
    last = p.comp("vector", lambda v: v.tensor_tensor(
        out_ap.rearrange("p (b k) -> p b k", k=16),
        r1.rearrange("p (b k) -> p b k", k=16),
        red.unsqueeze(2).broadcast_to([128, nb, 16]), OP.mult),
        waits=list(out_waits), inc=dv)
    return dv, last


def emit_l1wprep(p, nc, es, consts, jobs, tag):
    maxK = max(src.shape[1] for src, _ in jobs)
    maxN = max(src.shape[0] for src, _ in jobs)
    s_row = es.enter_context(nc.sbuf_tensor(f"l1p_row{tag}", [128, maxK], F32))
    s_wq = es.enter_context(nc.sbuf_tensor(f"l1p_wq{tag}", [128, (maxN // 128) * maxK], BF16))
    s_strip = es.enter_context(nc.sbuf_tensor(f"l1p_strip{tag}", [128, maxN], BF16))
    ps = [es.enter_context(nc.psum_tensor(f"l1p_ps{tag}{i}", [128, 128], BF16)) for i in range(2)]
    qs = make_quant_scratch(p, nc, es, maxK, "l1p" + tag)
    s_ld = p.sem(f"l1p_ld{tag}")
    s_mm = p.sem(f"l1p_mm{tag}")
    s_ev = p.sem(f"l1p_ev{tag}")
    s_st = p.sem(f"l1p_st{tag}")

    for src, dst in jobs:
        N, K = src.shape
        nb = N // 128
        wq3 = s_wq.ap()[:, 0:nb * K].rearrange("p (b k) -> p b k", k=K)
        for b in range(nb):
            lv = p.dma("gpsimd", lambda g, src=src, b=b, K=K: g.dma_start(
                out=s_row[:, 0:K], in_=src.ap()[b * 128:(b + 1) * 128, :]),
                waits=[(qs["dv"], qs["dv"].count)], inc=s_ld)
            emit_quant(p, qs, s_row.ap()[:, 0:K], wq3[:, b, :], K,
                       [(s_ld, lv)])
        qdone = (qs["dv"], qs["dv"].count)
        k = 0
        for ic in range(K // 128):
            st_guard = s_st.count
            for b in range(nb):
                buf = k % 2
                w = [qdone] if k == 0 else [(s_mm, s_mm.count)]
                mv = p.comp("tensor", lambda t, buf=buf, b=b, ic=ic, wq3=wq3: t.transpose(
                    ps[buf][:, :], wq3[:, b, ic * 128:(ic + 1) * 128], consts.ident_bf[:, :]),
                    waits=w + ([(s_ev, s_ev.count - 1)] if k >= 2 else []), inc=s_mm)
                ev_w = [(s_mm, mv)]
                if b == 0:
                    ev_w.append((s_st, st_guard))
                p.comp("vector", lambda v, buf=buf, b=b: v.tensor_copy(
                    s_strip[:, b * 128:(b + 1) * 128], ps[buf][:, :]),
                    waits=ev_w, inc=s_ev)
                k += 1
            p.dma("sync", lambda sy, dst=dst, ic=ic, N=N: sy.dma_start(
                out=dst.ap()[ic * 128:(ic + 1) * 128, :], in_=s_strip[:, 0:N]),
                waits=[(s_ev, s_ev.count)], inc=s_st)
    p.wait("sync", [(s_st, s_st.count)])
    return s_st


def emit_l1ffn(p, nc, es, consts, x_in, y_out, W, tag):
    NI = 2816
    NT = NTOK // 128
    FQ = 704

    s_wg = es.enter_context(nc.sbuf_tensor(f"l1_wg{tag}", [128, 8, NI], BF16))
    s_wu = es.enter_context(nc.sbuf_tensor(f"l1_wu{tag}", [128, 8, NI], BF16))
    s_wd = es.enter_context(nc.sbuf_tensor(f"l1_wd{tag}", [128, 22, D], BF16))
    s_xf = [es.enter_context(nc.sbuf_tensor(f"l1_xf{tag}{i}", [128, 8, 128], F32)) for i in range(2)]
    s_xtm = [es.enter_context(nc.sbuf_tensor(f"l1_xtm{tag}{i}", [128, D], F32)) for i in range(2)]
    s_xq = es.enter_context(nc.sbuf_tensor(f"l1_xq{tag}", [128, D], BF16))
    s_xqf = es.enter_context(nc.sbuf_tensor(f"l1_xqf{tag}", [128, 8, 128], BF16))
    s_htm = es.enter_context(nc.sbuf_tensor(f"l1_htm{tag}", [128, NI], F32))
    s_hq = es.enter_context(nc.sbuf_tensor(f"l1_hq{tag}", [128, NI], BF16))
    s_hqf = es.enter_context(nc.sbuf_tensor(f"l1_hqf{tag}", [128, 22, 128], BF16))
    s_ytm = es.enter_context(nc.sbuf_tensor(f"l1_ytm{tag}", [128, D], F32))
    s_g = es.enter_context(nc.sbuf_tensor(f"l1_g{tag}", [128, 512], F32))
    s_col = [es.enter_context(nc.sbuf_tensor(f"l1_col{i}{tag}", [128, 1], F32)) for i in range(3)]
    s_sq = es.enter_context(nc.sbuf_tensor(f"l1_sq{tag}", [128, D], F32))
    ident_f = es.enter_context(nc.sbuf_tensor(f"l1_idf{tag}", [128, 128], F32))
    qs = make_quant_scratch(p, nc, es, FQ, "l1" + tag)
    ps_t = [es.enter_context(nc.psum_tensor(f"l1_pst{tag}{i}", [128, 128], F32)) for i in range(2)]
    ps_tb = [es.enter_context(nc.psum_tensor(f"l1_ptb{tag}{i}", [128, 128], BF16)) for i in range(2)]
    ps_g = es.enter_context(nc.psum_tensor(f"l1_psg{tag}", [128, 512], F32))
    ps_u = es.enter_context(nc.psum_tensor(f"l1_psu{tag}", [128, 512], F32))

    s_ld = p.sem(f"l1_ld{tag}")
    s_mm = p.sem(f"l1_mm{tag}")
    s_ev = p.sem(f"l1_ev{tag}")
    s_dv = p.sem(f"l1_dv{tag}")
    s_ac = p.sem(f"l1_ac{tag}")
    s_st = p.sem(f"l1_st{tag}")

    idf_sem = p.sem(f"l1_idf_s{tag}")
    idf_done = p.comp("gpsimd", lambda g: g.tensor_copy(ident_f[:, :], consts.ident_bf[:, :]),
                      waits=[(consts.sem, consts.ready)], inc=idf_sem)

    wld_done = None
    for dst, key in ((s_wg, "wgT"), (s_wu, "wuT"), (s_wd, "wdT")):
        nch = dst.shape[1]
        wld_done = p.dma("gpsimd", lambda g, dst=dst, key=key, nch=nch: g.dma_start(
            out=dst[:, :, :],
            in_=bass.AP(W[key], 0, [[W[key].shape[1], 128],
                                    [128 * W[key].shape[1], nch],
                                    [1, W[key].shape[1]]])), inc=s_ld)

    def transpose_set(n, src_ap_fn, dst_ap_fn, psl, ident, waits):
        evs = []
        ev_entry = s_ev.count
        for c in range(n):
            buf = c % 2
            w = list(waits) if c == 0 else []
            w.append((s_ev, evs[c - 2]) if c >= 2 else (s_ev, ev_entry))
            mv = p.comp("tensor", lambda t, c=c, buf=buf: t.transpose(
                psl[buf][:, :], src_ap_fn(c), ident[:, :]), waits=w, inc=s_mm)
            ev = p.comp("vector",
                        lambda v, c=c, buf=buf: v.tensor_copy(dst_ap_fn(c), psl[buf][:, :]),
                        waits=[(s_mm, mv)], inc=s_ev)
            evs.append(ev)
        return evs[-1]

    prev = {}
    pv = lambda key: [prev[key]] if key in prev else []
    for ti in range(NT):
        par = ti % 2
        xf, xtm = s_xf[par], s_xtm[par]

        lw = []
        if ti >= 2:
            lw.append(prev[("xf_free", par)])
        xv = p.dma("gpsimd", lambda g, ti=ti, xf=xf: g.dma_start(
            out=xf[:, :, :],
            in_=bass.AP(x_in, ti * 128, [[NTOK, 128], [128 * NTOK, 8], [1, 128]])),
            waits=lw, inc=s_ld)

        tw = [(s_ld, xv), (idf_sem, idf_done)]
        if ti >= 2:
            tw.append(prev[("xtm_free", par)])
        tv = transpose_set(8, lambda c, xf=xf: xf[:, c, :],
                           lambda c, xtm=xtm: xtm[:, c * 128:(c + 1) * 128],
                           ps_t, ident_f, tw)
        prev[("xf_free", par)] = (s_mm, s_mm.count)

        lnw = [(s_ev, tv)] + pv("col_free") + pv("sq_free")
        a1 = p.comp("scalar", lambda s, xtm=xtm: s.activation(
            s_sq[:, :], xtm[:, :], AF.Copy, accum_out=s_col[0][:, :]),
            waits=lnw, inc=s_ac)
        a2 = p.comp("scalar", lambda s, xtm=xtm: s.activation(
            s_sq[:, :], xtm[:, :], AF.Square, accum_out=s_col[1][:, :]), inc=s_ac)
        c0 = p.comp("vector", lambda v: v.tensor_scalar(s_col[0][:, :], s_col[0][:, :], 1.0 / D, None, OP.mult),
                    waits=[(s_ac, a2)], inc=s_dv)
        c1 = p.comp("vector", lambda v: v.tensor_scalar(s_col[1][:, :], s_col[1][:, :], 1.0 / D, None, OP.mult),
                    waits=[(s_dv, c0)], inc=s_dv)
        c2 = p.comp("vector", lambda v: v.tensor_tensor(s_col[2][:, :], s_col[0][:, :], s_col[0][:, :], OP.mult),
                    waits=[(s_dv, c1)], inc=s_dv)
        d1 = p.comp("vector", lambda v: v.scalar_tensor_tensor(
            s_col[1][:, :], s_col[1][:, :], 1e-5, s_col[2][:, :], OP.add, OP.subtract),
            waits=[(s_dv, c2)], inc=s_dv)
        l1 = p.comp("scalar", lambda s: s.activation(s_col[1][:, :], s_col[1][:, :], AF.Ln),
                    waits=[(s_dv, d1)], inc=s_ac)
        a3 = p.comp("scalar", lambda s: s.activation(s_col[1][:, :], s_col[1][:, :], AF.Exp, scale=-0.5),
                    waits=[(s_ac, l1)], inc=s_ac)
        d2 = p.comp("vector", lambda v, xtm=xtm: v.tensor_scalar(
            s_sq[:, :], xtm[:, :], s_col[0][:, :], s_col[1][:, :], OP.subtract, OP.mult),
            waits=[(s_ac, a3)], inc=s_dv)
        prev["col_free"] = (s_dv, d2)

        qv_pair = (s_dv, d2)
        for h in range(2):
            qsem, qv = emit_quant(p, qs, s_sq.ap()[:, h * 512:(h + 1) * 512],
                                  s_xq.ap()[:, h * 512:(h + 1) * 512], 512,
                                  [qv_pair], out_waits=pv(("xq_free",)) if h == 0 else ())
            qv_pair = (qsem, qv)
        prev["sq_free"] = qv_pair

        tw = [qv_pair, (consts.sem, consts.ready)] + pv(("xqf_free",))
        transpose_set(8, lambda c: s_xq[:, c * 128:(c + 1) * 128],
                      lambda c: s_xqf[:, c, :],
                      ps_tb, consts.ident_bf, tw)
        prev[("xq_free",)] = (s_mm, s_mm.count)
        xq_t_done = (s_ev, s_ev.count)

        mw = [xq_t_done, (s_ld, wld_done)] + pv("gu_psum_free")
        nfull = NI // 512
        chunks = [(i * 512, 512) for i in range(nfull)] + ([(nfull * 512, NI - nfull * 512)] if NI % 512 else [])
        ev = hv = None
        htm_guard = pv("htm_free")
        for (n0, nw) in chunks:
            gw = list(mw) if n0 == 0 else [(s_ev, ev)]
            for kc in range(8):
                p.comp("tensor", lambda t, kc=kc, n0=n0, nw=nw: t.matmul(
                    ps_g[:, 0:nw], s_xqf[:, kc, :], s_wg[:, kc, n0:n0 + nw],
                    start=(kc == 0), stop=(kc == 7), skip_group_check=True),
                    waits=gw if kc == 0 else [])
            uw = [] if n0 == 0 else [(s_dv, hv)]
            for kc in range(8):
                p.comp("tensor", lambda t, kc=kc, n0=n0, nw=nw: t.matmul(
                    ps_u[:, 0:nw], s_xqf[:, kc, :], s_wu[:, kc, n0:n0 + nw],
                    start=(kc == 0), stop=(kc == 7), skip_group_check=True),
                    waits=uw if kc == 0 else [], inc=(s_mm if kc == 7 else None))
            mmv = s_mm.count
            sw = [(s_mm, mmv)]
            if hv is not None:
                sw.append((s_dv, hv))
            ev = p.comp("scalar", lambda s, n0=n0, nw=nw: s.activation(
                s_g[:, 0:nw], ps_g[:, 0:nw], AF.Silu), waits=sw, inc=s_ev)
            hvw = [(s_ev, ev)] + (htm_guard if n0 == 0 else [])
            hv = p.comp("vector", lambda v, n0=n0, nw=nw: v.tensor_tensor(
                s_htm[:, n0:n0 + nw], s_g[:, 0:nw], ps_u[:, 0:nw], OP.mult),
                waits=hvw, inc=s_dv)
        h_done = (s_dv, s_dv.count)
        prev["gu_psum_free"] = h_done

        qv_pair = h_done
        for hqi in range(4):
            qsem, qv = emit_quant(p, qs, s_htm.ap()[:, hqi * FQ:(hqi + 1) * FQ],
                                  s_hq.ap()[:, hqi * FQ:(hqi + 1) * FQ], FQ,
                                  [qv_pair], out_waits=pv(("hq_free",)) if hqi == 0 else ())
            qv_pair = (qsem, qv)
        prev["htm_free"] = (qs["gp"], qs["gp"].count)

        tw = [qv_pair] + pv(("hqf_free",))
        transpose_set(22, lambda c: s_hq[:, c * 128:(c + 1) * 128],
                      lambda c: s_hqf[:, c, :],
                      ps_tb, consts.ident_bf, tw)
        prev[("hq_free",)] = (s_mm, s_mm.count)
        hq_t_done = (s_ev, s_ev.count)

        rv = None
        for half in range(2):
            n0 = half * 512
            dw = ([hq_t_done] + pv("down_psum_free")) if half == 0 else [(s_dv, rv)]
            for kc in range(22):
                p.comp("tensor", lambda t, kc=kc, n0=n0: t.matmul(
                    ps_g[:, :], s_hqf[:, kc, :], s_wd[:, kc, n0:n0 + 512],
                    start=(kc == 0), stop=(kc == 21), skip_group_check=True),
                    waits=dw if kc == 0 else [],
                    inc=(s_mm if kc == 21 else None))
            mmv = s_mm.count
            rw = [(s_mm, mmv)] + (pv("ytm_free") if half == 0 else [])
            rv = p.comp("vector", lambda v, n0=n0, xtm=xtm: v.scalar_tensor_tensor(
                s_ytm[:, n0:n0 + 512], ps_g[:, :], 0.5, xtm[:, n0:n0 + 512], OP.mult, OP.add),
                waits=rw, inc=s_dv)
        y_done = (s_dv, s_dv.count)
        prev[("xtm_free", par)] = y_done
        prev["down_psum_free"] = y_done
        prev["gu_psum_free"] = y_done
        prev[("hqf_free",)] = (s_mm, s_mm.count)

        tv = transpose_set(8, lambda c: s_ytm[:, c * 128:(c + 1) * 128],
                           lambda c, xf=xf: xf[:, c, :],
                           ps_t, ident_f, [y_done])
        prev["ytm_free"] = (s_mm, s_mm.count)
        stv = p.dma("gpsimd", lambda g, ti=ti, xf=xf: g.dma_start(
            out=bass.AP(y_out, ti * 128, [[NTOK, 128], [128 * NTOK, 8], [1, 128]]),
            in_=xf[:, :, :]),
            waits=[(s_ev, tv)], inc=s_st)
        prev[("xf_free", par)] = (s_st, stv)

    p.wait("gpsimd", [(s_st, s_st.count)])
    return s_st



def emit_l1attn(p, nc, es, consts, x_in, y_out, W, scr, tag):
    SW = 2176
    Tg = 512
    NG = NTOK // Tg
    STRIDE_QE = 128 * SW

    s_wq = es.enter_context(nc.sbuf_tensor(f"la_wq{tag}", [128, 8, D], BF16))
    s_wk = es.enter_context(nc.sbuf_tensor(f"la_wk{tag}", [128, 8, D], BF16))
    s_wv = es.enter_context(nc.sbuf_tensor(f"la_wv{tag}", [128, 8, D], BF16))
    s_wo = es.enter_context(nc.sbuf_tensor(f"la_wo{tag}", [128, 8, D], BF16))
    s_bcols = {}
    for key in ("bq", "bk", "bv", "bo", "gam", "bet"):
        s_bcols[key] = es.enter_context(nc.sbuf_tensor(f"la_{key}{tag}", [128, 8], F32))
    s_drev = es.enter_context(nc.sbuf_tensor(f"la_drev{tag}", [64, 2047], BF16))
    s_dnat = es.enter_context(nc.sbuf_tensor(f"la_dnat{tag}", [64, 2047], BF16))
    s_scal = es.enter_context(nc.sbuf_tensor(f"la_sc{tag}", [1, 8], F32))

    s_ld = p.sem(f"la_ld{tag}")
    s_mm = p.sem(f"la_mm{tag}")
    s_ev = p.sem(f"la_ev{tag}")
    s_dv = p.sem(f"la_dv{tag}")
    s_gp = p.sem(f"la_gp{tag}")
    s_st = p.sem(f"la_st{tag}")

    for dst, key in ((s_wq, "Wq"), (s_wk, "Wk"), (s_wv, "Wv"), (s_wo, "Wo")):
        p.dma("gpsimd", lambda g, dst=dst, key=key: g.dma_start(
            out=dst[:, :, :],
            in_=bass.AP(W[key], 0, [[D, 128], [128 * D, 8], [1, D]])), inc=s_ld)
    for key in ("bq", "bk", "bv", "bo", "gam", "bet"):
        p.dma("gpsimd", lambda g, key=key: g.dma_start(
            out=s_bcols[key][:, :], in_=W[key].ap()), inc=s_ld)
    wld_done = s_ld.count

    with contextlib.ExitStack() as esp:
        s_dtile = esp.enter_context(nc.sbuf_tensor(f"la_dt{tag}", [128, 64], BF16))
        s_dsq = esp.enter_context(nc.sbuf_tensor(f"la_dsq{tag}", [128, 64], F32))
        s_dnorm = esp.enter_context(nc.sbuf_tensor(f"la_dn{tag}", [128, 16], F32))
        ps_dt = esp.enter_context(nc.psum_tensor(f"la_psdt{tag}", [64, 128], F32))
        ps_dn = esp.enter_context(nc.psum_tensor(f"la_psdn{tag}", [1, 16], F32))
        zs = p.sem(f"la_dz{tag}")
        p.comp("gpsimd", lambda g: g.memset(s_dnorm[:, :], 0.0))
        zv = p.comp("gpsimd", lambda g: g.tensor_copy(s_dnorm[:, 0:1], s_dnorm[:, 0:1]),
                    inc=zs)
        acc_prev = None
        for tile in range(16):
            n = 128 if tile < 15 else 127
            srcap = bass.AP(W["dist"], (128 * tile) * 64, [[64, n], [1, 64]])
            lv = p.dma("gpsimd", lambda g, srcap=srcap, n=n: g.dma_start(
                out=s_dtile[0:n, :], in_=srcap),
                waits=[(s_mm, s_mm.count), (s_ev, s_ev.count)], inc=s_ld)
            mv = p.comp("tensor", lambda t, n=n: t.matmul(
                ps_dt[:, 0:n], s_dtile[0:n, 0:64], consts.ident_bf[0:n, 0:n],
                start=True, stop=True, skip_group_check=True),
                waits=[(s_ld, lv), (consts.sem, consts.ready), (s_ev, s_ev.count)],
                inc=s_mm)
            p.comp("scalar", lambda s, tile=tile, n=n: s.activation(
                s_dnat[0:64, 128 * tile:128 * tile + n], ps_dt[0:64, 0:n], AF.Copy),
                waits=[(s_mm, mv)], inc=s_ev)
            anti = consts.anti128 if n == 128 else consts.anti127
            cc0 = 1919 - 128 * tile if n == 128 else 0
            mv = p.comp("tensor", lambda t, n=n, anti=anti: t.matmul(
                ps_dt[:, 0:n], s_dtile[0:n, 0:64], anti[0:n, 0:n],
                start=True, stop=True, skip_group_check=True), inc=s_mm)
            p.comp("scalar", lambda s, cc0=cc0, n=n: s.activation(
                s_drev[0:64, cc0:cc0 + n], ps_dt[0:64, 0:n], AF.Copy),
                waits=[(s_mm, mv)], inc=s_ev)
            w = [(s_ld, lv), (zs, zv)]
            if acc_prev is not None:
                w.append((s_ev, acc_prev))
            acc_prev = p.comp("scalar", lambda s, tile=tile, n=n: s.activation(
                s_dsq[0:n, :], s_dtile[0:n, :], AF.Square,
                accum_out=s_dnorm[0:n, tile:tile + 1]), waits=w, inc=s_ev)
        mv = p.comp("tensor", lambda t: t.matmul(ps_dn[:1, :], consts.ones_col[:, :],
                                                 s_dnorm[:, :], start=True, stop=True),
                    waits=[(s_ev, s_ev.count)], inc=s_mm)
        v0 = p.comp("vector", lambda v: v.reduce_sum(
            s_scal.ap()[0:1, 1:2].unsqueeze(2),
            ps_dn.ap()[0:1, 0:16].unsqueeze(1),
            mybir.AxisListType.X), waits=[(s_mm, mv)], inc=s_dv)
        av = p.comp("scalar", lambda s: s.activation(s_scal[0:1, 1:2], s_scal[0:1, 1:2],
                                                     AF.Sqrt), waits=[(s_dv, v0)], inc=s_ev)
        v0 = p.comp("vector", lambda v: v.tensor_scalar(s_scal[0:1, 1:2], s_scal[0:1, 1:2],
                                                        1.01, None, OP.mult),
                    waits=[(s_ev, av)], inc=s_dv)
        dmax_ready = (s_dv, v0)
    p.barrier(f"la_bar_prep{tag}")

    with contextlib.ExitStack() as esA:
        s_x = esA.enter_context(nc.sbuf_tensor(f"la_x{tag}", [128, 8, Tg], F32))
        s_xs = esA.enter_context(nc.sbuf_tensor(f"la_xsb{tag}", [128, 8, Tg], F32))
        s_xn = esA.enter_context(nc.sbuf_tensor(f"la_xn{tag}", [128, 8, Tg], BF16))
        s_qkv = esA.enter_context(nc.sbuf_tensor(f"la_qkv{tag}", [128, 8, Tg], BF16))
        s_vst = esA.enter_context(nc.sbuf_tensor(f"la_vst{tag}", [128, 4, D], BF16))
        ln = make_ln_ctx(p, nc, esA, consts, Tg, "la" + tag)
        ps_p = [esA.enter_context(nc.psum_tensor(f"la_pp{tag}{i}", [128, Tg], F32)) for i in range(2)]
        x3, xs3, xn3, qkv3 = s_x.ap(), s_xs.ap(), s_xn.ap(), s_qkv.ap()
        bankA_free = {}
        ln_last = None
        cast_last = None
        qkv_free = None
        vst_free = None
        for g in range(NG):
            xw = [] if ln_last is None else [(ln["sem_dv"], ln_last)]
            xv = p.dma("sync", lambda sy, g=g: sy.dma_start(
                out=s_x[:, :, :],
                in_=bass.AP(x_in, g * Tg, [[NTOK, 128], [128 * NTOK, 8], [1, Tg]])),
                waits=xw, inc=s_ld)
            wr = []
            if g > 0:
                wr = [(s_st, s_st.count), (s_gp, cast_last)]
            lnsem, lnval = emit_layernorm(p, ln, x3, xs3, [(s_ld, xv)], write_waits=wr)
            ln_last = lnval
            p.dma("sync", lambda sy, g=g: sy.dma_start(
                out=bass.AP(scr["xs"], g * Tg, [[NTOK, 128], [128 * NTOK, 8], [1, Tg]]),
                in_=s_xs[:, :, :]), waits=[(lnsem, lnval)], inc=s_st)
            cw = [(lnsem, lnval)]
            if g > 0:
                cw.append((s_mm, s_mm.count))
            cast_last = p.comp("gpsimd", lambda gg: gg.tensor_copy(s_xn[:, :, :], s_xs[:, :, :]),
                               waits=cw, inc=s_gp)
            for proj, wsb, bkey, dstd in (("q", s_wq, "bq", scr["qf"]), ("k", s_wk, "bk", scr["kf"])):
                for ot in range(8):
                    bi = ot % 2
                    for ic in range(8):
                        w = []
                        if ic == 0:
                            w = [(s_gp, cast_last), (s_ld, wld_done)]
                            if bi in bankA_free:
                                w.append(bankA_free[bi])
                        p.comp("tensor", lambda t, ic=ic, ot=ot, wsb=wsb, bi=bi: t.matmul(
                            ps_p[bi][:, :], wsb[:, ic, ot * 128:(ot + 1) * 128], xn3[:, ic, :],
                            start=(ic == 0), stop=(ic == 7), skip_group_check=True),
                            waits=w, inc=(s_mm if ic == 7 else None))
                    mmv = s_mm.count
                    ew = [(s_mm, mmv)]
                    if ot == 0 and qkv_free is not None:
                        ew.append(qkv_free)
                    e = p.comp("scalar", lambda s, ot=ot, bkey=bkey, bi=bi: s.activation(
                        qkv3[:, ot, :], ps_p[bi][:, :], AF.Identity,
                        bias=s_bcols[bkey][:, ot:ot + 1]),
                        waits=ew, inc=s_ev)
                    bankA_free[bi] = (s_ev, e)
                sv = p.dma("sync", lambda sy, g=g, dstd=dstd: sy.dma_start(
                    out=bass.AP(dstd, g * Tg, [[NTOK, 128], [128 * NTOK, 8], [1, Tg]]),
                    in_=s_qkv[:, :, :]), waits=[(s_ev, s_ev.count)], inc=s_st)
                qkv_free = (s_st, sv)
            for tt in range(Tg // 128):
                for half in range(2):
                    bi = half
                    for ic in range(8):
                        w = []
                        if ic == 0:
                            w = [(s_gp, cast_last)]
                            if bi in bankA_free:
                                w.append(bankA_free[bi])
                        p.comp("tensor", lambda t, ic=ic, tt=tt, half=half, bi=bi: t.matmul(
                            ps_p[bi][:, 0:512], xn3[:, ic, tt * 128:(tt + 1) * 128],
                            s_wv[:, ic, half * 512:(half + 1) * 512],
                            start=(ic == 0), stop=(ic == 7), skip_group_check=True),
                            waits=w, inc=(s_mm if ic == 7 else None))
                    mmv = s_mm.count
                    ew = [(s_mm, mmv)]
                    if tt == 0 and half == 0 and vst_free is not None:
                        ew.append(vst_free)
                    e = p.comp("scalar", lambda s, tt=tt, half=half, bi=bi: s.activation(
                        s_vst[:, tt, half * 512:(half + 1) * 512], ps_p[bi][:, 0:512],
                        AF.Copy), waits=ew, inc=s_ev)
                    bankA_free[bi] = (s_ev, e)
            sv = p.dma("sync", lambda sy, g=g: sy.dma_start(
                out=bass.AP(scr["vt"], (g * Tg) * D, [[D, 128], [128 * D, 4], [1, D]]),
                in_=s_vst[:, :, :]), waits=[(s_ev, s_ev.count)], inc=s_st)
            vst_free = (s_st, sv)
        phaseA_st = s_st.count
        phaseA_mm = s_mm.count
    p.barrier(f"la_bar_a{tag}")

    with contextlib.ExitStack() as esB:
        s_qh = esB.enter_context(nc.sbuf_tensor(f"la_qh{tag}", [64, 2048], BF16))
        s_kh = esB.enter_context(nc.sbuf_tensor(f"la_kh{tag}", [64, 2048], BF16))
        s_vh = esB.enter_context(nc.sbuf_tensor(f"la_vh{tag}", [128, 16, 64], BF16))
        s_rkT = esB.enter_context(nc.sbuf_tensor(f"la_rkT{tag}", [128, 16, 2048], BF16))
        s_asm = [esB.enter_context(nc.sbuf_tensor(f"la_asm{tag}{i}", [128, SW], BF16)) for i in range(2)]
        s_rq = esB.enter_context(nc.sbuf_tensor(f"la_rq{tag}", [128, 2048], BF16))
        s_p = esB.enter_context(nc.sbuf_tensor(f"la_p{tag}", [128, 2048], BF16))
        s_pT = esB.enter_context(nc.sbuf_tensor(f"la_pT{tag}", [128, 2048], BF16))
        s_ctx = esB.enter_context(nc.sbuf_tensor(f"la_ctx{tag}", [128, 64], BF16))
        s_cstrip = esB.enter_context(nc.sbuf_tensor(f"la_cst{tag}", [64, 2048], BF16))
        s_qn = esB.enter_context(nc.sbuf_tensor(f"la_qn{tag}", [1, 2048], F32))
        s_ksq = esB.enter_context(nc.sbuf_tensor(f"la_ksq{tag}", [64, 2048], F32))
        s_bias = esB.enter_context(nc.sbuf_tensor(f"la_bias{tag}", [128, 16], F32))
        s_sum = esB.enter_context(nc.sbuf_tensor(f"la_sum{tag}", [128, 1], F32))
        s_rcp = esB.enter_context(nc.sbuf_tensor(f"la_rcp{tag}", [128, 1], F32))
        ps_big = esB.enter_context(nc.psum_tensor(f"la_psb{tag}", [128, 2048], F32))
        ps_pt = [esB.enter_context(nc.psum_tensor(f"la_pspt{tag}{i}", [128, 1024], BF16)) for i in range(2)]
        ps_ctx = esB.enter_context(nc.psum_tensor(f"la_psctx{tag}", [128, 64], F32))
        ps_ctxT = esB.enter_context(nc.psum_tensor(f"la_psct{tag}", [64, 128], BF16))

        CH = [(0, 512), (512, 512), (1024, 512), (1536, 511)]
        SC = [(0, 512), (512, 512), (1024, 512), (1536, 512)]

        def strips(side, src_sb, dsb, dram_dst, first_waits):
            last_store = None
            for s16 in range(16):
                l0 = 128 * s16
                wlo = 1920 - l0
                par = s16 % 2
                for ci, (c0, cw_) in enumerate(CH):
                    w = list(first_waits) if (s16 == 0 and ci == 0) else []
                    if ci == 0 and (s16 > 0 or side == "k"):
                        w.append((s_ev, s_ev.count))
                    p.comp("tensor", lambda t, s16=s16, c0=c0, cw_=cw_, src_sb=src_sb, dsb=dsb: t.matmul(
                        ps_big[:, c0:c0 + cw_], src_sb[0:64, 128 * s16:128 * s16 + 128],
                        dsb[0:64, c0:c0 + cw_], start=True, stop=True,
                        skip_group_check=True),
                        waits=w, inc=(s_mm if ci == 3 else None))
                mmv = s_mm.count
                whi = wlo + SW - 1
                LA = max(0, 1024 - wlo)
                mB0 = max(wlo, 1024)
                LB = min(3070, whi) - mB0 + 1
                LC = max(0, whi - 3070)
                aB0 = LA
                aC0 = LA + LB
                ew = [(s_mm, mmv)]
                if last_store is not None and s16 >= 2:
                    ew.append(last_store[s16 - 2])
                if side == "q":
                    cB0 = mB0 - 1024
                    e1 = p.comp("scalar", lambda s, par=par, aB0=aB0, LB=LB, cB0=cB0: s.activation(
                        s_asm[par][:, aB0:aB0 + LB], ps_big[:, cB0:cB0 + LB], AF.Copy),
                        waits=ew, inc=s_ev)
                    last_ev = e1
                    if LC:
                        last_ev = p.comp("scalar", lambda s, par=par, aC0=aC0, LC=LC: s.activation(
                            s_asm[par][:, aC0:aC0 + LC], ps_big[:, 0:LC], AF.Copy), inc=s_ev)
                    if LA:
                        last_ev = p.comp("vector", lambda v, par=par, LA=LA, aB0=aB0: v.tensor_copy(
                            s_asm[par][:, 0:LA],
                            s_asm[par].ap()[:, aB0:aB0 + 1].broadcast_to([128, LA])),
                            waits=[(s_ev, last_ev)], inc=s_ev)
                else:
                    cB0 = mB0 - 1024
                    e1 = p.comp("scalar", lambda s, par=par, aB0=aB0, LB=LB, cB0=cB0: s.activation(
                        s_asm[par][:, aB0:aB0 + LB], ps_big[:, cB0:cB0 + LB], AF.Copy),
                        waits=ew, inc=s_ev)
                    last_ev = e1
                    if LA:
                        last_ev = p.comp("scalar", lambda s, par=par, LA=LA, wlo=wlo: s.activation(
                            s_asm[par][:, 0:LA], ps_big[:, wlo + 1023:wlo + 1023 + LA], AF.Copy),
                            inc=s_ev)
                    if LC:
                        last_ev = p.comp("vector", lambda v, par=par, aC0=aC0, LC=LC: v.tensor_copy(
                            s_asm[par][:, aC0:aC0 + LC],
                            s_asm[par].ap()[:, aC0 - 1:aC0].broadcast_to([128, LC])),
                            waits=[(s_ev, last_ev)], inc=s_ev)
                sv = p.dma("sync", lambda sy, s16=s16, par=par, dram_dst=dram_dst: sy.dma_start(
                    out=bass.AP(dram_dst, s16 * STRIDE_QE, [[SW, 128], [1, SW]]),
                    in_=s_asm[par][:, :]),
                    waits=[(s_ev, last_ev)], inc=s_st)
                if last_store is None:
                    last_store = {}
                last_store[s16] = (s_st, sv)
            return s_st.count

        bh_prev_done = None
        for b in range(SEQ_PER_CORE):
            for h in range(16):
                lw = [(s_st, phaseA_st)]
                if bh_prev_done is not None:
                    lw.append(bh_prev_done)
                qv = p.dma("gpsimd", lambda g, b=b, h=h: g.dma_start(
                    out=s_qh[:, :],
                    in_=bass.AP(scr["qf"], (64 * h) * NTOK + 2048 * b, [[NTOK, 64], [1, 2048]])),
                    waits=lw, inc=s_ld)
                kv = p.dma("gpsimd", lambda g, b=b, h=h: g.dma_start(
                    out=s_kh[:, :],
                    in_=bass.AP(scr["kf"], (64 * h) * NTOK + 2048 * b, [[NTOK, 64], [1, 2048]])),
                    inc=s_ld)
                vv = p.dma("gpsimd", lambda g, b=b, h=h: g.dma_start(
                    out=s_vh[:, :, :],
                    in_=bass.AP(scr["vt"], (2048 * b) * D + 64 * h, [[D, 128], [128 * D, 16], [1, 64]])),
                    inc=s_ld)
                qdone = strips("q", s_qh, s_drev, scr["qe"], [(s_ld, qv)])
                kdone = strips("k", s_kh, s_dnat, scr["ke"], [(s_ld, kv)])
                rkT_lv = None
                for s16 in range(16):
                    rkT_lv = p.dma("gpsimd", lambda g, s16=s16: g.dma_start(
                        out=s_rkT[:, s16, :],
                        in_=bass.AP(scr["ke"], s16 * STRIDE_QE + 127, [[SW - 1, 128], [1, 2048]])),
                        waits=[(s_st, kdone)] if s16 == 0 else [], inc=s_ld)
                a1 = p.comp("scalar", lambda s: s.activation(s_ksq[0:64, :], s_kh[0:64, :], AF.Square),
                            waits=[(s_ld, kv), (s_ev, s_ev.count)], inc=s_ev)
                mv = None
                for ci, (c0, cw_) in enumerate(SC):
                    mv = p.comp("tensor", lambda t, c0=c0, cw_=cw_: t.matmul(
                        ps_big[0:1, c0:c0 + cw_], consts.ones_col[0:64, :], s_ksq[0:64, c0:c0 + cw_],
                        start=True, stop=True, skip_group_check=True),
                        waits=[(s_ev, a1), (s_mm, s_mm.count)] if ci == 0 else [], inc=s_mm)
                v1 = p.comp("vector", lambda v: v.reduce_max(
                    s_scal.ap()[0:1, 2:3].unsqueeze(2),
                    ps_big.ap()[0:1, 0:2048].unsqueeze(1),
                    mybir.AxisListType.X), waits=[(s_mm, mv)], inc=s_dv)
                a2 = p.comp("scalar", lambda s: s.activation(s_scal[0:1, 2:3], s_scal[0:1, 2:3], AF.Sqrt),
                            waits=[(s_dv, v1)], inc=s_ev)
                a3 = p.comp("scalar", lambda s: s.activation(s_ksq[0:64, :], s_qh[0:64, :], AF.Square),
                            waits=[(s_ld, qv), (s_dv, v1)], inc=s_ev)
                for ci, (c0, cw_) in enumerate(SC):
                    mv = p.comp("tensor", lambda t, c0=c0, cw_=cw_: t.matmul(
                        ps_big[0:1, c0:c0 + cw_], consts.ones_col[0:64, :], s_ksq[0:64, c0:c0 + cw_],
                        start=True, stop=True, skip_group_check=True),
                        waits=[(s_ev, a3)] if ci == 0 else [], inc=s_mm)
                a4 = p.comp("scalar", lambda s: s.activation(s_qn[0:1, :], ps_big[0:1, :], AF.Sqrt),
                            waits=[(s_mm, mv), (s_ev, a2)], inc=s_ev)
                v2 = p.comp("vector", lambda v: v.tensor_copy(
                    s_scal[0:1, 3:4], s_scal[0:1, 2:3]),
                    waits=[(s_ev, a4)] + [dmax_ready], inc=s_dv)
                v2 = p.comp("vector", lambda v: v.tensor_tensor(
                    s_scal[0:1, 3:4], s_scal[0:1, 3:4], s_scal[0:1, 1:2], OP.add),
                    waits=[(s_dv, v2)], inc=s_dv)
                v3 = p.comp("vector", lambda v: v.tensor_tensor(
                    s_scal[0:1, 4:5], s_scal[0:1, 2:3], s_scal[0:1, 1:2], OP.mult),
                    waits=[(s_dv, v2)], inc=s_dv)
                v4 = p.comp("vector", lambda v: v.tensor_scalar(
                    s_qn[0:1, :], s_qn[0:1, :], s_scal[0:1, 3:4], s_scal[0:1, 4:5],
                    OP.mult, OP.add), waits=[(s_dv, v3)], inc=s_dv)
                v5 = p.comp("vector", lambda v: v.tensor_scalar(
                    s_qn[0:1, :], s_qn[0:1, :], -0.126, None, OP.mult),
                    waits=[(s_dv, v4)], inc=s_dv)
                bias_ready = None
                for s16 in range(16):
                    mv = p.comp("tensor", lambda t, s16=s16: t.matmul(
                        ps_ctx[:, 0:1], s_qn[0:1, 128 * s16:128 * s16 + 128],
                        consts.ones_row[0:1, 0:1], start=True, stop=True,
                        skip_group_check=True),
                        waits=([(s_dv, v5)] if s16 == 0 else []) + ([(s_dv, bias_ready)] if bias_ready else []),
                        inc=s_mm)
                    bias_ready = p.comp("vector", lambda v, s16=s16: v.tensor_copy(
                        s_bias[:, s16:s16 + 1], ps_ctx[:, 0:1]),
                        waits=[(s_mm, mv)], inc=s_dv)
                exp_prev = None
                rq_free = None
                pT_free = None
                ctx_free = None
                for qb in range(16):
                    rw = [(s_st, qdone)]
                    if rq_free is not None:
                        rw.append(rq_free)
                    rqv = p.dma("sync", lambda sy, qb=qb: sy.dma_start(
                        out=s_rq[:, :],
                        in_=bass.AP(scr["qe"], (qb) * STRIDE_QE + 127, [[SW - 1, 128], [1, 2048]])),
                        waits=rw, inc=s_ld)
                    for ci, (c0, cw_) in enumerate(SC):
                        w = []
                        if ci == 0:
                            w = [(s_ld, rkT_lv), (s_dv, bias_ready)]
                            if exp_prev is not None:
                                w.append((s_ev, exp_prev))
                        p.comp("tensor", lambda t, qb=qb, c0=c0, cw_=cw_: t.matmul(
                            ps_big[:, c0:c0 + cw_], s_qh[0:64, 128 * qb:128 * qb + 128],
                            s_kh[0:64, c0:c0 + cw_], start=True, stop=False,
                            skip_group_check=True), waits=w)
                    for sc in range(16):
                        p.comp("tensor", lambda t, sc=sc, qb=qb: t.matmul(
                            ps_big[:, 128 * sc:128 * sc + 128],
                            s_rkT[:, sc, 128 * qb:128 * qb + 128], consts.ident_bf[:, :],
                            start=False, stop=(sc % 4 == 3), skip_group_check=True),
                            inc=(s_mm if sc == 15 else None))
                    mmv = s_mm.count
                    aq = p.comp("vector", lambda v: v.tensor_tensor(
                        ps_big[:, :], ps_big[:, :], s_rq[:, :], OP.add),
                        waits=[(s_mm, mmv), (s_ld, rqv)], inc=s_dv)
                    rq_free = (s_dv, aq)
                    ew = [(s_dv, aq)]
                    if pT_free is not None:
                        ew.append(pT_free)
                    expv = p.comp("scalar", lambda s, qb=qb: s.activation(
                        s_p[:, :], ps_big[:, :], AF.Exp, scale=0.125,
                        bias=s_bias[:, qb:qb + 1], accum_out=s_sum[:, :]),
                        waits=ew, inc=s_ev)
                    exp_prev = expv
                    tv = {}
                    for half in range(2):
                        for rc in range(8):
                            rcg = half * 8 + rc
                            w = [(s_ev, expv)] if rcg == 0 else []
                            if rcg < 2 and ctx_free is not None:
                                w.append(ctx_free)
                            if rcg == 0 and pT_free is not None:
                                w.append(pT_free)
                            tv[half] = p.comp("tensor", lambda t, half=half, rc=rc, rcg=rcg: t.transpose(
                                ps_pt[half][:, 128 * rc:128 * rc + 128],
                                s_p[:, 128 * rcg:128 * rcg + 128], consts.ident_bf[:, :]),
                                waits=w, inc=(s_mm if rc == 7 else None))
                    mmv = s_mm.count
                    ev1 = p.comp("vector", lambda v: v.tensor_copy(s_pT[:, 0:1024], ps_pt[0][:, :]),
                                 waits=[(s_mm, mmv)], inc=s_dv)
                    ev2 = p.comp("vector", lambda v: v.tensor_copy(s_pT[:, 1024:2048], ps_pt[1][:, :]),
                                 inc=s_dv)
                    pT_free = (s_dv, ev2)
                    r1 = p.comp("vector", lambda v: v.reciprocal(s_rcp[:, :], s_sum[:, :]),
                                waits=[(s_ev, expv)], inc=s_dv)
                    for rc in range(16):
                        w = []
                        if rc == 0:
                            w = [(s_dv, ev2), (s_ld, vv)]
                            if ctx_free is not None:
                                w.append(ctx_free)
                        p.comp("tensor", lambda t, rc=rc: t.matmul(
                            ps_ctx[:, :], s_pT[:, 128 * rc:128 * rc + 128], s_vh[:, rc, :],
                            start=(rc == 0), stop=(rc == 15), skip_group_check=True),
                            waits=w, inc=(s_mm if rc == 15 else None))
                    mmv = s_mm.count
                    c1 = p.comp("vector", lambda v: v.tensor_scalar(
                        s_ctx[:, :], ps_ctx[:, :], s_rcp[:, :], None, OP.mult),
                        waits=[(s_mm, mmv), (s_dv, r1)], inc=s_dv)
                    ctv = p.comp("tensor", lambda t: t.transpose(
                        ps_ctxT[:, :], s_ctx[:, :], consts.ident_bf[:, :]),
                        waits=[(s_dv, c1), (s_ev, s_ev.count)], inc=s_mm)
                    bvslice = s_bcols["bv"].ap()[(h % 2) * 64:(h % 2) * 64 + 64, h // 2:h // 2 + 1]
                    ce = p.comp("scalar", lambda s, qb=qb, bvslice=bvslice: s.activation(
                        s_cstrip[0:64, 128 * qb:128 * qb + 128], ps_ctxT[0:64, :], AF.Identity,
                        bias=bvslice), waits=[(s_mm, ctv)], inc=s_ev)
                    ctx_free = (s_ev, ce)
                sv = p.dma("sync", lambda sy, b=b, h=h: sy.dma_start(
                    out=bass.AP(scr["cf"], (64 * h) * NTOK + 2048 * b, [[NTOK, 64], [1, 2048]]),
                    in_=s_cstrip[:, :]), waits=[(s_ev, s_ev.count)], inc=s_st)
                bh_prev_done = (s_st, sv)
        phaseB_st = s_st.count
    p.barrier(f"la_bar_b{tag}")

    with contextlib.ExitStack() as esC:
        s_cfm = esC.enter_context(nc.sbuf_tensor(f"la_cfm{tag}", [128, 8, Tg], BF16))
        s_x2 = esC.enter_context(nc.sbuf_tensor(f"la_x2{tag}", [128, 8, Tg], F32))
        s_xs2 = esC.enter_context(nc.sbuf_tensor(f"la_xs2{tag}", [128, 8, Tg], F32))
        s_h2 = esC.enter_context(nc.sbuf_tensor(f"la_h2{tag}", [128, 8, Tg], F32))
        s_y = esC.enter_context(nc.sbuf_tensor(f"la_y{tag}", [128, 8, Tg], F32))
        s_t = esC.enter_context(nc.sbuf_tensor(f"la_t{tag}", [128, Tg], F32))
        ln2 = make_ln_ctx(p, nc, esC, consts, Tg, "lc" + tag)
        ps_p = [esC.enter_context(nc.psum_tensor(f"la_pc{tag}{i}", [128, Tg], F32)) for i in range(2)]
        bankC_free = {}
        ln_last = None
        st_prev = None
        h2_free = None
        for g in range(NG):
            lw = [(s_st, phaseB_st)]
            if ln_last is not None:
                lw.append((ln2["sem_dv"], ln_last))
            cv = p.dma("sync", lambda sy, g=g: sy.dma_start(
                out=s_cfm[:, :, :],
                in_=bass.AP(scr["cf"], g * Tg, [[NTOK, 128], [128 * NTOK, 8], [1, Tg]])),
                waits=lw, inc=s_ld)
            xv = p.dma("sync", lambda sy, g=g: sy.dma_start(
                out=s_x2[:, :, :],
                in_=bass.AP(x_in, g * Tg, [[NTOK, 128], [128 * NTOK, 8], [1, Tg]])),
                inc=s_ld)
            xsv = p.dma("sync", lambda sy, g=g: sy.dma_start(
                out=s_xs2[:, :, :],
                in_=bass.AP(scr["xs"], g * Tg, [[NTOK, 128], [128 * NTOK, 8], [1, Tg]])),
                inc=s_ld)
            for ot in range(8):
                bi = ot % 2
                for ic in range(8):
                    w = []
                    if ic == 0:
                        w = [(s_ld, cv), (s_ld, s_ld.count)]
                        if bi in bankC_free:
                            w.append(bankC_free[bi])
                    p.comp("tensor", lambda t, ic=ic, ot=ot, bi=bi: t.matmul(
                        ps_p[bi][:, :], s_wo[:, ic, ot * 128:(ot + 1) * 128],
                        s_cfm.ap()[:, ic, :], start=(ic == 0), stop=(ic == 7),
                        skip_group_check=True), waits=w, inc=(s_mm if ic == 7 else None))
                mmv = s_mm.count
                e = p.comp("scalar", lambda s, ot=ot, bi=bi: s.activation(
                    s_t[:, :], ps_p[bi][:, :], AF.Identity, bias=s_bcols["bo"][:, ot:ot + 1]),
                    waits=[(s_mm, mmv)] + ([(s_dv, h2_free)] if (h2_free and ot == 0) else []), inc=s_ev)
                hv = p.comp("vector", lambda v, ot=ot: v.tensor_tensor(
                    s_h2.ap()[:, ot, :], s_t[:, :], s_xs2.ap()[:, ot, :], OP.add),
                    waits=[(s_ev, e), (s_ld, xsv)], inc=s_dv)
                bankC_free[bi] = (s_dv, hv)
            h2_done = s_dv.count
            wr = [(s_st, st_prev)] if st_prev is not None else []
            lnsem, lnval = emit_layernorm(
                p, ln2, s_h2.ap(), s_y.ap(), [(s_dv, h2_done), (s_ld, xv)],
                write_waits=wr, eps=1e-12,
                affine=(s_bcols["gam"], s_bcols["bet"]), resid3=s_x2.ap())
            ln_last = lnval
            h2_free = lnval
            sv = p.dma("sync", lambda sy, g=g: sy.dma_start(
                out=bass.AP(y_out, g * Tg, [[NTOK, 128], [128 * NTOK, 8], [1, Tg]]),
                in_=s_y[:, :, :]), waits=[(lnsem, lnval)], inc=s_st)
            st_prev = sv
    p.wait("sync", [(s_st, s_st.count)])
    return s_st


def emit_l0attn(p, nc, es, consts, x_in, y_out, W, scr, tag):
    Tg = 512
    NG = NTOK // Tg
    CJ = 128
    NCH = T // CJ

    s_ld = p.sem(f"da_ld{tag}")
    s_mm = p.sem(f"da_mm{tag}")
    s_ev = p.sem(f"da_ev{tag}")
    s_dv = p.sem(f"da_dv{tag}")
    s_gp = p.sem(f"da_gp{tag}")
    s_st = p.sem(f"da_st{tag}")

    cols = {}
    for key, ncol in (("ln1g", 8), ("ln1b", 8), ("ln2g", 8), ("ln2b", 8),
                      ("rmsw", 8), ("b2", 8), ("onw", 4), ("cq", 32), ("ck", 32),
                      ("cv", 64), ("dtb", 1), ("alog", 1)):
        cols[key] = es.enter_context(nc.sbuf_tensor(f"da_c{key}{tag}", [128, ncol], F32))
    maskLI = es.enter_context(nc.sbuf_tensor(f"da_mLI{tag}", [128, 128], F32))
    maskSU = es.enter_context(nc.sbuf_tensor(f"da_mSU{tag}", [128, 128], F32))
    maskLS = es.enter_context(nc.sbuf_tensor(f"da_mLS{tag}", [128, 128], F32))

    for key in ("ln1g", "ln1b", "ln2g", "ln2b", "rmsw", "b2", "onw", "cq", "ck", "cv", "dtb", "alog"):
        nr = W[key].shape[0]
        p.dma("gpsimd", lambda g, key=key, nr=nr: g.dma_start(
            out=cols[key][0:nr, :], in_=W[key].ap()), inc=s_ld)
    wld = s_ld.count
    msem = p.sem(f"da_msk{tag}")

    with contextlib.ExitStack() as esM:
        s_iota = esM.enter_context(nc.sbuf_tensor(f"da_iot{tag}", [1, 128], F32))
        s_ic = esM.enter_context(nc.sbuf_tensor(f"da_ioc{tag}", [128, 1], F32))
        s_dd = esM.enter_context(nc.sbuf_tensor(f"da_dd{tag}", [128, 128], F32))
        s_m01 = esM.enter_context(nc.sbuf_tensor(f"da_m01{tag}", [128, 128], F32))
        ps_m = esM.enter_context(nc.psum_tensor(f"da_psm{tag}", [128, 128], F32))
        v0 = p.comp("vector", lambda v: v.tensor_tensor_scan(
            s_iota[0:1, :], consts.ones_row[0:1, :], consts.ones_row[0:1, :],
            0.0, OP.mult, OP.add), waits=[(consts.sem, consts.ready)], inc=msem)
        v0 = p.comp("vector", lambda v: v.tensor_scalar(
            s_iota[0:1, :], s_iota[0:1, :], -1.0, None, OP.add),
            waits=[(msem, v0)], inc=msem)
        m1 = p.comp("tensor", lambda t: t.matmul(
            ps_m[:, 0:1], s_iota[0:1, :], consts.ones_row[0:1, 0:1],
            start=True, stop=True, skip_group_check=True),
            waits=[(msem, v0)], inc=s_mm)
        v0 = p.comp("vector", lambda v: v.tensor_copy(s_ic[:, :], ps_m[:, 0:1]),
                    waits=[(s_mm, m1)], inc=msem)
        m1 = p.comp("tensor", lambda t: t.matmul(
            ps_m[:, :], consts.ones_row[0:1, :], s_iota[0:1, :],
            start=True, stop=True, skip_group_check=True),
            waits=[(msem, v0)], inc=s_mm)
        v0 = p.comp("vector", lambda v: v.tensor_scalar(
            s_dd[:, :], ps_m[:, :], s_ic.ap()[:, 0:1], None, OP.subtract),
            waits=[(s_mm, m1)], inc=msem)
        v0 = p.comp("vector", lambda v: v.tensor_scalar(
            s_m01[:, :], s_dd[:, :], 0.0, None, OP.is_gt), waits=[(msem, v0)], inc=msem)
        v0 = p.comp("vector", lambda v: v.tensor_scalar(
            maskLI[:, :], s_m01[:, :], -1.0, 1e30, OP.add, OP.mult),
            waits=[(msem, v0)], inc=msem)
        v0 = p.comp("vector", lambda v: v.tensor_scalar(
            s_m01[:, :], s_dd[:, :], 0.0, None, OP.is_ge), waits=[(msem, v0)], inc=msem)
        v0 = p.comp("vector", lambda v: v.tensor_scalar(
            maskLS[:, :], s_m01[:, :], -1.0, 1e30, OP.add, OP.mult),
            waits=[(msem, v0)], inc=msem)
        v0 = p.comp("vector", lambda v: v.tensor_scalar(
            maskSU[:, :], s_m01[:, :], -1e30, None, OP.mult),
            waits=[(msem, v0)], inc=msem)
    mval = msem.count

    with contextlib.ExitStack() as esP:
        s_wq = esP.enter_context(nc.sbuf_tensor(f"da_wq{tag}", [128, 8, D], BF16))
        s_wk = esP.enter_context(nc.sbuf_tensor(f"da_wk{tag}", [128, 8, D], BF16))
        s_wv = esP.enter_context(nc.sbuf_tensor(f"da_wv{tag}", [128, 8, 2048], BF16))
        s_wg = esP.enter_context(nc.sbuf_tensor(f"da_wg{tag}", [128, 8, 2048], BF16))
        s_wab = esP.enter_context(nc.sbuf_tensor(f"da_wab{tag}", [128, 8, 8], BF16))
        for dst, key, nch in ((s_wq, "Wq", 8), (s_wk, "Wk", 8), (s_wv, "Wv", 8),
                              (s_wg, "Wg", 8), (s_wab, "Wab", 8)):
            ocols = W[key].shape[1]
            p.dma("gpsimd", lambda g, dst=dst, key=key, nch=nch, ocols=ocols: g.dma_start(
                out=dst[:, 0:nch, 0:ocols],
                in_=bass.AP(W[key], 0, [[ocols, 128], [128 * ocols, nch], [1, ocols]])), inc=s_ld)
        wld = s_ld.count
        s_x = esP.enter_context(nc.sbuf_tensor(f"da_x{tag}", [128, 8, Tg], F32))
        s_xs = esP.enter_context(nc.sbuf_tensor(f"da_xs{tag}", [128, 8, Tg], F32))
        s_h1 = s_x
        s_hn = esP.enter_context(nc.sbuf_tensor(f"da_hn{tag}", [128, 8, Tg], BF16))
        s_pr = esP.enter_context(nc.sbuf_tensor(f"da_pr{tag}", [128, 16, Tg], BF16))
        s_sq = esP.enter_context(nc.sbuf_tensor(f"da_sq{tag}", [128, Tg], F32))
        s_abuf = esP.enter_context(nc.sbuf_tensor(f"da_abuf{tag}", [8, Tg], F32))
        s_row = [esP.enter_context(nc.sbuf_tensor(f"da_row{i}{tag}", [1, Tg], F32)) for i in range(2)]
        ln = make_ln_ctx(p, nc, esP, consts, Tg, "dp" + tag)
        lnA = make_ln_ctx(p, nc, esP, consts, Tg, "dA" + tag, share=ln)
        ps_p = [esP.enter_context(nc.psum_tensor(f"da_pp{tag}{i}", [128, Tg], F32)) for i in range(2)]

        bank_free = {}
        ln_last = None
        pr_free = {}
        cast_prev = None
        h1_free = None
        for g in range(NG):
            xw = [] if ln_last is None else [(ln["sem_dv"], ln_last)]
            if h1_free is not None:
                xw.append(h1_free)
            xv = p.dma("sync", lambda sy, g=g: sy.dma_start(
                out=s_x[:, :, :],
                in_=bass.AP(x_in, g * Tg, [[NTOK, 128], [128 * NTOK, 8], [1, Tg]])),
                waits=xw, inc=s_ld)
            wr = [(s_st, s_st.count)] if g > 0 else []
            lnsem, lnval = emit_layernorm(p, ln, s_x.ap(), s_xs.ap(), [(s_ld, xv)],
                                          write_waits=wr)
            ln_last = lnval
            sxv = p.dma("sync", lambda sy, g=g: sy.dma_start(
                out=bass.AP(scr["xsub"], g * Tg, [[NTOK, 128], [128 * NTOK, 8], [1, Tg]]),
                in_=s_xs[:, :, :]), waits=[(lnsem, lnval)], inc=s_st)
            wr2 = [(lnA["sem_mm"], lnA["sem_mm"].count)] if g > 0 else []
            ln2sem, ln2val = emit_layernorm(p, lnA, s_xs.ap(), s_h1.ap(),
                                            [(lnsem, lnval)],
                                            write_waits=wr2 + [(lnsem, lnval)],
                                            affine=(cols["ln1g"], cols["ln1b"]))
            aq = None
            for c in range(8):
                w = [(ln2sem, ln2val)] if c == 0 else [(s_mm, aq)]
                av = p.comp("scalar", lambda s, c=c: s.activation(s_sq[:, :], s_h1.ap()[:, c, :], AF.Square),
                            waits=w, inc=s_ev)
                aq = p.comp("tensor", lambda t, c=c: t.matmul(
                    ps_p[0][0:1, :], consts.ones_col[:, :], s_sq[:, :],
                    start=(c == 0), stop=(c == 7), skip_group_check=True),
                    waits=[(s_ev, av)] + ([(s_dv, s_dv.count)] if c == 0 else []),
                    inc=s_mm)
            v1 = p.comp("vector", lambda v: v.tensor_scalar(s_row[0][:1, :], ps_p[0][0:1, :],
                                                            1.0 / 1024.0, 1e-6, OP.mult, OP.add),
                        waits=[(s_mm, aq)], inc=s_dv)
            a1 = p.comp("scalar", lambda s: s.activation(s_row[0][:1, :], s_row[0][:1, :], AF.Ln),
                        waits=[(s_dv, v1)], inc=s_ev)
            a2 = p.comp("scalar", lambda s: s.activation(s_row[0][:1, :], s_row[0][:1, :], AF.Exp,
                                                         scale=-0.5), waits=[(s_ev, a1)], inc=s_ev)
            bv_ = p.comp("tensor", lambda t: t.matmul(ps_p[0][:, :], consts.ones_row[:1, :],
                                                      s_row[0][:1, :], start=True, stop=True,
                                                      skip_group_check=True),
                         waits=[(s_ev, a2)], inc=s_mm)
            cast_last = None
            for c in range(8):
                w = [(s_mm, bv_)] if c == 0 else []
                if c == 0 and cast_prev is not None:
                    w.append(cast_prev)
                p.comp("vector", lambda v, c=c: v.tensor_tensor(s_sq[:, :], s_h1.ap()[:, c, :],
                                                                ps_p[0][:, :], OP.mult), waits=w)
                cast_last = p.comp("vector", lambda v, c=c: v.tensor_scalar(
                    s_hn.ap()[:, c, :], s_sq[:, :], cols["rmsw"][:, c:c + 1], None, OP.mult),
                    inc=s_dv)
            cast_prev = None

            def proj(wsb, n_ot, dstd, extra_w=()):
                stv = None
                for ot in range(n_ot):
                    bi = ot % 2
                    for ic in range(8):
                        w = []
                        if ic == 0:
                            w = [(s_dv, cast_last), (s_ld, wld)] + list(extra_w)
                            if bi in bank_free:
                                w.append(bank_free[bi])
                        p.comp("tensor", lambda t, ic=ic, ot=ot, wsb=wsb, bi=bi: t.matmul(
                            ps_p[bi][:, :], wsb[:, ic, ot * 128:(ot + 1) * 128],
                            s_hn.ap()[:, ic, :], start=(ic == 0), stop=(ic == 7),
                            skip_group_check=True),
                            waits=w, inc=(s_mm if ic == 7 else None))
                    mmv = s_mm.count
                    ew = [(s_mm, mmv)]
                    if (ot % 16) in pr_free:
                        ew.append(pr_free[ot % 16])
                    e = p.comp("scalar", lambda s, ot=ot, bi=bi: s.activation(
                        s_pr.ap()[:, ot % 16, :], ps_p[bi][:, :], AF.Copy),
                        waits=ew, inc=s_ev)
                    bank_free[bi] = (s_ev, e)
                    if ot % 16 == n_ot - 1 or ot % 16 == 15:
                        nch = min(n_ot, 16)
                        stv = p.dma("sync", lambda sy, g=g, dstd=dstd, nch=nch: sy.dma_start(
                            out=bass.AP(dstd, g * Tg, [[NTOK, 128], [128 * NTOK, nch], [1, Tg]]),
                            in_=s_pr[:, 0:nch, :]), waits=[(s_ev, s_ev.count)], inc=s_st)
                        for j in range(nch):
                            pr_free[j] = (s_st, stv)
                return stv
            proj(s_wq, 8, scr["qraw"])
            proj(s_wk, 8, scr["kraw"])
            proj(s_wv, 16, scr["vraw"])
            proj(s_wg, 16, scr["gate"])
            for ic in range(8):
                w = []
                if ic == 0:
                    w = [(s_dv, cast_last)]
                    if 0 in bank_free:
                        w.append(bank_free[0])
                p.comp("tensor", lambda t, ic=ic: t.matmul(
                    ps_p[0][0:8, :], s_wab[:, ic, :], s_hn.ap()[:, ic, :],
                    start=(ic == 0), stop=(ic == 7), skip_group_check=True),
                    waits=w, inc=(s_mm if ic == 7 else None))
            mmv = s_mm.count
            ew = [(s_mm, mmv)]
            if g > 0:
                ew.append((s_st, s_st.count))
            e = p.comp("scalar", lambda s: s.activation(s_abuf[0:8, :], ps_p[0][0:8, :], AF.Copy),
                       waits=ew, inc=s_ev)
            bank_free[0] = (s_ev, e)
            p.dma("sync", lambda sy, g=g: sy.dma_start(
                out=bass.AP(scr["ab"], g * Tg, [[NTOK, 8], [1, Tg]]),
                in_=s_abuf[0:8, :]),
                waits=[(s_ev, e)], inc=s_st)
            cast_prev = (s_mm, s_mm.count)
            h1_free = (s_dv, cast_last)
            if "hdump" in scr:
                p.dma("sync", lambda sy, g=g: sy.dma_start(
                    out=bass.AP(scr["hdump"], g * Tg, [[NTOK, 128], [128 * NTOK, 8], [1, Tg]]),
                    in_=s_hn[:, :, :]), waits=[(s_dv, cast_last)], inc=s_st)
        phaseP_st = s_st.count
    p.barrier(f"da_barP{tag}")

    with contextlib.ExitStack() as esG:
        s_ab = esG.enter_context(nc.sbuf_tensor(f"da_ab{tag}", [4, NTOK], F32))
        s_abb = esG.enter_context(nc.sbuf_tensor(f"da_abb{tag}", [4, NTOK], F32))
        s_g1 = esG.enter_context(nc.sbuf_tensor(f"da_g1{tag}", [4, NTOK], F32))
        s_g2 = esG.enter_context(nc.sbuf_tensor(f"da_g2{tag}", [4, NTOK], F32))
        s_B = esG.enter_context(nc.sbuf_tensor(f"da_B{tag}", [4, 2 * (T + 1)], F32))
        s_on = esG.enter_context(nc.sbuf_tensor(f"da_on{tag}", [4, T], F32))
        s_ea = esG.enter_context(nc.sbuf_tensor(f"da_ea{tag}", [4, 1], F32))
        lv = p.dma("gpsimd", lambda g: g.dma_start(
            out=s_ab[:, :], in_=bass.AP(scr["ab"], 0, [[NTOK, 4], [1, NTOK]])), inc=s_ld)
        lv = p.dma("gpsimd", lambda g: g.dma_start(
            out=s_abb[:, :], in_=bass.AP(scr["ab"], 4 * NTOK, [[NTOK, 4], [1, NTOK]])), inc=s_ld)
        ze = p.comp("gpsimd", lambda g: g.memset(s_on[:, :], 1.0))
        ze = p.comp("gpsimd", lambda g: g.memset(s_B[:, :], 0.0))
        zv = p.comp("gpsimd", lambda g: g.tensor_copy(s_B[0:4, 0:1], s_B[0:4, 0:1]), inc=s_gp)
        a0 = p.comp("scalar", lambda s: s.activation(s_ea[0:4, :], cols["alog"][0:4, 0:1], AF.Exp),
                    waits=[(s_ld, wld)], inc=s_ev)
        v0 = p.comp("vector", lambda v: v.tensor_scalar(s_ea[0:4, :], s_ea[0:4, :], -1.0, None, OP.mult),
                    waits=[(s_ev, a0)], inc=s_dv)
        v1 = p.comp("vector", lambda v: v.tensor_scalar(s_ab[0:4, :], s_ab[0:4, :],
                                                        cols["dtb"][0:4, 0:1], None, OP.add),
                    waits=[(s_ld, lv), (s_dv, v0)], inc=s_dv)
        a1 = p.comp("scalar", lambda s: s.activation(s_g1[0:4, :], s_ab[0:4, :], AF.Abs),
                    waits=[(s_dv, v1)], inc=s_ev)
        a2 = p.comp("scalar", lambda s: s.activation(s_g1[0:4, :], s_g1[0:4, :], AF.Exp, scale=-1.0),
                    waits=[(s_ev, a1)], inc=s_ev)
        v2 = p.comp("vector", lambda v: v.tensor_scalar(s_g1[0:4, :], s_g1[0:4, :], 1.0, None, OP.add),
                    waits=[(s_ev, a2)], inc=s_dv)
        a3 = p.comp("scalar", lambda s: s.activation(s_g1[0:4, :], s_g1[0:4, :], AF.Ln),
                    waits=[(s_dv, v2)], inc=s_ev)
        v3 = p.comp("vector", lambda v: v.tensor_scalar(s_g2[0:4, :], s_ab[0:4, :], 0.0, None, OP.max),
                    waits=[(s_ev, a3)], inc=s_dv)
        v4 = p.comp("vector", lambda v: v.tensor_tensor(s_g1[0:4, :], s_g1[0:4, :], s_g2[0:4, :], OP.add),
                    waits=[(s_dv, v3)], inc=s_dv)
        v5 = p.comp("vector", lambda v: v.tensor_scalar(s_g1[0:4, :], s_g1[0:4, :], s_ea[0:4, 0:1],
                                                        None, OP.mult),
                    waits=[(s_dv, v4)], inc=s_dv)
        sc1 = p.comp("vector", lambda v: v.tensor_tensor_scan(
            s_B[0:4, 1:T + 1], s_on[0:4, 0:T], s_g1[0:4, 0:T], 0.0, OP.mult, OP.add),
            waits=[(s_dv, v5), (s_gp, zv)], inc=s_dv)
        sc2 = p.comp("vector", lambda v: v.tensor_tensor_scan(
            s_B[0:4, T + 2:2 * T + 2], s_on[0:4, 0:T], s_g1[0:4, T:2 * T], 0.0, OP.mult, OP.add),
            waits=[(s_dv, sc1)], inc=s_dv)
        a4 = p.comp("scalar", lambda s: s.activation(s_g2[0:4, :], s_abb[0:4, :], AF.Sigmoid),
                    waits=[(s_dv, sc2)], inc=s_ev)
        p.dma("sync", lambda sy: sy.dma_start(
            out=bass.AP(scr["Bsc"], 0, [[2 * (T + 1), 4], [1, 2 * (T + 1)]]), in_=s_B[0:4, :]),
            waits=[(s_dv, sc2)], inc=s_st)
        p.dma("sync", lambda sy: sy.dma_start(
            out=bass.AP(scr["beta"], 0, [[NTOK, 4], [1, NTOK]]), in_=s_g2[0:4, :]),
            waits=[(s_ev, a4)], inc=s_st)
    p.barrier(f"da_barG{tag}")

    with contextlib.ExitStack() as esV:
        PADW = T + 3
        s_cb = esV.enter_context(nc.sbuf_tensor(f"da_cb{tag}", [128, 2, 2 * PADW], BF16))
        s_cv = esV.enter_context(nc.sbuf_tensor(f"da_cvb{tag}", [128, 2, NTOK], F32))
        s_bsq = esV.enter_context(nc.sbuf_tensor(f"da_bsq{tag}", [128, 512], F32))
        s_out = esV.enter_context(nc.sbuf_tensor(f"da_vout{tag}", [128, 2, NTOK], BF16))
        s_vrow = esV.enter_context(nc.sbuf_tensor(f"da_vrow{tag}", [1, 512], F32))
        ps_r = esV.enter_context(nc.psum_tensor(f"da_psr{tag}", [1, 512], F32))
        ps_bc = esV.enter_context(nc.psum_tensor(f"da_psbc{tag}", [128, 512], F32))
        zr = p.sem(f"da_vz{tag}")
        p.comp("gpsimd", lambda g: g.memset(s_cb[:, :, :], 0.0))
        zv = p.comp("gpsimd", lambda g: g.tensor_copy(s_cb[:, 0, 0:4], s_cb[:, 0, 0:4]), inc=zr)

        def conv_unit(src_dram, dst_dram, cwcol, chunks, do_norm, inv_scale, gate_done):
            nonlocal_busy = []
            lvs = []
            for ci, cg in enumerate(chunks):
                for sq_ in range(2):
                    lvs.append(p.dma("gpsimd", lambda g, ci=ci, cg=cg, sq_=sq_: g.dma_start(
                        out=s_cb[:, ci, sq_ * PADW + 3:sq_ * PADW + 3 + T],
                        in_=bass.AP(src_dram, cg * 128 * NTOK + sq_ * T,
                                    [[NTOK, 128], [1, T]])),
                        waits=[(zr, zv)] + list(gate_done), inc=s_ld))
            lv = lvs[-1]
            dvv = None
            for ci, cg in enumerate(chunks):
                for sq_ in range(2):
                    base = sq_ * PADW
                    w = [(s_ld, lv)] if (ci == 0 and sq_ == 0) else []
                    p.comp("vector", lambda v, ci=ci, cg=cg, base=base, sq_=sq_: v.tensor_scalar(
                        s_cv[:, ci, sq_ * T:(sq_ + 1) * T], s_cb[:, ci, base:base + T],
                        cwcol[:, cg * 4:cg * 4 + 1], None, OP.mult), waits=w)
                    for j in range(1, 4):
                        dvv = p.comp("vector", lambda v, ci=ci, cg=cg, base=base, j=j, sq_=sq_:
                                     v.scalar_tensor_tensor(
                                         s_cv[:, ci, sq_ * T:(sq_ + 1) * T],
                                         s_cb[:, ci, base + j:base + j + T],
                                         cwcol[:, cg * 4 + j:cg * 4 + j + 1],
                                         s_cv[:, ci, sq_ * T:(sq_ + 1) * T],
                                         OP.mult, OP.add),
                                     inc=(s_dv if (j == 3) else None))
            avs = []
            for ci in range(len(chunks)):
                avs.append(p.comp("scalar", lambda s, ci=ci: s.activation(
                    s_cv[:, ci, :], s_cv[:, ci, :], AF.Silu),
                    waits=[(s_dv, dvv)] if ci == 0 else [], inc=s_ev))
            a_last = avs[-1]
            if not do_norm:
                ov = p.comp("vector", lambda v: v.tensor_copy(s_out[:, 0:len(chunks), :],
                                                              s_cv[:, 0:len(chunks), :]),
                            waits=[(s_ev, a_last)], inc=s_dv)
                norm_done = (s_dv, ov)
            else:
                last = None
                for blk in range(8):
                    t0 = blk * 512
                    mv = None
                    for ci in range(2):
                        aq = p.comp("scalar", lambda s, ci=ci, t0=t0: s.activation(
                            s_bsq[:, :], s_cv[:, ci, t0:t0 + 512], AF.Square),
                            waits=([(s_ev, a_last)] if (blk == 0 and ci == 0) else []) +
                                  [(s_mm, s_mm.count)], inc=s_ev)
                        mv = p.comp("tensor", lambda t, ci=ci: t.matmul(
                            ps_r[0:1, :], consts.ones_col[:, :], s_bsq[:, :],
                            start=(ci == 0), stop=(ci == 1), skip_group_check=True),
                            waits=[(s_ev, aq)] + ([(s_dv, s_dv.count)] if ci == 0 else []),
                            inc=s_mm)
                    v1 = p.comp("vector", lambda v: v.tensor_scalar(
                        s_vrow[0:1, :], ps_r[0:1, :], 1.0, 1e-6, OP.mult, OP.add),
                        waits=[(s_mm, mv)], inc=s_dv)
                    a1 = p.comp("scalar", lambda s: s.activation(s_vrow[0:1, :], s_vrow[0:1, :], AF.Ln),
                                waits=[(s_dv, v1)], inc=s_ev)
                    a2 = p.comp("scalar", lambda s: s.activation(s_vrow[0:1, :], s_vrow[0:1, :],
                                                                 AF.Exp, scale=-0.5),
                                waits=[(s_ev, a1)], inc=s_ev)
                    v2 = p.comp("vector", lambda v, isc=inv_scale: v.tensor_scalar(
                        s_vrow[0:1, :], s_vrow[0:1, :], isc, None, OP.mult),
                        waits=[(s_ev, a2)], inc=s_dv)
                    bv_ = p.comp("tensor", lambda t: t.matmul(
                        ps_bc[:, :], consts.ones_row[:1, :], s_vrow[0:1, :],
                        start=True, stop=True, skip_group_check=True),
                        waits=[(s_dv, v2)], inc=s_mm)
                    for ci in range(2):
                        last = p.comp("vector", lambda v, ci=ci, t0=t0: v.tensor_tensor(
                            s_out[:, ci, t0:t0 + 512], s_cv[:, ci, t0:t0 + 512],
                            ps_bc[:, :], OP.mult),
                            waits=[(s_mm, bv_)] if ci == 0 else [], inc=s_dv)
                norm_done = (s_dv, last)
            stv = p.dma("sync", lambda sy, dst_dram=dst_dram, chunks=chunks: sy.dma_start(
                out=bass.AP(dst_dram, chunks[0] * 128 * NTOK,
                            [[NTOK, 128], [128 * NTOK, len(chunks)], [1, NTOK]]),
                in_=s_out[:, 0:len(chunks), :]), waits=[norm_done], inc=s_st)
            return (s_st, stv)

        gd = []
        for h in range(4):
            gd = [conv_unit(scr["qraw"], scr["qn"], cols["cq"], [2 * h, 2 * h + 1], True, 1.0 / 16.0, gd)]
        for h in range(4):
            gd = [conv_unit(scr["kraw"], scr["kn"], cols["ck"], [2 * h, 2 * h + 1], True, 1.0, gd)]
        for c2 in range(8):
            gd = [conv_unit(scr["vraw"], scr["vv"], cols["cv"], [2 * c2, 2 * c2 + 1], False, 1.0, gd)]
    p.barrier(f"da_barV{tag}")

    with contextlib.ExitStack() as esS:
        s_S = esS.enter_context(nc.sbuf_tensor(f"da_S{tag}", [128, 2, 512], F32))
        s_Sb = esS.enter_context(nc.sbuf_tensor(f"da_Sbf{tag}", [128, 2, 512], BF16))
        s_q = esS.enter_context(nc.sbuf_tensor(f"da_sq2{tag}", [128, 2, 128], BF16))
        s_k = esS.enter_context(nc.sbuf_tensor(f"da_sk2{tag}", [128, 2, 128], BF16))
        s_v = esS.enter_context(nc.sbuf_tensor(f"da_sv2{tag}", [128, 4, 128], BF16))
        s_gt = esS.enter_context(nc.sbuf_tensor(f"da_sgt{tag}", [128, 4, 128], BF16))
        s_gs = esS.enter_context(nc.sbuf_tensor(f"da_sgs{tag}", [128, 4, 128], BF16))
        s_qh2 = esS.enter_context(nc.sbuf_tensor(f"da_qh2{tag}", [128, 2, 128], BF16))
        s_kh2 = esS.enter_context(nc.sbuf_tensor(f"da_kh2{tag}", [128, 2, 128], BF16))
        s_kc2 = esS.enter_context(nc.sbuf_tensor(f"da_kc2{tag}", [128, 2, 128], BF16))
        s_kvT = esS.enter_context(nc.sbuf_tensor(f"da_kvT{tag}", [128, 2, 128], BF16))
        s_vtm = esS.enter_context(nc.sbuf_tensor(f"da_vtm{tag}", [128, 512], BF16))
        s_Brow = esS.enter_context(nc.sbuf_tensor(f"da_Brow{tag}", [1, T + 1], F32))
        s_brow = esS.enter_context(nc.sbuf_tensor(f"da_brow{tag}", [1, T], F32))
        s_blr = esS.enter_context(nc.sbuf_tensor(f"da_blr{tag}", [1, 128], F32))
        s_lamr = esS.enter_context(nc.sbuf_tensor(f"da_lamr{tag}", [1, 128], F32))
        s_ecr = esS.enter_context(nc.sbuf_tensor(f"da_ecr{tag}", [1, 128], F32))
        s_blm = esS.enter_context(nc.sbuf_tensor(f"da_blm{tag}", [1, 128], F32))
        s_LAM = esS.enter_context(nc.sbuf_tensor(f"da_LAM{tag}", [128, 128], F32))
        s_EC = esS.enter_context(nc.sbuf_tensor(f"da_EC{tag}", [128, 128], F32))
        s_BL = esS.enter_context(nc.sbuf_tensor(f"da_BL{tag}", [128, 128], F32))
        s_BB = esS.enter_context(nc.sbuf_tensor(f"da_BB{tag}", [128, 128], F32))
        s_blc = esS.enter_context(nc.sbuf_tensor(f"da_blc{tag}", [128, 1], F32))
        s_bbc = esS.enter_context(nc.sbuf_tensor(f"da_bbc{tag}", [128, 1], F32))
        s_D = esS.enter_context(nc.sbuf_tensor(f"da_D{tag}", [128, 128], F32))
        s_Eu = esS.enter_context(nc.sbuf_tensor(f"da_Eu{tag}", [128, 128], F32))
        s_El = esS.enter_context(nc.sbuf_tensor(f"da_El{tag}", [128, 128], F32))
        s_Ei = esS.enter_context(nc.sbuf_tensor(f"da_Ei{tag}", [128, 128], F32))
        s_N = [esS.enter_context(nc.sbuf_tensor(f"da_N{tag}{i}", [128, 128], BF16)) for i in range(2)]
        s_Nt = [esS.enter_context(nc.sbuf_tensor(f"da_Nt{tag}{i}", [128, 128], BF16)) for i in range(2)]
        s_Wt = esS.enter_context(nc.sbuf_tensor(f"da_Wt{tag}", [128, 128], BF16))
        s_U = esS.enter_context(nc.sbuf_tensor(f"da_U{tag}", [128, 512], BF16))
        s_Uf = esS.enter_context(nc.sbuf_tensor(f"da_Uf{tag}", [128, 512], F32))
        s_osq = esS.enter_context(nc.sbuf_tensor(f"da_osq{tag}", [128, 512], F32))
        s_ORS = esS.enter_context(nc.sbuf_tensor(f"da_ORS{tag}", [128, 128], F32))
        s_of = esS.enter_context(nc.sbuf_tensor(f"da_of{tag}", [128, 4, 128], BF16))
        s_tt = esS.enter_context(nc.sbuf_tensor(f"da_tt{tag}", [128, 128], F32))
        s_orow = esS.enter_context(nc.sbuf_tensor(f"da_orow{tag}", [1, 128], F32))
        ps_bc2 = esS.enter_context(nc.psum_tensor(f"da_pbc2{tag}", [128, 128], F32))
        ps_K = esS.enter_context(nc.psum_tensor(f"da_pK{tag}", [128, 128], F32))
        ps_U = esS.enter_context(nc.psum_tensor(f"da_pU{tag}", [128, 512], F32))
        ps_X = esS.enter_context(nc.psum_tensor(f"da_pX{tag}", [128, 128], F32))
        ps_O = esS.enter_context(nc.psum_tensor(f"da_pO{tag}", [128, 512], F32))
        ps_S2 = esS.enter_context(nc.psum_tensor(f"da_pS2{tag}", [128, 512], F32))
        ps_ro = esS.enter_context(nc.psum_tensor(f"da_pro{tag}", [1, 128], F32))
        ps_T = esS.enter_context(nc.psum_tensor(f"da_pT{tag}", [128, 128], BF16))

        zs = p.sem(f"da_zs{tag}")

        last_dv = [0]
        last_ev = [0]

        def dve(fn, waits=()):
            v = p.comp("vector", fn, waits=[(s_dv, last_dv[0])] + list(waits), inc=s_dv)
            last_dv[0] = v
            return v

        def act(fn, waits=()):
            v = p.comp("scalar", fn, waits=[(s_ev, last_ev[0])] + list(waits), inc=s_ev)
            last_ev[0] = v
            return v

        for b in range(SEQ_PER_CORE):
            for h in range(4):
                zz = p.comp("gpsimd", lambda g: g.memset(s_S[:, :, :], 0.0),
                            waits=[(s_dv, last_dv[0]), (s_mm, s_mm.count)])
                zz = p.comp("gpsimd", lambda g: g.memset(s_Sb[:, :, :], 0.0))
                zv = p.comp("gpsimd", lambda g: g.tensor_copy(s_S[:, 0, 0:4], s_S[:, 0, 0:4]),
                            inc=zs)
                zval = zs.count
                lB = p.dma("gpsimd", lambda g, b=b, h=h: g.dma_start(
                    out=s_Brow[0:1, :],
                    in_=bass.AP(scr["Bsc"], h * 2 * (T + 1) + b * (T + 1), [[T + 1, 1], [1, T + 1]])),
                    inc=s_ld)
                lb2 = p.dma("gpsimd", lambda g, b=b, h=h: g.dma_start(
                    out=s_brow[0:1, :],
                    in_=bass.AP(scr["beta"], h * NTOK + b * T, [[T, 1], [1, T]])),
                    inc=s_ld)
                for j in range(NCH):
                    c0 = j * CJ
                    tok0 = b * T + c0
                    lq = p.dma("gpsimd", lambda g, h=h, tok0=tok0: g.dma_start(
                        out=s_q[:, :, :],
                        in_=bass.AP(scr["qn"], (256 * h) * NTOK + tok0,
                                    [[NTOK, 128], [128 * NTOK, 2], [1, 128]])),
                        waits=[(s_mm, s_mm.count)], inc=s_ld)
                    lk = p.dma("gpsimd", lambda g, h=h, tok0=tok0: g.dma_start(
                        out=s_k[:, :, :],
                        in_=bass.AP(scr["kn"], (256 * h) * NTOK + tok0,
                                    [[NTOK, 128], [128 * NTOK, 2], [1, 128]])),
                        inc=s_ld)
                    lv_ = p.dma("gpsimd", lambda g, h=h, tok0=tok0: g.dma_start(
                        out=s_v[:, :, :],
                        in_=bass.AP(scr["vv"], (512 * h) * NTOK + tok0,
                                    [[NTOK, 128], [128 * NTOK, 4], [1, 128]])),
                        inc=s_ld)
                    lg = p.dma("gpsimd", lambda g, h=h, tok0=tok0: g.dma_start(
                        out=s_gt[:, :, :],
                        in_=bass.AP(scr["gate"], (512 * h) * NTOK + tok0,
                                    [[NTOK, 128], [128 * NTOK, 4], [1, 128]])),
                        inc=s_ld)
                    dve(lambda v, c0=c0: v.tensor_scalar(
                        s_blr[0:1, :], s_Brow[0:1, 1 + c0:1 + c0 + CJ],
                        s_Brow.ap()[0:1, c0:c0 + 1], None, OP.subtract),
                        waits=[(s_ld, lB)])
                    act(lambda s: s.activation(s_lamr[0:1, :], s_blr[0:1, :], AF.Exp),
                        waits=[(s_dv, last_dv[0])])
                    dve(lambda v: v.tensor_scalar(
                        s_ecr[0:1, :], s_blr[0:1, :], -1.0, s_blr.ap()[0:1, 127:128],
                        OP.mult, OP.add))
                    act(lambda s: s.activation(s_ecr[0:1, :], s_ecr[0:1, :], AF.Exp),
                        waits=[(s_dv, last_dv[0])])
                    dve(lambda v, b=b, c0=c0: v.tensor_tensor(
                        s_blm[0:1, :], s_brow[0:1, c0:c0 + CJ], s_lamr[0:1, :], OP.mult),
                        waits=[(s_ld, lb2), (s_ev, last_ev[0])])
                    rows_done = last_dv[0]
                    bc_jobs = [(s_lamr, s_LAM), (s_ecr, s_EC), (s_blm, s_BL)]
                    mvp = None
                    for rsrc, rdst in bc_jobs:
                        mv = p.comp("tensor", lambda t, rsrc=rsrc: t.matmul(
                            ps_bc2[:, :], consts.ones_row[:1, :], rsrc[0:1, :],
                            start=True, stop=True, skip_group_check=True),
                            waits=[(s_dv, rows_done), (s_ev, last_ev[0])] +
                                  ([(s_dv, last_dv[0])] if mvp else []), inc=s_mm)
                        dve(lambda v, rdst=rdst: v.tensor_copy(rdst[:, :], ps_bc2[:, :]),
                            waits=[(s_mm, mv)])
                        mvp = mv
                    mv = p.comp("tensor", lambda t, b=b, c0=c0: t.matmul(
                        ps_bc2[:, :], consts.ones_row[:1, :], s_brow[0:1, c0:c0 + CJ],
                        start=True, stop=True, skip_group_check=True),
                        waits=[(s_dv, last_dv[0])], inc=s_mm)
                    dve(lambda v: v.tensor_copy(s_BB[:, :], ps_bc2[:, :]), waits=[(s_mm, mv)])
                    mv = p.comp("tensor", lambda t: t.matmul(
                        ps_bc2[:, 0:1], s_blr[0:1, :], consts.ones_row[0:1, 0:1],
                        start=True, stop=True, skip_group_check=True),
                        waits=[(s_dv, last_dv[0])], inc=s_mm)
                    dve(lambda v: v.tensor_copy(s_blc[:, :], ps_bc2[:, 0:1]), waits=[(s_mm, mv)])
                    mv = p.comp("tensor", lambda t, b=b, c0=c0: t.matmul(
                        ps_bc2[:, 0:1], s_brow[0:1, c0:c0 + CJ], consts.ones_row[0:1, 0:1],
                        start=True, stop=True, skip_group_check=True),
                        waits=[(s_dv, last_dv[0])], inc=s_mm)
                    dve(lambda v: v.tensor_copy(s_bbc[:, :], ps_bc2[:, 0:1]), waits=[(s_mm, mv)])
                    mv = p.comp("tensor", lambda t: t.matmul(
                        ps_bc2[:, :], consts.ones_row[:1, :], s_blr[0:1, :],
                        start=True, stop=True, skip_group_check=True),
                        waits=[(s_dv, last_dv[0])], inc=s_mm)
                    dve(lambda v: v.tensor_scalar(s_D[:, :], ps_bc2[:, :], -1.0,
                                                  s_blc.ap()[:, 0:1], OP.mult, OP.add),
                        waits=[(s_mm, mv)])
                    dve(lambda v: v.scalar_tensor_tensor(s_Eu[:, :], s_D[:, :], -1.0,
                                                         maskLI[:, :], OP.mult, OP.add),
                        waits=[(msem, mval), (s_ev, last_ev[0])])
                    dve(lambda v: v.tensor_tensor(s_El[:, :], s_D[:, :], maskSU[:, :], OP.add))
                    dve(lambda v: v.scalar_tensor_tensor(s_Ei[:, :], s_D[:, :], -1.0,
                                                         maskLS[:, :], OP.mult, OP.add))
                    act(lambda s: s.activation(s_Eu[:, :], s_Eu[:, :], AF.Exp),
                        waits=[(s_dv, last_dv[0])])
                    act(lambda s: s.activation(s_El[:, :], s_El[:, :], AF.Exp))
                    act(lambda s: s.activation(s_Ei[:, :], s_Ei[:, :], AF.Exp))
                    etiles = last_ev[0]
                    for ci in range(2):
                        p.comp("tensor", lambda t, ci=ci: t.matmul(
                            ps_K[:, :], s_k[:, ci, :], s_k[:, ci, :],
                            start=(ci == 0), stop=(ci == 1), skip_group_check=True),
                            waits=[(s_ld, lk), (s_dv, last_dv[0])] if ci == 0 else [],
                            inc=(s_mm if ci == 1 else None))
                    kkv = s_mm.count
                    dve(lambda v: v.tensor_tensor(s_tt[:, :], ps_K[:, :], s_El[:, :], OP.mult),
                        waits=[(s_mm, kkv), (s_ev, etiles)])
                    dve(lambda v: v.tensor_scalar(s_N[0][:, :], s_tt[:, :],
                                                  s_bbc.ap()[:, 0:1], None, OP.mult))
                    dve(lambda v: v.tensor_tensor(s_tt[:, :], ps_K[:, :], s_Eu[:, :], OP.mult))
                    ndone = dve(lambda v: v.tensor_tensor(s_Nt[0][:, :], s_tt[:, :], s_BB[:, :],
                                                          OP.mult))
                    for ci in range(2):
                        p.comp("tensor", lambda t, ci=ci: t.matmul(
                            ps_K[:, :], s_k[:, ci, :], s_q[:, ci, :],
                            start=(ci == 0), stop=(ci == 1), skip_group_check=True),
                            waits=[(s_ld, lq), (s_dv, ndone)] if ci == 0 else [],
                            inc=(s_mm if ci == 1 else None))
                    kqv = s_mm.count
                    wtv = dve(lambda v: v.tensor_tensor(s_Wt[:, :], ps_K[:, :], s_Ei[:, :], OP.mult),
                              waits=[(s_mm, kqv)])
                    for ci in range(2):
                        dve(lambda v, ci=ci: v.tensor_tensor(s_qh2[:, ci, :], s_q[:, ci, :],
                                                             s_LAM[:, :], OP.mult))
                        dve(lambda v, ci=ci: v.tensor_tensor(s_kh2[:, ci, :], s_k[:, ci, :],
                                                             s_BL[:, :], OP.mult))
                        dve(lambda v, ci=ci: v.tensor_tensor(s_kc2[:, ci, :], s_k[:, ci, :],
                                                             s_EC[:, :], OP.mult))
                    hats = last_dv[0]
                    tv = None
                    for ci in range(2):
                        tv = p.comp("tensor", lambda t, ci=ci: t.transpose(
                            ps_T[:, :], s_kc2[:, ci, :], consts.ident_bf[:, :]),
                            waits=[(s_dv, last_dv[0])], inc=s_mm)
                        dve(lambda v, ci=ci: v.tensor_copy(s_kvT[:, ci, :], ps_T[:, :]),
                            waits=[(s_mm, tv)])
                    for ci in range(4):
                        tv = p.comp("tensor", lambda t, ci=ci: t.transpose(
                            ps_T[:, :], s_v[:, ci, :], consts.ident_bf[:, :]),
                            waits=[(s_ld, lv_), (s_dv, last_dv[0])], inc=s_mm)
                        dve(lambda v, ci=ci: v.tensor_copy(s_vtm[:, ci * 128:(ci + 1) * 128],
                                                           ps_T[:, :]),
                            waits=[(s_mm, tv)])
                    vt_done = last_dv[0]
                    for ci in range(2):
                        p.comp("tensor", lambda t, ci=ci: t.matmul(
                            ps_U[:, :], s_kh2[:, ci, :], s_Sb[:, ci, :],
                            start=(ci == 0), stop=(ci == 1), skip_group_check=True),
                            waits=[(s_dv, vt_done), (zs, zval)] if ci == 0 else [],
                            inc=(s_mm if ci == 1 else None))
                    o2v = s_mm.count
                    dve(lambda v: v.scalar_tensor_tensor(
                        s_Uf[:, :], s_vtm[:, :], s_bbc.ap()[:, 0:1], ps_U[:, :],
                        OP.mult, OP.subtract),
                        waits=[(s_mm, o2v)])
                    dve(lambda v: v.tensor_copy(s_U[:, :], s_Uf[:, :]))
                    cpar = 0
                    for it in range(7):
                        mv = p.comp("tensor", lambda t, cpar=cpar: t.matmul(
                            ps_U[:, :], s_Nt[cpar][:, :], s_U[:, :],
                            start=True, stop=True, skip_group_check=True),
                            waits=[(s_dv, last_dv[0])], inc=s_mm)
                        dve(lambda v, it=it: v.tensor_tensor(
                            s_Uf[:, :], s_Uf[:, :], ps_U[:, :],
                            OP.subtract if it == 0 else OP.add),
                            waits=[(s_mm, mv)])
                        dve(lambda v: v.tensor_copy(s_U[:, :], s_Uf[:, :]))
                        if it < 6:
                            mv = p.comp("tensor", lambda t, cpar=cpar: t.matmul(
                                ps_X[:, :], s_Nt[cpar][:, :], s_N[cpar][:, :],
                                start=True, stop=True, skip_group_check=True),
                                waits=[(s_dv, last_dv[0])], inc=s_mm)
                            dve(lambda v, cpar=cpar: v.tensor_copy(s_N[1 - cpar][:, :], ps_X[:, :]),
                                waits=[(s_mm, mv)])
                            mv = p.comp("tensor", lambda t, cpar=cpar: t.transpose(
                                ps_T[:, :], s_N[1 - cpar][:, :], consts.ident_bf[:, :]),
                                waits=[(s_dv, last_dv[0])], inc=s_mm)
                            dve(lambda v, cpar=cpar: v.tensor_copy(s_Nt[1 - cpar][:, :], ps_T[:, :]),
                                waits=[(s_mm, mv)])
                            cpar = 1 - cpar
                    u_done = last_dv[0]
                    for vt in range(4):
                        for ci in range(2):
                            p.comp("tensor", lambda t, vt=vt, ci=ci: t.matmul(
                                ps_O[:, vt * 128:(vt + 1) * 128],
                                s_Sb[:, ci, vt * 128:(vt + 1) * 128], s_qh2[:, ci, :],
                                start=(ci == 0), stop=False, skip_group_check=True),
                                waits=[(s_dv, u_done)] if (vt == 0 and ci == 0) else [])
                        p.comp("tensor", lambda t, vt=vt: t.matmul(
                            ps_O[:, vt * 128:(vt + 1) * 128],
                            s_U[:, vt * 128:(vt + 1) * 128], s_Wt[:, :],
                            start=False, stop=True, skip_group_check=True),
                            inc=(s_mm if vt == 3 else None))
                    ov = s_mm.count
                    if "dbg" in scr and b == 1 and h == 0 and j == 3:
                        for off, t_sb in ((0, s_Wt), (1, s_U), (2, s_Ei), (3, s_LAM), (5, maskLS)):
                            width = 512 if t_sb is s_U else 128
                            p.dma("gpsimd", lambda g, off=off, t_sb=t_sb, width=width: g.dma_start(
                                out=bass.AP(scr["dbg"], off * 128 * 512, [[width, 128], [1, width]]),
                                in_=t_sb[:, 0:width] if t_sb is not s_U else t_sb[:, :]),
                                waits=[(s_mm, ov), (s_dv, last_dv[0]), (s_ev, last_ev[0])], inc=s_st)
                        p.dma("gpsimd", lambda g: g.dma_start(
                            out=bass.AP(scr["dbg"], 4 * 128 * 512, [[128, 1], [1, 128]]),
                            in_=s_blr[0:1, :]), inc=s_st)
                    for ci in range(2):
                        mv = p.comp("tensor", lambda t, ci=ci: t.matmul(
                            ps_S2[:, :], s_kvT[:, ci, :], s_U[:, :],
                            start=True, stop=True, skip_group_check=True),
                            waits=[(s_dv, last_dv[0])], inc=s_mm)
                        dve(lambda v, ci=ci: v.scalar_tensor_tensor(
                            s_S[:, ci, :], s_S[:, ci, :], s_LAM.ap()[:, 127:128],
                            ps_S2[:, :], OP.mult, OP.add),
                            waits=[(s_mm, mv)])
                        dve(lambda v, ci=ci: v.tensor_copy(s_Sb[:, ci, :], s_S[:, ci, :]))
                    dump_wait = []
                    if "oraw" in scr:
                        aqd = act(lambda s: s.activation(s_osq[:, :], ps_O[:, :], AF.Copy),
                                  waits=[(s_mm, ov)])
                        odma = p.dma("sync", lambda sy, h=h, tok0=tok0: sy.dma_start(
                            out=bass.AP(scr["oraw"], (512 * h) * NTOK + tok0,
                                        [[NTOK, 128], [128 * NTOK, 4], [1, 128]]),
                            in_=s_osq.ap()[:, 0:512].rearrange("p (a b) -> p a b", b=128)),
                            waits=[(s_ev, aqd)], inc=s_st)
                        dump_wait = [(s_st, odma)]
                    aq = act(lambda s: s.activation(s_osq[:, :], ps_O[:, :], AF.Square),
                             waits=[(s_mm, ov)] + dump_wait)
                    mv = None
                    for vt in range(4):
                        mv = p.comp("tensor", lambda t, vt=vt: t.matmul(
                            ps_ro[0:1, :], consts.ones_col[:, :],
                            s_osq[:, vt * 128:(vt + 1) * 128],
                            start=(vt == 0), stop=(vt == 3), skip_group_check=True),
                            waits=[(s_ev, aq), (s_dv, last_dv[0])] if vt == 0 else [],
                            inc=(s_mm if vt == 3 else None))
                    dve(lambda v: v.tensor_scalar(s_orow[0:1, :], ps_ro[0:1, :],
                                                  1.0 / 512.0, 1e-6, OP.mult, OP.add),
                        waits=[(s_mm, mv)])
                    act(lambda s: s.activation(s_orow[0:1, :], s_orow[0:1, :], AF.Ln),
                        waits=[(s_dv, last_dv[0])])
                    act(lambda s: s.activation(s_orow[0:1, :], s_orow[0:1, :], AF.Exp, scale=-0.5))
                    mv = p.comp("tensor", lambda t: t.matmul(
                        ps_bc2[:, :], consts.ones_row[:1, :], s_orow[0:1, :],
                        start=True, stop=True, skip_group_check=True),
                        waits=[(s_ev, last_ev[0]), (s_dv, last_dv[0])], inc=s_mm)
                    dve(lambda v: v.tensor_copy(s_ORS[:, :], ps_bc2[:, :]), waits=[(s_mm, mv)])
                    av = act(lambda s: s.activation(s_gs[:, :, :], s_gt[:, :, :], AF.Silu),
                             waits=[(s_ld, lg)])
                    for vt in range(4):
                        dve(lambda v, vt=vt: v.tensor_tensor(
                            s_tt[:, :], ps_O[:, vt * 128:(vt + 1) * 128], s_ORS[:, :], OP.mult))
                        dve(lambda v, vt=vt: v.scalar_tensor_tensor(
                            s_of[:, vt, :], s_tt[:, :], cols["onw"][:, vt:vt + 1],
                            s_gs[:, vt, :], OP.mult, OP.mult),
                            waits=[(s_ev, av)])
                    stv = p.dma("sync", lambda sy, h=h, tok0=tok0: sy.dma_start(
                        out=bass.AP(scr["osc"], (512 * h) * NTOK + tok0,
                                    [[NTOK, 128], [128 * NTOK, 4], [1, 128]]),
                        in_=s_of[:, :, :]), waits=[(s_dv, last_dv[0])], inc=s_st)
    p.barrier(f"da_barS{tag}")

    with contextlib.ExitStack() as esO:
        s_wo = esO.enter_context(nc.sbuf_tensor(f"da_wo{tag}", [128, 16, D], BF16))
        s_w2 = esO.enter_context(nc.sbuf_tensor(f"da_w2{tag}", [128, 8, D], BF16))
        for dst, key, nch in ((s_wo, "Wo", 16), (s_w2, "W2", 8)):
            p.dma("gpsimd", lambda g, dst=dst, key=key, nch=nch: g.dma_start(
                out=dst[:, :, :],
                in_=bass.AP(W[key], 0, [[D, 128], [128 * D, nch], [1, D]])), inc=s_ld)
        wld2 = s_ld.count
        s_o = esO.enter_context(nc.sbuf_tensor(f"da_og{tag}", [128, 16, Tg], BF16))
        s_xso = esO.enter_context(nc.sbuf_tensor(f"da_xso{tag}", [128, 8, Tg], F32))
        s_xo = esO.enter_context(nc.sbuf_tensor(f"da_xo{tag}", [128, 8, Tg], F32))
        s_x2 = esO.enter_context(nc.sbuf_tensor(f"da_x2b{tag}", [128, 8, Tg], F32))
        s_h2n = esO.enter_context(nc.sbuf_tensor(f"da_h2n{tag}", [128, 8, Tg], BF16))
        s_t2 = esO.enter_context(nc.sbuf_tensor(f"da_t2{tag}", [128, Tg], F32))
        s_y2 = esO.enter_context(nc.sbuf_tensor(f"da_y2{tag}", [128, 8, Tg], F32))
        lnO = make_ln_ctx(p, nc, esO, consts, Tg, "dO" + tag)
        ps_p2 = [esO.enter_context(nc.psum_tensor(f"da_po{tag}{i}", [128, Tg], F32)) for i in range(2)]
        bank_free = {}
        ln_last = None
        st_prev = None
        for g in range(NG):
            lw = [(s_ld, wld2)]
            if ln_last is not None:
                lw.append((s_dv, s_dv.count))
            lo = p.dma("sync", lambda sy, g=g: sy.dma_start(
                out=s_o[:, :, :],
                in_=bass.AP(scr["osc"], g * Tg, [[NTOK, 128], [128 * NTOK, 16], [1, Tg]])),
                waits=lw, inc=s_ld)
            lxs = p.dma("sync", lambda sy, g=g: sy.dma_start(
                out=s_xso[:, :, :],
                in_=bass.AP(scr["xsub"], g * Tg, [[NTOK, 128], [128 * NTOK, 8], [1, Tg]])),
                inc=s_ld)
            lx = p.dma("sync", lambda sy, g=g: sy.dma_start(
                out=s_xo[:, :, :],
                in_=bass.AP(x_in, g * Tg, [[NTOK, 128], [128 * NTOK, 8], [1, Tg]])),
                inc=s_ld)
            x2_done = None
            for ot in range(8):
                bi = ot % 2
                for ic in range(16):
                    w = []
                    if ic == 0:
                        w = [(s_ld, lo)]
                        if bi in bank_free:
                            w.append(bank_free[bi])
                    p.comp("tensor", lambda t, ic=ic, ot=ot, bi=bi: t.matmul(
                        ps_p2[bi][:, :], s_wo[:, ic, ot * 128:(ot + 1) * 128],
                        s_o.ap()[:, ic, :], start=(ic == 0), stop=(ic == 15),
                        skip_group_check=True), waits=w, inc=(s_mm if ic == 15 else None))
                mmv = s_mm.count
                hv = p.comp("vector", lambda v, ot=ot, bi=bi: v.tensor_tensor(
                    s_x2.ap()[:, ot, :], ps_p2[bi][:, :], s_xso.ap()[:, ot, :], OP.add),
                    waits=[(s_mm, mmv), (s_ld, lxs)] +
                          ([(lnO["sem_dv"], x2_done)] if (x2_done and ot == 0) else []),
                    inc=s_dv)
                bank_free[bi] = (s_dv, hv)
            x2v = s_dv.count
            wr = []
            wr = [(s_mm, s_mm.count)] if g > 0 else []
            lnsem, lnval = emit_layernorm(p, lnO, s_x2.ap(), s_h2n.ap(),
                                          [(s_dv, x2v)], write_waits=wr,
                                          affine=(cols["ln2g"], cols["ln2b"]))
            ln_last = lnval
            xxv = None
            for c in range(8):
                xxv = p.comp("vector", lambda v, c=c: v.tensor_tensor(
                    s_xo.ap()[:, c, :], s_x2.ap()[:, c, :], s_xo.ap()[:, c, :], OP.add),
                    waits=[(s_dv, x2v), (s_ld, lx)] if c == 0 else [], inc=s_dv)
            yv = None
            for ot in range(8):
                bi = ot % 2
                for ic in range(8):
                    w = []
                    if ic == 0:
                        w = [(lnsem, lnval)]
                        if bi in bank_free:
                            w.append(bank_free[bi])
                    p.comp("tensor", lambda t, ic=ic, ot=ot, bi=bi: t.matmul(
                        ps_p2[bi][:, :], s_w2[:, ic, ot * 128:(ot + 1) * 128],
                        s_h2n.ap()[:, ic, :], start=(ic == 0), stop=(ic == 7),
                        skip_group_check=True), waits=w, inc=(s_mm if ic == 7 else None))
                mmv = s_mm.count
                gw = [(s_mm, mmv)]
                if yv is not None:
                    gw.append((s_dv, yv))
                ge = p.comp("scalar", lambda s, ot=ot, bi=bi: s.activation(
                    s_t2[:, :], ps_p2[bi][:, :], AF.Gelu, bias=cols["b2"][:, ot:ot + 1]),
                    waits=gw, inc=s_ev)
                yv = p.comp("vector", lambda v, ot=ot: v.tensor_tensor(
                    s_y2.ap()[:, ot, :], s_t2[:, :], s_xo.ap()[:, ot, :], OP.add),
                    waits=[(s_ev, ge), (s_dv, xxv)] +
                          ([(s_st, st_prev)] if (st_prev and ot == 0) else []), inc=s_dv)
                bank_free[bi] = (s_dv, yv)
            st_prev = p.dma("sync", lambda sy, g=g: sy.dma_start(
                out=bass.AP(y_out, g * Tg, [[NTOK, 128], [128 * NTOK, 8], [1, Tg]]),
                in_=s_y2[:, :, :]), waits=[(s_dv, s_dv.count)], inc=s_st)
            x2_done = lnval
    p.wait("sync", [(s_st, s_st.count)])
    return s_st


# version 05 (final)
# speedup vs baseline: 1.1451x; 1.1451x over previous
import contextlib
import math
import os

import numpy as np

os.environ.setdefault("GAUGE_NTFF_IGNORE_DMA", "1")

import concourse.bass as bass
import concourse.mybir as mybir
from concourse.bass_utils import run_bass_kernel_spmd

F32 = mybir.dt.float32
BF16 = mybir.dt.bfloat16
U8 = mybir.dt.uint8
AF = mybir.ActivationFunctionType
OP = mybir.AluOpType

NCORES = 8
SEQ_PER_CORE = 2
T = 2048
NTOK = SEQ_PER_CORE * T
D = 1024


class Sem:
    __slots__ = ("name", "count", "h")

    def __init__(self, name):
        self.name = name
        self.count = 0
        self.h = None


class _Op:
    __slots__ = ("fn", "waits", "inc", "inc_k", "pure")

    def __init__(self, fn, waits, inc, inc_k, pure):
        self.fn = fn
        self.waits = waits
        self.inc = inc
        self.inc_k = inc_k
        self.pure = pure


class Prog:
    ENGINES = ("tensor", "vector", "scalar", "gpsimd", "sync")

    def __init__(self, nc):
        self.nc = nc
        self.ops = {e: [] for e in self.ENGINES}
        self.sems = []
        self._pending_waits = {e: [] for e in self.ENGINES}

    def sem(self, name):
        s = Sem(name)
        self.sems.append(s)
        return s

    def on(self, eng, fn, waits=(), inc=None, inc_k=1, pure=False):
        wl = [(s, int(v)) for (s, v) in waits]
        if self._pending_waits[eng]:
            wl = self._pending_waits[eng] + wl
            self._pending_waits[eng] = []
        self.ops[eng].append(_Op(fn, wl, inc, inc_k, pure))
        if inc is not None:
            inc.count += inc_k
            return inc.count
        return None

    def comp(self, eng, fn, waits=(), inc=None):
        return self.on(eng, fn, waits, inc, 1)

    def dma(self, eng, fn, waits=(), inc=None):
        return self.on(eng, fn, waits, inc, 16)

    def wait(self, eng, pairs):
        self.on(eng, lambda e: None, waits=pairs, pure=True)

    def barrier(self, name):
        if not hasattr(self, "_bar_sem"):
            self._bar_sem = self.sem("p_barrier")
        done = self._bar_sem
        pairs = []
        for e in self.ENGINES:
            lst = self.ops[e]
            idx = None
            for i in range(len(lst) - 1, -1, -1):
                if not lst[i].pure:
                    idx = i
                    break
            if idx is None:
                continue
            op = lst[idx]
            if op.inc is None:
                op.inc = done
                op.inc_k = 1
                done.count += 1
                pairs.append((done, done.count))
            else:
                pairs.append((op.inc, op.inc.count))
        best = {}
        for s, v in pairs:
            best[s] = max(best.get(s, 0), v)
        pairs = [(s, v) for s, v in best.items()]
        for e in self.ENGINES:
            self._pending_waits[e] = self._pending_waits[e] + [(s, int(v)) for s, v in pairs]
        return pairs

    def build(self):
        nc = self.nc
        with contextlib.ExitStack() as es:
            for s in self.sems:
                s.h = es.enter_context(nc.semaphore(s.name))
            block = es.enter_context(nc.Block())

            def mk(eng_name):
                def emit(engine):
                    for op in self.ops[eng_name]:
                        for (s, v) in op.waits:
                            engine.wait_ge(s.h, v)
                        instr = op.fn(engine)
                        if op.inc is not None:
                            assert instr is not None, f"op on {eng_name} with inc returned no instr"
                            instr.then_inc(op.inc.h, op.inc_k)
                return emit

            for e in self.ENGINES:
                if self.ops[e]:
                    getattr(block, e)(mk(e))


class Consts:
    def __init__(self, p, nc, es):
        self.ones_col = es.enter_context(nc.sbuf_tensor("c_ones_col", [128, 1], F32))
        self.ones_row = es.enter_context(nc.sbuf_tensor("c_ones_row", [1, 128], F32))
        self.ident_bf = es.enter_context(nc.sbuf_tensor("c_ident_bf", [128, 128], BF16))
        self.anti128 = es.enter_context(nc.sbuf_tensor("c_anti128", [128, 128], BF16))
        self.anti127 = es.enter_context(nc.sbuf_tensor("c_anti127", [128, 128], BF16))
        sem = p.sem("c_const")

        def emit(g):
            g.memset(self.ones_col[:, :], 1.0)
            g.memset(self.ones_row[:1, :], 1.0)
            g.memset(self.ident_bf[:, :], 0.0)
            g.memset(self.anti128[:, :], 0.0)
            g.memset(self.anti127[:, :], 0.0)
            g.affine_select(
                out=self.anti128[:, :], in_=self.anti128[:, :],
                compare_op=OP.not_equal, fill=1.0, base=-127,
                pattern=[[1, 128]], channel_multiplier=1)
            g.affine_select(
                out=self.anti127[:, :], in_=self.anti127[:, :],
                compare_op=OP.not_equal, fill=1.0, base=-126,
                pattern=[[1, 128]], channel_multiplier=1)
            return g.affine_select(
                out=self.ident_bf[:, :], in_=self.ident_bf[:, :],
                compare_op=OP.not_equal, fill=1.0, base=0,
                pattern=[[-1, 128]], channel_multiplier=1)

        def emit2(g):
            g.tensor_copy(self.ones_col[:, :], self.ones_col[:, :])
            return g.tensor_copy(self.ones_row[:1, :], self.ones_row[:1, :])

        p.comp("gpsimd", emit)
        self.ready = p.comp("gpsimd", emit2, inc=sem)
        self.sem = sem


def make_ln_ctx(p, nc, es, consts, Tg, tag, share=None):
    if share is not None:
        psums = {k: share[k] for k in ("ps_s", "ps_q", "ps_A", "ps_B")}
    else:
        psums = {
            "ps_s": es.enter_context(nc.psum_tensor(f"ln_ps_s{tag}", [1, Tg], F32)),
            "ps_q": es.enter_context(nc.psum_tensor(f"ln_ps_q{tag}", [1, Tg], F32)),
            "ps_A": es.enter_context(nc.psum_tensor(f"ln_ps_A{tag}", [128, Tg], F32)),
            "ps_B": es.enter_context(nc.psum_tensor(f"ln_ps_B{tag}", [128, Tg], F32)),
        }
    ctx = {
        "consts": consts,
        "s_xsq": es.enter_context(nc.sbuf_tensor(f"ln_xsq{tag}", [128, Tg], F32)),
        "s_t1": es.enter_context(nc.sbuf_tensor(f"ln_t1{tag}", [128, Tg], F32)),
        "s_rows": [es.enter_context(nc.sbuf_tensor(f"ln_row{i}{tag}", [1, Tg], F32)) for i in range(4)],
        **psums,
        "sem_sq": p.sem(f"ln_sq{tag}"),
        "sem_mm": p.sem(f"ln_mm{tag}"),
        "sem_dv": p.sem(f"ln_dv{tag}"),
        "sem_ac": p.sem(f"ln_ac{tag}"),
        "Tg": Tg,
    }
    return ctx


def emit_layernorm(p, ctx, s_x3, s_xn3, xin_ready, write_waits=(), n_chunks=8,
                   eps=1e-5, affine=None, resid3=None):
    consts = ctx["consts"]
    s_xsq = ctx["s_xsq"]
    r_m, r_rstd, r_b, r_t = (r[:1, :] for r in ctx["s_rows"])
    ps_s, ps_q, ps_A, ps_B = ctx["ps_s"], ctx["ps_q"], ctx["ps_A"], ctx["ps_B"]
    sq, mm, dv, ac = ctx["sem_sq"], ctx["sem_mm"], ctx["sem_dv"], ctx["sem_ac"]

    stats_guard = [(dv, dv.count)]

    for c in range(n_chunks):
        w = list(xin_ready) if c == 0 else [(mm, mm.count)]
        sv = p.comp("scalar", lambda s, c=c: s.activation(s_xsq[:, :], s_x3[:, c, :], AF.Square),
                    waits=w, inc=sq)
        p.comp("tensor", lambda t, c=c, n=n_chunks: t.matmul(
            ps_s[:1, :], consts.ones_col[:, :], s_x3[:, c, :],
            start=(c == 0), stop=(c == n - 1), skip_group_check=True),
            waits=(list(xin_ready) + stats_guard) if c == 0 else [])
        p.comp("tensor", lambda t, c=c, n=n_chunks: t.matmul(
            ps_q[:1, :], consts.ones_col[:, :], s_xsq[:, :],
            start=(c == 0), stop=(c == n - 1), skip_group_check=True),
            waits=[(sq, sv)], inc=mm)
    stats_done = mm.count

    nd = float(n_chunks * 128)
    p.comp("vector", lambda v: v.tensor_scalar(r_m, ps_s[:1, :], 1.0 / nd, None, OP.mult),
           waits=[(mm, stats_done)])
    p.comp("vector", lambda v: v.tensor_scalar(r_t, ps_q[:1, :], 1.0 / nd, None, OP.mult))
    p.comp("vector", lambda v: v.tensor_tensor(r_b, r_m, r_m, OP.mult))
    dvv = p.comp("vector", lambda v: v.scalar_tensor_tensor(r_t, r_t, eps, r_b,
                                                            OP.add, OP.subtract), inc=dv)
    p.comp("scalar", lambda s: s.activation(r_t, r_t, AF.Ln), waits=[(dv, dvv)])
    acv = p.comp("scalar", lambda s: s.activation(r_rstd, r_t, AF.Exp, scale=-0.5), inc=ac)
    p.comp("vector", lambda v: v.tensor_tensor(r_b, r_m, r_rstd, OP.mult), waits=[(ac, acv)])
    dvv = p.comp("vector", lambda v: v.tensor_scalar(r_b, r_b, -1.0, None, OP.mult), inc=dv)
    p.comp("tensor", lambda t: t.matmul(ps_A[:, :], consts.ones_row[:1, :], r_rstd,
                                        start=True, stop=True), waits=[(dv, dvv)])
    mv = p.comp("tensor", lambda t: t.matmul(ps_B[:, :], consts.ones_row[:1, :], r_b,
                                             start=True, stop=True), inc=mm)
    t1 = ctx["s_t1"]
    chunk_done = []
    for c in range(n_chunks):
        w = ([(mm, mv)] + list(write_waits)) if c == 0 else []
        p.comp("vector", lambda v, c=c: v.tensor_tensor(t1[:, :], s_x3[:, c, :], ps_A[:, :], OP.mult),
               waits=w)
        if affine is None and resid3 is None:
            cv = p.comp("vector", lambda v, c=c: v.tensor_tensor(s_xn3[:, c, :], t1[:, :], ps_B[:, :], OP.add),
                        inc=dv)
        else:
            p.comp("vector", lambda v, c=c: v.tensor_tensor(t1[:, :], t1[:, :], ps_B[:, :], OP.add))
            if affine is not None:
                gam, bet = affine
                if resid3 is None:
                    cv = p.comp("vector", lambda v, c=c, gam=gam, bet=bet: v.tensor_scalar(
                        s_xn3[:, c, :], t1[:, :], gam[:, c:c + 1], bet[:, c:c + 1],
                        OP.mult, OP.add), inc=dv)
                else:
                    p.comp("vector", lambda v, c=c, gam=gam, bet=bet: v.tensor_scalar(
                        t1[:, :], t1[:, :], gam[:, c:c + 1], bet[:, c:c + 1], OP.mult, OP.add))
                    cv = p.comp("vector", lambda v, c=c: v.tensor_tensor(
                        s_xn3[:, c, :], t1[:, :], resid3[:, c, :], OP.add), inc=dv)
            else:
                cv = p.comp("vector", lambda v, c=c: v.tensor_tensor(
                    s_xn3[:, c, :], t1[:, :], resid3[:, c, :], OP.add), inc=dv)
        chunk_done.append(cv)
    return dv, chunk_done[-1]


def emit_wprep(p, nc, es, consts, jobs, tag):
    s_src = [es.enter_context(nc.sbuf_tensor(f"wp_src{tag}{i}", [128, 128], BF16)) for i in range(2)]
    maxO = max(src.shape[0] for src, _ in jobs)
    s_strip = es.enter_context(nc.sbuf_tensor(f"wp_strip{tag}", [128, maxO], BF16))
    ps = [es.enter_context(nc.psum_tensor(f"wp_ps{tag}{i}", [128, 128], BF16)) for i in range(2)]
    s_ld = p.sem(f"wp_ld{tag}")
    s_mm = p.sem(f"wp_mm{tag}")
    s_ev = p.sem(f"wp_ev{tag}")
    s_st = p.sem(f"wp_st{tag}")

    k = 0
    mm_at = {}
    for src, dst in jobs:
        O, I = src.shape
        n_ob = O // 128
        for ic in range(I // 128):
            st_guard = s_st.count
            for ob in range(n_ob):
                buf = k % 2
                w = [(consts.sem, consts.ready)] if k < 2 else [(s_mm, mm_at[k - 2])]
                lv = p.dma("gpsimd",
                           lambda g, src=src, ob=ob, ic=ic, buf=buf: g.dma_start(
                               out=s_src[buf][:, :],
                               in_=src.ap()[ob * 128:(ob + 1) * 128, ic * 128:(ic + 1) * 128]),
                           waits=w, inc=s_ld)
                w = [(s_ld, lv)]
                if k >= 2:
                    w.append((s_ev, s_ev.count - 1))
                mv = p.comp("tensor",
                            lambda t, buf=buf: t.transpose(ps[buf][:, :], s_src[buf][:, :],
                                                           consts.ident_bf[:, :]),
                            waits=w, inc=s_mm)
                mm_at[k] = mv
                ev_w = [(s_mm, mv)]
                if ob == 0:
                    ev_w.append((s_st, st_guard))
                p.comp("vector",
                       lambda v, buf=buf, ob=ob: v.tensor_copy(
                           s_strip[:, ob * 128:(ob + 1) * 128], ps[buf][:, :]),
                       waits=ev_w, inc=s_ev)
                k += 1
            p.dma("sync",
                  lambda sy, dst=dst, ic=ic, O=O: sy.dma_start(
                      out=dst.ap()[ic * 128:(ic + 1) * 128, :],
                      in_=s_strip[:, 0:O]),
                  waits=[(s_ev, s_ev.count)], inc=s_st)
    p.wait("sync", [(s_st, s_st.count)])
    return s_st


def emit_l0ffn(p, nc, es, consts, x_in, y_out, W, tag):
    Tg = 512
    NG = NTOK // Tg

    s_x = es.enter_context(nc.sbuf_tensor(f"ff_x{tag}", [128, 8, Tg], F32))
    s_xn = es.enter_context(nc.sbuf_tensor(f"ff_xn{tag}", [128, 8, Tg], BF16))
    s_ha = es.enter_context(nc.sbuf_tensor(f"ff_ha{tag}", [128, 32, Tg], BF16))
    s_hb = es.enter_context(nc.sbuf_tensor(f"ff_hb{tag}", [128, 32, Tg], BF16))
    s_u = es.enter_context(nc.sbuf_tensor(f"ff_u{tag}", [128, Tg], BF16))
    s_v = es.enter_context(nc.sbuf_tensor(f"ff_v{tag}", [128, Tg], BF16))
    s_y = es.enter_context(nc.sbuf_tensor(f"ff_y{tag}", [128, 8, Tg], F32))
    s_wsl = [es.enter_context(nc.sbuf_tensor(f"ff_w{tag}{i}", [128, 32, 128], BF16))
             for i in range(4)]
    s_s1 = es.enter_context(nc.sbuf_tensor(f"ff_s1{tag}", [128, 32], F32))
    s_s2 = es.enter_context(nc.sbuf_tensor(f"ff_s2{tag}", [128, 32], F32))
    s_sh1 = es.enter_context(nc.sbuf_tensor(f"ff_sh1{tag}", [128, 32], F32))
    s_sh2 = es.enter_context(nc.sbuf_tensor(f"ff_sh2{tag}", [128, 32], F32))
    s_s3h = es.enter_context(nc.sbuf_tensor(f"ff_s3h{tag}", [128, 8], F32))
    ps_mm = [es.enter_context(nc.psum_tensor(f"ff_ps{tag}{i}", [128, Tg], F32)) for i in range(4)]

    ln = make_ln_ctx(p, nc, es, consts, Tg, "f" + tag)

    s_ld = p.sem(f"ff_ld{tag}")
    s_wld = p.sem(f"ff_wld{tag}")
    s_mm = p.sem(f"ff_mm{tag}")
    s_ev = p.sem(f"ff_ev{tag}")
    s_dv = p.sem(f"ff_dv{tag}")
    s_st = p.sem(f"ff_st{tag}")

    for dst, key in ((s_s1, "s1"), (s_s2, "s2"), (s_sh1, "sh1"), (s_sh2, "sh2")):
        p.dma("gpsimd", lambda g, dst=dst, key=key: g.dma_start(out=dst[:, :], in_=W[key].ap()),
              inc=s_ld)
    sc_v = p.dma("gpsimd", lambda g: g.dma_start(out=s_s3h[:, :], in_=W["s3"].ap()), inc=s_ld)
    p.comp("vector", lambda v: v.tensor_scalar(s_s3h[:, :], s_s3h[:, :], 0.5, None, OP.mult),
           waits=[(s_ld, sc_v)], inc=s_dv)

    slab_jobs = []
    for g in range(NG):
        for o in range(32):
            slab_jobs.append(("w1", W["w1T"], 8, o))
            slab_jobs.append(("w2", W["w2T"], 8, o))
        for o in range(32):
            slab_jobs.append(("h1", W["h1T"], 32, o))
        for o in range(32):
            slab_jobs.append(("h2", W["h2T"], 32, o))
        for o in range(8):
            slab_jobs.append(("w3", W["w3T"], 32, o))
    NSLAB_BUF = 4
    slab_free = [None] * len(slab_jobs)
    slab_ld = [None] * len(slab_jobs)
    next_slab = [0]

    def issue_slab(consts_gate=False):
        k = next_slab[0]
        next_slab[0] += 1
        layer, wt, nch, o = slab_jobs[k]
        buf = k % NSLAB_BUF
        waits = []
        if k >= NSLAB_BUF:
            waits.append(slab_free[k - NSLAB_BUF])
        elif consts_gate:
            waits.append((consts.sem, consts.ready))
        Ocols = wt.shape[1]

        def fn(sy, wt=wt, nch=nch, o=o, buf=buf, Ocols=Ocols):
            src = bass.AP(wt, o * 128, [[Ocols, 128], [128 * Ocols, nch], [1, 128]])
            return sy.dma_start(out=s_wsl[buf][:, 0:nch, :], in_=src)
        slab_ld[k] = p.dma("sync", fn, waits=waits, inc=s_wld)
        return k

    for _ in range(NSLAB_BUF):
        issue_slab(consts_gate=True)

    bank_free = {}

    def stage_matmuls(bank_i, nch, src3, slab_k, extra_waits):
        buf = slab_k % NSLAB_BUF
        v = None
        for c in range(nch):
            w = []
            if c == 0:
                w.append((s_wld, slab_ld[slab_k]))
                if bank_i in bank_free:
                    w.append(bank_free[bank_i])
                w += list(extra_waits)
            v = p.comp("tensor",
                       lambda t, c=c, bank_i=bank_i, buf=buf, nch=nch, src3=src3:
                           t.matmul(ps_mm[bank_i][:, :], s_wsl[buf][:, c, :], src3[:, c, :],
                                    start=(c == 0), stop=(c == nch - 1), skip_group_check=True),
                       waits=w, inc=(s_mm if c == nch - 1 else None))
        return v

    xn3 = s_xn.ap()
    x3 = s_x.ap()
    ha3 = s_ha.ap()
    hb3 = s_hb.ap()
    y3 = s_y.ap()

    store_val = None
    resid_done = None
    for g in range(NG):
        xwaits = [] if resid_done is None else [(s_dv, resid_done)]
        xv = p.dma("gpsimd", lambda gg, g=g: gg.dma_start(
            out=s_x[:, :, :],
            in_=bass.AP(x_in, g * Tg, [[NTOK, 128], [128 * NTOK, 8], [1, Tg]])),
            waits=xwaits, inc=s_ld)

        ln_in = [(s_ld, xv)]
        wr = [(s_mm, s_mm.count)] if g > 0 else []
        lnsem, lnval = emit_layernorm(p, ln, x3, xn3, ln_in, write_waits=wr)

        uv_free = {}
        for o in range(32):
            k1 = next_slab[0] - NSLAB_BUF
            k_w1 = g * 136 + 2 * o
            k_w2 = k_w1 + 1
            b1 = (2 * o) % 4
            b2 = (2 * o + 1) % 4
            mv1 = stage_matmuls(b1, 8, xn3, k_w1, [(lnsem, lnval)] if o == 0 else [])
            mv2 = stage_matmuls(b2, 8, xn3, k_w2, [])
            slab_free[k_w1] = (s_mm, mv1)
            slab_free[k_w2] = (s_mm, mv2)
            if next_slab[0] < len(slab_jobs):
                issue_slab()
            if next_slab[0] < len(slab_jobs):
                issue_slab()
            ew1 = [(s_mm, mv1)]
            if o >= 1:
                ew1.append(uv_free[0])
            e1 = p.comp("scalar", lambda s, o=o, b1=b1: s.activation(
                s_u[:, :], ps_mm[b1][:, :], AF.Silu, scale=s_s1[:, o:o + 1]),
                waits=ew1, inc=s_ev)
            ew2 = [(s_mm, mv2)]
            if o >= 1:
                ew2.append(uv_free[1])
            e2 = p.comp("scalar", lambda s, o=o, b2=b2: s.activation(
                s_v[:, :], ps_mm[b2][:, :], AF.Copy, scale=s_s2[:, o:o + 1]),
                waits=ew2, inc=s_ev)
            bank_free[b1] = (s_ev, e1)
            bank_free[b2] = (s_ev, e2)
            dv = p.comp("vector", lambda v, o=o: v.tensor_tensor(
                ha3[:, o, :], s_u[:, :], s_v[:, :], OP.mult),
                waits=[(s_ev, e2)], inc=s_dv)
            uv_free = {0: (s_dv, dv), 1: (s_dv, dv)}
        stage1_done = s_dv.count

        for o in range(32):
            k_h1 = g * 136 + 64 + o
            b = o % 4
            mv = stage_matmuls(b, 32, ha3, k_h1,
                               [(s_dv, stage1_done)] if o == 0 else [])
            slab_free[k_h1] = (s_mm, mv)
            if next_slab[0] < len(slab_jobs):
                issue_slab()
            e = p.comp("scalar", lambda s, o=o, b=b: s.activation(
                hb3[:, o, :], ps_mm[b][:, :], AF.Gelu, scale=s_sh1[:, o:o + 1]),
                waits=[(s_mm, mv)], inc=s_ev)
            bank_free[b] = (s_ev, e)
        h1_done = s_ev.count

        for o in range(32):
            k_h2 = g * 136 + 96 + o
            b = o % 4
            mv = stage_matmuls(b, 32, hb3, k_h2,
                               [(s_ev, h1_done)] if o == 0 else [])
            slab_free[k_h2] = (s_mm, mv)
            if next_slab[0] < len(slab_jobs):
                issue_slab()
            e = p.comp("scalar", lambda s, o=o, b=b: s.activation(
                ha3[:, o, :], ps_mm[b][:, :], AF.Gelu, scale=s_sh2[:, o:o + 1]),
                waits=[(s_mm, mv)], inc=s_ev)
            bank_free[b] = (s_ev, e)
        h2_done = s_ev.count

        for o in range(8):
            k_w3 = g * 136 + 128 + o
            b = o % 4
            mv = stage_matmuls(b, 32, ha3, k_w3,
                               [(s_ev, h2_done)] if o == 0 else [])
            slab_free[k_w3] = (s_mm, mv)
            if next_slab[0] < len(slab_jobs):
                issue_slab()
            w = [(s_mm, mv)]
            if g > 0 and o == 0:
                w.append((s_st, store_val))
            dv = p.comp("vector", lambda v, o=o, b=b: v.scalar_tensor_tensor(
                y3[:, o, :], ps_mm[b][:, :], s_s3h[:, o:o + 1], x3[:, o, :],
                OP.mult, OP.add),
                waits=w, inc=s_dv)
            bank_free[b] = (s_dv, dv)
        resid_done = s_dv.count

        store_val = p.dma("gpsimd", lambda gg, g=g: gg.dma_start(
            out=bass.AP(y_out, g * Tg, [[NTOK, 128], [128 * NTOK, 8], [1, Tg]]),
            in_=s_y[:, :, :]),
            waits=[(s_dv, resid_done)], inc=s_st)

    p.wait("gpsimd", [(s_st, s_st.count)])
    return s_st


def _featmajor(x):
    B, Tt, Dd = x.shape
    return np.ascontiguousarray(x.reshape(B * Tt, Dd).T)


def _scale_cols(s):
    return np.ascontiguousarray(s.reshape(-1, 128).T)


def build_prep():
    nc = bass.Bass("TRN2", target_bir_lowering=False)
    ins = {}
    for nm, shape in (("w1", (4096, 1024)), ("w2", (4096, 1024)), ("h1", (4096, 4096)),
                      ("h2", (4096, 4096)), ("w3", (1024, 4096))):
        ins[nm] = nc.dram_tensor(nm, list(shape), F32, kind="ExternalInput")
    outs = {}
    for nm, shape in (("w1T", (1024, 4096)), ("w2T", (1024, 4096)), ("h1T", (4096, 4096)),
                      ("h2T", (4096, 4096)), ("w3T", (4096, 1024))):
        outs[nm] = nc.dram_tensor(nm, list(shape), BF16, kind="ExternalOutput")
    for nm, shape in (("wg", (2816, 1024)), ("wu", (2816, 1024)), ("wd", (1024, 2816))):
        ins[nm] = nc.dram_tensor(nm, list(shape), F32, kind="ExternalInput")
    for nm, shape in (("wgT", (1024, 2816)), ("wuT", (1024, 2816)), ("wdT", (2816, 1024))):
        outs[nm] = nc.dram_tensor(nm, list(shape), BF16, kind="ExternalOutput")
    p = Prog(nc)
    with contextlib.ExitStack() as es:
        consts = Consts(p, nc, es)
        emit_wprep(p, nc, es, consts,
                   [(ins[a], outs[a + "T"]) for a in ("w1", "w2", "h1", "h2", "w3")], "p")
        p.barrier("prep_bar")
        emit_l1wprep(p, nc, es, consts,
                     [(ins[a], outs[a + "T"]) for a in ("wg", "wu", "wd")], "p")
        p.build()
    return nc


def build_l0ffn():
    nc = bass.Bass("TRN2", target_bir_lowering=False)
    x_in = nc.dram_tensor("x", [1024 * NTOK], F32, kind="ExternalInput")
    y_out = nc.dram_tensor("y", [1024 * NTOK], F32, kind="ExternalOutput")
    W = {}
    for nm, shape, dt in (("w1T", (1024, 4096), BF16), ("w2T", (1024, 4096), BF16),
                          ("h1T", (4096, 4096), BF16), ("h2T", (4096, 4096), BF16),
                          ("w3T", (4096, 1024), BF16), ("s1", (128, 32), F32),
                          ("s2", (128, 32), F32), ("sh1", (128, 32), F32),
                          ("sh2", (128, 32), F32), ("s3", (128, 8), F32)):
        W[nm] = nc.dram_tensor(nm, list(shape), dt, kind="ExternalInput")
    p = Prog(nc)
    with contextlib.ExitStack() as es:
        consts = Consts(p, nc, es)
        emit_l0ffn(p, nc, es, consts, x_in, y_out, W, "a")
        p.build()
    return nc


def build_l0attn():
    nc = bass.Bass("TRN2", target_bir_lowering=False)
    x_in = nc.dram_tensor("x", [1024 * NTOK], F32, kind="ExternalInput")
    y_out = nc.dram_tensor("y", [1024 * NTOK], F32, kind="ExternalOutput")
    W = {}
    for nm, shape in (("Wq", (1024, 1024)), ("Wk", (1024, 1024)), ("Wv", (1024, 2048)),
                      ("Wg", (1024, 2048)), ("Wab", (1024, 8)), ("Wo", (2048, 1024)),
                      ("W2", (1024, 1024))):
        W[nm] = nc.dram_tensor(nm, list(shape), F32, kind="ExternalInput")
    for nm, shape in (("ln1g", (128, 8)), ("ln1b", (128, 8)), ("ln2g", (128, 8)),
                      ("ln2b", (128, 8)), ("rmsw", (128, 8)), ("b2", (128, 8)),
                      ("onw", (128, 4)), ("cq", (128, 32)), ("ck", (128, 32)),
                      ("cv", (128, 64)), ("dtb", (4, 1)), ("alog", (4, 1))):
        W[nm] = nc.dram_tensor(nm, list(shape), F32, kind="ExternalInput")
    scr = {}
    for nm, size, dt in (("xsub", 1024 * NTOK, F32), ("qraw", 1024 * NTOK, BF16),
                         ("kraw", 1024 * NTOK, BF16), ("vraw", 2048 * NTOK, BF16),
                         ("gate", 2048 * NTOK, BF16), ("qn", 1024 * NTOK, BF16),
                         ("kn", 1024 * NTOK, BF16), ("vv", 2048 * NTOK, BF16),
                         ("ab", 8 * NTOK, F32), ("Bsc", 4 * 2 * (T + 1), F32),
                         ("beta", 4 * NTOK, F32), ("osc", 2048 * NTOK, BF16)):
        scr[nm] = nc.dram_tensor("scr_" + nm, [size], dt, kind="Internal")
    p = Prog(nc)
    with contextlib.ExitStack() as es:
        consts = Consts(p, nc, es)
        emit_l0attn(p, nc, es, consts, x_in, y_out, W, scr, "a")
        p.build()
    return nc


def build_l1ffn():
    nc = bass.Bass("TRN2", target_bir_lowering=False)
    x_in = nc.dram_tensor("x", [1024 * NTOK], F32, kind="ExternalInput")
    y_out = nc.dram_tensor("y", [1024 * NTOK], F32, kind="ExternalOutput")
    W = {
        "wgT": nc.dram_tensor("wgT", [1024, 2816], BF16, kind="ExternalInput"),
        "wuT": nc.dram_tensor("wuT", [1024, 2816], BF16, kind="ExternalInput"),
        "wdT": nc.dram_tensor("wdT", [2816, 1024], BF16, kind="ExternalInput"),
    }
    p = Prog(nc)
    with contextlib.ExitStack() as es:
        consts = Consts(p, nc, es)
        emit_l1ffn(p, nc, es, consts, x_in, y_out, W, "a")
        p.build()
    return nc


def build_l1attn():
    nc = bass.Bass("TRN2", target_bir_lowering=False)
    x_in = nc.dram_tensor("x", [1024 * NTOK], F32, kind="ExternalInput")
    y_out = nc.dram_tensor("y", [1024 * NTOK], F32, kind="ExternalOutput")
    W = {}
    for nm in ("Wq", "Wk", "Wv", "Wo"):
        W[nm] = nc.dram_tensor(nm, [1024, 1024], F32, kind="ExternalInput")
    W["dist"] = nc.dram_tensor("dist", [2047 * 64], F32, kind="ExternalInput")
    for nm in ("bq", "bk", "bv", "bo", "gam", "bet"):
        W[nm] = nc.dram_tensor(nm, [128, 8], F32, kind="ExternalInput")
    scr = {
        "xs": nc.dram_tensor("scr_xs", [1024 * NTOK], F32),
        "qf": nc.dram_tensor("scr_qf", [1024 * NTOK], BF16),
        "kf": nc.dram_tensor("scr_kf", [1024 * NTOK], BF16),
        "vt": nc.dram_tensor("scr_vt", [NTOK * 1024], BF16),
        "cf": nc.dram_tensor("scr_cf", [1024 * NTOK], BF16),
        "qe": nc.dram_tensor("scr_qe", [16 * 128 * 2176], BF16),
        "ke": nc.dram_tensor("scr_ke", [16 * 128 * 2176], BF16),
    }
    p = Prog(nc)
    with contextlib.ExitStack() as es:
        consts = Consts(p, nc, es)
        emit_l1attn(p, nc, es, consts, x_in, y_out, W, scr, "a")
        p.build()
    return nc


def build_fused():
    nc = bass.Bass("TRN2", target_bir_lowering=False)
    x_in = nc.dram_tensor("x", [1024 * NTOK], F32, kind="ExternalInput")
    y_out = nc.dram_tensor("y", [1024 * NTOK], F32, kind="ExternalOutput")
    raw = {}
    for nm, shape in (("w1", (4096, 1024)), ("w2", (4096, 1024)), ("h1", (4096, 4096)),
                      ("h2", (4096, 4096)), ("w3", (1024, 4096)),
                      ("wg", (2816, 1024)), ("wu", (2816, 1024)), ("wd", (1024, 2816))):
        raw[nm] = nc.dram_tensor(nm, list(shape), F32, kind="ExternalInput")
    cache = {}
    for nm, shape in (("w1T", (1024, 4096)), ("w2T", (1024, 4096)), ("h1T", (4096, 4096)),
                      ("h2T", (4096, 4096)), ("w3T", (4096, 1024)),
                      ("wgT", (1024, 2816)), ("wuT", (1024, 2816)), ("wdT", (2816, 1024))):
        cache[nm] = nc.dram_tensor(nm, list(shape), BF16, kind="Internal")
    W0 = {nm: cache[nm] for nm in ("w1T", "w2T", "h1T", "h2T", "w3T")}
    for nm, shape in (("s1", (128, 32)), ("s2", (128, 32)), ("sh1", (128, 32)),
                      ("sh2", (128, 32)), ("s3", (128, 8))):
        W0[nm] = nc.dram_tensor(nm, list(shape), F32, kind="ExternalInput")
    W1 = {nm: cache[nm] for nm in ("wgT", "wuT", "wdT")}
    WA = {}
    for nm, shape in (("Wq", (1024, 1024)), ("Wk", (1024, 1024)), ("Wv", (1024, 2048)),
                      ("Wg", (1024, 2048)), ("Wab", (1024, 8)), ("Wo", (2048, 1024)),
                      ("W2", (1024, 1024))):
        WA[nm] = nc.dram_tensor("a" + nm, list(shape), F32, kind="ExternalInput")
    for nm, shape in (("ln1g", (128, 8)), ("ln1b", (128, 8)), ("ln2g", (128, 8)),
                      ("ln2b", (128, 8)), ("rmsw", (128, 8)), ("b2", (128, 8)),
                      ("onw", (128, 4)), ("cq", (128, 32)), ("ck", (128, 32)),
                      ("cv", (128, 64)), ("dtb", (4, 1)), ("alog", (4, 1))):
        WA[nm] = nc.dram_tensor("a" + nm, list(shape), F32, kind="ExternalInput")
    scrA = {}
    for nm, size, dt in (("xsub", 1024 * NTOK, F32), ("qraw", 1024 * NTOK, BF16),
                         ("kraw", 1024 * NTOK, BF16), ("vraw", 2048 * NTOK, BF16),
                         ("gate", 2048 * NTOK, BF16), ("qn", 1024 * NTOK, BF16),
                         ("kn", 1024 * NTOK, BF16), ("vv", 2048 * NTOK, BF16),
                         ("ab", 8 * NTOK, F32), ("Bsc", 4 * 2 * (T + 1), F32),
                         ("beta", 4 * NTOK, F32), ("osc", 2048 * NTOK, BF16)):
        scrA[nm] = nc.dram_tensor("ascr_" + nm, [size], dt, kind="Internal")
    WB = {}
    for nm in ("Wq", "Wk", "Wv", "Wo"):
        WB[nm] = nc.dram_tensor("b" + nm, [1024, 1024], F32, kind="ExternalInput")
    WB["dist"] = nc.dram_tensor("bdist", [2047 * 64], F32, kind="ExternalInput")
    for nm in ("bq", "bk", "bv", "bo", "gam", "bet"):
        WB[nm] = nc.dram_tensor("b" + nm, [128, 8], F32, kind="ExternalInput")
    scrB = {
        "xs": nc.dram_tensor("bscr_xs", [1024 * NTOK], F32),
        "qf": nc.dram_tensor("bscr_qf", [1024 * NTOK], BF16),
        "kf": nc.dram_tensor("bscr_kf", [1024 * NTOK], BF16),
        "vt": nc.dram_tensor("bscr_vt", [NTOK * 1024], BF16),
        "cf": nc.dram_tensor("bscr_cf", [1024 * NTOK], BF16),
        "qe": nc.dram_tensor("bscr_qe", [16 * 128 * 2176], BF16),
        "ke": nc.dram_tensor("bscr_ke", [16 * 128 * 2176], BF16),
    }
    r = [nc.dram_tensor(f"r{i}", [1024 * NTOK], F32, kind="Internal") for i in range(5)]

    p = Prog(nc)
    with contextlib.ExitStack() as esC:
        consts = Consts(p, nc, esC)
        with contextlib.ExitStack() as es:
            emit_wprep(p, nc, es, consts,
                       [(raw[a], cache[a + "T"]) for a in ("w1", "w2", "h1", "h2", "w3")], "p")
            p.barrier("fz0")
            emit_l1wprep(p, nc, es, consts,
                         [(raw[a], cache[a + "T"]) for a in ("wg", "wu", "wd")], "p")
        p.barrier("fz1")
        with contextlib.ExitStack() as es:
            emit_l0ffn(p, nc, es, consts, x_in, r[0], W0, "f0")
        p.barrier("fz2")
        with contextlib.ExitStack() as es:
            emit_l0attn(p, nc, es, consts, r[0], r[1], WA, scrA, "fa")
        p.barrier("fz3")
        with contextlib.ExitStack() as es:
            emit_l0ffn(p, nc, es, consts, r[1], r[2], W0, "f1")
        p.barrier("fz4")
        with contextlib.ExitStack() as es:
            emit_l1ffn(p, nc, es, consts, r[2], r[3], W1, "g0")
        p.barrier("fz5")
        with contextlib.ExitStack() as es:
            emit_l1attn(p, nc, es, consts, r[3], r[4], WB, scrB, "ga")
        p.barrier("fz6")
        with contextlib.ExitStack() as es:
            emit_l1ffn(p, nc, es, consts, r[4], y_out, W1, "g1")
        p.barrier("fz7")
        p.wait("vector", [])
        p.build()
    return nc


def _scale_cols32(s):
    return np.ascontiguousarray(np.asarray(s, np.float32).reshape(-1, 128).T)


def _conv_cols(w):
    w = np.asarray(w, np.float32)
    nch = w.shape[0] // 128
    return np.ascontiguousarray(w.reshape(nch, 128, 4).transpose(1, 0, 2).reshape(128, nch * 4))


def kernel(**inputs):
    N = NCORES
    g = lambda k: np.asarray(inputs[k], dtype=np.float32)

    x = g("L0_attn_x")
    xsh = [np.ascontiguousarray(x[2 * c:2 * c + 2].reshape(NTOK, D).T).reshape(-1)
           for c in range(N)]

    nc = build_fused()
    common = {
        "w1": g("L0_ffn_w1_w"), "w2": g("L0_ffn_w2_w"), "h1": g("L0_ffn_h1_w"),
        "h2": g("L0_ffn_h2_w"), "w3": g("L0_ffn_w3_w"),
        "wg": g("L1_ffn_w_gate"), "wu": g("L1_ffn_w_up"), "wd": g("L1_ffn_w_down"),
        "s1": _scale_cols32(g("L0_ffn_w1_s")), "s2": _scale_cols32(g("L0_ffn_w2_s")),
        "sh1": _scale_cols32(g("L0_ffn_h1_s")), "sh2": _scale_cols32(g("L0_ffn_h2_s")),
        "s3": _scale_cols32(g("L0_ffn_w3_s")),
        "aWq": g("L0_attn_Wq"), "aWk": g("L0_attn_Wk"), "aWv": g("L0_attn_Wv"),
        "aWg": g("L0_attn_Wg"),
        "aWab": np.ascontiguousarray(np.concatenate(
            [g("L0_attn_Wa"), g("L0_attn_Wb")], axis=1)),
        "aWo": g("L0_attn_Wo"), "aW2": g("L0_attn_W2"),
        "aln1g": _scale_cols32(g("L0_attn_ln1_g")), "aln1b": _scale_cols32(g("L0_attn_ln1_b")),
        "aln2g": _scale_cols32(g("L0_attn_ln2_g")), "aln2b": _scale_cols32(g("L0_attn_ln2_b")),
        "armsw": _scale_cols32(g("L0_attn_rms_in_w")), "ab2": _scale_cols32(g("L0_attn_b2")),
        "aonw": _scale_cols32(g("L0_attn_o_norm_w")),
        "acq": _conv_cols(g("L0_attn_conv_q_w")), "ack": _conv_cols(g("L0_attn_conv_k_w")),
        "acv": _conv_cols(g("L0_attn_conv_v_w")),
        "adtb": np.ascontiguousarray(g("L0_attn_dt_bias").reshape(4, 1)),
        "aalog": np.ascontiguousarray(g("L0_attn_A_log").reshape(4, 1)),
        "bWq": g("L1_attn_Wq"), "bWk": g("L1_attn_Wk"), "bWv": g("L1_attn_Wv"),
        "bWo": g("L1_attn_Wo"), "bdist": g("L1_attn_dist_emb").reshape(-1),
        "bbq": _scale_cols32(g("L1_attn_bq")), "bbk": _scale_cols32(g("L1_attn_bk")),
        "bbv": _scale_cols32(g("L1_attn_bv")), "bbo": _scale_cols32(g("L1_attn_bo")),
        "bgam": _scale_cols32(g("L1_attn_ln_gamma")),
        "bbet": _scale_cols32(g("L1_attn_ln_beta")),
    }
    maps = [dict(common, x=xsh[c]) for c in range(N)]
    r = run_bass_kernel_spmd(nc, maps, core_ids=list(range(N))).results
    out = np.empty((16, 2048, 1024), dtype=np.float32)
    for c in range(N):
        out[2 * c:2 * c + 2] = r[c]["y"].reshape(1024, NTOK).T.reshape(2, 2048, 1024)
    return out


def kernel_multi(**inputs):
    N = NCORES
    g = lambda k: np.asarray(inputs[k], dtype=np.float32)

    x = g("L0_attn_x")
    xsh = [np.ascontiguousarray(x[2 * c:2 * c + 2].reshape(NTOK, D).T).reshape(-1)
           for c in range(N)]

    ncA = build_prep()
    inA = {
        "w1": g("L0_ffn_w1_w"), "w2": g("L0_ffn_w2_w"), "h1": g("L0_ffn_h1_w"),
        "h2": g("L0_ffn_h2_w"), "w3": g("L0_ffn_w3_w"),
        "wg": g("L1_ffn_w_gate"), "wu": g("L1_ffn_w_up"), "wd": g("L1_ffn_w_down"),
    }
    resA = run_bass_kernel_spmd(ncA, [inA] * N, core_ids=list(range(N))).results

    ncB = build_l0ffn()
    l0ffn_w = {
        "s1": _scale_cols32(g("L0_ffn_w1_s")), "s2": _scale_cols32(g("L0_ffn_w2_s")),
        "sh1": _scale_cols32(g("L0_ffn_h1_s")), "sh2": _scale_cols32(g("L0_ffn_h2_s")),
        "s3": _scale_cols32(g("L0_ffn_w3_s")),
    }

    def run_l0ffn(xs):
        maps = []
        for c in range(N):
            m = dict(l0ffn_w)
            for nm in ("w1T", "w2T", "h1T", "h2T", "w3T"):
                m[nm] = resA[c][nm]
            m["x"] = xs[c]
            maps.append(m)
        r = run_bass_kernel_spmd(ncB, maps, core_ids=list(range(N))).results
        return [r[c]["y"] for c in range(N)]

    y = run_l0ffn(xsh)

    ncC = build_l0attn()
    inC = {
        "Wq": g("L0_attn_Wq"), "Wk": g("L0_attn_Wk"), "Wv": g("L0_attn_Wv"),
        "Wg": g("L0_attn_Wg"),
        "Wab": np.ascontiguousarray(np.concatenate([g("L0_attn_Wa"), g("L0_attn_Wb")], axis=1)),
        "Wo": g("L0_attn_Wo"), "W2": g("L0_attn_W2"),
        "ln1g": _scale_cols32(g("L0_attn_ln1_g")), "ln1b": _scale_cols32(g("L0_attn_ln1_b")),
        "ln2g": _scale_cols32(g("L0_attn_ln2_g")), "ln2b": _scale_cols32(g("L0_attn_ln2_b")),
        "rmsw": _scale_cols32(g("L0_attn_rms_in_w")), "b2": _scale_cols32(g("L0_attn_b2")),
        "onw": _scale_cols32(g("L0_attn_o_norm_w")),
        "cq": _conv_cols(g("L0_attn_conv_q_w")), "ck": _conv_cols(g("L0_attn_conv_k_w")),
        "cv": _conv_cols(g("L0_attn_conv_v_w")),
        "dtb": np.ascontiguousarray(g("L0_attn_dt_bias").reshape(4, 1)),
        "alog": np.ascontiguousarray(g("L0_attn_A_log").reshape(4, 1)),
    }
    maps = [dict(inC, x=y[c]) for c in range(N)]
    r = run_bass_kernel_spmd(ncC, maps, core_ids=list(range(N))).results
    y = [r[c]["y"] for c in range(N)]

    y = run_l0ffn(y)

    ncE = build_l1ffn()

    def run_l1ffn(xs):
        maps = []
        for c in range(N):
            m = {"wgT": resA[c]["wgT"], "wuT": resA[c]["wuT"], "wdT": resA[c]["wdT"],
                 "x": xs[c]}
            maps.append(m)
        r = run_bass_kernel_spmd(ncE, maps, core_ids=list(range(N))).results
        return [r[c]["y"] for c in range(N)]

    y = run_l1ffn(y)

    ncF = build_l1attn()
    inF = {
        "Wq": g("L1_attn_Wq"), "Wk": g("L1_attn_Wk"), "Wv": g("L1_attn_Wv"),
        "Wo": g("L1_attn_Wo"), "dist": g("L1_attn_dist_emb").reshape(-1),
        "bq": _scale_cols32(g("L1_attn_bq")), "bk": _scale_cols32(g("L1_attn_bk")),
        "bv": _scale_cols32(g("L1_attn_bv")), "bo": _scale_cols32(g("L1_attn_bo")),
        "gam": _scale_cols32(g("L1_attn_ln_gamma")), "bet": _scale_cols32(g("L1_attn_ln_beta")),
    }
    maps = [dict(inF, x=y[c]) for c in range(N)]
    r = run_bass_kernel_spmd(ncF, maps, core_ids=list(range(N))).results
    y = [r[c]["y"] for c in range(N)]

    y = run_l1ffn(y)

    out = np.empty((16, 2048, 1024), dtype=np.float32)
    for c in range(N):
        out[2 * c:2 * c + 2] = y[c].reshape(1024, NTOK).T.reshape(2, 2048, 1024)
    return out


_RNE_C = 12582912.0


def make_quant_scratch(p, nc, es, Fmax, tag):
    qs = {
        "red": es.enter_context(nc.sbuf_tensor(f"q_red{tag}", [128, Fmax // 16], F32)),
        "inv": es.enter_context(nc.sbuf_tensor(f"q_inv{tag}", [128, Fmax // 16], F32)),
        "u": es.enter_context(nc.sbuf_tensor(f"q_u{tag}", [128, Fmax], F32)),
        "sgn": es.enter_context(nc.sbuf_tensor(f"q_sgn{tag}", [128, Fmax], F32)),
        "r1": es.enter_context(nc.sbuf_tensor(f"q_r1{tag}", [128, Fmax], F32)),
        "r2": es.enter_context(nc.sbuf_tensor(f"q_r2{tag}", [128, Fmax], F32)),
        "m1": es.enter_context(nc.sbuf_tensor(f"q_m1{tag}", [128, Fmax], U8)),
        "m2": es.enter_context(nc.sbuf_tensor(f"q_m2{tag}", [128, Fmax], U8)),
        "dv": p.sem(f"q_dv{tag}"),
        "gp": p.sem(f"q_gp{tag}"),
        "ac": p.sem(f"q_ac{tag}"),
        "Fmax": Fmax,
    }
    return qs


def emit_quant(p, qs, x_ap, out_ap, F, in_ready, out_waits=()):
    nb = F // 16
    dv, gp, ac = qs["dv"], qs["gp"], qs["ac"]
    red = qs["red"].ap()[:, 0:nb]
    inv = qs["inv"].ap()[:, 0:nb]
    u = qs["u"].ap()[:, 0:F]
    sgn = qs["sgn"].ap()[:, 0:F]
    r1 = qs["r1"].ap()[:, 0:F]
    r2 = qs["r2"].ap()[:, 0:F]
    m1 = qs["m1"].ap()[:, 0:F]
    m2 = qs["m2"].ap()[:, 0:F]
    x3 = x_ap.rearrange("p (b k) -> p b k", k=16)

    v0 = p.comp("vector", lambda v: v.reduce_max(red.unsqueeze(2), x3, mybir.AxisListType.X,
                                                 apply_absolute_value=True),
                waits=in_ready, inc=dv)
    v1 = p.comp("vector", lambda v: v.tensor_scalar(red, red, 1.0 / 6.0, 1e-12, OP.mult, OP.max),
                waits=[(dv, v0)], inc=dv)
    dv_inv = p.comp("vector", lambda v: v.reciprocal(inv, red), waits=[(dv, v1)], inc=dv)
    dv_xn = p.comp("vector", lambda v: v.tensor_tensor(
        x3, x3, inv.unsqueeze(2).broadcast_to([128, nb, 16]), OP.mult),
        waits=[(dv, dv_inv)], inc=dv)
    ac_u = p.comp("scalar", lambda s: s.activation(u, x_ap, AF.Abs), waits=[(dv, dv_xn)], inc=ac)
    ac_s = p.comp("scalar", lambda s: s.activation(sgn, x_ap, AF.Sign), inc=ac)
    p.comp("vector", lambda v: v.tensor_scalar(r1, x_ap, 2.0, _RNE_C, OP.mult, OP.add),
           waits=[(dv, dv_xn)])
    p.comp("vector", lambda v: v.tensor_scalar(r1, r1, _RNE_C, 4.0, OP.subtract, OP.min))
    gp_r1 = p.comp("vector", lambda v: v.tensor_scalar(r1, r1, -4.0, 0.5, OP.max, OP.mult), inc=gp)
    p.comp("gpsimd", lambda g: g.tensor_scalar(r2, x_ap, _RNE_C, None, OP.add),
           waits=[(dv, dv_xn)])
    p.comp("gpsimd", lambda g: g.tensor_scalar(r2, r2, _RNE_C, 4.0, OP.subtract, OP.min))
    gp_r2 = p.comp("gpsimd", lambda g: g.tensor_scalar(r2, r2, -4.0, None, OP.max), inc=gp)
    p.comp("vector", lambda v: v.tensor_scalar(m1, u, 2.5, None, OP.is_gt), waits=[(ac, ac_u)])
    p.comp("vector", lambda v: v.tensor_scalar(m2, u, 5.0, None, OP.is_gt))
    p.comp("vector", lambda v: v.select(r1, m1, r2, r1), waits=[(gp, gp_r2)])
    p.comp("vector", lambda v: v.tensor_scalar(sgn, sgn, 6.0, None, OP.mult),
           waits=[(ac, ac_s)])
    p.comp("vector", lambda v: v.select(r1, m2, sgn, r1))
    last = p.comp("vector", lambda v: v.tensor_tensor(
        out_ap.rearrange("p (b k) -> p b k", k=16),
        r1.rearrange("p (b k) -> p b k", k=16),
        red.unsqueeze(2).broadcast_to([128, nb, 16]), OP.mult),
        waits=list(out_waits), inc=dv)
    return dv, last


def emit_l1wprep(p, nc, es, consts, jobs, tag):
    maxK = max(src.shape[1] for src, _ in jobs)
    maxN = max(src.shape[0] for src, _ in jobs)
    s_row = es.enter_context(nc.sbuf_tensor(f"l1p_row{tag}", [128, maxK], F32))
    s_wq = es.enter_context(nc.sbuf_tensor(f"l1p_wq{tag}", [128, (maxN // 128) * maxK], BF16))
    s_strip = es.enter_context(nc.sbuf_tensor(f"l1p_strip{tag}", [128, maxN], BF16))
    ps = [es.enter_context(nc.psum_tensor(f"l1p_ps{tag}{i}", [128, 128], BF16)) for i in range(2)]
    qs = make_quant_scratch(p, nc, es, maxK, "l1p" + tag)
    s_ld = p.sem(f"l1p_ld{tag}")
    s_mm = p.sem(f"l1p_mm{tag}")
    s_ev = p.sem(f"l1p_ev{tag}")
    s_st = p.sem(f"l1p_st{tag}")

    for src, dst in jobs:
        N, K = src.shape
        nb = N // 128
        wq3 = s_wq.ap()[:, 0:nb * K].rearrange("p (b k) -> p b k", k=K)
        for b in range(nb):
            lv = p.dma("gpsimd", lambda g, src=src, b=b, K=K: g.dma_start(
                out=s_row[:, 0:K], in_=src.ap()[b * 128:(b + 1) * 128, :]),
                waits=[(qs["dv"], qs["dv"].count)], inc=s_ld)
            emit_quant(p, qs, s_row.ap()[:, 0:K], wq3[:, b, :], K,
                       [(s_ld, lv)])
        qdone = (qs["dv"], qs["dv"].count)
        k = 0
        for ic in range(K // 128):
            st_guard = s_st.count
            for b in range(nb):
                buf = k % 2
                w = [qdone] if k == 0 else [(s_mm, s_mm.count)]
                mv = p.comp("tensor", lambda t, buf=buf, b=b, ic=ic, wq3=wq3: t.transpose(
                    ps[buf][:, :], wq3[:, b, ic * 128:(ic + 1) * 128], consts.ident_bf[:, :]),
                    waits=w + ([(s_ev, s_ev.count - 1)] if k >= 2 else []), inc=s_mm)
                ev_w = [(s_mm, mv)]
                if b == 0:
                    ev_w.append((s_st, st_guard))
                p.comp("vector", lambda v, buf=buf, b=b: v.tensor_copy(
                    s_strip[:, b * 128:(b + 1) * 128], ps[buf][:, :]),
                    waits=ev_w, inc=s_ev)
                k += 1
            p.dma("sync", lambda sy, dst=dst, ic=ic, N=N: sy.dma_start(
                out=dst.ap()[ic * 128:(ic + 1) * 128, :], in_=s_strip[:, 0:N]),
                waits=[(s_ev, s_ev.count)], inc=s_st)
    p.wait("sync", [(s_st, s_st.count)])
    return s_st


def emit_l1ffn(p, nc, es, consts, x_in, y_out, W, tag):
    NI = 2816
    NT = NTOK // 128
    FQ = 704

    s_wg = es.enter_context(nc.sbuf_tensor(f"l1_wg{tag}", [128, 8, NI], BF16))
    s_wu = es.enter_context(nc.sbuf_tensor(f"l1_wu{tag}", [128, 8, NI], BF16))
    s_wd = es.enter_context(nc.sbuf_tensor(f"l1_wd{tag}", [128, 22, D], BF16))
    s_xf = [es.enter_context(nc.sbuf_tensor(f"l1_xf{tag}{i}", [128, 8, 128], F32)) for i in range(2)]
    s_xtm = [es.enter_context(nc.sbuf_tensor(f"l1_xtm{tag}{i}", [128, D], F32)) for i in range(2)]
    s_xq = es.enter_context(nc.sbuf_tensor(f"l1_xq{tag}", [128, D], BF16))
    s_xqf = es.enter_context(nc.sbuf_tensor(f"l1_xqf{tag}", [128, 8, 128], BF16))
    s_htm = es.enter_context(nc.sbuf_tensor(f"l1_htm{tag}", [128, NI], F32))
    s_hq = es.enter_context(nc.sbuf_tensor(f"l1_hq{tag}", [128, NI], BF16))
    s_hqf = es.enter_context(nc.sbuf_tensor(f"l1_hqf{tag}", [128, 22, 128], BF16))
    s_ytm = es.enter_context(nc.sbuf_tensor(f"l1_ytm{tag}", [128, D], F32))
    s_g = es.enter_context(nc.sbuf_tensor(f"l1_g{tag}", [128, 512], F32))
    s_col = [es.enter_context(nc.sbuf_tensor(f"l1_col{i}{tag}", [128, 1], F32)) for i in range(3)]
    s_sq = es.enter_context(nc.sbuf_tensor(f"l1_sq{tag}", [128, D], F32))
    ident_f = es.enter_context(nc.sbuf_tensor(f"l1_idf{tag}", [128, 128], F32))
    qs = make_quant_scratch(p, nc, es, FQ, "l1" + tag)
    ps_t = [es.enter_context(nc.psum_tensor(f"l1_pst{tag}{i}", [128, 128], F32)) for i in range(2)]
    ps_tb = [es.enter_context(nc.psum_tensor(f"l1_ptb{tag}{i}", [128, 128], BF16)) for i in range(2)]
    ps_g = es.enter_context(nc.psum_tensor(f"l1_psg{tag}", [128, 512], F32))
    ps_u = es.enter_context(nc.psum_tensor(f"l1_psu{tag}", [128, 512], F32))

    s_ld = p.sem(f"l1_ld{tag}")
    s_mm = p.sem(f"l1_mm{tag}")
    s_ev = p.sem(f"l1_ev{tag}")
    s_dv = p.sem(f"l1_dv{tag}")
    s_ac = p.sem(f"l1_ac{tag}")
    s_st = p.sem(f"l1_st{tag}")

    idf_sem = p.sem(f"l1_idf_s{tag}")
    idf_done = p.comp("gpsimd", lambda g: g.tensor_copy(ident_f[:, :], consts.ident_bf[:, :]),
                      waits=[(consts.sem, consts.ready)], inc=idf_sem)

    wld_done = None
    for dst, key in ((s_wg, "wgT"), (s_wu, "wuT"), (s_wd, "wdT")):
        nch = dst.shape[1]
        wld_done = p.dma("gpsimd", lambda g, dst=dst, key=key, nch=nch: g.dma_start(
            out=dst[:, :, :],
            in_=bass.AP(W[key], 0, [[W[key].shape[1], 128],
                                    [128 * W[key].shape[1], nch],
                                    [1, W[key].shape[1]]])), inc=s_ld)

    def transpose_set(n, src_ap_fn, dst_ap_fn, psl, ident, waits):
        evs = []
        ev_entry = s_ev.count
        for c in range(n):
            buf = c % 2
            w = list(waits) if c == 0 else []
            w.append((s_ev, evs[c - 2]) if c >= 2 else (s_ev, ev_entry))
            mv = p.comp("tensor", lambda t, c=c, buf=buf: t.transpose(
                psl[buf][:, :], src_ap_fn(c), ident[:, :]), waits=w, inc=s_mm)
            ev = p.comp("vector",
                        lambda v, c=c, buf=buf: v.tensor_copy(dst_ap_fn(c), psl[buf][:, :]),
                        waits=[(s_mm, mv)], inc=s_ev)
            evs.append(ev)
        return evs[-1]

    prev = {}
    pv = lambda key: [prev[key]] if key in prev else []
    for ti in range(NT):
        par = ti % 2
        xf, xtm = s_xf[par], s_xtm[par]

        lw = []
        if ti >= 2:
            lw.append(prev[("xf_free", par)])
        xv = p.dma("gpsimd", lambda g, ti=ti, xf=xf: g.dma_start(
            out=xf[:, :, :],
            in_=bass.AP(x_in, ti * 128, [[NTOK, 128], [128 * NTOK, 8], [1, 128]])),
            waits=lw, inc=s_ld)

        tw = [(s_ld, xv), (idf_sem, idf_done)]
        if ti >= 2:
            tw.append(prev[("xtm_free", par)])
        tv = transpose_set(8, lambda c, xf=xf: xf[:, c, :],
                           lambda c, xtm=xtm: xtm[:, c * 128:(c + 1) * 128],
                           ps_t, ident_f, tw)
        prev[("xf_free", par)] = (s_mm, s_mm.count)

        lnw = [(s_ev, tv)] + pv("col_free") + pv("sq_free")
        a1 = p.comp("scalar", lambda s, xtm=xtm: s.activation(
            s_sq[:, :], xtm[:, :], AF.Copy, accum_out=s_col[0][:, :]),
            waits=lnw, inc=s_ac)
        a2 = p.comp("scalar", lambda s, xtm=xtm: s.activation(
            s_sq[:, :], xtm[:, :], AF.Square, accum_out=s_col[1][:, :]), inc=s_ac)
        c0 = p.comp("vector", lambda v: v.tensor_scalar(s_col[0][:, :], s_col[0][:, :], 1.0 / D, None, OP.mult),
                    waits=[(s_ac, a2)], inc=s_dv)
        c1 = p.comp("vector", lambda v: v.tensor_scalar(s_col[1][:, :], s_col[1][:, :], 1.0 / D, None, OP.mult),
                    waits=[(s_dv, c0)], inc=s_dv)
        c2 = p.comp("vector", lambda v: v.tensor_tensor(s_col[2][:, :], s_col[0][:, :], s_col[0][:, :], OP.mult),
                    waits=[(s_dv, c1)], inc=s_dv)
        d1 = p.comp("vector", lambda v: v.scalar_tensor_tensor(
            s_col[1][:, :], s_col[1][:, :], 1e-5, s_col[2][:, :], OP.add, OP.subtract),
            waits=[(s_dv, c2)], inc=s_dv)
        l1 = p.comp("scalar", lambda s: s.activation(s_col[1][:, :], s_col[1][:, :], AF.Ln),
                    waits=[(s_dv, d1)], inc=s_ac)
        a3 = p.comp("scalar", lambda s: s.activation(s_col[1][:, :], s_col[1][:, :], AF.Exp, scale=-0.5),
                    waits=[(s_ac, l1)], inc=s_ac)
        d2 = p.comp("vector", lambda v, xtm=xtm: v.tensor_scalar(
            s_sq[:, :], xtm[:, :], s_col[0][:, :], s_col[1][:, :], OP.subtract, OP.mult),
            waits=[(s_ac, a3)], inc=s_dv)
        prev["col_free"] = (s_dv, d2)

        qv_pair = (s_dv, d2)
        for h in range(2):
            qsem, qv = emit_quant(p, qs, s_sq.ap()[:, h * 512:(h + 1) * 512],
                                  s_xq.ap()[:, h * 512:(h + 1) * 512], 512,
                                  [qv_pair], out_waits=pv(("xq_free",)) if h == 0 else ())
            qv_pair = (qsem, qv)
        prev["sq_free"] = qv_pair

        tw = [qv_pair, (consts.sem, consts.ready)] + pv(("xqf_free",))
        transpose_set(8, lambda c: s_xq[:, c * 128:(c + 1) * 128],
                      lambda c: s_xqf[:, c, :],
                      ps_tb, consts.ident_bf, tw)
        prev[("xq_free",)] = (s_mm, s_mm.count)
        xq_t_done = (s_ev, s_ev.count)

        mw = [xq_t_done, (s_ld, wld_done)] + pv("gu_psum_free")
        nfull = NI // 512
        chunks = [(i * 512, 512) for i in range(nfull)] + ([(nfull * 512, NI - nfull * 512)] if NI % 512 else [])
        ev = hv = None
        htm_guard = pv("htm_free")
        for (n0, nw) in chunks:
            gw = list(mw) if n0 == 0 else [(s_ev, ev)]
            for kc in range(8):
                p.comp("tensor", lambda t, kc=kc, n0=n0, nw=nw: t.matmul(
                    ps_g[:, 0:nw], s_xqf[:, kc, :], s_wg[:, kc, n0:n0 + nw],
                    start=(kc == 0), stop=(kc == 7), skip_group_check=True),
                    waits=gw if kc == 0 else [])
            uw = [] if n0 == 0 else [(s_dv, hv)]
            for kc in range(8):
                p.comp("tensor", lambda t, kc=kc, n0=n0, nw=nw: t.matmul(
                    ps_u[:, 0:nw], s_xqf[:, kc, :], s_wu[:, kc, n0:n0 + nw],
                    start=(kc == 0), stop=(kc == 7), skip_group_check=True),
                    waits=uw if kc == 0 else [], inc=(s_mm if kc == 7 else None))
            mmv = s_mm.count
            sw = [(s_mm, mmv)]
            if hv is not None:
                sw.append((s_dv, hv))
            ev = p.comp("scalar", lambda s, n0=n0, nw=nw: s.activation(
                s_g[:, 0:nw], ps_g[:, 0:nw], AF.Silu), waits=sw, inc=s_ev)
            hvw = [(s_ev, ev)] + (htm_guard if n0 == 0 else [])
            hv = p.comp("vector", lambda v, n0=n0, nw=nw: v.tensor_tensor(
                s_htm[:, n0:n0 + nw], s_g[:, 0:nw], ps_u[:, 0:nw], OP.mult),
                waits=hvw, inc=s_dv)
        h_done = (s_dv, s_dv.count)
        prev["gu_psum_free"] = h_done

        qv_pair = h_done
        for hqi in range(4):
            qsem, qv = emit_quant(p, qs, s_htm.ap()[:, hqi * FQ:(hqi + 1) * FQ],
                                  s_hq.ap()[:, hqi * FQ:(hqi + 1) * FQ], FQ,
                                  [qv_pair], out_waits=pv(("hq_free",)) if hqi == 0 else ())
            qv_pair = (qsem, qv)
        prev["htm_free"] = (qs["gp"], qs["gp"].count)

        tw = [qv_pair] + pv(("hqf_free",))
        transpose_set(22, lambda c: s_hq[:, c * 128:(c + 1) * 128],
                      lambda c: s_hqf[:, c, :],
                      ps_tb, consts.ident_bf, tw)
        prev[("hq_free",)] = (s_mm, s_mm.count)
        hq_t_done = (s_ev, s_ev.count)

        rv = None
        for half in range(2):
            n0 = half * 512
            dw = ([hq_t_done] + pv("down_psum_free")) if half == 0 else [(s_dv, rv)]
            for kc in range(22):
                p.comp("tensor", lambda t, kc=kc, n0=n0: t.matmul(
                    ps_g[:, :], s_hqf[:, kc, :], s_wd[:, kc, n0:n0 + 512],
                    start=(kc == 0), stop=(kc == 21), skip_group_check=True),
                    waits=dw if kc == 0 else [],
                    inc=(s_mm if kc == 21 else None))
            mmv = s_mm.count
            rw = [(s_mm, mmv)] + (pv("ytm_free") if half == 0 else [])
            rv = p.comp("vector", lambda v, n0=n0, xtm=xtm: v.scalar_tensor_tensor(
                s_ytm[:, n0:n0 + 512], ps_g[:, :], 0.5, xtm[:, n0:n0 + 512], OP.mult, OP.add),
                waits=rw, inc=s_dv)
        y_done = (s_dv, s_dv.count)
        prev[("xtm_free", par)] = y_done
        prev["down_psum_free"] = y_done
        prev["gu_psum_free"] = y_done
        prev[("hqf_free",)] = (s_mm, s_mm.count)

        tv = transpose_set(8, lambda c: s_ytm[:, c * 128:(c + 1) * 128],
                           lambda c, xf=xf: xf[:, c, :],
                           ps_t, ident_f, [y_done])
        prev["ytm_free"] = (s_mm, s_mm.count)
        stv = p.dma("gpsimd", lambda g, ti=ti, xf=xf: g.dma_start(
            out=bass.AP(y_out, ti * 128, [[NTOK, 128], [128 * NTOK, 8], [1, 128]]),
            in_=xf[:, :, :]),
            waits=[(s_ev, tv)], inc=s_st)
        prev[("xf_free", par)] = (s_st, stv)

    p.wait("gpsimd", [(s_st, s_st.count)])
    return s_st



def emit_l1attn(p, nc, es, consts, x_in, y_out, W, scr, tag):
    SW = 2176
    Tg = 512
    NG = NTOK // Tg
    STRIDE_QE = 128 * SW

    s_wq = es.enter_context(nc.sbuf_tensor(f"la_wq{tag}", [128, 8, D], BF16))
    s_wk = es.enter_context(nc.sbuf_tensor(f"la_wk{tag}", [128, 8, D], BF16))
    s_wv = es.enter_context(nc.sbuf_tensor(f"la_wv{tag}", [128, 8, D], BF16))
    s_wo = es.enter_context(nc.sbuf_tensor(f"la_wo{tag}", [128, 8, D], BF16))
    s_bcols = {}
    for key in ("bq", "bk", "bv", "bo", "gam", "bet"):
        s_bcols[key] = es.enter_context(nc.sbuf_tensor(f"la_{key}{tag}", [128, 8], F32))
    s_drev = es.enter_context(nc.sbuf_tensor(f"la_drev{tag}", [64, 2047], BF16))
    s_dnat = es.enter_context(nc.sbuf_tensor(f"la_dnat{tag}", [64, 2047], BF16))
    s_scal = es.enter_context(nc.sbuf_tensor(f"la_sc{tag}", [1, 8], F32))

    s_ld = p.sem(f"la_ld{tag}")
    s_mm = p.sem(f"la_mm{tag}")
    s_ev = p.sem(f"la_ev{tag}")
    s_dv = p.sem(f"la_dv{tag}")
    s_gp = p.sem(f"la_gp{tag}")
    s_st = p.sem(f"la_st{tag}")

    for dst, key in ((s_wq, "Wq"), (s_wk, "Wk"), (s_wv, "Wv"), (s_wo, "Wo")):
        p.dma("gpsimd", lambda g, dst=dst, key=key: g.dma_start(
            out=dst[:, :, :],
            in_=bass.AP(W[key], 0, [[D, 128], [128 * D, 8], [1, D]])), inc=s_ld)
    for key in ("bq", "bk", "bv", "bo", "gam", "bet"):
        p.dma("gpsimd", lambda g, key=key: g.dma_start(
            out=s_bcols[key][:, :], in_=W[key].ap()), inc=s_ld)
    wld_done = s_ld.count

    with contextlib.ExitStack() as esp:
        s_dtile = esp.enter_context(nc.sbuf_tensor(f"la_dt{tag}", [128, 64], BF16))
        s_dsq = esp.enter_context(nc.sbuf_tensor(f"la_dsq{tag}", [128, 64], F32))
        s_dnorm = esp.enter_context(nc.sbuf_tensor(f"la_dn{tag}", [128, 16], F32))
        ps_dt = esp.enter_context(nc.psum_tensor(f"la_psdt{tag}", [64, 128], F32))
        ps_dn = esp.enter_context(nc.psum_tensor(f"la_psdn{tag}", [1, 16], F32))
        zs = p.sem(f"la_dz{tag}")
        p.comp("gpsimd", lambda g: g.memset(s_dnorm[:, :], 0.0))
        zv = p.comp("gpsimd", lambda g: g.tensor_copy(s_dnorm[:, 0:1], s_dnorm[:, 0:1]),
                    inc=zs)
        acc_prev = None
        for tile in range(16):
            n = 128 if tile < 15 else 127
            srcap = bass.AP(W["dist"], (128 * tile) * 64, [[64, n], [1, 64]])
            lv = p.dma("gpsimd", lambda g, srcap=srcap, n=n: g.dma_start(
                out=s_dtile[0:n, :], in_=srcap),
                waits=[(s_mm, s_mm.count), (s_ev, s_ev.count)], inc=s_ld)
            mv = p.comp("tensor", lambda t, n=n: t.matmul(
                ps_dt[:, 0:n], s_dtile[0:n, 0:64], consts.ident_bf[0:n, 0:n],
                start=True, stop=True, skip_group_check=True),
                waits=[(s_ld, lv), (consts.sem, consts.ready), (s_ev, s_ev.count)],
                inc=s_mm)
            p.comp("scalar", lambda s, tile=tile, n=n: s.activation(
                s_dnat[0:64, 128 * tile:128 * tile + n], ps_dt[0:64, 0:n], AF.Copy),
                waits=[(s_mm, mv)], inc=s_ev)
            anti = consts.anti128 if n == 128 else consts.anti127
            cc0 = 1919 - 128 * tile if n == 128 else 0
            mv = p.comp("tensor", lambda t, n=n, anti=anti: t.matmul(
                ps_dt[:, 0:n], s_dtile[0:n, 0:64], anti[0:n, 0:n],
                start=True, stop=True, skip_group_check=True), inc=s_mm)
            p.comp("scalar", lambda s, cc0=cc0, n=n: s.activation(
                s_drev[0:64, cc0:cc0 + n], ps_dt[0:64, 0:n], AF.Copy),
                waits=[(s_mm, mv)], inc=s_ev)
            w = [(s_ld, lv), (zs, zv)]
            if acc_prev is not None:
                w.append((s_ev, acc_prev))
            acc_prev = p.comp("scalar", lambda s, tile=tile, n=n: s.activation(
                s_dsq[0:n, :], s_dtile[0:n, :], AF.Square,
                accum_out=s_dnorm[0:n, tile:tile + 1]), waits=w, inc=s_ev)
        mv = p.comp("tensor", lambda t: t.matmul(ps_dn[:1, :], consts.ones_col[:, :],
                                                 s_dnorm[:, :], start=True, stop=True),
                    waits=[(s_ev, s_ev.count)], inc=s_mm)
        v0 = p.comp("vector", lambda v: v.reduce_sum(
            s_scal.ap()[0:1, 1:2].unsqueeze(2),
            ps_dn.ap()[0:1, 0:16].unsqueeze(1),
            mybir.AxisListType.X), waits=[(s_mm, mv)], inc=s_dv)
        av = p.comp("scalar", lambda s: s.activation(s_scal[0:1, 1:2], s_scal[0:1, 1:2],
                                                     AF.Sqrt), waits=[(s_dv, v0)], inc=s_ev)
        v0 = p.comp("vector", lambda v: v.tensor_scalar(s_scal[0:1, 1:2], s_scal[0:1, 1:2],
                                                        1.01, None, OP.mult),
                    waits=[(s_ev, av)], inc=s_dv)
        dmax_ready = (s_dv, v0)
    p.barrier(f"la_bar_prep{tag}")

    with contextlib.ExitStack() as esA:
        s_x = esA.enter_context(nc.sbuf_tensor(f"la_x{tag}", [128, 8, Tg], F32))
        s_xs = esA.enter_context(nc.sbuf_tensor(f"la_xsb{tag}", [128, 8, Tg], F32))
        s_xn = esA.enter_context(nc.sbuf_tensor(f"la_xn{tag}", [128, 8, Tg], BF16))
        s_qkv = esA.enter_context(nc.sbuf_tensor(f"la_qkv{tag}", [128, 8, Tg], BF16))
        s_vst = esA.enter_context(nc.sbuf_tensor(f"la_vst{tag}", [128, 4, D], BF16))
        ln = make_ln_ctx(p, nc, esA, consts, Tg, "la" + tag)
        ps_p = [esA.enter_context(nc.psum_tensor(f"la_pp{tag}{i}", [128, Tg], F32)) for i in range(2)]
        x3, xs3, xn3, qkv3 = s_x.ap(), s_xs.ap(), s_xn.ap(), s_qkv.ap()
        bankA_free = {}
        ln_last = None
        cast_last = None
        qkv_free = None
        vst_free = None
        for g in range(NG):
            xw = [] if ln_last is None else [(ln["sem_dv"], ln_last)]
            xv = p.dma("sync", lambda sy, g=g: sy.dma_start(
                out=s_x[:, :, :],
                in_=bass.AP(x_in, g * Tg, [[NTOK, 128], [128 * NTOK, 8], [1, Tg]])),
                waits=xw, inc=s_ld)
            wr = []
            if g > 0:
                wr = [(s_st, s_st.count), (s_gp, cast_last)]
            lnsem, lnval = emit_layernorm(p, ln, x3, xs3, [(s_ld, xv)], write_waits=wr)
            ln_last = lnval
            p.dma("sync", lambda sy, g=g: sy.dma_start(
                out=bass.AP(scr["xs"], g * Tg, [[NTOK, 128], [128 * NTOK, 8], [1, Tg]]),
                in_=s_xs[:, :, :]), waits=[(lnsem, lnval)], inc=s_st)
            cw = [(lnsem, lnval)]
            if g > 0:
                cw.append((s_mm, s_mm.count))
            cast_last = p.comp("gpsimd", lambda gg: gg.tensor_copy(s_xn[:, :, :], s_xs[:, :, :]),
                               waits=cw, inc=s_gp)
            for proj, wsb, bkey, dstd in (("q", s_wq, "bq", scr["qf"]), ("k", s_wk, "bk", scr["kf"])):
                for ot in range(8):
                    bi = ot % 2
                    for ic in range(8):
                        w = []
                        if ic == 0:
                            w = [(s_gp, cast_last), (s_ld, wld_done)]
                            if bi in bankA_free:
                                w.append(bankA_free[bi])
                        p.comp("tensor", lambda t, ic=ic, ot=ot, wsb=wsb, bi=bi: t.matmul(
                            ps_p[bi][:, :], wsb[:, ic, ot * 128:(ot + 1) * 128], xn3[:, ic, :],
                            start=(ic == 0), stop=(ic == 7), skip_group_check=True),
                            waits=w, inc=(s_mm if ic == 7 else None))
                    mmv = s_mm.count
                    ew = [(s_mm, mmv)]
                    if ot == 0 and qkv_free is not None:
                        ew.append(qkv_free)
                    e = p.comp("scalar", lambda s, ot=ot, bkey=bkey, bi=bi: s.activation(
                        qkv3[:, ot, :], ps_p[bi][:, :], AF.Identity,
                        bias=s_bcols[bkey][:, ot:ot + 1]),
                        waits=ew, inc=s_ev)
                    bankA_free[bi] = (s_ev, e)
                sv = p.dma("sync", lambda sy, g=g, dstd=dstd: sy.dma_start(
                    out=bass.AP(dstd, g * Tg, [[NTOK, 128], [128 * NTOK, 8], [1, Tg]]),
                    in_=s_qkv[:, :, :]), waits=[(s_ev, s_ev.count)], inc=s_st)
                qkv_free = (s_st, sv)
            for tt in range(Tg // 128):
                for half in range(2):
                    bi = half
                    for ic in range(8):
                        w = []
                        if ic == 0:
                            w = [(s_gp, cast_last)]
                            if bi in bankA_free:
                                w.append(bankA_free[bi])
                        p.comp("tensor", lambda t, ic=ic, tt=tt, half=half, bi=bi: t.matmul(
                            ps_p[bi][:, 0:512], xn3[:, ic, tt * 128:(tt + 1) * 128],
                            s_wv[:, ic, half * 512:(half + 1) * 512],
                            start=(ic == 0), stop=(ic == 7), skip_group_check=True),
                            waits=w, inc=(s_mm if ic == 7 else None))
                    mmv = s_mm.count
                    ew = [(s_mm, mmv)]
                    if tt == 0 and half == 0 and vst_free is not None:
                        ew.append(vst_free)
                    e = p.comp("scalar", lambda s, tt=tt, half=half, bi=bi: s.activation(
                        s_vst[:, tt, half * 512:(half + 1) * 512], ps_p[bi][:, 0:512],
                        AF.Copy), waits=ew, inc=s_ev)
                    bankA_free[bi] = (s_ev, e)
            sv = p.dma("sync", lambda sy, g=g: sy.dma_start(
                out=bass.AP(scr["vt"], (g * Tg) * D, [[D, 128], [128 * D, 4], [1, D]]),
                in_=s_vst[:, :, :]), waits=[(s_ev, s_ev.count)], inc=s_st)
            vst_free = (s_st, sv)
        phaseA_st = s_st.count
        phaseA_mm = s_mm.count
    p.barrier(f"la_bar_a{tag}")

    with contextlib.ExitStack() as esB:
        s_qh = esB.enter_context(nc.sbuf_tensor(f"la_qh{tag}", [64, 2048], BF16))
        s_kh = esB.enter_context(nc.sbuf_tensor(f"la_kh{tag}", [64, 2048], BF16))
        s_vh = esB.enter_context(nc.sbuf_tensor(f"la_vh{tag}", [128, 16, 64], BF16))
        s_rkT = esB.enter_context(nc.sbuf_tensor(f"la_rkT{tag}", [128, 16, 2048], BF16))
        s_asm = [esB.enter_context(nc.sbuf_tensor(f"la_asm{tag}{i}", [128, SW], BF16)) for i in range(2)]
        s_rq = esB.enter_context(nc.sbuf_tensor(f"la_rq{tag}", [128, 2048], BF16))
        s_p = esB.enter_context(nc.sbuf_tensor(f"la_p{tag}", [128, 2048], BF16))
        s_pT = esB.enter_context(nc.sbuf_tensor(f"la_pT{tag}", [128, 2048], BF16))
        s_ctx = esB.enter_context(nc.sbuf_tensor(f"la_ctx{tag}", [128, 64], BF16))
        s_cstrip = esB.enter_context(nc.sbuf_tensor(f"la_cst{tag}", [64, 2048], BF16))
        s_qn = esB.enter_context(nc.sbuf_tensor(f"la_qn{tag}", [1, 2048], F32))
        s_ksq = esB.enter_context(nc.sbuf_tensor(f"la_ksq{tag}", [64, 2048], F32))
        s_bias = esB.enter_context(nc.sbuf_tensor(f"la_bias{tag}", [128, 16], F32))
        s_sum = esB.enter_context(nc.sbuf_tensor(f"la_sum{tag}", [128, 1], F32))
        s_rcp = esB.enter_context(nc.sbuf_tensor(f"la_rcp{tag}", [128, 1], F32))
        ps_big = esB.enter_context(nc.psum_tensor(f"la_psb{tag}", [128, 2048], F32))
        ps_pt = [esB.enter_context(nc.psum_tensor(f"la_pspt{tag}{i}", [128, 1024], BF16)) for i in range(2)]
        ps_ctx = esB.enter_context(nc.psum_tensor(f"la_psctx{tag}", [128, 64], F32))
        ps_ctxT = esB.enter_context(nc.psum_tensor(f"la_psct{tag}", [64, 128], BF16))

        CH = [(0, 512), (512, 512), (1024, 512), (1536, 511)]
        SC = [(0, 512), (512, 512), (1024, 512), (1536, 512)]

        def strips(side, src_sb, dsb, dram_dst, first_waits):
            last_store = None
            for s16 in range(16):
                l0 = 128 * s16
                wlo = 1920 - l0
                par = s16 % 2
                for ci, (c0, cw_) in enumerate(CH):
                    w = list(first_waits) if (s16 == 0 and ci == 0) else []
                    if ci == 0 and (s16 > 0 or side == "k"):
                        w.append((s_ev, s_ev.count))
                    p.comp("tensor", lambda t, s16=s16, c0=c0, cw_=cw_, src_sb=src_sb, dsb=dsb: t.matmul(
                        ps_big[:, c0:c0 + cw_], src_sb[0:64, 128 * s16:128 * s16 + 128],
                        dsb[0:64, c0:c0 + cw_], start=True, stop=True,
                        skip_group_check=True),
                        waits=w, inc=(s_mm if ci == 3 else None))
                mmv = s_mm.count
                whi = wlo + SW - 1
                LA = max(0, 1024 - wlo)
                mB0 = max(wlo, 1024)
                LB = min(3070, whi) - mB0 + 1
                LC = max(0, whi - 3070)
                aB0 = LA
                aC0 = LA + LB
                ew = [(s_mm, mmv)]
                if last_store is not None and s16 >= 2:
                    ew.append(last_store[s16 - 2])
                if side == "q":
                    cB0 = mB0 - 1024
                    e1 = p.comp("scalar", lambda s, par=par, aB0=aB0, LB=LB, cB0=cB0: s.activation(
                        s_asm[par][:, aB0:aB0 + LB], ps_big[:, cB0:cB0 + LB], AF.Copy),
                        waits=ew, inc=s_ev)
                    last_ev = e1
                    if LC:
                        last_ev = p.comp("scalar", lambda s, par=par, aC0=aC0, LC=LC: s.activation(
                            s_asm[par][:, aC0:aC0 + LC], ps_big[:, 0:LC], AF.Copy), inc=s_ev)
                    if LA:
                        last_ev = p.comp("vector", lambda v, par=par, LA=LA, aB0=aB0: v.tensor_copy(
                            s_asm[par][:, 0:LA],
                            s_asm[par].ap()[:, aB0:aB0 + 1].broadcast_to([128, LA])),
                            waits=[(s_ev, last_ev)], inc=s_ev)
                else:
                    cB0 = mB0 - 1024
                    e1 = p.comp("scalar", lambda s, par=par, aB0=aB0, LB=LB, cB0=cB0: s.activation(
                        s_asm[par][:, aB0:aB0 + LB], ps_big[:, cB0:cB0 + LB], AF.Copy),
                        waits=ew, inc=s_ev)
                    last_ev = e1
                    if LA:
                        last_ev = p.comp("scalar", lambda s, par=par, LA=LA, wlo=wlo: s.activation(
                            s_asm[par][:, 0:LA], ps_big[:, wlo + 1023:wlo + 1023 + LA], AF.Copy),
                            inc=s_ev)
                    if LC:
                        last_ev = p.comp("vector", lambda v, par=par, aC0=aC0, LC=LC: v.tensor_copy(
                            s_asm[par][:, aC0:aC0 + LC],
                            s_asm[par].ap()[:, aC0 - 1:aC0].broadcast_to([128, LC])),
                            waits=[(s_ev, last_ev)], inc=s_ev)
                sv = p.dma("sync", lambda sy, s16=s16, par=par, dram_dst=dram_dst: sy.dma_start(
                    out=bass.AP(dram_dst, s16 * STRIDE_QE, [[SW, 128], [1, SW]]),
                    in_=s_asm[par][:, :]),
                    waits=[(s_ev, last_ev)], inc=s_st)
                if last_store is None:
                    last_store = {}
                last_store[s16] = (s_st, sv)
            return s_st.count

        bh_prev_done = None
        for b in range(SEQ_PER_CORE):
            for h in range(16):
                lw = [(s_st, phaseA_st)]
                if bh_prev_done is not None:
                    lw.append(bh_prev_done)
                qv = p.dma("gpsimd", lambda g, b=b, h=h: g.dma_start(
                    out=s_qh[:, :],
                    in_=bass.AP(scr["qf"], (64 * h) * NTOK + 2048 * b, [[NTOK, 64], [1, 2048]])),
                    waits=lw, inc=s_ld)
                kv = p.dma("gpsimd", lambda g, b=b, h=h: g.dma_start(
                    out=s_kh[:, :],
                    in_=bass.AP(scr["kf"], (64 * h) * NTOK + 2048 * b, [[NTOK, 64], [1, 2048]])),
                    inc=s_ld)
                vv = p.dma("gpsimd", lambda g, b=b, h=h: g.dma_start(
                    out=s_vh[:, :, :],
                    in_=bass.AP(scr["vt"], (2048 * b) * D + 64 * h, [[D, 128], [128 * D, 16], [1, 64]])),
                    inc=s_ld)
                qdone = strips("q", s_qh, s_drev, scr["qe"], [(s_ld, qv)])
                kdone = strips("k", s_kh, s_dnat, scr["ke"], [(s_ld, kv)])
                rkT_lv = None
                for s16 in range(16):
                    rkT_lv = p.dma("gpsimd", lambda g, s16=s16: g.dma_start(
                        out=s_rkT[:, s16, :],
                        in_=bass.AP(scr["ke"], s16 * STRIDE_QE + 127, [[SW - 1, 128], [1, 2048]])),
                        waits=[(s_st, kdone)] if s16 == 0 else [], inc=s_ld)
                a1 = p.comp("scalar", lambda s: s.activation(s_ksq[0:64, :], s_kh[0:64, :], AF.Square),
                            waits=[(s_ld, kv), (s_ev, s_ev.count)], inc=s_ev)
                mv = None
                for ci, (c0, cw_) in enumerate(SC):
                    mv = p.comp("tensor", lambda t, c0=c0, cw_=cw_: t.matmul(
                        ps_big[0:1, c0:c0 + cw_], consts.ones_col[0:64, :], s_ksq[0:64, c0:c0 + cw_],
                        start=True, stop=True, skip_group_check=True),
                        waits=[(s_ev, a1), (s_mm, s_mm.count)] if ci == 0 else [], inc=s_mm)
                v1 = p.comp("vector", lambda v: v.reduce_max(
                    s_scal.ap()[0:1, 2:3].unsqueeze(2),
                    ps_big.ap()[0:1, 0:2048].unsqueeze(1),
                    mybir.AxisListType.X), waits=[(s_mm, mv)], inc=s_dv)
                a2 = p.comp("scalar", lambda s: s.activation(s_scal[0:1, 2:3], s_scal[0:1, 2:3], AF.Sqrt),
                            waits=[(s_dv, v1)], inc=s_ev)
                a3 = p.comp("scalar", lambda s: s.activation(s_ksq[0:64, :], s_qh[0:64, :], AF.Square),
                            waits=[(s_ld, qv), (s_dv, v1)], inc=s_ev)
                for ci, (c0, cw_) in enumerate(SC):
                    mv = p.comp("tensor", lambda t, c0=c0, cw_=cw_: t.matmul(
                        ps_big[0:1, c0:c0 + cw_], consts.ones_col[0:64, :], s_ksq[0:64, c0:c0 + cw_],
                        start=True, stop=True, skip_group_check=True),
                        waits=[(s_ev, a3)] if ci == 0 else [], inc=s_mm)
                a4 = p.comp("scalar", lambda s: s.activation(s_qn[0:1, :], ps_big[0:1, :], AF.Sqrt),
                            waits=[(s_mm, mv), (s_ev, a2)], inc=s_ev)
                v2 = p.comp("vector", lambda v: v.tensor_copy(
                    s_scal[0:1, 3:4], s_scal[0:1, 2:3]),
                    waits=[(s_ev, a4)] + [dmax_ready], inc=s_dv)
                v2 = p.comp("vector", lambda v: v.tensor_tensor(
                    s_scal[0:1, 3:4], s_scal[0:1, 3:4], s_scal[0:1, 1:2], OP.add),
                    waits=[(s_dv, v2)], inc=s_dv)
                v3 = p.comp("vector", lambda v: v.tensor_tensor(
                    s_scal[0:1, 4:5], s_scal[0:1, 2:3], s_scal[0:1, 1:2], OP.mult),
                    waits=[(s_dv, v2)], inc=s_dv)
                v4 = p.comp("vector", lambda v: v.tensor_scalar(
                    s_qn[0:1, :], s_qn[0:1, :], s_scal[0:1, 3:4], s_scal[0:1, 4:5],
                    OP.mult, OP.add), waits=[(s_dv, v3)], inc=s_dv)
                v5 = p.comp("vector", lambda v: v.tensor_scalar(
                    s_qn[0:1, :], s_qn[0:1, :], -0.126, None, OP.mult),
                    waits=[(s_dv, v4)], inc=s_dv)
                bias_ready = None
                for s16 in range(16):
                    mv = p.comp("tensor", lambda t, s16=s16: t.matmul(
                        ps_ctx[:, 0:1], s_qn[0:1, 128 * s16:128 * s16 + 128],
                        consts.ones_row[0:1, 0:1], start=True, stop=True,
                        skip_group_check=True),
                        waits=([(s_dv, v5)] if s16 == 0 else []) + ([(s_dv, bias_ready)] if bias_ready else []),
                        inc=s_mm)
                    bias_ready = p.comp("vector", lambda v, s16=s16: v.tensor_copy(
                        s_bias[:, s16:s16 + 1], ps_ctx[:, 0:1]),
                        waits=[(s_mm, mv)], inc=s_dv)
                exp_prev = None
                rq_free = None
                pT_free = None
                ctx_free = None
                for qb in range(16):
                    rw = [(s_st, qdone)]
                    if rq_free is not None:
                        rw.append(rq_free)
                    rqv = p.dma("sync", lambda sy, qb=qb: sy.dma_start(
                        out=s_rq[:, :],
                        in_=bass.AP(scr["qe"], (qb) * STRIDE_QE + 127, [[SW - 1, 128], [1, 2048]])),
                        waits=rw, inc=s_ld)
                    for ci, (c0, cw_) in enumerate(SC):
                        w = []
                        if ci == 0:
                            w = [(s_ld, rkT_lv), (s_dv, bias_ready)]
                            if exp_prev is not None:
                                w.append((s_ev, exp_prev))
                        p.comp("tensor", lambda t, qb=qb, c0=c0, cw_=cw_: t.matmul(
                            ps_big[:, c0:c0 + cw_], s_qh[0:64, 128 * qb:128 * qb + 128],
                            s_kh[0:64, c0:c0 + cw_], start=True, stop=False,
                            skip_group_check=True), waits=w)
                    for sc in range(16):
                        p.comp("tensor", lambda t, sc=sc, qb=qb: t.matmul(
                            ps_big[:, 128 * sc:128 * sc + 128],
                            s_rkT[:, sc, 128 * qb:128 * qb + 128], consts.ident_bf[:, :],
                            start=False, stop=(sc % 4 == 3), skip_group_check=True),
                            inc=(s_mm if sc == 15 else None))
                    mmv = s_mm.count
                    aq = p.comp("vector", lambda v: v.tensor_tensor(
                        ps_big[:, :], ps_big[:, :], s_rq[:, :], OP.add),
                        waits=[(s_mm, mmv), (s_ld, rqv)], inc=s_dv)
                    rq_free = (s_dv, aq)
                    ew = [(s_dv, aq)]
                    if pT_free is not None:
                        ew.append(pT_free)
                    expv = p.comp("scalar", lambda s, qb=qb: s.activation(
                        s_p[:, :], ps_big[:, :], AF.Exp, scale=0.125,
                        bias=s_bias[:, qb:qb + 1], accum_out=s_sum[:, :]),
                        waits=ew, inc=s_ev)
                    exp_prev = expv
                    tv = {}
                    for half in range(2):
                        for rc in range(8):
                            rcg = half * 8 + rc
                            w = [(s_ev, expv)] if rcg == 0 else []
                            if rcg < 2 and ctx_free is not None:
                                w.append(ctx_free)
                            if rcg == 0 and pT_free is not None:
                                w.append(pT_free)
                            tv[half] = p.comp("tensor", lambda t, half=half, rc=rc, rcg=rcg: t.transpose(
                                ps_pt[half][:, 128 * rc:128 * rc + 128],
                                s_p[:, 128 * rcg:128 * rcg + 128], consts.ident_bf[:, :]),
                                waits=w, inc=(s_mm if rc == 7 else None))
                    mmv = s_mm.count
                    ev1 = p.comp("vector", lambda v: v.tensor_copy(s_pT[:, 0:1024], ps_pt[0][:, :]),
                                 waits=[(s_mm, mmv)], inc=s_dv)
                    ev2 = p.comp("vector", lambda v: v.tensor_copy(s_pT[:, 1024:2048], ps_pt[1][:, :]),
                                 inc=s_dv)
                    pT_free = (s_dv, ev2)
                    r1 = p.comp("vector", lambda v: v.reciprocal(s_rcp[:, :], s_sum[:, :]),
                                waits=[(s_ev, expv)], inc=s_dv)
                    for rc in range(16):
                        w = []
                        if rc == 0:
                            w = [(s_dv, ev2), (s_ld, vv)]
                            if ctx_free is not None:
                                w.append(ctx_free)
                        p.comp("tensor", lambda t, rc=rc: t.matmul(
                            ps_ctx[:, :], s_pT[:, 128 * rc:128 * rc + 128], s_vh[:, rc, :],
                            start=(rc == 0), stop=(rc == 15), skip_group_check=True),
                            waits=w, inc=(s_mm if rc == 15 else None))
                    mmv = s_mm.count
                    c1 = p.comp("vector", lambda v: v.tensor_scalar(
                        s_ctx[:, :], ps_ctx[:, :], s_rcp[:, :], None, OP.mult),
                        waits=[(s_mm, mmv), (s_dv, r1)], inc=s_dv)
                    ctv = p.comp("tensor", lambda t: t.transpose(
                        ps_ctxT[:, :], s_ctx[:, :], consts.ident_bf[:, :]),
                        waits=[(s_dv, c1), (s_ev, s_ev.count)], inc=s_mm)
                    bvslice = s_bcols["bv"].ap()[(h % 2) * 64:(h % 2) * 64 + 64, h // 2:h // 2 + 1]
                    ce = p.comp("scalar", lambda s, qb=qb, bvslice=bvslice: s.activation(
                        s_cstrip[0:64, 128 * qb:128 * qb + 128], ps_ctxT[0:64, :], AF.Identity,
                        bias=bvslice), waits=[(s_mm, ctv)], inc=s_ev)
                    ctx_free = (s_ev, ce)
                sv = p.dma("sync", lambda sy, b=b, h=h: sy.dma_start(
                    out=bass.AP(scr["cf"], (64 * h) * NTOK + 2048 * b, [[NTOK, 64], [1, 2048]]),
                    in_=s_cstrip[:, :]), waits=[(s_ev, s_ev.count)], inc=s_st)
                bh_prev_done = (s_st, sv)
        phaseB_st = s_st.count
    p.barrier(f"la_bar_b{tag}")

    with contextlib.ExitStack() as esC:
        s_cfm = esC.enter_context(nc.sbuf_tensor(f"la_cfm{tag}", [128, 8, Tg], BF16))
        s_x2 = esC.enter_context(nc.sbuf_tensor(f"la_x2{tag}", [128, 8, Tg], F32))
        s_xs2 = esC.enter_context(nc.sbuf_tensor(f"la_xs2{tag}", [128, 8, Tg], F32))
        s_h2 = esC.enter_context(nc.sbuf_tensor(f"la_h2{tag}", [128, 8, Tg], F32))
        s_y = esC.enter_context(nc.sbuf_tensor(f"la_y{tag}", [128, 8, Tg], F32))
        s_t = esC.enter_context(nc.sbuf_tensor(f"la_t{tag}", [128, Tg], F32))
        ln2 = make_ln_ctx(p, nc, esC, consts, Tg, "lc" + tag)
        ps_p = [esC.enter_context(nc.psum_tensor(f"la_pc{tag}{i}", [128, Tg], F32)) for i in range(2)]
        bankC_free = {}
        ln_last = None
        st_prev = None
        h2_free = None
        for g in range(NG):
            lw = [(s_st, phaseB_st)]
            if ln_last is not None:
                lw.append((ln2["sem_dv"], ln_last))
            cv = p.dma("sync", lambda sy, g=g: sy.dma_start(
                out=s_cfm[:, :, :],
                in_=bass.AP(scr["cf"], g * Tg, [[NTOK, 128], [128 * NTOK, 8], [1, Tg]])),
                waits=lw, inc=s_ld)
            xv = p.dma("sync", lambda sy, g=g: sy.dma_start(
                out=s_x2[:, :, :],
                in_=bass.AP(x_in, g * Tg, [[NTOK, 128], [128 * NTOK, 8], [1, Tg]])),
                inc=s_ld)
            xsv = p.dma("sync", lambda sy, g=g: sy.dma_start(
                out=s_xs2[:, :, :],
                in_=bass.AP(scr["xs"], g * Tg, [[NTOK, 128], [128 * NTOK, 8], [1, Tg]])),
                inc=s_ld)
            for ot in range(8):
                bi = ot % 2
                for ic in range(8):
                    w = []
                    if ic == 0:
                        w = [(s_ld, cv), (s_ld, s_ld.count)]
                        if bi in bankC_free:
                            w.append(bankC_free[bi])
                    p.comp("tensor", lambda t, ic=ic, ot=ot, bi=bi: t.matmul(
                        ps_p[bi][:, :], s_wo[:, ic, ot * 128:(ot + 1) * 128],
                        s_cfm.ap()[:, ic, :], start=(ic == 0), stop=(ic == 7),
                        skip_group_check=True), waits=w, inc=(s_mm if ic == 7 else None))
                mmv = s_mm.count
                e = p.comp("scalar", lambda s, ot=ot, bi=bi: s.activation(
                    s_t[:, :], ps_p[bi][:, :], AF.Identity, bias=s_bcols["bo"][:, ot:ot + 1]),
                    waits=[(s_mm, mmv)] + ([(s_dv, h2_free)] if (h2_free and ot == 0) else []), inc=s_ev)
                hv = p.comp("vector", lambda v, ot=ot: v.tensor_tensor(
                    s_h2.ap()[:, ot, :], s_t[:, :], s_xs2.ap()[:, ot, :], OP.add),
                    waits=[(s_ev, e), (s_ld, xsv)], inc=s_dv)
                bankC_free[bi] = (s_dv, hv)
            h2_done = s_dv.count
            wr = [(s_st, st_prev)] if st_prev is not None else []
            lnsem, lnval = emit_layernorm(
                p, ln2, s_h2.ap(), s_y.ap(), [(s_dv, h2_done), (s_ld, xv)],
                write_waits=wr, eps=1e-12,
                affine=(s_bcols["gam"], s_bcols["bet"]), resid3=s_x2.ap())
            ln_last = lnval
            h2_free = lnval
            sv = p.dma("sync", lambda sy, g=g: sy.dma_start(
                out=bass.AP(y_out, g * Tg, [[NTOK, 128], [128 * NTOK, 8], [1, Tg]]),
                in_=s_y[:, :, :]), waits=[(lnsem, lnval)], inc=s_st)
            st_prev = sv
    p.wait("sync", [(s_st, s_st.count)])
    return s_st


def emit_l0attn(p, nc, es, consts, x_in, y_out, W, scr, tag):
    Tg = 512
    NG = NTOK // Tg
    CJ = 128
    NCH = T // CJ

    s_ld = p.sem(f"da_ld{tag}")
    s_mm = p.sem(f"da_mm{tag}")
    s_ev = p.sem(f"da_ev{tag}")
    s_dv = p.sem(f"da_dv{tag}")
    s_gp = p.sem(f"da_gp{tag}")
    s_st = p.sem(f"da_st{tag}")

    cols = {}
    for key, ncol in (("ln1g", 8), ("ln1b", 8), ("ln2g", 8), ("ln2b", 8),
                      ("rmsw", 8), ("b2", 8), ("onw", 4), ("cq", 32), ("ck", 32),
                      ("cv", 64), ("dtb", 1), ("alog", 1)):
        cols[key] = es.enter_context(nc.sbuf_tensor(f"da_c{key}{tag}", [128, ncol], F32))
    maskLI = es.enter_context(nc.sbuf_tensor(f"da_mLI{tag}", [128, 128], F32))
    maskSU = es.enter_context(nc.sbuf_tensor(f"da_mSU{tag}", [128, 128], F32))
    maskLS = es.enter_context(nc.sbuf_tensor(f"da_mLS{tag}", [128, 128], F32))

    for key in ("ln1g", "ln1b", "ln2g", "ln2b", "rmsw", "b2", "onw", "cq", "ck", "cv", "dtb", "alog"):
        nr = W[key].shape[0]
        p.dma("gpsimd", lambda g, key=key, nr=nr: g.dma_start(
            out=cols[key][0:nr, :], in_=W[key].ap()), inc=s_ld)
    wld = s_ld.count
    msem = p.sem(f"da_msk{tag}")

    with contextlib.ExitStack() as esM:
        s_iota = esM.enter_context(nc.sbuf_tensor(f"da_iot{tag}", [1, 128], F32))
        s_ic = esM.enter_context(nc.sbuf_tensor(f"da_ioc{tag}", [128, 1], F32))
        s_dd = esM.enter_context(nc.sbuf_tensor(f"da_dd{tag}", [128, 128], F32))
        s_m01 = esM.enter_context(nc.sbuf_tensor(f"da_m01{tag}", [128, 128], F32))
        ps_m = esM.enter_context(nc.psum_tensor(f"da_psm{tag}", [128, 128], F32))
        v0 = p.comp("vector", lambda v: v.tensor_tensor_scan(
            s_iota[0:1, :], consts.ones_row[0:1, :], consts.ones_row[0:1, :],
            0.0, OP.mult, OP.add), waits=[(consts.sem, consts.ready)], inc=msem)
        v0 = p.comp("vector", lambda v: v.tensor_scalar(
            s_iota[0:1, :], s_iota[0:1, :], -1.0, None, OP.add),
            waits=[(msem, v0)], inc=msem)
        m1 = p.comp("tensor", lambda t: t.matmul(
            ps_m[:, 0:1], s_iota[0:1, :], consts.ones_row[0:1, 0:1],
            start=True, stop=True, skip_group_check=True),
            waits=[(msem, v0)], inc=s_mm)
        v0 = p.comp("vector", lambda v: v.tensor_copy(s_ic[:, :], ps_m[:, 0:1]),
                    waits=[(s_mm, m1)], inc=msem)
        m1 = p.comp("tensor", lambda t: t.matmul(
            ps_m[:, :], consts.ones_row[0:1, :], s_iota[0:1, :],
            start=True, stop=True, skip_group_check=True),
            waits=[(msem, v0)], inc=s_mm)
        v0 = p.comp("vector", lambda v: v.tensor_scalar(
            s_dd[:, :], ps_m[:, :], s_ic.ap()[:, 0:1], None, OP.subtract),
            waits=[(s_mm, m1)], inc=msem)
        v0 = p.comp("vector", lambda v: v.tensor_scalar(
            s_m01[:, :], s_dd[:, :], 0.0, None, OP.is_gt), waits=[(msem, v0)], inc=msem)
        v0 = p.comp("vector", lambda v: v.tensor_scalar(
            maskLI[:, :], s_m01[:, :], -1.0, 1e30, OP.add, OP.mult),
            waits=[(msem, v0)], inc=msem)
        v0 = p.comp("vector", lambda v: v.tensor_scalar(
            s_m01[:, :], s_dd[:, :], 0.0, None, OP.is_ge), waits=[(msem, v0)], inc=msem)
        v0 = p.comp("vector", lambda v: v.tensor_scalar(
            maskLS[:, :], s_m01[:, :], -1.0, 1e30, OP.add, OP.mult),
            waits=[(msem, v0)], inc=msem)
        v0 = p.comp("vector", lambda v: v.tensor_scalar(
            maskSU[:, :], s_m01[:, :], -1e30, None, OP.mult),
            waits=[(msem, v0)], inc=msem)
    mval = msem.count

    with contextlib.ExitStack() as esP:
        s_wq = esP.enter_context(nc.sbuf_tensor(f"da_wq{tag}", [128, 8, D], BF16))
        s_wk = esP.enter_context(nc.sbuf_tensor(f"da_wk{tag}", [128, 8, D], BF16))
        s_wv = esP.enter_context(nc.sbuf_tensor(f"da_wv{tag}", [128, 8, 2048], BF16))
        s_wg = esP.enter_context(nc.sbuf_tensor(f"da_wg{tag}", [128, 8, 2048], BF16))
        s_wab = esP.enter_context(nc.sbuf_tensor(f"da_wab{tag}", [128, 8, 8], BF16))
        for dst, key, nch in ((s_wq, "Wq", 8), (s_wk, "Wk", 8), (s_wv, "Wv", 8),
                              (s_wg, "Wg", 8), (s_wab, "Wab", 8)):
            ocols = W[key].shape[1]
            p.dma("gpsimd", lambda g, dst=dst, key=key, nch=nch, ocols=ocols: g.dma_start(
                out=dst[:, 0:nch, 0:ocols],
                in_=bass.AP(W[key], 0, [[ocols, 128], [128 * ocols, nch], [1, ocols]])), inc=s_ld)
        wld = s_ld.count
        s_x = esP.enter_context(nc.sbuf_tensor(f"da_x{tag}", [128, 8, Tg], F32))
        s_xs = esP.enter_context(nc.sbuf_tensor(f"da_xs{tag}", [128, 8, Tg], F32))
        s_h1 = s_x
        s_hn = esP.enter_context(nc.sbuf_tensor(f"da_hn{tag}", [128, 8, Tg], BF16))
        s_pr = esP.enter_context(nc.sbuf_tensor(f"da_pr{tag}", [128, 16, Tg], BF16))
        s_sq = esP.enter_context(nc.sbuf_tensor(f"da_sq{tag}", [128, Tg], F32))
        s_abuf = esP.enter_context(nc.sbuf_tensor(f"da_abuf{tag}", [8, Tg], F32))
        s_row = [esP.enter_context(nc.sbuf_tensor(f"da_row{i}{tag}", [1, Tg], F32)) for i in range(2)]
        ln = make_ln_ctx(p, nc, esP, consts, Tg, "dp" + tag)
        lnA = make_ln_ctx(p, nc, esP, consts, Tg, "dA" + tag, share=ln)
        ps_p = [esP.enter_context(nc.psum_tensor(f"da_pp{tag}{i}", [128, Tg], F32)) for i in range(2)]

        bank_free = {}
        ln_last = None
        pr_free = {}
        cast_prev = None
        h1_free = None
        for g in range(NG):
            xw = [] if ln_last is None else [(ln["sem_dv"], ln_last)]
            if h1_free is not None:
                xw.append(h1_free)
            xv = p.dma("sync", lambda sy, g=g: sy.dma_start(
                out=s_x[:, :, :],
                in_=bass.AP(x_in, g * Tg, [[NTOK, 128], [128 * NTOK, 8], [1, Tg]])),
                waits=xw, inc=s_ld)
            wr = [(s_st, s_st.count)] if g > 0 else []
            lnsem, lnval = emit_layernorm(p, ln, s_x.ap(), s_xs.ap(), [(s_ld, xv)],
                                          write_waits=wr)
            ln_last = lnval
            sxv = p.dma("sync", lambda sy, g=g: sy.dma_start(
                out=bass.AP(scr["xsub"], g * Tg, [[NTOK, 128], [128 * NTOK, 8], [1, Tg]]),
                in_=s_xs[:, :, :]), waits=[(lnsem, lnval)], inc=s_st)
            wr2 = [(lnA["sem_mm"], lnA["sem_mm"].count)] if g > 0 else []
            ln2sem, ln2val = emit_layernorm(p, lnA, s_xs.ap(), s_h1.ap(),
                                            [(lnsem, lnval)],
                                            write_waits=wr2 + [(lnsem, lnval)],
                                            affine=(cols["ln1g"], cols["ln1b"]))
            aq = None
            for c in range(8):
                w = [(ln2sem, ln2val)] if c == 0 else [(s_mm, aq)]
                av = p.comp("scalar", lambda s, c=c: s.activation(s_sq[:, :], s_h1.ap()[:, c, :], AF.Square),
                            waits=w, inc=s_ev)
                aq = p.comp("tensor", lambda t, c=c: t.matmul(
                    ps_p[0][0:1, :], consts.ones_col[:, :], s_sq[:, :],
                    start=(c == 0), stop=(c == 7), skip_group_check=True),
                    waits=[(s_ev, av)] + ([(s_dv, s_dv.count)] if c == 0 else []),
                    inc=s_mm)
            v1 = p.comp("vector", lambda v: v.tensor_scalar(s_row[0][:1, :], ps_p[0][0:1, :],
                                                            1.0 / 1024.0, 1e-6, OP.mult, OP.add),
                        waits=[(s_mm, aq)], inc=s_dv)
            a1 = p.comp("scalar", lambda s: s.activation(s_row[0][:1, :], s_row[0][:1, :], AF.Ln),
                        waits=[(s_dv, v1)], inc=s_ev)
            a2 = p.comp("scalar", lambda s: s.activation(s_row[0][:1, :], s_row[0][:1, :], AF.Exp,
                                                         scale=-0.5), waits=[(s_ev, a1)], inc=s_ev)
            bv_ = p.comp("tensor", lambda t: t.matmul(ps_p[0][:, :], consts.ones_row[:1, :],
                                                      s_row[0][:1, :], start=True, stop=True,
                                                      skip_group_check=True),
                         waits=[(s_ev, a2)], inc=s_mm)
            cast_last = None
            for c in range(8):
                w = [(s_mm, bv_)] if c == 0 else []
                if c == 0 and cast_prev is not None:
                    w.append(cast_prev)
                p.comp("vector", lambda v, c=c: v.tensor_tensor(s_sq[:, :], s_h1.ap()[:, c, :],
                                                                ps_p[0][:, :], OP.mult), waits=w)
                cast_last = p.comp("vector", lambda v, c=c: v.tensor_scalar(
                    s_hn.ap()[:, c, :], s_sq[:, :], cols["rmsw"][:, c:c + 1], None, OP.mult),
                    inc=s_dv)
            cast_prev = None

            def proj(wsb, n_ot, dstd, extra_w=()):
                stv = None
                for ot in range(n_ot):
                    bi = ot % 2
                    for ic in range(8):
                        w = []
                        if ic == 0:
                            w = [(s_dv, cast_last), (s_ld, wld)] + list(extra_w)
                            if bi in bank_free:
                                w.append(bank_free[bi])
                        p.comp("tensor", lambda t, ic=ic, ot=ot, wsb=wsb, bi=bi: t.matmul(
                            ps_p[bi][:, :], wsb[:, ic, ot * 128:(ot + 1) * 128],
                            s_hn.ap()[:, ic, :], start=(ic == 0), stop=(ic == 7),
                            skip_group_check=True),
                            waits=w, inc=(s_mm if ic == 7 else None))
                    mmv = s_mm.count
                    ew = [(s_mm, mmv)]
                    if (ot % 16) in pr_free:
                        ew.append(pr_free[ot % 16])
                    e = p.comp("scalar", lambda s, ot=ot, bi=bi: s.activation(
                        s_pr.ap()[:, ot % 16, :], ps_p[bi][:, :], AF.Copy),
                        waits=ew, inc=s_ev)
                    bank_free[bi] = (s_ev, e)
                    if ot % 16 == n_ot - 1 or ot % 16 == 15:
                        nch = min(n_ot, 16)
                        stv = p.dma("sync", lambda sy, g=g, dstd=dstd, nch=nch: sy.dma_start(
                            out=bass.AP(dstd, g * Tg, [[NTOK, 128], [128 * NTOK, nch], [1, Tg]]),
                            in_=s_pr[:, 0:nch, :]), waits=[(s_ev, s_ev.count)], inc=s_st)
                        for j in range(nch):
                            pr_free[j] = (s_st, stv)
                return stv
            proj(s_wq, 8, scr["qraw"])
            proj(s_wk, 8, scr["kraw"])
            proj(s_wv, 16, scr["vraw"])
            proj(s_wg, 16, scr["gate"])
            for ic in range(8):
                w = []
                if ic == 0:
                    w = [(s_dv, cast_last)]
                    if 0 in bank_free:
                        w.append(bank_free[0])
                p.comp("tensor", lambda t, ic=ic: t.matmul(
                    ps_p[0][0:8, :], s_wab[:, ic, :], s_hn.ap()[:, ic, :],
                    start=(ic == 0), stop=(ic == 7), skip_group_check=True),
                    waits=w, inc=(s_mm if ic == 7 else None))
            mmv = s_mm.count
            ew = [(s_mm, mmv)]
            if g > 0:
                ew.append((s_st, s_st.count))
            e = p.comp("scalar", lambda s: s.activation(s_abuf[0:8, :], ps_p[0][0:8, :], AF.Copy),
                       waits=ew, inc=s_ev)
            bank_free[0] = (s_ev, e)
            p.dma("sync", lambda sy, g=g: sy.dma_start(
                out=bass.AP(scr["ab"], g * Tg, [[NTOK, 8], [1, Tg]]),
                in_=s_abuf[0:8, :]),
                waits=[(s_ev, e)], inc=s_st)
            cast_prev = (s_mm, s_mm.count)
            h1_free = (s_dv, cast_last)
            if "hdump" in scr:
                p.dma("sync", lambda sy, g=g: sy.dma_start(
                    out=bass.AP(scr["hdump"], g * Tg, [[NTOK, 128], [128 * NTOK, 8], [1, Tg]]),
                    in_=s_hn[:, :, :]), waits=[(s_dv, cast_last)], inc=s_st)
        phaseP_st = s_st.count
    p.barrier(f"da_barP{tag}")

    with contextlib.ExitStack() as esG:
        s_ab = esG.enter_context(nc.sbuf_tensor(f"da_ab{tag}", [4, NTOK], F32))
        s_abb = esG.enter_context(nc.sbuf_tensor(f"da_abb{tag}", [4, NTOK], F32))
        s_g1 = esG.enter_context(nc.sbuf_tensor(f"da_g1{tag}", [4, NTOK], F32))
        s_g2 = esG.enter_context(nc.sbuf_tensor(f"da_g2{tag}", [4, NTOK], F32))
        s_B = esG.enter_context(nc.sbuf_tensor(f"da_B{tag}", [4, 2 * (T + 1)], F32))
        s_on = esG.enter_context(nc.sbuf_tensor(f"da_on{tag}", [4, T], F32))
        s_ea = esG.enter_context(nc.sbuf_tensor(f"da_ea{tag}", [4, 1], F32))
        lv = p.dma("gpsimd", lambda g: g.dma_start(
            out=s_ab[:, :], in_=bass.AP(scr["ab"], 0, [[NTOK, 4], [1, NTOK]])), inc=s_ld)
        lv = p.dma("gpsimd", lambda g: g.dma_start(
            out=s_abb[:, :], in_=bass.AP(scr["ab"], 4 * NTOK, [[NTOK, 4], [1, NTOK]])), inc=s_ld)
        ze = p.comp("gpsimd", lambda g: g.memset(s_on[:, :], 1.0))
        ze = p.comp("gpsimd", lambda g: g.memset(s_B[:, :], 0.0))
        zv = p.comp("gpsimd", lambda g: g.tensor_copy(s_B[0:4, 0:1], s_B[0:4, 0:1]), inc=s_gp)
        a0 = p.comp("scalar", lambda s: s.activation(s_ea[0:4, :], cols["alog"][0:4, 0:1], AF.Exp),
                    waits=[(s_ld, wld)], inc=s_ev)
        v0 = p.comp("vector", lambda v: v.tensor_scalar(s_ea[0:4, :], s_ea[0:4, :], -1.0, None, OP.mult),
                    waits=[(s_ev, a0)], inc=s_dv)
        v1 = p.comp("vector", lambda v: v.tensor_scalar(s_ab[0:4, :], s_ab[0:4, :],
                                                        cols["dtb"][0:4, 0:1], None, OP.add),
                    waits=[(s_ld, lv), (s_dv, v0)], inc=s_dv)
        a1 = p.comp("scalar", lambda s: s.activation(s_g1[0:4, :], s_ab[0:4, :], AF.Abs),
                    waits=[(s_dv, v1)], inc=s_ev)
        a2 = p.comp("scalar", lambda s: s.activation(s_g1[0:4, :], s_g1[0:4, :], AF.Exp, scale=-1.0),
                    waits=[(s_ev, a1)], inc=s_ev)
        v2 = p.comp("vector", lambda v: v.tensor_scalar(s_g1[0:4, :], s_g1[0:4, :], 1.0, None, OP.add),
                    waits=[(s_ev, a2)], inc=s_dv)
        a3 = p.comp("scalar", lambda s: s.activation(s_g1[0:4, :], s_g1[0:4, :], AF.Ln),
                    waits=[(s_dv, v2)], inc=s_ev)
        v3 = p.comp("vector", lambda v: v.tensor_scalar(s_g2[0:4, :], s_ab[0:4, :], 0.0, None, OP.max),
                    waits=[(s_ev, a3)], inc=s_dv)
        v4 = p.comp("vector", lambda v: v.tensor_tensor(s_g1[0:4, :], s_g1[0:4, :], s_g2[0:4, :], OP.add),
                    waits=[(s_dv, v3)], inc=s_dv)
        v5 = p.comp("vector", lambda v: v.tensor_scalar(s_g1[0:4, :], s_g1[0:4, :], s_ea[0:4, 0:1],
                                                        None, OP.mult),
                    waits=[(s_dv, v4)], inc=s_dv)
        sc1 = p.comp("vector", lambda v: v.tensor_tensor_scan(
            s_B[0:4, 1:T + 1], s_on[0:4, 0:T], s_g1[0:4, 0:T], 0.0, OP.mult, OP.add),
            waits=[(s_dv, v5), (s_gp, zv)], inc=s_dv)
        sc2 = p.comp("vector", lambda v: v.tensor_tensor_scan(
            s_B[0:4, T + 2:2 * T + 2], s_on[0:4, 0:T], s_g1[0:4, T:2 * T], 0.0, OP.mult, OP.add),
            waits=[(s_dv, sc1)], inc=s_dv)
        a4 = p.comp("scalar", lambda s: s.activation(s_g2[0:4, :], s_abb[0:4, :], AF.Sigmoid),
                    waits=[(s_dv, sc2)], inc=s_ev)
        p.dma("sync", lambda sy: sy.dma_start(
            out=bass.AP(scr["Bsc"], 0, [[2 * (T + 1), 4], [1, 2 * (T + 1)]]), in_=s_B[0:4, :]),
            waits=[(s_dv, sc2)], inc=s_st)
        p.dma("sync", lambda sy: sy.dma_start(
            out=bass.AP(scr["beta"], 0, [[NTOK, 4], [1, NTOK]]), in_=s_g2[0:4, :]),
            waits=[(s_ev, a4)], inc=s_st)
    p.barrier(f"da_barG{tag}")

    with contextlib.ExitStack() as esV:
        PADW = T + 3
        s_cb = esV.enter_context(nc.sbuf_tensor(f"da_cb{tag}", [128, 2, 2 * PADW], BF16))
        s_cv = esV.enter_context(nc.sbuf_tensor(f"da_cvb{tag}", [128, 2, NTOK], F32))
        s_bsq = esV.enter_context(nc.sbuf_tensor(f"da_bsq{tag}", [128, 512], F32))
        s_out = esV.enter_context(nc.sbuf_tensor(f"da_vout{tag}", [128, 2, NTOK], BF16))
        s_vrow = esV.enter_context(nc.sbuf_tensor(f"da_vrow{tag}", [1, 512], F32))
        ps_r = esV.enter_context(nc.psum_tensor(f"da_psr{tag}", [1, 512], F32))
        ps_bc = esV.enter_context(nc.psum_tensor(f"da_psbc{tag}", [128, 512], F32))
        zr = p.sem(f"da_vz{tag}")
        p.comp("gpsimd", lambda g: g.memset(s_cb[:, :, :], 0.0))
        zv = p.comp("gpsimd", lambda g: g.tensor_copy(s_cb[:, 0, 0:4], s_cb[:, 0, 0:4]), inc=zr)

        def conv_unit(src_dram, dst_dram, cwcol, chunks, do_norm, inv_scale, gate_done):
            nonlocal_busy = []
            lvs = []
            for ci, cg in enumerate(chunks):
                for sq_ in range(2):
                    lvs.append(p.dma("gpsimd", lambda g, ci=ci, cg=cg, sq_=sq_: g.dma_start(
                        out=s_cb[:, ci, sq_ * PADW + 3:sq_ * PADW + 3 + T],
                        in_=bass.AP(src_dram, cg * 128 * NTOK + sq_ * T,
                                    [[NTOK, 128], [1, T]])),
                        waits=[(zr, zv)] + list(gate_done), inc=s_ld))
            lv = lvs[-1]
            dvv = None
            for ci, cg in enumerate(chunks):
                for sq_ in range(2):
                    base = sq_ * PADW
                    w = [(s_ld, lv)] if (ci == 0 and sq_ == 0) else []
                    p.comp("vector", lambda v, ci=ci, cg=cg, base=base, sq_=sq_: v.tensor_scalar(
                        s_cv[:, ci, sq_ * T:(sq_ + 1) * T], s_cb[:, ci, base:base + T],
                        cwcol[:, cg * 4:cg * 4 + 1], None, OP.mult), waits=w)
                    for j in range(1, 4):
                        dvv = p.comp("vector", lambda v, ci=ci, cg=cg, base=base, j=j, sq_=sq_:
                                     v.scalar_tensor_tensor(
                                         s_cv[:, ci, sq_ * T:(sq_ + 1) * T],
                                         s_cb[:, ci, base + j:base + j + T],
                                         cwcol[:, cg * 4 + j:cg * 4 + j + 1],
                                         s_cv[:, ci, sq_ * T:(sq_ + 1) * T],
                                         OP.mult, OP.add),
                                     inc=(s_dv if (j == 3) else None))
            avs = []
            for ci in range(len(chunks)):
                avs.append(p.comp("scalar", lambda s, ci=ci: s.activation(
                    s_cv[:, ci, :], s_cv[:, ci, :], AF.Silu),
                    waits=[(s_dv, dvv)] if ci == 0 else [], inc=s_ev))
            a_last = avs[-1]
            if not do_norm:
                ov = p.comp("vector", lambda v: v.tensor_copy(s_out[:, 0:len(chunks), :],
                                                              s_cv[:, 0:len(chunks), :]),
                            waits=[(s_ev, a_last)], inc=s_dv)
                norm_done = (s_dv, ov)
            else:
                last = None
                for blk in range(8):
                    t0 = blk * 512
                    mv = None
                    for ci in range(2):
                        aq = p.comp("scalar", lambda s, ci=ci, t0=t0: s.activation(
                            s_bsq[:, :], s_cv[:, ci, t0:t0 + 512], AF.Square),
                            waits=([(s_ev, a_last)] if (blk == 0 and ci == 0) else []) +
                                  [(s_mm, s_mm.count)], inc=s_ev)
                        mv = p.comp("tensor", lambda t, ci=ci: t.matmul(
                            ps_r[0:1, :], consts.ones_col[:, :], s_bsq[:, :],
                            start=(ci == 0), stop=(ci == 1), skip_group_check=True),
                            waits=[(s_ev, aq)] + ([(s_dv, s_dv.count)] if ci == 0 else []),
                            inc=s_mm)
                    v1 = p.comp("vector", lambda v: v.tensor_scalar(
                        s_vrow[0:1, :], ps_r[0:1, :], 1.0, 1e-6, OP.mult, OP.add),
                        waits=[(s_mm, mv)], inc=s_dv)
                    a1 = p.comp("scalar", lambda s: s.activation(s_vrow[0:1, :], s_vrow[0:1, :], AF.Ln),
                                waits=[(s_dv, v1)], inc=s_ev)
                    a2 = p.comp("scalar", lambda s: s.activation(s_vrow[0:1, :], s_vrow[0:1, :],
                                                                 AF.Exp, scale=-0.5),
                                waits=[(s_ev, a1)], inc=s_ev)
                    v2 = p.comp("vector", lambda v, isc=inv_scale: v.tensor_scalar(
                        s_vrow[0:1, :], s_vrow[0:1, :], isc, None, OP.mult),
                        waits=[(s_ev, a2)], inc=s_dv)
                    bv_ = p.comp("tensor", lambda t: t.matmul(
                        ps_bc[:, :], consts.ones_row[:1, :], s_vrow[0:1, :],
                        start=True, stop=True, skip_group_check=True),
                        waits=[(s_dv, v2)], inc=s_mm)
                    for ci in range(2):
                        last = p.comp("vector", lambda v, ci=ci, t0=t0: v.tensor_tensor(
                            s_out[:, ci, t0:t0 + 512], s_cv[:, ci, t0:t0 + 512],
                            ps_bc[:, :], OP.mult),
                            waits=[(s_mm, bv_)] if ci == 0 else [], inc=s_dv)
                norm_done = (s_dv, last)
            stv = p.dma("sync", lambda sy, dst_dram=dst_dram, chunks=chunks: sy.dma_start(
                out=bass.AP(dst_dram, chunks[0] * 128 * NTOK,
                            [[NTOK, 128], [128 * NTOK, len(chunks)], [1, NTOK]]),
                in_=s_out[:, 0:len(chunks), :]), waits=[norm_done], inc=s_st)
            return (s_st, stv)

        gd = []
        for h in range(4):
            gd = [conv_unit(scr["qraw"], scr["qn"], cols["cq"], [2 * h, 2 * h + 1], True, 1.0 / 16.0, gd)]
        for h in range(4):
            gd = [conv_unit(scr["kraw"], scr["kn"], cols["ck"], [2 * h, 2 * h + 1], True, 1.0, gd)]
        for c2 in range(8):
            gd = [conv_unit(scr["vraw"], scr["vv"], cols["cv"], [2 * c2, 2 * c2 + 1], False, 1.0, gd)]
    p.barrier(f"da_barV{tag}")

    with contextlib.ExitStack() as esS:
        s_S = esS.enter_context(nc.sbuf_tensor(f"da_S{tag}", [128, 2, 512], F32))
        s_Sb = esS.enter_context(nc.sbuf_tensor(f"da_Sbf{tag}", [128, 2, 512], BF16))
        s_q = esS.enter_context(nc.sbuf_tensor(f"da_sq2{tag}", [128, 2, 128], BF16))
        s_k = esS.enter_context(nc.sbuf_tensor(f"da_sk2{tag}", [128, 2, 128], BF16))
        s_v = esS.enter_context(nc.sbuf_tensor(f"da_sv2{tag}", [128, 4, 128], BF16))
        s_gt = esS.enter_context(nc.sbuf_tensor(f"da_sgt{tag}", [128, 4, 128], BF16))
        s_gs = esS.enter_context(nc.sbuf_tensor(f"da_sgs{tag}", [128, 4, 128], BF16))
        s_qh2 = esS.enter_context(nc.sbuf_tensor(f"da_qh2{tag}", [128, 2, 128], BF16))
        s_kh2 = esS.enter_context(nc.sbuf_tensor(f"da_kh2{tag}", [128, 2, 128], BF16))
        s_kc2 = esS.enter_context(nc.sbuf_tensor(f"da_kc2{tag}", [128, 2, 128], BF16))
        s_kvT = esS.enter_context(nc.sbuf_tensor(f"da_kvT{tag}", [128, 2, 128], BF16))
        s_vtm = esS.enter_context(nc.sbuf_tensor(f"da_vtm{tag}", [128, 512], BF16))
        s_Brow = esS.enter_context(nc.sbuf_tensor(f"da_Brow{tag}", [1, T + 1], F32))
        s_brow = esS.enter_context(nc.sbuf_tensor(f"da_brow{tag}", [1, T], F32))
        s_blr = esS.enter_context(nc.sbuf_tensor(f"da_blr{tag}", [1, 128], F32))
        s_lamr = esS.enter_context(nc.sbuf_tensor(f"da_lamr{tag}", [1, 128], F32))
        s_ecr = esS.enter_context(nc.sbuf_tensor(f"da_ecr{tag}", [1, 128], F32))
        s_blm = esS.enter_context(nc.sbuf_tensor(f"da_blm{tag}", [1, 128], F32))
        s_LAM = esS.enter_context(nc.sbuf_tensor(f"da_LAM{tag}", [128, 128], F32))
        s_EC = esS.enter_context(nc.sbuf_tensor(f"da_EC{tag}", [128, 128], F32))
        s_BL = esS.enter_context(nc.sbuf_tensor(f"da_BL{tag}", [128, 128], F32))
        s_BB = esS.enter_context(nc.sbuf_tensor(f"da_BB{tag}", [128, 128], F32))
        s_blc = esS.enter_context(nc.sbuf_tensor(f"da_blc{tag}", [128, 1], F32))
        s_bbc = esS.enter_context(nc.sbuf_tensor(f"da_bbc{tag}", [128, 1], F32))
        s_D = esS.enter_context(nc.sbuf_tensor(f"da_D{tag}", [128, 128], F32))
        s_Eu = esS.enter_context(nc.sbuf_tensor(f"da_Eu{tag}", [128, 128], F32))
        s_El = esS.enter_context(nc.sbuf_tensor(f"da_El{tag}", [128, 128], F32))
        s_Ei = esS.enter_context(nc.sbuf_tensor(f"da_Ei{tag}", [128, 128], F32))
        s_N = [esS.enter_context(nc.sbuf_tensor(f"da_N{tag}{i}", [128, 128], BF16)) for i in range(2)]
        s_Nt = [esS.enter_context(nc.sbuf_tensor(f"da_Nt{tag}{i}", [128, 128], BF16)) for i in range(2)]
        s_Wt = esS.enter_context(nc.sbuf_tensor(f"da_Wt{tag}", [128, 128], BF16))
        s_U = esS.enter_context(nc.sbuf_tensor(f"da_U{tag}", [128, 512], BF16))
        s_Uf = esS.enter_context(nc.sbuf_tensor(f"da_Uf{tag}", [128, 512], F32))
        s_osq = esS.enter_context(nc.sbuf_tensor(f"da_osq{tag}", [128, 512], F32))
        s_ORS = esS.enter_context(nc.sbuf_tensor(f"da_ORS{tag}", [128, 128], F32))
        s_of = esS.enter_context(nc.sbuf_tensor(f"da_of{tag}", [128, 4, 128], BF16))
        s_tt = esS.enter_context(nc.sbuf_tensor(f"da_tt{tag}", [128, 128], F32))
        s_orow = esS.enter_context(nc.sbuf_tensor(f"da_orow{tag}", [1, 128], F32))
        ps_bc2 = esS.enter_context(nc.psum_tensor(f"da_pbc2{tag}", [128, 128], F32))
        ps_K = esS.enter_context(nc.psum_tensor(f"da_pK{tag}", [128, 128], F32))
        ps_U = esS.enter_context(nc.psum_tensor(f"da_pU{tag}", [128, 512], F32))
        ps_X = esS.enter_context(nc.psum_tensor(f"da_pX{tag}", [128, 128], F32))
        ps_O = esS.enter_context(nc.psum_tensor(f"da_pO{tag}", [128, 512], F32))
        ps_S2 = esS.enter_context(nc.psum_tensor(f"da_pS2{tag}", [128, 512], F32))
        ps_ro = esS.enter_context(nc.psum_tensor(f"da_pro{tag}", [1, 128], F32))
        ps_T = esS.enter_context(nc.psum_tensor(f"da_pT{tag}", [128, 128], BF16))

        zs = p.sem(f"da_zs{tag}")

        last_dv = [0]
        last_ev = [0]

        def dve(fn, waits=()):
            v = p.comp("vector", fn, waits=[(s_dv, last_dv[0])] + list(waits), inc=s_dv)
            last_dv[0] = v
            return v

        def act(fn, waits=()):
            v = p.comp("scalar", fn, waits=[(s_ev, last_ev[0])] + list(waits), inc=s_ev)
            last_ev[0] = v
            return v

        for b in range(SEQ_PER_CORE):
            for h in range(4):
                zz = p.comp("gpsimd", lambda g: g.memset(s_S[:, :, :], 0.0),
                            waits=[(s_dv, last_dv[0]), (s_mm, s_mm.count)])
                zz = p.comp("gpsimd", lambda g: g.memset(s_Sb[:, :, :], 0.0))
                zv = p.comp("gpsimd", lambda g: g.tensor_copy(s_S[:, 0, 0:4], s_S[:, 0, 0:4]),
                            inc=zs)
                zval = zs.count
                lB = p.dma("gpsimd", lambda g, b=b, h=h: g.dma_start(
                    out=s_Brow[0:1, :],
                    in_=bass.AP(scr["Bsc"], h * 2 * (T + 1) + b * (T + 1), [[T + 1, 1], [1, T + 1]])),
                    inc=s_ld)
                lb2 = p.dma("gpsimd", lambda g, b=b, h=h: g.dma_start(
                    out=s_brow[0:1, :],
                    in_=bass.AP(scr["beta"], h * NTOK + b * T, [[T, 1], [1, T]])),
                    inc=s_ld)
                for j in range(NCH):
                    c0 = j * CJ
                    tok0 = b * T + c0
                    lq = p.dma("gpsimd", lambda g, h=h, tok0=tok0: g.dma_start(
                        out=s_q[:, :, :],
                        in_=bass.AP(scr["qn"], (256 * h) * NTOK + tok0,
                                    [[NTOK, 128], [128 * NTOK, 2], [1, 128]])),
                        waits=[(s_mm, s_mm.count)], inc=s_ld)
                    lk = p.dma("gpsimd", lambda g, h=h, tok0=tok0: g.dma_start(
                        out=s_k[:, :, :],
                        in_=bass.AP(scr["kn"], (256 * h) * NTOK + tok0,
                                    [[NTOK, 128], [128 * NTOK, 2], [1, 128]])),
                        inc=s_ld)
                    lv_ = p.dma("gpsimd", lambda g, h=h, tok0=tok0: g.dma_start(
                        out=s_v[:, :, :],
                        in_=bass.AP(scr["vv"], (512 * h) * NTOK + tok0,
                                    [[NTOK, 128], [128 * NTOK, 4], [1, 128]])),
                        inc=s_ld)
                    lg = p.dma("gpsimd", lambda g, h=h, tok0=tok0: g.dma_start(
                        out=s_gt[:, :, :],
                        in_=bass.AP(scr["gate"], (512 * h) * NTOK + tok0,
                                    [[NTOK, 128], [128 * NTOK, 4], [1, 128]])),
                        inc=s_ld)
                    dve(lambda v, c0=c0: v.tensor_scalar(
                        s_blr[0:1, :], s_Brow[0:1, 1 + c0:1 + c0 + CJ],
                        s_Brow.ap()[0:1, c0:c0 + 1], None, OP.subtract),
                        waits=[(s_ld, lB)])
                    act(lambda s: s.activation(s_lamr[0:1, :], s_blr[0:1, :], AF.Exp),
                        waits=[(s_dv, last_dv[0])])
                    dve(lambda v: v.tensor_scalar(
                        s_ecr[0:1, :], s_blr[0:1, :], -1.0, s_blr.ap()[0:1, 127:128],
                        OP.mult, OP.add))
                    act(lambda s: s.activation(s_ecr[0:1, :], s_ecr[0:1, :], AF.Exp),
                        waits=[(s_dv, last_dv[0])])
                    dve(lambda v, b=b, c0=c0: v.tensor_tensor(
                        s_blm[0:1, :], s_brow[0:1, c0:c0 + CJ], s_lamr[0:1, :], OP.mult),
                        waits=[(s_ld, lb2), (s_ev, last_ev[0])])
                    rows_done = last_dv[0]
                    bc_jobs = [(s_lamr, s_LAM), (s_ecr, s_EC), (s_blm, s_BL)]
                    mvp = None
                    for rsrc, rdst in bc_jobs:
                        mv = p.comp("tensor", lambda t, rsrc=rsrc: t.matmul(
                            ps_bc2[:, :], consts.ones_row[:1, :], rsrc[0:1, :],
                            start=True, stop=True, skip_group_check=True),
                            waits=[(s_dv, rows_done), (s_ev, last_ev[0])] +
                                  ([(s_dv, last_dv[0])] if mvp else []), inc=s_mm)
                        dve(lambda v, rdst=rdst: v.tensor_copy(rdst[:, :], ps_bc2[:, :]),
                            waits=[(s_mm, mv)])
                        mvp = mv
                    mv = p.comp("tensor", lambda t, b=b, c0=c0: t.matmul(
                        ps_bc2[:, :], consts.ones_row[:1, :], s_brow[0:1, c0:c0 + CJ],
                        start=True, stop=True, skip_group_check=True),
                        waits=[(s_dv, last_dv[0])], inc=s_mm)
                    dve(lambda v: v.tensor_copy(s_BB[:, :], ps_bc2[:, :]), waits=[(s_mm, mv)])
                    mv = p.comp("tensor", lambda t: t.matmul(
                        ps_bc2[:, 0:1], s_blr[0:1, :], consts.ones_row[0:1, 0:1],
                        start=True, stop=True, skip_group_check=True),
                        waits=[(s_dv, last_dv[0])], inc=s_mm)
                    dve(lambda v: v.tensor_copy(s_blc[:, :], ps_bc2[:, 0:1]), waits=[(s_mm, mv)])
                    mv = p.comp("tensor", lambda t, b=b, c0=c0: t.matmul(
                        ps_bc2[:, 0:1], s_brow[0:1, c0:c0 + CJ], consts.ones_row[0:1, 0:1],
                        start=True, stop=True, skip_group_check=True),
                        waits=[(s_dv, last_dv[0])], inc=s_mm)
                    dve(lambda v: v.tensor_copy(s_bbc[:, :], ps_bc2[:, 0:1]), waits=[(s_mm, mv)])
                    mv = p.comp("tensor", lambda t: t.matmul(
                        ps_bc2[:, :], consts.ones_row[:1, :], s_blr[0:1, :],
                        start=True, stop=True, skip_group_check=True),
                        waits=[(s_dv, last_dv[0])], inc=s_mm)
                    dve(lambda v: v.tensor_scalar(s_D[:, :], ps_bc2[:, :], -1.0,
                                                  s_blc.ap()[:, 0:1], OP.mult, OP.add),
                        waits=[(s_mm, mv)])
                    dve(lambda v: v.scalar_tensor_tensor(s_Eu[:, :], s_D[:, :], -1.0,
                                                         maskLI[:, :], OP.mult, OP.add),
                        waits=[(msem, mval), (s_ev, last_ev[0])])
                    dve(lambda v: v.tensor_tensor(s_El[:, :], s_D[:, :], maskSU[:, :], OP.add))
                    dve(lambda v: v.scalar_tensor_tensor(s_Ei[:, :], s_D[:, :], -1.0,
                                                         maskLS[:, :], OP.mult, OP.add))
                    act(lambda s: s.activation(s_Eu[:, :], s_Eu[:, :], AF.Exp),
                        waits=[(s_dv, last_dv[0])])
                    act(lambda s: s.activation(s_El[:, :], s_El[:, :], AF.Exp))
                    act(lambda s: s.activation(s_Ei[:, :], s_Ei[:, :], AF.Exp))
                    etiles = last_ev[0]
                    for ci in range(2):
                        p.comp("tensor", lambda t, ci=ci: t.matmul(
                            ps_K[:, :], s_k[:, ci, :], s_k[:, ci, :],
                            start=(ci == 0), stop=(ci == 1), skip_group_check=True),
                            waits=[(s_ld, lk), (s_dv, last_dv[0])] if ci == 0 else [],
                            inc=(s_mm if ci == 1 else None))
                    kkv = s_mm.count
                    dve(lambda v: v.tensor_tensor(s_tt[:, :], ps_K[:, :], s_El[:, :], OP.mult),
                        waits=[(s_mm, kkv), (s_ev, etiles)])
                    dve(lambda v: v.tensor_scalar(s_N[0][:, :], s_tt[:, :],
                                                  s_bbc.ap()[:, 0:1], None, OP.mult))
                    dve(lambda v: v.tensor_tensor(s_tt[:, :], ps_K[:, :], s_Eu[:, :], OP.mult))
                    ndone = dve(lambda v: v.tensor_tensor(s_Nt[0][:, :], s_tt[:, :], s_BB[:, :],
                                                          OP.mult))
                    for ci in range(2):
                        p.comp("tensor", lambda t, ci=ci: t.matmul(
                            ps_K[:, :], s_k[:, ci, :], s_q[:, ci, :],
                            start=(ci == 0), stop=(ci == 1), skip_group_check=True),
                            waits=[(s_ld, lq), (s_dv, ndone)] if ci == 0 else [],
                            inc=(s_mm if ci == 1 else None))
                    kqv = s_mm.count
                    wtv = dve(lambda v: v.tensor_tensor(s_Wt[:, :], ps_K[:, :], s_Ei[:, :], OP.mult),
                              waits=[(s_mm, kqv)])
                    for ci in range(2):
                        dve(lambda v, ci=ci: v.tensor_tensor(s_qh2[:, ci, :], s_q[:, ci, :],
                                                             s_LAM[:, :], OP.mult))
                        dve(lambda v, ci=ci: v.tensor_tensor(s_kh2[:, ci, :], s_k[:, ci, :],
                                                             s_BL[:, :], OP.mult))
                        dve(lambda v, ci=ci: v.tensor_tensor(s_kc2[:, ci, :], s_k[:, ci, :],
                                                             s_EC[:, :], OP.mult))
                    hats = last_dv[0]
                    tv = None
                    for ci in range(2):
                        tv = p.comp("tensor", lambda t, ci=ci: t.transpose(
                            ps_T[:, :], s_kc2[:, ci, :], consts.ident_bf[:, :]),
                            waits=[(s_dv, last_dv[0])], inc=s_mm)
                        dve(lambda v, ci=ci: v.tensor_copy(s_kvT[:, ci, :], ps_T[:, :]),
                            waits=[(s_mm, tv)])
                    for ci in range(4):
                        tv = p.comp("tensor", lambda t, ci=ci: t.transpose(
                            ps_T[:, :], s_v[:, ci, :], consts.ident_bf[:, :]),
                            waits=[(s_ld, lv_), (s_dv, last_dv[0])], inc=s_mm)
                        dve(lambda v, ci=ci: v.tensor_copy(s_vtm[:, ci * 128:(ci + 1) * 128],
                                                           ps_T[:, :]),
                            waits=[(s_mm, tv)])
                    vt_done = last_dv[0]
                    for ci in range(2):
                        p.comp("tensor", lambda t, ci=ci: t.matmul(
                            ps_U[:, :], s_kh2[:, ci, :], s_Sb[:, ci, :],
                            start=(ci == 0), stop=(ci == 1), skip_group_check=True),
                            waits=[(s_dv, vt_done), (zs, zval)] if ci == 0 else [],
                            inc=(s_mm if ci == 1 else None))
                    o2v = s_mm.count
                    dve(lambda v: v.scalar_tensor_tensor(
                        s_Uf[:, :], s_vtm[:, :], s_bbc.ap()[:, 0:1], ps_U[:, :],
                        OP.mult, OP.subtract),
                        waits=[(s_mm, o2v)])
                    dve(lambda v: v.tensor_copy(s_U[:, :], s_Uf[:, :]))
                    cpar = 0
                    for it in range(7):
                        mv = p.comp("tensor", lambda t, cpar=cpar: t.matmul(
                            ps_U[:, :], s_Nt[cpar][:, :], s_U[:, :],
                            start=True, stop=True, skip_group_check=True),
                            waits=[(s_dv, last_dv[0])], inc=s_mm)
                        dve(lambda v, it=it: v.tensor_tensor(
                            s_Uf[:, :], s_Uf[:, :], ps_U[:, :],
                            OP.subtract if it == 0 else OP.add),
                            waits=[(s_mm, mv)])
                        dve(lambda v: v.tensor_copy(s_U[:, :], s_Uf[:, :]))
                        if it < 6:
                            mv = p.comp("tensor", lambda t, cpar=cpar: t.matmul(
                                ps_X[:, :], s_Nt[cpar][:, :], s_N[cpar][:, :],
                                start=True, stop=True, skip_group_check=True),
                                waits=[(s_dv, last_dv[0])], inc=s_mm)
                            dve(lambda v, cpar=cpar: v.tensor_copy(s_N[1 - cpar][:, :], ps_X[:, :]),
                                waits=[(s_mm, mv)])
                            mv = p.comp("tensor", lambda t, cpar=cpar: t.transpose(
                                ps_T[:, :], s_N[1 - cpar][:, :], consts.ident_bf[:, :]),
                                waits=[(s_dv, last_dv[0])], inc=s_mm)
                            dve(lambda v, cpar=cpar: v.tensor_copy(s_Nt[1 - cpar][:, :], ps_T[:, :]),
                                waits=[(s_mm, mv)])
                            cpar = 1 - cpar
                    u_done = last_dv[0]
                    for vt in range(4):
                        for ci in range(2):
                            p.comp("tensor", lambda t, vt=vt, ci=ci: t.matmul(
                                ps_O[:, vt * 128:(vt + 1) * 128],
                                s_Sb[:, ci, vt * 128:(vt + 1) * 128], s_qh2[:, ci, :],
                                start=(ci == 0), stop=False, skip_group_check=True),
                                waits=[(s_dv, u_done)] if (vt == 0 and ci == 0) else [])
                        p.comp("tensor", lambda t, vt=vt: t.matmul(
                            ps_O[:, vt * 128:(vt + 1) * 128],
                            s_U[:, vt * 128:(vt + 1) * 128], s_Wt[:, :],
                            start=False, stop=True, skip_group_check=True),
                            inc=(s_mm if vt == 3 else None))
                    ov = s_mm.count
                    if "dbg" in scr and b == 1 and h == 0 and j == 3:
                        for off, t_sb in ((0, s_Wt), (1, s_U), (2, s_Ei), (3, s_LAM), (5, maskLS)):
                            width = 512 if t_sb is s_U else 128
                            p.dma("gpsimd", lambda g, off=off, t_sb=t_sb, width=width: g.dma_start(
                                out=bass.AP(scr["dbg"], off * 128 * 512, [[width, 128], [1, width]]),
                                in_=t_sb[:, 0:width] if t_sb is not s_U else t_sb[:, :]),
                                waits=[(s_mm, ov), (s_dv, last_dv[0]), (s_ev, last_ev[0])], inc=s_st)
                        p.dma("gpsimd", lambda g: g.dma_start(
                            out=bass.AP(scr["dbg"], 4 * 128 * 512, [[128, 1], [1, 128]]),
                            in_=s_blr[0:1, :]), inc=s_st)
                    for ci in range(2):
                        mv = p.comp("tensor", lambda t, ci=ci: t.matmul(
                            ps_S2[:, :], s_kvT[:, ci, :], s_U[:, :],
                            start=True, stop=True, skip_group_check=True),
                            waits=[(s_dv, last_dv[0])], inc=s_mm)
                        dve(lambda v, ci=ci: v.scalar_tensor_tensor(
                            s_S[:, ci, :], s_S[:, ci, :], s_LAM.ap()[:, 127:128],
                            ps_S2[:, :], OP.mult, OP.add),
                            waits=[(s_mm, mv)])
                        dve(lambda v, ci=ci: v.tensor_copy(s_Sb[:, ci, :], s_S[:, ci, :]))
                    dump_wait = []
                    if "oraw" in scr:
                        aqd = act(lambda s: s.activation(s_osq[:, :], ps_O[:, :], AF.Copy),
                                  waits=[(s_mm, ov)])
                        odma = p.dma("sync", lambda sy, h=h, tok0=tok0: sy.dma_start(
                            out=bass.AP(scr["oraw"], (512 * h) * NTOK + tok0,
                                        [[NTOK, 128], [128 * NTOK, 4], [1, 128]]),
                            in_=s_osq.ap()[:, 0:512].rearrange("p (a b) -> p a b", b=128)),
                            waits=[(s_ev, aqd)], inc=s_st)
                        dump_wait = [(s_st, odma)]
                    aq = act(lambda s: s.activation(s_osq[:, :], ps_O[:, :], AF.Square),
                             waits=[(s_mm, ov)] + dump_wait)
                    mv = None
                    for vt in range(4):
                        mv = p.comp("tensor", lambda t, vt=vt: t.matmul(
                            ps_ro[0:1, :], consts.ones_col[:, :],
                            s_osq[:, vt * 128:(vt + 1) * 128],
                            start=(vt == 0), stop=(vt == 3), skip_group_check=True),
                            waits=[(s_ev, aq), (s_dv, last_dv[0])] if vt == 0 else [],
                            inc=(s_mm if vt == 3 else None))
                    dve(lambda v: v.tensor_scalar(s_orow[0:1, :], ps_ro[0:1, :],
                                                  1.0 / 512.0, 1e-6, OP.mult, OP.add),
                        waits=[(s_mm, mv)])
                    act(lambda s: s.activation(s_orow[0:1, :], s_orow[0:1, :], AF.Ln),
                        waits=[(s_dv, last_dv[0])])
                    act(lambda s: s.activation(s_orow[0:1, :], s_orow[0:1, :], AF.Exp, scale=-0.5))
                    mv = p.comp("tensor", lambda t: t.matmul(
                        ps_bc2[:, :], consts.ones_row[:1, :], s_orow[0:1, :],
                        start=True, stop=True, skip_group_check=True),
                        waits=[(s_ev, last_ev[0]), (s_dv, last_dv[0])], inc=s_mm)
                    dve(lambda v: v.tensor_copy(s_ORS[:, :], ps_bc2[:, :]), waits=[(s_mm, mv)])
                    av = act(lambda s: s.activation(s_gs[:, :, :], s_gt[:, :, :], AF.Silu),
                             waits=[(s_ld, lg)])
                    for vt in range(4):
                        dve(lambda v, vt=vt: v.tensor_tensor(
                            s_tt[:, :], ps_O[:, vt * 128:(vt + 1) * 128], s_ORS[:, :], OP.mult))
                        dve(lambda v, vt=vt: v.scalar_tensor_tensor(
                            s_of[:, vt, :], s_tt[:, :], cols["onw"][:, vt:vt + 1],
                            s_gs[:, vt, :], OP.mult, OP.mult),
                            waits=[(s_ev, av)])
                    stv = p.dma("sync", lambda sy, h=h, tok0=tok0: sy.dma_start(
                        out=bass.AP(scr["osc"], (512 * h) * NTOK + tok0,
                                    [[NTOK, 128], [128 * NTOK, 4], [1, 128]]),
                        in_=s_of[:, :, :]), waits=[(s_dv, last_dv[0])], inc=s_st)
    p.barrier(f"da_barS{tag}")

    with contextlib.ExitStack() as esO:
        s_wo = esO.enter_context(nc.sbuf_tensor(f"da_wo{tag}", [128, 16, D], BF16))
        s_w2 = esO.enter_context(nc.sbuf_tensor(f"da_w2{tag}", [128, 8, D], BF16))
        for dst, key, nch in ((s_wo, "Wo", 16), (s_w2, "W2", 8)):
            p.dma("gpsimd", lambda g, dst=dst, key=key, nch=nch: g.dma_start(
                out=dst[:, :, :],
                in_=bass.AP(W[key], 0, [[D, 128], [128 * D, nch], [1, D]])), inc=s_ld)
        wld2 = s_ld.count
        s_o = esO.enter_context(nc.sbuf_tensor(f"da_og{tag}", [128, 16, Tg], BF16))
        s_xso = esO.enter_context(nc.sbuf_tensor(f"da_xso{tag}", [128, 8, Tg], F32))
        s_xo = esO.enter_context(nc.sbuf_tensor(f"da_xo{tag}", [128, 8, Tg], F32))
        s_x2 = esO.enter_context(nc.sbuf_tensor(f"da_x2b{tag}", [128, 8, Tg], F32))
        s_h2n = esO.enter_context(nc.sbuf_tensor(f"da_h2n{tag}", [128, 8, Tg], BF16))
        s_t2 = esO.enter_context(nc.sbuf_tensor(f"da_t2{tag}", [128, Tg], F32))
        s_y2 = esO.enter_context(nc.sbuf_tensor(f"da_y2{tag}", [128, 8, Tg], F32))
        lnO = make_ln_ctx(p, nc, esO, consts, Tg, "dO" + tag)
        ps_p2 = [esO.enter_context(nc.psum_tensor(f"da_po{tag}{i}", [128, Tg], F32)) for i in range(2)]
        bank_free = {}
        ln_last = None
        st_prev = None
        for g in range(NG):
            lw = [(s_ld, wld2)]
            if ln_last is not None:
                lw.append((s_dv, s_dv.count))
            lo = p.dma("sync", lambda sy, g=g: sy.dma_start(
                out=s_o[:, :, :],
                in_=bass.AP(scr["osc"], g * Tg, [[NTOK, 128], [128 * NTOK, 16], [1, Tg]])),
                waits=lw, inc=s_ld)
            lxs = p.dma("sync", lambda sy, g=g: sy.dma_start(
                out=s_xso[:, :, :],
                in_=bass.AP(scr["xsub"], g * Tg, [[NTOK, 128], [128 * NTOK, 8], [1, Tg]])),
                inc=s_ld)
            lx = p.dma("sync", lambda sy, g=g: sy.dma_start(
                out=s_xo[:, :, :],
                in_=bass.AP(x_in, g * Tg, [[NTOK, 128], [128 * NTOK, 8], [1, Tg]])),
                inc=s_ld)
            x2_done = None
            for ot in range(8):
                bi = ot % 2
                for ic in range(16):
                    w = []
                    if ic == 0:
                        w = [(s_ld, lo)]
                        if bi in bank_free:
                            w.append(bank_free[bi])
                    p.comp("tensor", lambda t, ic=ic, ot=ot, bi=bi: t.matmul(
                        ps_p2[bi][:, :], s_wo[:, ic, ot * 128:(ot + 1) * 128],
                        s_o.ap()[:, ic, :], start=(ic == 0), stop=(ic == 15),
                        skip_group_check=True), waits=w, inc=(s_mm if ic == 15 else None))
                mmv = s_mm.count
                hv = p.comp("vector", lambda v, ot=ot, bi=bi: v.tensor_tensor(
                    s_x2.ap()[:, ot, :], ps_p2[bi][:, :], s_xso.ap()[:, ot, :], OP.add),
                    waits=[(s_mm, mmv), (s_ld, lxs)] +
                          ([(lnO["sem_dv"], x2_done)] if (x2_done and ot == 0) else []),
                    inc=s_dv)
                bank_free[bi] = (s_dv, hv)
            x2v = s_dv.count
            wr = []
            wr = [(s_mm, s_mm.count)] if g > 0 else []
            lnsem, lnval = emit_layernorm(p, lnO, s_x2.ap(), s_h2n.ap(),
                                          [(s_dv, x2v)], write_waits=wr,
                                          affine=(cols["ln2g"], cols["ln2b"]))
            ln_last = lnval
            xxv = None
            for c in range(8):
                xxv = p.comp("vector", lambda v, c=c: v.tensor_tensor(
                    s_xo.ap()[:, c, :], s_x2.ap()[:, c, :], s_xo.ap()[:, c, :], OP.add),
                    waits=[(s_dv, x2v), (s_ld, lx)] if c == 0 else [], inc=s_dv)
            yv = None
            for ot in range(8):
                bi = ot % 2
                for ic in range(8):
                    w = []
                    if ic == 0:
                        w = [(lnsem, lnval)]
                        if bi in bank_free:
                            w.append(bank_free[bi])
                    p.comp("tensor", lambda t, ic=ic, ot=ot, bi=bi: t.matmul(
                        ps_p2[bi][:, :], s_w2[:, ic, ot * 128:(ot + 1) * 128],
                        s_h2n.ap()[:, ic, :], start=(ic == 0), stop=(ic == 7),
                        skip_group_check=True), waits=w, inc=(s_mm if ic == 7 else None))
                mmv = s_mm.count
                gw = [(s_mm, mmv)]
                if yv is not None:
                    gw.append((s_dv, yv))
                ge = p.comp("scalar", lambda s, ot=ot, bi=bi: s.activation(
                    s_t2[:, :], ps_p2[bi][:, :], AF.Gelu, bias=cols["b2"][:, ot:ot + 1]),
                    waits=gw, inc=s_ev)
                yv = p.comp("vector", lambda v, ot=ot: v.tensor_tensor(
                    s_y2.ap()[:, ot, :], s_t2[:, :], s_xo.ap()[:, ot, :], OP.add),
                    waits=[(s_ev, ge), (s_dv, xxv)] +
                          ([(s_st, st_prev)] if (st_prev and ot == 0) else []), inc=s_dv)
                bank_free[bi] = (s_dv, yv)
            st_prev = p.dma("sync", lambda sy, g=g: sy.dma_start(
                out=bass.AP(y_out, g * Tg, [[NTOK, 128], [128 * NTOK, 8], [1, Tg]]),
                in_=s_y2[:, :, :]), waits=[(s_dv, s_dv.count)], inc=s_st)
            x2_done = lnval
    p.wait("sync", [(s_st, s_st.count)])
    return s_st


# version 06
# speedup vs baseline: 1.1494x; 1.0038x over previous
import contextlib
import math
import os

import numpy as np

os.environ.setdefault("GAUGE_NTFF_IGNORE_DMA", "1")

import concourse.bass as bass
import concourse.mybir as mybir
from concourse.bass_utils import run_bass_kernel_spmd

F32 = mybir.dt.float32
BF16 = mybir.dt.bfloat16
U8 = mybir.dt.uint8
AF = mybir.ActivationFunctionType
OP = mybir.AluOpType

NCORES = 8
SEQ_PER_CORE = 2
T = 2048
NTOK = SEQ_PER_CORE * T
D = 1024


class Sem:
    __slots__ = ("name", "count", "h")

    def __init__(self, name):
        self.name = name
        self.count = 0
        self.h = None


class _Op:
    __slots__ = ("fn", "waits", "inc", "inc_k", "pure")

    def __init__(self, fn, waits, inc, inc_k, pure):
        self.fn = fn
        self.waits = waits
        self.inc = inc
        self.inc_k = inc_k
        self.pure = pure


class Prog:
    ENGINES = ("tensor", "vector", "scalar", "gpsimd", "sync")

    def __init__(self, nc):
        self.nc = nc
        self.ops = {e: [] for e in self.ENGINES}
        self.sems = []
        self._pending_waits = {e: [] for e in self.ENGINES}

    def sem(self, name):
        s = Sem(name)
        self.sems.append(s)
        return s

    def on(self, eng, fn, waits=(), inc=None, inc_k=1, pure=False):
        wl = [(s, int(v)) for (s, v) in waits]
        if self._pending_waits[eng]:
            wl = self._pending_waits[eng] + wl
            self._pending_waits[eng] = []
        self.ops[eng].append(_Op(fn, wl, inc, inc_k, pure))
        if inc is not None:
            inc.count += inc_k
            return inc.count
        return None

    def comp(self, eng, fn, waits=(), inc=None):
        return self.on(eng, fn, waits, inc, 1)

    def dma(self, eng, fn, waits=(), inc=None):
        return self.on(eng, fn, waits, inc, 16)

    def wait(self, eng, pairs):
        self.on(eng, lambda e: None, waits=pairs, pure=True)

    def barrier(self, name):
        if not hasattr(self, "_bar_sem"):
            self._bar_sem = self.sem("p_barrier")
        done = self._bar_sem
        pairs = []
        for e in self.ENGINES:
            lst = self.ops[e]
            idx = None
            for i in range(len(lst) - 1, -1, -1):
                if not lst[i].pure:
                    idx = i
                    break
            if idx is None:
                continue
            op = lst[idx]
            if op.inc is None:
                op.inc = done
                op.inc_k = 1
                done.count += 1
                pairs.append((done, done.count))
            else:
                pairs.append((op.inc, op.inc.count))
        best = {}
        for s, v in pairs:
            best[s] = max(best.get(s, 0), v)
        pairs = [(s, v) for s, v in best.items()]
        for e in self.ENGINES:
            self._pending_waits[e] = self._pending_waits[e] + [(s, int(v)) for s, v in pairs]
        return pairs

    def build(self):
        nc = self.nc
        with contextlib.ExitStack() as es:
            for s in self.sems:
                s.h = es.enter_context(nc.semaphore(s.name))
            block = es.enter_context(nc.Block())

            def mk(eng_name):
                def emit(engine):
                    for op in self.ops[eng_name]:
                        for (s, v) in op.waits:
                            engine.wait_ge(s.h, v)
                        instr = op.fn(engine)
                        if op.inc is not None:
                            assert instr is not None, f"op on {eng_name} with inc returned no instr"
                            instr.then_inc(op.inc.h, op.inc_k)
                return emit

            for e in self.ENGINES:
                if self.ops[e]:
                    getattr(block, e)(mk(e))


class Consts:
    def __init__(self, p, nc, es):
        self.ones_col = es.enter_context(nc.sbuf_tensor("c_ones_col", [128, 1], F32))
        self.ones_row = es.enter_context(nc.sbuf_tensor("c_ones_row", [1, 128], F32))
        self.ident_bf = es.enter_context(nc.sbuf_tensor("c_ident_bf", [128, 128], BF16))
        self.anti128 = es.enter_context(nc.sbuf_tensor("c_anti128", [128, 128], BF16))
        self.anti127 = es.enter_context(nc.sbuf_tensor("c_anti127", [128, 128], BF16))
        sem = p.sem("c_const")

        def emit(g):
            g.memset(self.ones_col[:, :], 1.0)
            g.memset(self.ones_row[:1, :], 1.0)
            g.memset(self.ident_bf[:, :], 0.0)
            g.memset(self.anti128[:, :], 0.0)
            g.memset(self.anti127[:, :], 0.0)
            g.affine_select(
                out=self.anti128[:, :], in_=self.anti128[:, :],
                compare_op=OP.not_equal, fill=1.0, base=-127,
                pattern=[[1, 128]], channel_multiplier=1)
            g.affine_select(
                out=self.anti127[:, :], in_=self.anti127[:, :],
                compare_op=OP.not_equal, fill=1.0, base=-126,
                pattern=[[1, 128]], channel_multiplier=1)
            return g.affine_select(
                out=self.ident_bf[:, :], in_=self.ident_bf[:, :],
                compare_op=OP.not_equal, fill=1.0, base=0,
                pattern=[[-1, 128]], channel_multiplier=1)

        def emit2(g):
            g.tensor_copy(self.ones_col[:, :], self.ones_col[:, :])
            return g.tensor_copy(self.ones_row[:1, :], self.ones_row[:1, :])

        p.comp("gpsimd", emit)
        self.ready = p.comp("gpsimd", emit2, inc=sem)
        self.sem = sem


def make_ln_ctx(p, nc, es, consts, Tg, tag, share=None):
    if share is not None:
        psums = {k: share[k] for k in ("ps_s", "ps_q", "ps_A", "ps_B")}
    else:
        psums = {
            "ps_s": es.enter_context(nc.psum_tensor(f"ln_ps_s{tag}", [1, Tg], F32)),
            "ps_q": es.enter_context(nc.psum_tensor(f"ln_ps_q{tag}", [1, Tg], F32)),
            "ps_A": es.enter_context(nc.psum_tensor(f"ln_ps_A{tag}", [128, Tg], F32)),
            "ps_B": es.enter_context(nc.psum_tensor(f"ln_ps_B{tag}", [128, Tg], F32)),
        }
    ctx = {
        "consts": consts,
        "s_xsq": es.enter_context(nc.sbuf_tensor(f"ln_xsq{tag}", [128, Tg], F32)),
        "s_t1": es.enter_context(nc.sbuf_tensor(f"ln_t1{tag}", [128, Tg], F32)),
        "s_rows": [es.enter_context(nc.sbuf_tensor(f"ln_row{i}{tag}", [1, Tg], F32)) for i in range(4)],
        **psums,
        "sem_sq": p.sem(f"ln_sq{tag}"),
        "sem_mm": p.sem(f"ln_mm{tag}"),
        "sem_dv": p.sem(f"ln_dv{tag}"),
        "sem_ac": p.sem(f"ln_ac{tag}"),
        "Tg": Tg,
    }
    return ctx


def emit_layernorm(p, ctx, s_x3, s_xn3, xin_ready, write_waits=(), n_chunks=8,
                   eps=1e-5, affine=None, resid3=None):
    consts = ctx["consts"]
    s_xsq = ctx["s_xsq"]
    r_m, r_rstd, r_b, r_t = (r[:1, :] for r in ctx["s_rows"])
    ps_s, ps_q, ps_A, ps_B = ctx["ps_s"], ctx["ps_q"], ctx["ps_A"], ctx["ps_B"]
    sq, mm, dv, ac = ctx["sem_sq"], ctx["sem_mm"], ctx["sem_dv"], ctx["sem_ac"]

    stats_guard = [(dv, dv.count)]

    for c in range(n_chunks):
        w = list(xin_ready) if c == 0 else [(mm, mm.count)]
        sv = p.comp("scalar", lambda s, c=c: s.activation(s_xsq[:, :], s_x3[:, c, :], AF.Square),
                    waits=w, inc=sq)
        p.comp("tensor", lambda t, c=c, n=n_chunks: t.matmul(
            ps_s[:1, :], consts.ones_col[:, :], s_x3[:, c, :],
            start=(c == 0), stop=(c == n - 1), skip_group_check=True),
            waits=(list(xin_ready) + stats_guard) if c == 0 else [])
        p.comp("tensor", lambda t, c=c, n=n_chunks: t.matmul(
            ps_q[:1, :], consts.ones_col[:, :], s_xsq[:, :],
            start=(c == 0), stop=(c == n - 1), skip_group_check=True),
            waits=[(sq, sv)], inc=mm)
    stats_done = mm.count

    nd = float(n_chunks * 128)
    p.comp("vector", lambda v: v.tensor_scalar(r_m, ps_s[:1, :], 1.0 / nd, None, OP.mult),
           waits=[(mm, stats_done)])
    p.comp("vector", lambda v: v.tensor_scalar(r_t, ps_q[:1, :], 1.0 / nd, None, OP.mult))
    p.comp("vector", lambda v: v.tensor_tensor(r_b, r_m, r_m, OP.mult))
    dvv = p.comp("vector", lambda v: v.scalar_tensor_tensor(r_t, r_t, eps, r_b,
                                                            OP.add, OP.subtract), inc=dv)
    p.comp("scalar", lambda s: s.activation(r_t, r_t, AF.Ln), waits=[(dv, dvv)])
    acv = p.comp("scalar", lambda s: s.activation(r_rstd, r_t, AF.Exp, scale=-0.5), inc=ac)
    p.comp("vector", lambda v: v.tensor_tensor(r_b, r_m, r_rstd, OP.mult), waits=[(ac, acv)])
    dvv = p.comp("vector", lambda v: v.tensor_scalar(r_b, r_b, -1.0, None, OP.mult), inc=dv)
    p.comp("tensor", lambda t: t.matmul(ps_A[:, :], consts.ones_row[:1, :], r_rstd,
                                        start=True, stop=True), waits=[(dv, dvv)])
    mv = p.comp("tensor", lambda t: t.matmul(ps_B[:, :], consts.ones_row[:1, :], r_b,
                                             start=True, stop=True), inc=mm)
    t1 = ctx["s_t1"]
    chunk_done = []
    for c in range(n_chunks):
        w = ([(mm, mv)] + list(write_waits)) if c == 0 else []
        p.comp("vector", lambda v, c=c: v.tensor_tensor(t1[:, :], s_x3[:, c, :], ps_A[:, :], OP.mult),
               waits=w)
        if affine is None and resid3 is None:
            cv = p.comp("vector", lambda v, c=c: v.tensor_tensor(s_xn3[:, c, :], t1[:, :], ps_B[:, :], OP.add),
                        inc=dv)
        else:
            p.comp("vector", lambda v, c=c: v.tensor_tensor(t1[:, :], t1[:, :], ps_B[:, :], OP.add))
            if affine is not None:
                gam, bet = affine
                if resid3 is None:
                    cv = p.comp("vector", lambda v, c=c, gam=gam, bet=bet: v.tensor_scalar(
                        s_xn3[:, c, :], t1[:, :], gam[:, c:c + 1], bet[:, c:c + 1],
                        OP.mult, OP.add), inc=dv)
                else:
                    p.comp("vector", lambda v, c=c, gam=gam, bet=bet: v.tensor_scalar(
                        t1[:, :], t1[:, :], gam[:, c:c + 1], bet[:, c:c + 1], OP.mult, OP.add))
                    cv = p.comp("vector", lambda v, c=c: v.tensor_tensor(
                        s_xn3[:, c, :], t1[:, :], resid3[:, c, :], OP.add), inc=dv)
            else:
                cv = p.comp("vector", lambda v, c=c: v.tensor_tensor(
                    s_xn3[:, c, :], t1[:, :], resid3[:, c, :], OP.add), inc=dv)
        chunk_done.append(cv)
    return dv, chunk_done[-1]


def emit_wprep(p, nc, es, consts, jobs, tag):
    s_src = [es.enter_context(nc.sbuf_tensor(f"wp_src{tag}{i}", [128, 128], BF16)) for i in range(2)]
    maxO = max(src.shape[0] for src, _ in jobs)
    s_strip = es.enter_context(nc.sbuf_tensor(f"wp_strip{tag}", [128, maxO], BF16))
    ps = [es.enter_context(nc.psum_tensor(f"wp_ps{tag}{i}", [128, 128], BF16)) for i in range(2)]
    s_ld = p.sem(f"wp_ld{tag}")
    s_mm = p.sem(f"wp_mm{tag}")
    s_ev = p.sem(f"wp_ev{tag}")
    s_st = p.sem(f"wp_st{tag}")

    k = 0
    mm_at = {}
    for src, dst in jobs:
        O, I = src.shape
        n_ob = O // 128
        for ic in range(I // 128):
            st_guard = s_st.count
            for ob in range(n_ob):
                buf = k % 2
                w = [(consts.sem, consts.ready)] if k < 2 else [(s_mm, mm_at[k - 2])]
                lv = p.dma("gpsimd",
                           lambda g, src=src, ob=ob, ic=ic, buf=buf: g.dma_start(
                               out=s_src[buf][:, :],
                               in_=src.ap()[ob * 128:(ob + 1) * 128, ic * 128:(ic + 1) * 128]),
                           waits=w, inc=s_ld)
                w = [(s_ld, lv)]
                if k >= 2:
                    w.append((s_ev, s_ev.count - 1))
                mv = p.comp("tensor",
                            lambda t, buf=buf: t.transpose(ps[buf][:, :], s_src[buf][:, :],
                                                           consts.ident_bf[:, :]),
                            waits=w, inc=s_mm)
                mm_at[k] = mv
                ev_w = [(s_mm, mv)]
                if ob == 0:
                    ev_w.append((s_st, st_guard))
                p.comp("vector",
                       lambda v, buf=buf, ob=ob: v.tensor_copy(
                           s_strip[:, ob * 128:(ob + 1) * 128], ps[buf][:, :]),
                       waits=ev_w, inc=s_ev)
                k += 1
            p.dma("sync",
                  lambda sy, dst=dst, ic=ic, O=O: sy.dma_start(
                      out=dst.ap()[ic * 128:(ic + 1) * 128, :],
                      in_=s_strip[:, 0:O]),
                  waits=[(s_ev, s_ev.count)], inc=s_st)
    p.wait("sync", [(s_st, s_st.count)])
    return s_st


def emit_l0ffn(p, nc, es, consts, x_in, y_out, W, tag):
    Tg = 512
    NG = NTOK // Tg

    s_x = es.enter_context(nc.sbuf_tensor(f"ff_x{tag}", [128, 8, Tg], F32))
    s_xn = es.enter_context(nc.sbuf_tensor(f"ff_xn{tag}", [128, 8, Tg], BF16))
    s_ha = es.enter_context(nc.sbuf_tensor(f"ff_ha{tag}", [128, 32, Tg], BF16))
    s_hb = es.enter_context(nc.sbuf_tensor(f"ff_hb{tag}", [128, 32, Tg], BF16))
    s_u = es.enter_context(nc.sbuf_tensor(f"ff_u{tag}", [128, Tg], BF16))
    s_v = es.enter_context(nc.sbuf_tensor(f"ff_v{tag}", [128, Tg], BF16))
    s_y = es.enter_context(nc.sbuf_tensor(f"ff_y{tag}", [128, 8, Tg], F32))
    s_wsl = [es.enter_context(nc.sbuf_tensor(f"ff_w{tag}{i}", [128, 32, 128], BF16))
             for i in range(4)]
    s_s1 = es.enter_context(nc.sbuf_tensor(f"ff_s1{tag}", [128, 32], F32))
    s_s2 = es.enter_context(nc.sbuf_tensor(f"ff_s2{tag}", [128, 32], F32))
    s_sh1 = es.enter_context(nc.sbuf_tensor(f"ff_sh1{tag}", [128, 32], F32))
    s_sh2 = es.enter_context(nc.sbuf_tensor(f"ff_sh2{tag}", [128, 32], F32))
    s_s3h = es.enter_context(nc.sbuf_tensor(f"ff_s3h{tag}", [128, 8], F32))
    ps_mm = [es.enter_context(nc.psum_tensor(f"ff_ps{tag}{i}", [128, Tg], F32)) for i in range(4)]

    ln = make_ln_ctx(p, nc, es, consts, Tg, "f" + tag)

    s_ld = p.sem(f"ff_ld{tag}")
    s_wld = p.sem(f"ff_wld{tag}")
    s_mm = p.sem(f"ff_mm{tag}")
    s_ev = p.sem(f"ff_ev{tag}")
    s_dv = p.sem(f"ff_dv{tag}")
    s_st = p.sem(f"ff_st{tag}")

    for dst, key in ((s_s1, "s1"), (s_s2, "s2"), (s_sh1, "sh1"), (s_sh2, "sh2")):
        p.dma("gpsimd", lambda g, dst=dst, key=key: g.dma_start(out=dst[:, :], in_=W[key].ap()),
              inc=s_ld)
    sc_v = p.dma("gpsimd", lambda g: g.dma_start(out=s_s3h[:, :], in_=W["s3"].ap()), inc=s_ld)
    p.comp("vector", lambda v: v.tensor_scalar(s_s3h[:, :], s_s3h[:, :], 0.5, None, OP.mult),
           waits=[(s_ld, sc_v)], inc=s_dv)

    slab_jobs = []
    for g in range(NG):
        for o in range(32):
            slab_jobs.append(("w1", W["w1T"], 8, o))
            slab_jobs.append(("w2", W["w2T"], 8, o))
        for o in range(32):
            slab_jobs.append(("h1", W["h1T"], 32, o))
        for o in range(32):
            slab_jobs.append(("h2", W["h2T"], 32, o))
        for o in range(8):
            slab_jobs.append(("w3", W["w3T"], 32, o))
    NSLAB_BUF = 4
    slab_free = [None] * len(slab_jobs)
    slab_ld = [None] * len(slab_jobs)
    next_slab = [0]

    def issue_slab(consts_gate=False):
        k = next_slab[0]
        next_slab[0] += 1
        layer, wt, nch, o = slab_jobs[k]
        buf = k % NSLAB_BUF
        waits = []
        if k >= NSLAB_BUF:
            waits.append(slab_free[k - NSLAB_BUF])
        elif consts_gate:
            waits.append((consts.sem, consts.ready))
        Ocols = wt.shape[1]

        def fn(sy, wt=wt, nch=nch, o=o, buf=buf, Ocols=Ocols):
            src = bass.AP(wt, o * 128, [[Ocols, 128], [128 * Ocols, nch], [1, 128]])
            return sy.dma_start(out=s_wsl[buf][:, 0:nch, :], in_=src)
        slab_ld[k] = p.dma("sync", fn, waits=waits, inc=s_wld)
        return k

    for _ in range(NSLAB_BUF):
        issue_slab(consts_gate=True)

    bank_free = {}

    def stage_matmuls(bank_i, nch, src3, slab_k, extra_waits):
        buf = slab_k % NSLAB_BUF
        v = None
        for c in range(nch):
            w = []
            if c == 0:
                w.append((s_wld, slab_ld[slab_k]))
                if bank_i in bank_free:
                    w.append(bank_free[bank_i])
                w += list(extra_waits)
            v = p.comp("tensor",
                       lambda t, c=c, bank_i=bank_i, buf=buf, nch=nch, src3=src3:
                           t.matmul(ps_mm[bank_i][:, :], s_wsl[buf][:, c, :], src3[:, c, :],
                                    start=(c == 0), stop=(c == nch - 1), skip_group_check=True),
                       waits=w, inc=(s_mm if c == nch - 1 else None))
        return v

    xn3 = s_xn.ap()
    x3 = s_x.ap()
    ha3 = s_ha.ap()
    hb3 = s_hb.ap()
    y3 = s_y.ap()

    store_val = None
    resid_done = None
    for g in range(NG):
        xwaits = [] if resid_done is None else [(s_dv, resid_done)]
        xv = p.dma("gpsimd", lambda gg, g=g: gg.dma_start(
            out=s_x[:, :, :],
            in_=bass.AP(x_in, g * Tg, [[NTOK, 128], [128 * NTOK, 8], [1, Tg]])),
            waits=xwaits, inc=s_ld)

        ln_in = [(s_ld, xv)]
        wr = [(s_mm, s_mm.count)] if g > 0 else []
        lnsem, lnval = emit_layernorm(p, ln, x3, xn3, ln_in, write_waits=wr)

        uv_free = {}
        for o in range(32):
            k1 = next_slab[0] - NSLAB_BUF
            k_w1 = g * 136 + 2 * o
            k_w2 = k_w1 + 1
            b1 = (2 * o) % 4
            b2 = (2 * o + 1) % 4
            mv1 = stage_matmuls(b1, 8, xn3, k_w1, [(lnsem, lnval)] if o == 0 else [])
            mv2 = stage_matmuls(b2, 8, xn3, k_w2, [])
            slab_free[k_w1] = (s_mm, mv1)
            slab_free[k_w2] = (s_mm, mv2)
            if next_slab[0] < len(slab_jobs):
                issue_slab()
            if next_slab[0] < len(slab_jobs):
                issue_slab()
            ew1 = [(s_mm, mv1)]
            if o >= 1:
                ew1.append(uv_free[0])
            e1 = p.comp("scalar", lambda s, o=o, b1=b1: s.activation(
                s_u[:, :], ps_mm[b1][:, :], AF.Silu, scale=s_s1[:, o:o + 1]),
                waits=ew1, inc=s_ev)
            ew2 = [(s_mm, mv2)]
            if o >= 1:
                ew2.append(uv_free[1])
            e2 = p.comp("scalar", lambda s, o=o, b2=b2: s.activation(
                s_v[:, :], ps_mm[b2][:, :], AF.Copy, scale=s_s2[:, o:o + 1]),
                waits=ew2, inc=s_ev)
            bank_free[b1] = (s_ev, e1)
            bank_free[b2] = (s_ev, e2)
            dv = p.comp("vector", lambda v, o=o: v.tensor_tensor(
                ha3[:, o, :], s_u[:, :], s_v[:, :], OP.mult),
                waits=[(s_ev, e2)], inc=s_dv)
            uv_free = {0: (s_dv, dv), 1: (s_dv, dv)}
        stage1_done = s_dv.count

        for o in range(32):
            k_h1 = g * 136 + 64 + o
            b = o % 4
            mv = stage_matmuls(b, 32, ha3, k_h1,
                               [(s_dv, stage1_done)] if o == 0 else [])
            slab_free[k_h1] = (s_mm, mv)
            if next_slab[0] < len(slab_jobs):
                issue_slab()
            e = p.comp("scalar", lambda s, o=o, b=b: s.activation(
                hb3[:, o, :], ps_mm[b][:, :], AF.Gelu, scale=s_sh1[:, o:o + 1]),
                waits=[(s_mm, mv)], inc=s_ev)
            bank_free[b] = (s_ev, e)
        h1_done = s_ev.count

        for o in range(32):
            k_h2 = g * 136 + 96 + o
            b = o % 4
            mv = stage_matmuls(b, 32, hb3, k_h2,
                               [(s_ev, h1_done)] if o == 0 else [])
            slab_free[k_h2] = (s_mm, mv)
            if next_slab[0] < len(slab_jobs):
                issue_slab()
            e = p.comp("scalar", lambda s, o=o, b=b: s.activation(
                ha3[:, o, :], ps_mm[b][:, :], AF.Gelu, scale=s_sh2[:, o:o + 1]),
                waits=[(s_mm, mv)], inc=s_ev)
            bank_free[b] = (s_ev, e)
        h2_done = s_ev.count

        for o in range(8):
            k_w3 = g * 136 + 128 + o
            b = o % 4
            mv = stage_matmuls(b, 32, ha3, k_w3,
                               [(s_ev, h2_done)] if o == 0 else [])
            slab_free[k_w3] = (s_mm, mv)
            if next_slab[0] < len(slab_jobs):
                issue_slab()
            w = [(s_mm, mv)]
            if g > 0 and o == 0:
                w.append((s_st, store_val))
            dv = p.comp("vector", lambda v, o=o, b=b: v.scalar_tensor_tensor(
                y3[:, o, :], ps_mm[b][:, :], s_s3h[:, o:o + 1], x3[:, o, :],
                OP.mult, OP.add),
                waits=w, inc=s_dv)
            bank_free[b] = (s_dv, dv)
        resid_done = s_dv.count

        store_val = p.dma("gpsimd", lambda gg, g=g: gg.dma_start(
            out=bass.AP(y_out, g * Tg, [[NTOK, 128], [128 * NTOK, 8], [1, Tg]]),
            in_=s_y[:, :, :]),
            waits=[(s_dv, resid_done)], inc=s_st)

    p.wait("gpsimd", [(s_st, s_st.count)])
    return s_st


def _featmajor(x):
    B, Tt, Dd = x.shape
    return np.ascontiguousarray(x.reshape(B * Tt, Dd).T)


def _scale_cols(s):
    return np.ascontiguousarray(s.reshape(-1, 128).T)


def build_prep():
    nc = bass.Bass("TRN2", target_bir_lowering=False)
    ins = {}
    for nm, shape in (("w1", (4096, 1024)), ("w2", (4096, 1024)), ("h1", (4096, 4096)),
                      ("h2", (4096, 4096)), ("w3", (1024, 4096))):
        ins[nm] = nc.dram_tensor(nm, list(shape), F32, kind="ExternalInput")
    outs = {}
    for nm, shape in (("w1T", (1024, 4096)), ("w2T", (1024, 4096)), ("h1T", (4096, 4096)),
                      ("h2T", (4096, 4096)), ("w3T", (4096, 1024))):
        outs[nm] = nc.dram_tensor(nm, list(shape), BF16, kind="ExternalOutput")
    for nm, shape in (("wg", (2816, 1024)), ("wu", (2816, 1024)), ("wd", (1024, 2816))):
        ins[nm] = nc.dram_tensor(nm, list(shape), F32, kind="ExternalInput")
    for nm, shape in (("wgT", (1024, 2816)), ("wuT", (1024, 2816)), ("wdT", (2816, 1024))):
        outs[nm] = nc.dram_tensor(nm, list(shape), BF16, kind="ExternalOutput")
    p = Prog(nc)
    with contextlib.ExitStack() as es:
        consts = Consts(p, nc, es)
        emit_wprep(p, nc, es, consts,
                   [(ins[a], outs[a + "T"]) for a in ("w1", "w2", "h1", "h2", "w3")], "p")
        p.barrier("prep_bar")
        emit_l1wprep(p, nc, es, consts,
                     [(ins[a], outs[a + "T"]) for a in ("wg", "wu", "wd")], "p")
        p.build()
    return nc


def build_l0ffn():
    nc = bass.Bass("TRN2", target_bir_lowering=False)
    x_in = nc.dram_tensor("x", [1024 * NTOK], F32, kind="ExternalInput")
    y_out = nc.dram_tensor("y", [1024 * NTOK], F32, kind="ExternalOutput")
    W = {}
    for nm, shape, dt in (("w1T", (1024, 4096), BF16), ("w2T", (1024, 4096), BF16),
                          ("h1T", (4096, 4096), BF16), ("h2T", (4096, 4096), BF16),
                          ("w3T", (4096, 1024), BF16), ("s1", (128, 32), F32),
                          ("s2", (128, 32), F32), ("sh1", (128, 32), F32),
                          ("sh2", (128, 32), F32), ("s3", (128, 8), F32)):
        W[nm] = nc.dram_tensor(nm, list(shape), dt, kind="ExternalInput")
    p = Prog(nc)
    with contextlib.ExitStack() as es:
        consts = Consts(p, nc, es)
        emit_l0ffn(p, nc, es, consts, x_in, y_out, W, "a")
        p.build()
    return nc


def build_l0attn():
    nc = bass.Bass("TRN2", target_bir_lowering=False)
    x_in = nc.dram_tensor("x", [1024 * NTOK], F32, kind="ExternalInput")
    y_out = nc.dram_tensor("y", [1024 * NTOK], F32, kind="ExternalOutput")
    W = {}
    for nm, shape in (("Wq", (1024, 1024)), ("Wk", (1024, 1024)), ("Wv", (1024, 2048)),
                      ("Wg", (1024, 2048)), ("Wab", (1024, 8)), ("Wo", (2048, 1024)),
                      ("W2", (1024, 1024))):
        W[nm] = nc.dram_tensor(nm, list(shape), F32, kind="ExternalInput")
    for nm, shape in (("ln1g", (128, 8)), ("ln1b", (128, 8)), ("ln2g", (128, 8)),
                      ("ln2b", (128, 8)), ("rmsw", (128, 8)), ("b2", (128, 8)),
                      ("onw", (128, 4)), ("cq", (128, 32)), ("ck", (128, 32)),
                      ("cv", (128, 64)), ("dtb", (4, 1)), ("alog", (4, 1))):
        W[nm] = nc.dram_tensor(nm, list(shape), F32, kind="ExternalInput")
    scr = {}
    for nm, size, dt in (("xsub", 1024 * NTOK, F32), ("qraw", 1024 * NTOK, BF16),
                         ("kraw", 1024 * NTOK, BF16), ("vraw", 2048 * NTOK, BF16),
                         ("gate", 2048 * NTOK, BF16), ("qn", 1024 * NTOK, BF16),
                         ("kn", 1024 * NTOK, BF16), ("vv", 2048 * NTOK, BF16),
                         ("ab", 8 * NTOK, F32), ("Bsc", 4 * 2 * (T + 1), F32),
                         ("beta", 4 * NTOK, F32), ("osc", 2048 * NTOK, BF16)):
        scr[nm] = nc.dram_tensor("scr_" + nm, [size], dt, kind="Internal")
    p = Prog(nc)
    with contextlib.ExitStack() as es:
        consts = Consts(p, nc, es)
        emit_l0attn(p, nc, es, consts, x_in, y_out, W, scr, "a")
        p.build()
    return nc


def build_l1ffn():
    nc = bass.Bass("TRN2", target_bir_lowering=False)
    x_in = nc.dram_tensor("x", [1024 * NTOK], F32, kind="ExternalInput")
    y_out = nc.dram_tensor("y", [1024 * NTOK], F32, kind="ExternalOutput")
    W = {
        "wgT": nc.dram_tensor("wgT", [1024, 2816], BF16, kind="ExternalInput"),
        "wuT": nc.dram_tensor("wuT", [1024, 2816], BF16, kind="ExternalInput"),
        "wdT": nc.dram_tensor("wdT", [2816, 1024], BF16, kind="ExternalInput"),
    }
    p = Prog(nc)
    with contextlib.ExitStack() as es:
        consts = Consts(p, nc, es)
        emit_l1ffn(p, nc, es, consts, x_in, y_out, W, "a")
        p.build()
    return nc


def build_l1attn():
    nc = bass.Bass("TRN2", target_bir_lowering=False)
    x_in = nc.dram_tensor("x", [1024 * NTOK], F32, kind="ExternalInput")
    y_out = nc.dram_tensor("y", [1024 * NTOK], F32, kind="ExternalOutput")
    W = {}
    for nm in ("Wq", "Wk", "Wv", "Wo"):
        W[nm] = nc.dram_tensor(nm, [1024, 1024], F32, kind="ExternalInput")
    W["dist"] = nc.dram_tensor("dist", [2047 * 64], F32, kind="ExternalInput")
    for nm in ("bq", "bk", "bv", "bo", "gam", "bet"):
        W[nm] = nc.dram_tensor(nm, [128, 8], F32, kind="ExternalInput")
    scr = {
        "xs": nc.dram_tensor("scr_xs", [1024 * NTOK], F32),
        "qf": nc.dram_tensor("scr_qf", [1024 * NTOK], BF16),
        "kf": nc.dram_tensor("scr_kf", [1024 * NTOK], BF16),
        "vt": nc.dram_tensor("scr_vt", [NTOK * 1024], BF16),
        "cf": nc.dram_tensor("scr_cf", [1024 * NTOK], BF16),
        "qe": nc.dram_tensor("scr_qe", [16 * 128 * 2176], BF16),
        "ke": nc.dram_tensor("scr_ke", [16 * 128 * 2176], BF16),
    }
    p = Prog(nc)
    with contextlib.ExitStack() as es:
        consts = Consts(p, nc, es)
        emit_l1attn(p, nc, es, consts, x_in, y_out, W, scr, "a")
        p.build()
    return nc


def build_fused():
    nc = bass.Bass("TRN2", target_bir_lowering=False)
    x_in = nc.dram_tensor("x", [1024 * NTOK], F32, kind="ExternalInput")
    y_out = nc.dram_tensor("y", [1024 * NTOK], F32, kind="ExternalOutput")
    raw = {}
    for nm, shape in (("w1", (4096, 1024)), ("w2", (4096, 1024)), ("h1", (4096, 4096)),
                      ("h2", (4096, 4096)), ("w3", (1024, 4096)),
                      ("wg", (2816, 1024)), ("wu", (2816, 1024)), ("wd", (1024, 2816))):
        raw[nm] = nc.dram_tensor(nm, list(shape), F32, kind="ExternalInput")
    cache = {}
    for nm, shape in (("w1T", (1024, 4096)), ("w2T", (1024, 4096)), ("h1T", (4096, 4096)),
                      ("h2T", (4096, 4096)), ("w3T", (4096, 1024)),
                      ("wgT", (1024, 2816)), ("wuT", (1024, 2816)), ("wdT", (2816, 1024))):
        cache[nm] = nc.dram_tensor(nm, list(shape), BF16, kind="Internal")
    W0 = {nm: cache[nm] for nm in ("w1T", "w2T", "h1T", "h2T", "w3T")}
    for nm, shape in (("s1", (128, 32)), ("s2", (128, 32)), ("sh1", (128, 32)),
                      ("sh2", (128, 32)), ("s3", (128, 8))):
        W0[nm] = nc.dram_tensor(nm, list(shape), F32, kind="ExternalInput")
    W1 = {nm: cache[nm] for nm in ("wgT", "wuT", "wdT")}
    WA = {}
    for nm, shape in (("Wq", (1024, 1024)), ("Wk", (1024, 1024)), ("Wv", (1024, 2048)),
                      ("Wg", (1024, 2048)), ("Wab", (1024, 8)), ("Wo", (2048, 1024)),
                      ("W2", (1024, 1024))):
        WA[nm] = nc.dram_tensor("a" + nm, list(shape), F32, kind="ExternalInput")
    for nm, shape in (("ln1g", (128, 8)), ("ln1b", (128, 8)), ("ln2g", (128, 8)),
                      ("ln2b", (128, 8)), ("rmsw", (128, 8)), ("b2", (128, 8)),
                      ("onw", (128, 4)), ("cq", (128, 32)), ("ck", (128, 32)),
                      ("cv", (128, 64)), ("dtb", (4, 1)), ("alog", (4, 1))):
        WA[nm] = nc.dram_tensor("a" + nm, list(shape), F32, kind="ExternalInput")
    scrA = {}
    for nm, size, dt in (("xsub", 1024 * NTOK, F32), ("qraw", 1024 * NTOK, BF16),
                         ("kraw", 1024 * NTOK, BF16), ("vraw", 2048 * NTOK, BF16),
                         ("gate", 2048 * NTOK, BF16), ("qn", 1024 * NTOK, BF16),
                         ("kn", 1024 * NTOK, BF16), ("vv", 2048 * NTOK, BF16),
                         ("ab", 8 * NTOK, F32), ("Bsc", 4 * 2 * (T + 1), F32),
                         ("beta", 4 * NTOK, F32), ("osc", 2048 * NTOK, BF16)):
        scrA[nm] = nc.dram_tensor("ascr_" + nm, [size], dt, kind="Internal")
    WB = {}
    for nm in ("Wq", "Wk", "Wv", "Wo"):
        WB[nm] = nc.dram_tensor("b" + nm, [1024, 1024], F32, kind="ExternalInput")
    WB["dist"] = nc.dram_tensor("bdist", [2047 * 64], F32, kind="ExternalInput")
    for nm in ("bq", "bk", "bv", "bo", "gam", "bet"):
        WB[nm] = nc.dram_tensor("b" + nm, [128, 8], F32, kind="ExternalInput")
    scrB = {
        "xs": nc.dram_tensor("bscr_xs", [1024 * NTOK], F32),
        "qf": nc.dram_tensor("bscr_qf", [1024 * NTOK], BF16),
        "kf": nc.dram_tensor("bscr_kf", [1024 * NTOK], BF16),
        "vt": nc.dram_tensor("bscr_vt", [NTOK * 1024], BF16),
        "cf": nc.dram_tensor("bscr_cf", [1024 * NTOK], BF16),
        "qe": nc.dram_tensor("bscr_qe", [16 * 128 * 2176], BF16),
        "ke": nc.dram_tensor("bscr_ke", [16 * 128 * 2176], BF16),
    }
    r = [nc.dram_tensor(f"r{i}", [1024 * NTOK], F32, kind="Internal") for i in range(5)]

    p = Prog(nc)
    with contextlib.ExitStack() as esC:
        consts = Consts(p, nc, esC)
        with contextlib.ExitStack() as es:
            emit_wprep(p, nc, es, consts,
                       [(raw[a], cache[a + "T"]) for a in ("w1", "w2", "h1", "h2", "w3")], "p")
            p.barrier("fz0")
            emit_l1wprep(p, nc, es, consts,
                         [(raw[a], cache[a + "T"]) for a in ("wg", "wu", "wd")], "p")
        p.barrier("fz1")
        with contextlib.ExitStack() as es:
            emit_l0ffn(p, nc, es, consts, x_in, r[0], W0, "f0")
        p.barrier("fz2")
        with contextlib.ExitStack() as es:
            emit_l0attn(p, nc, es, consts, r[0], r[1], WA, scrA, "fa")
        p.barrier("fz3")
        with contextlib.ExitStack() as es:
            emit_l0ffn(p, nc, es, consts, r[1], r[2], W0, "f1")
        p.barrier("fz4")
        with contextlib.ExitStack() as es:
            emit_l1ffn(p, nc, es, consts, r[2], r[3], W1, "g0")
        p.barrier("fz5")
        with contextlib.ExitStack() as es:
            emit_l1attn(p, nc, es, consts, r[3], r[4], WB, scrB, "ga")
        p.barrier("fz6")
        with contextlib.ExitStack() as es:
            emit_l1ffn(p, nc, es, consts, r[4], y_out, W1, "g1")
        p.barrier("fz7")
        p.wait("vector", [])
        p.build()
    return nc


def _scale_cols32(s):
    return np.ascontiguousarray(np.asarray(s, np.float32).reshape(-1, 128).T)


def _conv_cols(w):
    w = np.asarray(w, np.float32)
    nch = w.shape[0] // 128
    return np.ascontiguousarray(w.reshape(nch, 128, 4).transpose(1, 0, 2).reshape(128, nch * 4))


def kernel(**inputs):
    N = NCORES
    g = lambda k: np.asarray(inputs[k], dtype=np.float32)

    x = g("L0_attn_x")
    xsh = [np.ascontiguousarray(x[2 * c:2 * c + 2].reshape(NTOK, D).T).reshape(-1)
           for c in range(N)]

    nc = build_fused()
    common = {
        "w1": g("L0_ffn_w1_w"), "w2": g("L0_ffn_w2_w"), "h1": g("L0_ffn_h1_w"),
        "h2": g("L0_ffn_h2_w"), "w3": g("L0_ffn_w3_w"),
        "wg": g("L1_ffn_w_gate"), "wu": g("L1_ffn_w_up"), "wd": g("L1_ffn_w_down"),
        "s1": _scale_cols32(g("L0_ffn_w1_s")), "s2": _scale_cols32(g("L0_ffn_w2_s")),
        "sh1": _scale_cols32(g("L0_ffn_h1_s")), "sh2": _scale_cols32(g("L0_ffn_h2_s")),
        "s3": _scale_cols32(g("L0_ffn_w3_s")),
        "aWq": g("L0_attn_Wq"), "aWk": g("L0_attn_Wk"), "aWv": g("L0_attn_Wv"),
        "aWg": g("L0_attn_Wg"),
        "aWab": np.ascontiguousarray(np.concatenate(
            [g("L0_attn_Wa"), g("L0_attn_Wb")], axis=1)),
        "aWo": g("L0_attn_Wo"), "aW2": g("L0_attn_W2"),
        "aln1g": _scale_cols32(g("L0_attn_ln1_g")), "aln1b": _scale_cols32(g("L0_attn_ln1_b")),
        "aln2g": _scale_cols32(g("L0_attn_ln2_g")), "aln2b": _scale_cols32(g("L0_attn_ln2_b")),
        "armsw": _scale_cols32(g("L0_attn_rms_in_w")), "ab2": _scale_cols32(g("L0_attn_b2")),
        "aonw": _scale_cols32(g("L0_attn_o_norm_w")),
        "acq": _conv_cols(g("L0_attn_conv_q_w")), "ack": _conv_cols(g("L0_attn_conv_k_w")),
        "acv": _conv_cols(g("L0_attn_conv_v_w")),
        "adtb": np.ascontiguousarray(g("L0_attn_dt_bias").reshape(4, 1)),
        "aalog": np.ascontiguousarray(g("L0_attn_A_log").reshape(4, 1)),
        "bWq": g("L1_attn_Wq"), "bWk": g("L1_attn_Wk"), "bWv": g("L1_attn_Wv"),
        "bWo": g("L1_attn_Wo"), "bdist": g("L1_attn_dist_emb").reshape(-1),
        "bbq": _scale_cols32(g("L1_attn_bq")), "bbk": _scale_cols32(g("L1_attn_bk")),
        "bbv": _scale_cols32(g("L1_attn_bv")), "bbo": _scale_cols32(g("L1_attn_bo")),
        "bgam": _scale_cols32(g("L1_attn_ln_gamma")),
        "bbet": _scale_cols32(g("L1_attn_ln_beta")),
    }
    maps = [dict(common, x=xsh[c]) for c in range(N)]
    r = run_bass_kernel_spmd(nc, maps, core_ids=list(range(N))).results
    out = np.empty((16, 2048, 1024), dtype=np.float32)
    for c in range(N):
        out[2 * c:2 * c + 2] = r[c]["y"].reshape(1024, NTOK).T.reshape(2, 2048, 1024)
    return out


def kernel_multi(**inputs):
    N = NCORES
    g = lambda k: np.asarray(inputs[k], dtype=np.float32)

    x = g("L0_attn_x")
    xsh = [np.ascontiguousarray(x[2 * c:2 * c + 2].reshape(NTOK, D).T).reshape(-1)
           for c in range(N)]

    ncA = build_prep()
    inA = {
        "w1": g("L0_ffn_w1_w"), "w2": g("L0_ffn_w2_w"), "h1": g("L0_ffn_h1_w"),
        "h2": g("L0_ffn_h2_w"), "w3": g("L0_ffn_w3_w"),
        "wg": g("L1_ffn_w_gate"), "wu": g("L1_ffn_w_up"), "wd": g("L1_ffn_w_down"),
    }
    resA = run_bass_kernel_spmd(ncA, [inA] * N, core_ids=list(range(N))).results

    ncB = build_l0ffn()
    l0ffn_w = {
        "s1": _scale_cols32(g("L0_ffn_w1_s")), "s2": _scale_cols32(g("L0_ffn_w2_s")),
        "sh1": _scale_cols32(g("L0_ffn_h1_s")), "sh2": _scale_cols32(g("L0_ffn_h2_s")),
        "s3": _scale_cols32(g("L0_ffn_w3_s")),
    }

    def run_l0ffn(xs):
        maps = []
        for c in range(N):
            m = dict(l0ffn_w)
            for nm in ("w1T", "w2T", "h1T", "h2T", "w3T"):
                m[nm] = resA[c][nm]
            m["x"] = xs[c]
            maps.append(m)
        r = run_bass_kernel_spmd(ncB, maps, core_ids=list(range(N))).results
        return [r[c]["y"] for c in range(N)]

    y = run_l0ffn(xsh)

    ncC = build_l0attn()
    inC = {
        "Wq": g("L0_attn_Wq"), "Wk": g("L0_attn_Wk"), "Wv": g("L0_attn_Wv"),
        "Wg": g("L0_attn_Wg"),
        "Wab": np.ascontiguousarray(np.concatenate([g("L0_attn_Wa"), g("L0_attn_Wb")], axis=1)),
        "Wo": g("L0_attn_Wo"), "W2": g("L0_attn_W2"),
        "ln1g": _scale_cols32(g("L0_attn_ln1_g")), "ln1b": _scale_cols32(g("L0_attn_ln1_b")),
        "ln2g": _scale_cols32(g("L0_attn_ln2_g")), "ln2b": _scale_cols32(g("L0_attn_ln2_b")),
        "rmsw": _scale_cols32(g("L0_attn_rms_in_w")), "b2": _scale_cols32(g("L0_attn_b2")),
        "onw": _scale_cols32(g("L0_attn_o_norm_w")),
        "cq": _conv_cols(g("L0_attn_conv_q_w")), "ck": _conv_cols(g("L0_attn_conv_k_w")),
        "cv": _conv_cols(g("L0_attn_conv_v_w")),
        "dtb": np.ascontiguousarray(g("L0_attn_dt_bias").reshape(4, 1)),
        "alog": np.ascontiguousarray(g("L0_attn_A_log").reshape(4, 1)),
    }
    maps = [dict(inC, x=y[c]) for c in range(N)]
    r = run_bass_kernel_spmd(ncC, maps, core_ids=list(range(N))).results
    y = [r[c]["y"] for c in range(N)]

    y = run_l0ffn(y)

    ncE = build_l1ffn()

    def run_l1ffn(xs):
        maps = []
        for c in range(N):
            m = {"wgT": resA[c]["wgT"], "wuT": resA[c]["wuT"], "wdT": resA[c]["wdT"],
                 "x": xs[c]}
            maps.append(m)
        r = run_bass_kernel_spmd(ncE, maps, core_ids=list(range(N))).results
        return [r[c]["y"] for c in range(N)]

    y = run_l1ffn(y)

    ncF = build_l1attn()
    inF = {
        "Wq": g("L1_attn_Wq"), "Wk": g("L1_attn_Wk"), "Wv": g("L1_attn_Wv"),
        "Wo": g("L1_attn_Wo"), "dist": g("L1_attn_dist_emb").reshape(-1),
        "bq": _scale_cols32(g("L1_attn_bq")), "bk": _scale_cols32(g("L1_attn_bk")),
        "bv": _scale_cols32(g("L1_attn_bv")), "bo": _scale_cols32(g("L1_attn_bo")),
        "gam": _scale_cols32(g("L1_attn_ln_gamma")), "bet": _scale_cols32(g("L1_attn_ln_beta")),
    }
    maps = [dict(inF, x=y[c]) for c in range(N)]
    r = run_bass_kernel_spmd(ncF, maps, core_ids=list(range(N))).results
    y = [r[c]["y"] for c in range(N)]

    y = run_l1ffn(y)

    out = np.empty((16, 2048, 1024), dtype=np.float32)
    for c in range(N):
        out[2 * c:2 * c + 2] = y[c].reshape(1024, NTOK).T.reshape(2, 2048, 1024)
    return out


_RNE_C = 12582912.0


def make_quant_scratch(p, nc, es, Fmax, tag):
    qs = {
        "red": es.enter_context(nc.sbuf_tensor(f"q_red{tag}", [128, Fmax // 16], F32)),
        "inv": es.enter_context(nc.sbuf_tensor(f"q_inv{tag}", [128, Fmax // 16], F32)),
        "u": es.enter_context(nc.sbuf_tensor(f"q_u{tag}", [128, Fmax], F32)),
        "sgn": es.enter_context(nc.sbuf_tensor(f"q_sgn{tag}", [128, Fmax], F32)),
        "r1": es.enter_context(nc.sbuf_tensor(f"q_r1{tag}", [128, Fmax], F32)),
        "r2": es.enter_context(nc.sbuf_tensor(f"q_r2{tag}", [128, Fmax], F32)),
        "m1": es.enter_context(nc.sbuf_tensor(f"q_m1{tag}", [128, Fmax], U8)),
        "m2": es.enter_context(nc.sbuf_tensor(f"q_m2{tag}", [128, Fmax], U8)),
        "dv": p.sem(f"q_dv{tag}"),
        "gp": p.sem(f"q_gp{tag}"),
        "ac": p.sem(f"q_ac{tag}"),
        "Fmax": Fmax,
    }
    return qs


def emit_quant(p, qs, x_ap, out_ap, F, in_ready, out_waits=()):
    nb = F // 16
    dv, gp, ac = qs["dv"], qs["gp"], qs["ac"]
    red = qs["red"].ap()[:, 0:nb]
    inv = qs["inv"].ap()[:, 0:nb]
    u = qs["u"].ap()[:, 0:F]
    sgn = qs["sgn"].ap()[:, 0:F]
    r1 = qs["r1"].ap()[:, 0:F]
    r2 = qs["r2"].ap()[:, 0:F]
    m1 = qs["m1"].ap()[:, 0:F]
    m2 = qs["m2"].ap()[:, 0:F]
    x3 = x_ap.rearrange("p (b k) -> p b k", k=16)

    v0 = p.comp("vector", lambda v: v.reduce_max(red.unsqueeze(2), x3, mybir.AxisListType.X,
                                                 apply_absolute_value=True),
                waits=in_ready, inc=dv)
    v1 = p.comp("vector", lambda v: v.tensor_scalar(red, red, 1.0 / 6.0, 1e-12, OP.mult, OP.max),
                waits=[(dv, v0)], inc=dv)
    dv_inv = p.comp("vector", lambda v: v.reciprocal(inv, red), waits=[(dv, v1)], inc=dv)
    dv_xn = p.comp("vector", lambda v: v.tensor_tensor(
        x3, x3, inv.unsqueeze(2).broadcast_to([128, nb, 16]), OP.mult),
        waits=[(dv, dv_inv)], inc=dv)
    ac_u = p.comp("scalar", lambda s: s.activation(u, x_ap, AF.Abs), waits=[(dv, dv_xn)], inc=ac)
    ac_s = p.comp("scalar", lambda s: s.activation(sgn, x_ap, AF.Sign), inc=ac)
    p.comp("vector", lambda v: v.tensor_scalar(r1, x_ap, 2.0, _RNE_C, OP.mult, OP.add),
           waits=[(dv, dv_xn)])
    p.comp("vector", lambda v: v.tensor_scalar(r1, r1, _RNE_C, 4.0, OP.subtract, OP.min))
    gp_r1 = p.comp("vector", lambda v: v.tensor_scalar(r1, r1, -4.0, 0.5, OP.max, OP.mult), inc=gp)
    p.comp("gpsimd", lambda g: g.tensor_scalar(r2, x_ap, _RNE_C, None, OP.add),
           waits=[(dv, dv_xn)])
    p.comp("gpsimd", lambda g: g.tensor_scalar(r2, r2, _RNE_C, 4.0, OP.subtract, OP.min))
    gp_r2 = p.comp("gpsimd", lambda g: g.tensor_scalar(r2, r2, -4.0, None, OP.max), inc=gp)
    p.comp("vector", lambda v: v.tensor_scalar(m1, u, 2.5, None, OP.is_gt), waits=[(ac, ac_u)])
    p.comp("vector", lambda v: v.tensor_scalar(m2, u, 5.0, None, OP.is_gt))
    p.comp("vector", lambda v: v.select(r1, m1, r2, r1), waits=[(gp, gp_r2)])
    ac_s6 = p.comp("scalar", lambda s: s.activation(sgn, sgn, AF.Copy, scale=6.0), inc=ac)
    p.comp("vector", lambda v: v.select(r1, m2, sgn, r1), waits=[(ac, ac_s6)])
    last = p.comp("vector", lambda v: v.tensor_tensor(
        out_ap.rearrange("p (b k) -> p b k", k=16),
        r1.rearrange("p (b k) -> p b k", k=16),
        red.unsqueeze(2).broadcast_to([128, nb, 16]), OP.mult),
        waits=list(out_waits), inc=dv)
    return dv, last


def emit_l1wprep(p, nc, es, consts, jobs, tag):
    maxK = max(src.shape[1] for src, _ in jobs)
    maxN = max(src.shape[0] for src, _ in jobs)
    s_row = es.enter_context(nc.sbuf_tensor(f"l1p_row{tag}", [128, maxK], F32))
    s_wq = es.enter_context(nc.sbuf_tensor(f"l1p_wq{tag}", [128, (maxN // 128) * maxK], BF16))
    s_strip = es.enter_context(nc.sbuf_tensor(f"l1p_strip{tag}", [128, maxN], BF16))
    ps = [es.enter_context(nc.psum_tensor(f"l1p_ps{tag}{i}", [128, 128], BF16)) for i in range(2)]
    qs = make_quant_scratch(p, nc, es, maxK, "l1p" + tag)
    s_ld = p.sem(f"l1p_ld{tag}")
    s_mm = p.sem(f"l1p_mm{tag}")
    s_ev = p.sem(f"l1p_ev{tag}")
    s_st = p.sem(f"l1p_st{tag}")

    for src, dst in jobs:
        N, K = src.shape
        nb = N // 128
        wq3 = s_wq.ap()[:, 0:nb * K].rearrange("p (b k) -> p b k", k=K)
        for b in range(nb):
            lv = p.dma("gpsimd", lambda g, src=src, b=b, K=K: g.dma_start(
                out=s_row[:, 0:K], in_=src.ap()[b * 128:(b + 1) * 128, :]),
                waits=[(qs["dv"], qs["dv"].count)], inc=s_ld)
            emit_quant(p, qs, s_row.ap()[:, 0:K], wq3[:, b, :], K,
                       [(s_ld, lv)])
        qdone = (qs["dv"], qs["dv"].count)
        k = 0
        for ic in range(K // 128):
            st_guard = s_st.count
            for b in range(nb):
                buf = k % 2
                w = [qdone] if k == 0 else [(s_mm, s_mm.count)]
                mv = p.comp("tensor", lambda t, buf=buf, b=b, ic=ic, wq3=wq3: t.transpose(
                    ps[buf][:, :], wq3[:, b, ic * 128:(ic + 1) * 128], consts.ident_bf[:, :]),
                    waits=w + ([(s_ev, s_ev.count - 1)] if k >= 2 else []), inc=s_mm)
                ev_w = [(s_mm, mv)]
                if b == 0:
                    ev_w.append((s_st, st_guard))
                p.comp("scalar", lambda s, buf=buf, b=b: s.activation(
                    s_strip[:, b * 128:(b + 1) * 128], ps[buf][:, :], AF.Copy),
                    waits=ev_w, inc=s_ev)
                k += 1
            p.dma("sync", lambda sy, dst=dst, ic=ic, N=N: sy.dma_start(
                out=dst.ap()[ic * 128:(ic + 1) * 128, :], in_=s_strip[:, 0:N]),
                waits=[(s_ev, s_ev.count)], inc=s_st)
    p.wait("sync", [(s_st, s_st.count)])
    return s_st


def emit_l1ffn(p, nc, es, consts, x_in, y_out, W, tag):
    NI = 2816
    NT = NTOK // 128
    FQ = 704

    s_wg = es.enter_context(nc.sbuf_tensor(f"l1_wg{tag}", [128, 8, NI], BF16))
    s_wu = es.enter_context(nc.sbuf_tensor(f"l1_wu{tag}", [128, 8, NI], BF16))
    s_wd = es.enter_context(nc.sbuf_tensor(f"l1_wd{tag}", [128, 22, D], BF16))
    s_xf = [es.enter_context(nc.sbuf_tensor(f"l1_xf{tag}{i}", [128, 8, 128], F32)) for i in range(2)]
    s_xtm = [es.enter_context(nc.sbuf_tensor(f"l1_xtm{tag}{i}", [128, D], F32)) for i in range(2)]
    s_xq = es.enter_context(nc.sbuf_tensor(f"l1_xq{tag}", [128, D], BF16))
    s_xqf = es.enter_context(nc.sbuf_tensor(f"l1_xqf{tag}", [128, 8, 128], BF16))
    s_htm = es.enter_context(nc.sbuf_tensor(f"l1_htm{tag}", [128, NI], F32))
    s_hq = es.enter_context(nc.sbuf_tensor(f"l1_hq{tag}", [128, NI], BF16))
    s_hqf = es.enter_context(nc.sbuf_tensor(f"l1_hqf{tag}", [128, 22, 128], BF16))
    s_ytm = es.enter_context(nc.sbuf_tensor(f"l1_ytm{tag}", [128, D], F32))
    s_g = es.enter_context(nc.sbuf_tensor(f"l1_g{tag}", [128, 512], F32))
    s_col = [es.enter_context(nc.sbuf_tensor(f"l1_col{i}{tag}", [128, 1], F32)) for i in range(3)]
    s_sq = es.enter_context(nc.sbuf_tensor(f"l1_sq{tag}", [128, D], F32))
    ident_f = es.enter_context(nc.sbuf_tensor(f"l1_idf{tag}", [128, 128], F32))
    qs = make_quant_scratch(p, nc, es, FQ, "l1" + tag)
    ps_t = [es.enter_context(nc.psum_tensor(f"l1_pst{tag}{i}", [128, 128], F32)) for i in range(2)]
    ps_tb = [es.enter_context(nc.psum_tensor(f"l1_ptb{tag}{i}", [128, 128], BF16)) for i in range(2)]
    ps_g = es.enter_context(nc.psum_tensor(f"l1_psg{tag}", [128, 512], F32))
    ps_u = es.enter_context(nc.psum_tensor(f"l1_psu{tag}", [128, 512], F32))

    s_ld = p.sem(f"l1_ld{tag}")
    s_mm = p.sem(f"l1_mm{tag}")
    s_ev = p.sem(f"l1_ev{tag}")
    s_dv = p.sem(f"l1_dv{tag}")
    s_ac = p.sem(f"l1_ac{tag}")
    s_st = p.sem(f"l1_st{tag}")

    idf_sem = p.sem(f"l1_idf_s{tag}")
    idf_done = p.comp("gpsimd", lambda g: g.tensor_copy(ident_f[:, :], consts.ident_bf[:, :]),
                      waits=[(consts.sem, consts.ready)], inc=idf_sem)

    wld_done = None
    for dst, key in ((s_wg, "wgT"), (s_wu, "wuT"), (s_wd, "wdT")):
        nch = dst.shape[1]
        wld_done = p.dma("gpsimd", lambda g, dst=dst, key=key, nch=nch: g.dma_start(
            out=dst[:, :, :],
            in_=bass.AP(W[key], 0, [[W[key].shape[1], 128],
                                    [128 * W[key].shape[1], nch],
                                    [1, W[key].shape[1]]])), inc=s_ld)

    def transpose_set(n, src_ap_fn, dst_ap_fn, psl, ident, waits):
        evs = []
        ev_entry = s_ev.count
        for c in range(n):
            buf = c % 2
            w = list(waits) if c == 0 else []
            w.append((s_ev, evs[c - 2]) if c >= 2 else (s_ev, ev_entry))
            mv = p.comp("tensor", lambda t, c=c, buf=buf: t.transpose(
                psl[buf][:, :], src_ap_fn(c), ident[:, :]), waits=w, inc=s_mm)
            ev = p.comp("scalar",
                        lambda s, c=c, buf=buf: s.activation(dst_ap_fn(c), psl[buf][:, :],
                                                             AF.Copy),
                        waits=[(s_mm, mv)], inc=s_ev)
            evs.append(ev)
        return evs[-1]

    prev = {}
    pv = lambda key: [prev[key]] if key in prev else []
    for ti in range(NT):
        par = ti % 2
        xf, xtm = s_xf[par], s_xtm[par]

        lw = []
        if ti >= 2:
            lw.append(prev[("xf_free", par)])
        xv = p.dma("gpsimd", lambda g, ti=ti, xf=xf: g.dma_start(
            out=xf[:, :, :],
            in_=bass.AP(x_in, ti * 128, [[NTOK, 128], [128 * NTOK, 8], [1, 128]])),
            waits=lw, inc=s_ld)

        tw = [(s_ld, xv), (idf_sem, idf_done)]
        if ti >= 2:
            tw.append(prev[("xtm_free", par)])
        tv = transpose_set(8, lambda c, xf=xf: xf[:, c, :],
                           lambda c, xtm=xtm: xtm[:, c * 128:(c + 1) * 128],
                           ps_t, ident_f, tw)
        prev[("xf_free", par)] = (s_mm, s_mm.count)

        lnw = [(s_ev, tv)] + pv("col_free") + pv("sq_free")
        a1 = p.comp("scalar", lambda s, xtm=xtm: s.activation(
            s_sq[:, :], xtm[:, :], AF.Copy, accum_out=s_col[0][:, :]),
            waits=lnw, inc=s_ac)
        a2 = p.comp("scalar", lambda s, xtm=xtm: s.activation(
            s_sq[:, :], xtm[:, :], AF.Square, accum_out=s_col[1][:, :]), inc=s_ac)
        c0 = p.comp("vector", lambda v: v.tensor_scalar(s_col[0][:, :], s_col[0][:, :], 1.0 / D, None, OP.mult),
                    waits=[(s_ac, a2)], inc=s_dv)
        c1 = p.comp("vector", lambda v: v.tensor_scalar(s_col[1][:, :], s_col[1][:, :], 1.0 / D, None, OP.mult),
                    waits=[(s_dv, c0)], inc=s_dv)
        c2 = p.comp("vector", lambda v: v.tensor_tensor(s_col[2][:, :], s_col[0][:, :], s_col[0][:, :], OP.mult),
                    waits=[(s_dv, c1)], inc=s_dv)
        d1 = p.comp("vector", lambda v: v.scalar_tensor_tensor(
            s_col[1][:, :], s_col[1][:, :], 1e-5, s_col[2][:, :], OP.add, OP.subtract),
            waits=[(s_dv, c2)], inc=s_dv)
        l1 = p.comp("scalar", lambda s: s.activation(s_col[1][:, :], s_col[1][:, :], AF.Ln),
                    waits=[(s_dv, d1)], inc=s_ac)
        a3 = p.comp("scalar", lambda s: s.activation(s_col[1][:, :], s_col[1][:, :], AF.Exp, scale=-0.5),
                    waits=[(s_ac, l1)], inc=s_ac)
        d2 = p.comp("vector", lambda v, xtm=xtm: v.tensor_scalar(
            s_sq[:, :], xtm[:, :], s_col[0][:, :], s_col[1][:, :], OP.subtract, OP.mult),
            waits=[(s_ac, a3)], inc=s_dv)
        prev["col_free"] = (s_dv, d2)

        qv_pair = (s_dv, d2)
        for h in range(2):
            qsem, qv = emit_quant(p, qs, s_sq.ap()[:, h * 512:(h + 1) * 512],
                                  s_xq.ap()[:, h * 512:(h + 1) * 512], 512,
                                  [qv_pair], out_waits=pv(("xq_free",)) if h == 0 else ())
            qv_pair = (qsem, qv)
        prev["sq_free"] = qv_pair

        tw = [qv_pair, (consts.sem, consts.ready)] + pv(("xqf_free",))
        transpose_set(8, lambda c: s_xq[:, c * 128:(c + 1) * 128],
                      lambda c: s_xqf[:, c, :],
                      ps_tb, consts.ident_bf, tw)
        prev[("xq_free",)] = (s_mm, s_mm.count)
        xq_t_done = (s_ev, s_ev.count)

        mw = [xq_t_done, (s_ld, wld_done)] + pv("gu_psum_free")
        nfull = NI // 512
        chunks = [(i * 512, 512) for i in range(nfull)] + ([(nfull * 512, NI - nfull * 512)] if NI % 512 else [])
        ev = hv = None
        htm_guard = pv("htm_free")
        for (n0, nw) in chunks:
            gw = list(mw) if n0 == 0 else [(s_ev, ev)]
            for kc in range(8):
                p.comp("tensor", lambda t, kc=kc, n0=n0, nw=nw: t.matmul(
                    ps_g[:, 0:nw], s_xqf[:, kc, :], s_wg[:, kc, n0:n0 + nw],
                    start=(kc == 0), stop=(kc == 7), skip_group_check=True),
                    waits=gw if kc == 0 else [])
            uw = [] if n0 == 0 else [(s_dv, hv)]
            for kc in range(8):
                p.comp("tensor", lambda t, kc=kc, n0=n0, nw=nw: t.matmul(
                    ps_u[:, 0:nw], s_xqf[:, kc, :], s_wu[:, kc, n0:n0 + nw],
                    start=(kc == 0), stop=(kc == 7), skip_group_check=True),
                    waits=uw if kc == 0 else [], inc=(s_mm if kc == 7 else None))
            mmv = s_mm.count
            sw = [(s_mm, mmv)]
            if hv is not None:
                sw.append((s_dv, hv))
            ev = p.comp("scalar", lambda s, n0=n0, nw=nw: s.activation(
                s_g[:, 0:nw], ps_g[:, 0:nw], AF.Silu), waits=sw, inc=s_ev)
            hvw = [(s_ev, ev)] + (htm_guard if n0 == 0 else [])
            hv = p.comp("vector", lambda v, n0=n0, nw=nw: v.tensor_tensor(
                s_htm[:, n0:n0 + nw], s_g[:, 0:nw], ps_u[:, 0:nw], OP.mult),
                waits=hvw, inc=s_dv)
        h_done = (s_dv, s_dv.count)
        prev["gu_psum_free"] = h_done

        qv_pair = h_done
        for hqi in range(4):
            qsem, qv = emit_quant(p, qs, s_htm.ap()[:, hqi * FQ:(hqi + 1) * FQ],
                                  s_hq.ap()[:, hqi * FQ:(hqi + 1) * FQ], FQ,
                                  [qv_pair], out_waits=pv(("hq_free",)) if hqi == 0 else ())
            qv_pair = (qsem, qv)
        prev["htm_free"] = (qs["gp"], qs["gp"].count)

        tw = [qv_pair] + pv(("hqf_free",))
        transpose_set(22, lambda c: s_hq[:, c * 128:(c + 1) * 128],
                      lambda c: s_hqf[:, c, :],
                      ps_tb, consts.ident_bf, tw)
        prev[("hq_free",)] = (s_mm, s_mm.count)
        hq_t_done = (s_ev, s_ev.count)

        rv = None
        for half in range(2):
            n0 = half * 512
            dw = ([hq_t_done] + pv("down_psum_free")) if half == 0 else [(s_dv, rv)]
            for kc in range(22):
                p.comp("tensor", lambda t, kc=kc, n0=n0: t.matmul(
                    ps_g[:, :], s_hqf[:, kc, :], s_wd[:, kc, n0:n0 + 512],
                    start=(kc == 0), stop=(kc == 21), skip_group_check=True),
                    waits=dw if kc == 0 else [],
                    inc=(s_mm if kc == 21 else None))
            mmv = s_mm.count
            rw = [(s_mm, mmv)] + (pv("ytm_free") if half == 0 else [])
            rv = p.comp("vector", lambda v, n0=n0, xtm=xtm: v.scalar_tensor_tensor(
                s_ytm[:, n0:n0 + 512], ps_g[:, :], 0.5, xtm[:, n0:n0 + 512], OP.mult, OP.add),
                waits=rw, inc=s_dv)
        y_done = (s_dv, s_dv.count)
        prev[("xtm_free", par)] = y_done
        prev["down_psum_free"] = y_done
        prev["gu_psum_free"] = y_done
        prev[("hqf_free",)] = (s_mm, s_mm.count)

        tv = transpose_set(8, lambda c: s_ytm[:, c * 128:(c + 1) * 128],
                           lambda c, xf=xf: xf[:, c, :],
                           ps_t, ident_f, [y_done])
        prev["ytm_free"] = (s_mm, s_mm.count)
        stv = p.dma("gpsimd", lambda g, ti=ti, xf=xf: g.dma_start(
            out=bass.AP(y_out, ti * 128, [[NTOK, 128], [128 * NTOK, 8], [1, 128]]),
            in_=xf[:, :, :]),
            waits=[(s_ev, tv)], inc=s_st)
        prev[("xf_free", par)] = (s_st, stv)

    p.wait("gpsimd", [(s_st, s_st.count)])
    return s_st



def emit_l1attn(p, nc, es, consts, x_in, y_out, W, scr, tag):
    SW = 2176
    Tg = 512
    NG = NTOK // Tg
    STRIDE_QE = 128 * SW

    s_wq = es.enter_context(nc.sbuf_tensor(f"la_wq{tag}", [128, 8, D], BF16))
    s_wk = es.enter_context(nc.sbuf_tensor(f"la_wk{tag}", [128, 8, D], BF16))
    s_wv = es.enter_context(nc.sbuf_tensor(f"la_wv{tag}", [128, 8, D], BF16))
    s_wo = es.enter_context(nc.sbuf_tensor(f"la_wo{tag}", [128, 8, D], BF16))
    s_bcols = {}
    for key in ("bq", "bk", "bv", "bo", "gam", "bet"):
        s_bcols[key] = es.enter_context(nc.sbuf_tensor(f"la_{key}{tag}", [128, 8], F32))
    s_drev = es.enter_context(nc.sbuf_tensor(f"la_drev{tag}", [64, 2047], BF16))
    s_dnat = es.enter_context(nc.sbuf_tensor(f"la_dnat{tag}", [64, 2047], BF16))
    s_scal = es.enter_context(nc.sbuf_tensor(f"la_sc{tag}", [1, 8], F32))

    s_ld = p.sem(f"la_ld{tag}")
    s_mm = p.sem(f"la_mm{tag}")
    s_ev = p.sem(f"la_ev{tag}")
    s_dv = p.sem(f"la_dv{tag}")
    s_gp = p.sem(f"la_gp{tag}")
    s_st = p.sem(f"la_st{tag}")

    for dst, key in ((s_wq, "Wq"), (s_wk, "Wk"), (s_wv, "Wv"), (s_wo, "Wo")):
        p.dma("gpsimd", lambda g, dst=dst, key=key: g.dma_start(
            out=dst[:, :, :],
            in_=bass.AP(W[key], 0, [[D, 128], [128 * D, 8], [1, D]])), inc=s_ld)
    for key in ("bq", "bk", "bv", "bo", "gam", "bet"):
        p.dma("gpsimd", lambda g, key=key: g.dma_start(
            out=s_bcols[key][:, :], in_=W[key].ap()), inc=s_ld)
    wld_done = s_ld.count

    with contextlib.ExitStack() as esp:
        s_dtile = esp.enter_context(nc.sbuf_tensor(f"la_dt{tag}", [128, 64], BF16))
        s_dsq = esp.enter_context(nc.sbuf_tensor(f"la_dsq{tag}", [128, 64], F32))
        s_dnorm = esp.enter_context(nc.sbuf_tensor(f"la_dn{tag}", [128, 16], F32))
        ps_dt = esp.enter_context(nc.psum_tensor(f"la_psdt{tag}", [64, 128], F32))
        ps_dn = esp.enter_context(nc.psum_tensor(f"la_psdn{tag}", [1, 16], F32))
        zs = p.sem(f"la_dz{tag}")
        p.comp("gpsimd", lambda g: g.memset(s_dnorm[:, :], 0.0))
        zv = p.comp("gpsimd", lambda g: g.tensor_copy(s_dnorm[:, 0:1], s_dnorm[:, 0:1]),
                    inc=zs)
        acc_prev = None
        for tile in range(16):
            n = 128 if tile < 15 else 127
            srcap = bass.AP(W["dist"], (128 * tile) * 64, [[64, n], [1, 64]])
            lv = p.dma("gpsimd", lambda g, srcap=srcap, n=n: g.dma_start(
                out=s_dtile[0:n, :], in_=srcap),
                waits=[(s_mm, s_mm.count), (s_ev, s_ev.count)], inc=s_ld)
            mv = p.comp("tensor", lambda t, n=n: t.matmul(
                ps_dt[:, 0:n], s_dtile[0:n, 0:64], consts.ident_bf[0:n, 0:n],
                start=True, stop=True, skip_group_check=True),
                waits=[(s_ld, lv), (consts.sem, consts.ready), (s_ev, s_ev.count)],
                inc=s_mm)
            p.comp("scalar", lambda s, tile=tile, n=n: s.activation(
                s_dnat[0:64, 128 * tile:128 * tile + n], ps_dt[0:64, 0:n], AF.Copy),
                waits=[(s_mm, mv)], inc=s_ev)
            anti = consts.anti128 if n == 128 else consts.anti127
            cc0 = 1919 - 128 * tile if n == 128 else 0
            mv = p.comp("tensor", lambda t, n=n, anti=anti: t.matmul(
                ps_dt[:, 0:n], s_dtile[0:n, 0:64], anti[0:n, 0:n],
                start=True, stop=True, skip_group_check=True), inc=s_mm)
            p.comp("scalar", lambda s, cc0=cc0, n=n: s.activation(
                s_drev[0:64, cc0:cc0 + n], ps_dt[0:64, 0:n], AF.Copy),
                waits=[(s_mm, mv)], inc=s_ev)
            w = [(s_ld, lv), (zs, zv)]
            if acc_prev is not None:
                w.append((s_ev, acc_prev))
            acc_prev = p.comp("scalar", lambda s, tile=tile, n=n: s.activation(
                s_dsq[0:n, :], s_dtile[0:n, :], AF.Square,
                accum_out=s_dnorm[0:n, tile:tile + 1]), waits=w, inc=s_ev)
        mv = p.comp("tensor", lambda t: t.matmul(ps_dn[:1, :], consts.ones_col[:, :],
                                                 s_dnorm[:, :], start=True, stop=True),
                    waits=[(s_ev, s_ev.count)], inc=s_mm)
        v0 = p.comp("vector", lambda v: v.reduce_sum(
            s_scal.ap()[0:1, 1:2].unsqueeze(2),
            ps_dn.ap()[0:1, 0:16].unsqueeze(1),
            mybir.AxisListType.X), waits=[(s_mm, mv)], inc=s_dv)
        av = p.comp("scalar", lambda s: s.activation(s_scal[0:1, 1:2], s_scal[0:1, 1:2],
                                                     AF.Sqrt), waits=[(s_dv, v0)], inc=s_ev)
        v0 = p.comp("vector", lambda v: v.tensor_scalar(s_scal[0:1, 1:2], s_scal[0:1, 1:2],
                                                        1.01, None, OP.mult),
                    waits=[(s_ev, av)], inc=s_dv)
        dmax_ready = (s_dv, v0)
    p.barrier(f"la_bar_prep{tag}")

    with contextlib.ExitStack() as esA:
        s_x = esA.enter_context(nc.sbuf_tensor(f"la_x{tag}", [128, 8, Tg], F32))
        s_xs = esA.enter_context(nc.sbuf_tensor(f"la_xsb{tag}", [128, 8, Tg], F32))
        s_xn = esA.enter_context(nc.sbuf_tensor(f"la_xn{tag}", [128, 8, Tg], BF16))
        s_qkv = esA.enter_context(nc.sbuf_tensor(f"la_qkv{tag}", [128, 8, Tg], BF16))
        s_vst = esA.enter_context(nc.sbuf_tensor(f"la_vst{tag}", [128, 4, D], BF16))
        ln = make_ln_ctx(p, nc, esA, consts, Tg, "la" + tag)
        ps_p = [esA.enter_context(nc.psum_tensor(f"la_pp{tag}{i}", [128, Tg], F32)) for i in range(2)]
        x3, xs3, xn3, qkv3 = s_x.ap(), s_xs.ap(), s_xn.ap(), s_qkv.ap()
        bankA_free = {}
        ln_last = None
        cast_last = None
        qkv_free = None
        vst_free = None
        for g in range(NG):
            xw = [] if ln_last is None else [(ln["sem_dv"], ln_last)]
            xv = p.dma("sync", lambda sy, g=g: sy.dma_start(
                out=s_x[:, :, :],
                in_=bass.AP(x_in, g * Tg, [[NTOK, 128], [128 * NTOK, 8], [1, Tg]])),
                waits=xw, inc=s_ld)
            wr = []
            if g > 0:
                wr = [(s_st, s_st.count), (s_gp, cast_last)]
            lnsem, lnval = emit_layernorm(p, ln, x3, xs3, [(s_ld, xv)], write_waits=wr)
            ln_last = lnval
            p.dma("sync", lambda sy, g=g: sy.dma_start(
                out=bass.AP(scr["xs"], g * Tg, [[NTOK, 128], [128 * NTOK, 8], [1, Tg]]),
                in_=s_xs[:, :, :]), waits=[(lnsem, lnval)], inc=s_st)
            cw = [(lnsem, lnval)]
            if g > 0:
                cw.append((s_mm, s_mm.count))
            cast_last = p.comp("gpsimd", lambda gg: gg.tensor_copy(s_xn[:, :, :], s_xs[:, :, :]),
                               waits=cw, inc=s_gp)
            for proj, wsb, bkey, dstd in (("q", s_wq, "bq", scr["qf"]), ("k", s_wk, "bk", scr["kf"])):
                for ot in range(8):
                    bi = ot % 2
                    for ic in range(8):
                        w = []
                        if ic == 0:
                            w = [(s_gp, cast_last), (s_ld, wld_done)]
                            if bi in bankA_free:
                                w.append(bankA_free[bi])
                        p.comp("tensor", lambda t, ic=ic, ot=ot, wsb=wsb, bi=bi: t.matmul(
                            ps_p[bi][:, :], wsb[:, ic, ot * 128:(ot + 1) * 128], xn3[:, ic, :],
                            start=(ic == 0), stop=(ic == 7), skip_group_check=True),
                            waits=w, inc=(s_mm if ic == 7 else None))
                    mmv = s_mm.count
                    ew = [(s_mm, mmv)]
                    if ot == 0 and qkv_free is not None:
                        ew.append(qkv_free)
                    e = p.comp("scalar", lambda s, ot=ot, bkey=bkey, bi=bi: s.activation(
                        qkv3[:, ot, :], ps_p[bi][:, :], AF.Identity,
                        bias=s_bcols[bkey][:, ot:ot + 1]),
                        waits=ew, inc=s_ev)
                    bankA_free[bi] = (s_ev, e)
                sv = p.dma("sync", lambda sy, g=g, dstd=dstd: sy.dma_start(
                    out=bass.AP(dstd, g * Tg, [[NTOK, 128], [128 * NTOK, 8], [1, Tg]]),
                    in_=s_qkv[:, :, :]), waits=[(s_ev, s_ev.count)], inc=s_st)
                qkv_free = (s_st, sv)
            for tt in range(Tg // 128):
                for half in range(2):
                    bi = half
                    for ic in range(8):
                        w = []
                        if ic == 0:
                            w = [(s_gp, cast_last)]
                            if bi in bankA_free:
                                w.append(bankA_free[bi])
                        p.comp("tensor", lambda t, ic=ic, tt=tt, half=half, bi=bi: t.matmul(
                            ps_p[bi][:, 0:512], xn3[:, ic, tt * 128:(tt + 1) * 128],
                            s_wv[:, ic, half * 512:(half + 1) * 512],
                            start=(ic == 0), stop=(ic == 7), skip_group_check=True),
                            waits=w, inc=(s_mm if ic == 7 else None))
                    mmv = s_mm.count
                    ew = [(s_mm, mmv)]
                    if tt == 0 and half == 0 and vst_free is not None:
                        ew.append(vst_free)
                    e = p.comp("scalar", lambda s, tt=tt, half=half, bi=bi: s.activation(
                        s_vst[:, tt, half * 512:(half + 1) * 512], ps_p[bi][:, 0:512],
                        AF.Copy), waits=ew, inc=s_ev)
                    bankA_free[bi] = (s_ev, e)
            sv = p.dma("sync", lambda sy, g=g: sy.dma_start(
                out=bass.AP(scr["vt"], (g * Tg) * D, [[D, 128], [128 * D, 4], [1, D]]),
                in_=s_vst[:, :, :]), waits=[(s_ev, s_ev.count)], inc=s_st)
            vst_free = (s_st, sv)
        phaseA_st = s_st.count
        phaseA_mm = s_mm.count
    p.barrier(f"la_bar_a{tag}")

    with contextlib.ExitStack() as esB:
        s_qh = esB.enter_context(nc.sbuf_tensor(f"la_qh{tag}", [64, 2048], BF16))
        s_kh = esB.enter_context(nc.sbuf_tensor(f"la_kh{tag}", [64, 2048], BF16))
        s_vh = esB.enter_context(nc.sbuf_tensor(f"la_vh{tag}", [128, 16, 64], BF16))
        s_rkT = esB.enter_context(nc.sbuf_tensor(f"la_rkT{tag}", [128, 16, 2048], BF16))
        s_asm = [esB.enter_context(nc.sbuf_tensor(f"la_asm{tag}{i}", [128, SW], BF16)) for i in range(2)]
        s_rq = esB.enter_context(nc.sbuf_tensor(f"la_rq{tag}", [128, 2048], BF16))
        s_p = esB.enter_context(nc.sbuf_tensor(f"la_p{tag}", [128, 2048], BF16))
        s_pT = esB.enter_context(nc.sbuf_tensor(f"la_pT{tag}", [128, 2048], BF16))
        s_ctx = esB.enter_context(nc.sbuf_tensor(f"la_ctx{tag}", [128, 64], BF16))
        s_cstrip = esB.enter_context(nc.sbuf_tensor(f"la_cst{tag}", [64, 2048], BF16))
        s_qn = esB.enter_context(nc.sbuf_tensor(f"la_qn{tag}", [1, 2048], F32))
        s_ksq = esB.enter_context(nc.sbuf_tensor(f"la_ksq{tag}", [64, 2048], F32))
        s_bias = esB.enter_context(nc.sbuf_tensor(f"la_bias{tag}", [128, 16], F32))
        s_sum = esB.enter_context(nc.sbuf_tensor(f"la_sum{tag}", [128, 1], F32))
        s_rcp = esB.enter_context(nc.sbuf_tensor(f"la_rcp{tag}", [128, 1], F32))
        ps_big = esB.enter_context(nc.psum_tensor(f"la_psb{tag}", [128, 2048], F32))
        ps_pt = [esB.enter_context(nc.psum_tensor(f"la_pspt{tag}{i}", [128, 1024], BF16)) for i in range(2)]
        ps_ctx = esB.enter_context(nc.psum_tensor(f"la_psctx{tag}", [128, 64], F32))
        ps_ctxT = esB.enter_context(nc.psum_tensor(f"la_psct{tag}", [64, 128], BF16))

        CH = [(0, 512), (512, 512), (1024, 512), (1536, 511)]
        SC = [(0, 512), (512, 512), (1024, 512), (1536, 512)]

        def strips(side, src_sb, dsb, dram_dst, first_waits):
            last_store = None
            for s16 in range(16):
                l0 = 128 * s16
                wlo = 1920 - l0
                par = s16 % 2
                for ci, (c0, cw_) in enumerate(CH):
                    w = list(first_waits) if (s16 == 0 and ci == 0) else []
                    if ci == 0 and (s16 > 0 or side == "k"):
                        w.append((s_ev, s_ev.count))
                    p.comp("tensor", lambda t, s16=s16, c0=c0, cw_=cw_, src_sb=src_sb, dsb=dsb: t.matmul(
                        ps_big[:, c0:c0 + cw_], src_sb[0:64, 128 * s16:128 * s16 + 128],
                        dsb[0:64, c0:c0 + cw_], start=True, stop=True,
                        skip_group_check=True),
                        waits=w, inc=(s_mm if ci == 3 else None))
                mmv = s_mm.count
                whi = wlo + SW - 1
                LA = max(0, 1024 - wlo)
                mB0 = max(wlo, 1024)
                LB = min(3070, whi) - mB0 + 1
                LC = max(0, whi - 3070)
                aB0 = LA
                aC0 = LA + LB
                ew = [(s_mm, mmv)]
                if last_store is not None and s16 >= 2:
                    ew.append(last_store[s16 - 2])
                if side == "q":
                    cB0 = mB0 - 1024
                    e1 = p.comp("scalar", lambda s, par=par, aB0=aB0, LB=LB, cB0=cB0: s.activation(
                        s_asm[par][:, aB0:aB0 + LB], ps_big[:, cB0:cB0 + LB], AF.Copy),
                        waits=ew, inc=s_ev)
                    last_ev = e1
                    if LC:
                        last_ev = p.comp("scalar", lambda s, par=par, aC0=aC0, LC=LC: s.activation(
                            s_asm[par][:, aC0:aC0 + LC], ps_big[:, 0:LC], AF.Copy), inc=s_ev)
                    if LA:
                        last_ev = p.comp("vector", lambda v, par=par, LA=LA, aB0=aB0: v.tensor_copy(
                            s_asm[par][:, 0:LA],
                            s_asm[par].ap()[:, aB0:aB0 + 1].broadcast_to([128, LA])),
                            waits=[(s_ev, last_ev)], inc=s_ev)
                else:
                    cB0 = mB0 - 1024
                    e1 = p.comp("scalar", lambda s, par=par, aB0=aB0, LB=LB, cB0=cB0: s.activation(
                        s_asm[par][:, aB0:aB0 + LB], ps_big[:, cB0:cB0 + LB], AF.Copy),
                        waits=ew, inc=s_ev)
                    last_ev = e1
                    if LA:
                        last_ev = p.comp("scalar", lambda s, par=par, LA=LA, wlo=wlo: s.activation(
                            s_asm[par][:, 0:LA], ps_big[:, wlo + 1023:wlo + 1023 + LA], AF.Copy),
                            inc=s_ev)
                    if LC:
                        last_ev = p.comp("vector", lambda v, par=par, aC0=aC0, LC=LC: v.tensor_copy(
                            s_asm[par][:, aC0:aC0 + LC],
                            s_asm[par].ap()[:, aC0 - 1:aC0].broadcast_to([128, LC])),
                            waits=[(s_ev, last_ev)], inc=s_ev)
                sv = p.dma("sync", lambda sy, s16=s16, par=par, dram_dst=dram_dst: sy.dma_start(
                    out=bass.AP(dram_dst, s16 * STRIDE_QE, [[SW, 128], [1, SW]]),
                    in_=s_asm[par][:, :]),
                    waits=[(s_ev, last_ev)], inc=s_st)
                if last_store is None:
                    last_store = {}
                last_store[s16] = (s_st, sv)
            return s_st.count

        bh_prev_done = None
        for b in range(SEQ_PER_CORE):
            for h in range(16):
                lw = [(s_st, phaseA_st)]
                if bh_prev_done is not None:
                    lw.append(bh_prev_done)
                qv = p.dma("gpsimd", lambda g, b=b, h=h: g.dma_start(
                    out=s_qh[:, :],
                    in_=bass.AP(scr["qf"], (64 * h) * NTOK + 2048 * b, [[NTOK, 64], [1, 2048]])),
                    waits=lw, inc=s_ld)
                kv = p.dma("gpsimd", lambda g, b=b, h=h: g.dma_start(
                    out=s_kh[:, :],
                    in_=bass.AP(scr["kf"], (64 * h) * NTOK + 2048 * b, [[NTOK, 64], [1, 2048]])),
                    inc=s_ld)
                vv = p.dma("gpsimd", lambda g, b=b, h=h: g.dma_start(
                    out=s_vh[:, :, :],
                    in_=bass.AP(scr["vt"], (2048 * b) * D + 64 * h, [[D, 128], [128 * D, 16], [1, 64]])),
                    inc=s_ld)
                qdone = strips("q", s_qh, s_drev, scr["qe"], [(s_ld, qv)])
                kdone = strips("k", s_kh, s_dnat, scr["ke"], [(s_ld, kv)])
                rkT_lv = None
                for s16 in range(16):
                    rkT_lv = p.dma("gpsimd", lambda g, s16=s16: g.dma_start(
                        out=s_rkT[:, s16, :],
                        in_=bass.AP(scr["ke"], s16 * STRIDE_QE + 127, [[SW - 1, 128], [1, 2048]])),
                        waits=[(s_st, kdone)] if s16 == 0 else [], inc=s_ld)
                a1 = p.comp("scalar", lambda s: s.activation(s_ksq[0:64, :], s_kh[0:64, :], AF.Square),
                            waits=[(s_ld, kv), (s_ev, s_ev.count)], inc=s_ev)
                mv = None
                for ci, (c0, cw_) in enumerate(SC):
                    mv = p.comp("tensor", lambda t, c0=c0, cw_=cw_: t.matmul(
                        ps_big[0:1, c0:c0 + cw_], consts.ones_col[0:64, :], s_ksq[0:64, c0:c0 + cw_],
                        start=True, stop=True, skip_group_check=True),
                        waits=[(s_ev, a1), (s_mm, s_mm.count)] if ci == 0 else [], inc=s_mm)
                v1 = p.comp("vector", lambda v: v.reduce_max(
                    s_scal.ap()[0:1, 2:3].unsqueeze(2),
                    ps_big.ap()[0:1, 0:2048].unsqueeze(1),
                    mybir.AxisListType.X), waits=[(s_mm, mv)], inc=s_dv)
                a2 = p.comp("scalar", lambda s: s.activation(s_scal[0:1, 2:3], s_scal[0:1, 2:3], AF.Sqrt),
                            waits=[(s_dv, v1)], inc=s_ev)
                a3 = p.comp("scalar", lambda s: s.activation(s_ksq[0:64, :], s_qh[0:64, :], AF.Square),
                            waits=[(s_ld, qv), (s_dv, v1)], inc=s_ev)
                for ci, (c0, cw_) in enumerate(SC):
                    mv = p.comp("tensor", lambda t, c0=c0, cw_=cw_: t.matmul(
                        ps_big[0:1, c0:c0 + cw_], consts.ones_col[0:64, :], s_ksq[0:64, c0:c0 + cw_],
                        start=True, stop=True, skip_group_check=True),
                        waits=[(s_ev, a3)] if ci == 0 else [], inc=s_mm)
                a4 = p.comp("scalar", lambda s: s.activation(s_qn[0:1, :], ps_big[0:1, :], AF.Sqrt),
                            waits=[(s_mm, mv), (s_ev, a2)], inc=s_ev)
                v2 = p.comp("vector", lambda v: v.tensor_copy(
                    s_scal[0:1, 3:4], s_scal[0:1, 2:3]),
                    waits=[(s_ev, a4)] + [dmax_ready], inc=s_dv)
                v2 = p.comp("vector", lambda v: v.tensor_tensor(
                    s_scal[0:1, 3:4], s_scal[0:1, 3:4], s_scal[0:1, 1:2], OP.add),
                    waits=[(s_dv, v2)], inc=s_dv)
                v3 = p.comp("vector", lambda v: v.tensor_tensor(
                    s_scal[0:1, 4:5], s_scal[0:1, 2:3], s_scal[0:1, 1:2], OP.mult),
                    waits=[(s_dv, v2)], inc=s_dv)
                v4 = p.comp("vector", lambda v: v.tensor_scalar(
                    s_qn[0:1, :], s_qn[0:1, :], s_scal[0:1, 3:4], s_scal[0:1, 4:5],
                    OP.mult, OP.add), waits=[(s_dv, v3)], inc=s_dv)
                v5 = p.comp("vector", lambda v: v.tensor_scalar(
                    s_qn[0:1, :], s_qn[0:1, :], -0.126, None, OP.mult),
                    waits=[(s_dv, v4)], inc=s_dv)
                bias_ready = None
                for s16 in range(16):
                    mv = p.comp("tensor", lambda t, s16=s16: t.matmul(
                        ps_ctx[:, 0:1], s_qn[0:1, 128 * s16:128 * s16 + 128],
                        consts.ones_row[0:1, 0:1], start=True, stop=True,
                        skip_group_check=True),
                        waits=([(s_dv, v5)] if s16 == 0 else []) + ([(s_dv, bias_ready)] if bias_ready else []),
                        inc=s_mm)
                    bias_ready = p.comp("vector", lambda v, s16=s16: v.tensor_copy(
                        s_bias[:, s16:s16 + 1], ps_ctx[:, 0:1]),
                        waits=[(s_mm, mv)], inc=s_dv)
                exp_prev = None
                rq_free = None
                pT_free = None
                ctx_free = None
                for qb in range(16):
                    rw = [(s_st, qdone)]
                    if rq_free is not None:
                        rw.append(rq_free)
                    rqv = p.dma("sync", lambda sy, qb=qb: sy.dma_start(
                        out=s_rq[:, :],
                        in_=bass.AP(scr["qe"], (qb) * STRIDE_QE + 127, [[SW - 1, 128], [1, 2048]])),
                        waits=rw, inc=s_ld)
                    for ci, (c0, cw_) in enumerate(SC):
                        w = []
                        if ci == 0:
                            w = [(s_ld, rkT_lv), (s_dv, bias_ready)]
                            if exp_prev is not None:
                                w.append((s_ev, exp_prev))
                        p.comp("tensor", lambda t, qb=qb, c0=c0, cw_=cw_: t.matmul(
                            ps_big[:, c0:c0 + cw_], s_qh[0:64, 128 * qb:128 * qb + 128],
                            s_kh[0:64, c0:c0 + cw_], start=True, stop=False,
                            skip_group_check=True), waits=w)
                    for sc in range(16):
                        p.comp("tensor", lambda t, sc=sc, qb=qb: t.matmul(
                            ps_big[:, 128 * sc:128 * sc + 128],
                            s_rkT[:, sc, 128 * qb:128 * qb + 128], consts.ident_bf[:, :],
                            start=False, stop=(sc % 4 == 3), skip_group_check=True),
                            inc=(s_mm if sc == 15 else None))
                    mmv = s_mm.count
                    aq = p.comp("vector", lambda v: v.tensor_tensor(
                        ps_big[:, :], ps_big[:, :], s_rq[:, :], OP.add),
                        waits=[(s_mm, mmv), (s_ld, rqv)], inc=s_dv)
                    rq_free = (s_dv, aq)
                    ew = [(s_dv, aq)]
                    if pT_free is not None:
                        ew.append(pT_free)
                    expv = p.comp("scalar", lambda s, qb=qb: s.activation(
                        s_p[:, :], ps_big[:, :], AF.Exp, scale=0.125,
                        bias=s_bias[:, qb:qb + 1], accum_out=s_sum[:, :]),
                        waits=ew, inc=s_ev)
                    exp_prev = expv
                    tv = {}
                    for half in range(2):
                        for rc in range(8):
                            rcg = half * 8 + rc
                            w = [(s_ev, expv)] if rcg == 0 else []
                            if rcg < 2 and ctx_free is not None:
                                w.append(ctx_free)
                            if rcg == 0 and pT_free is not None:
                                w.append(pT_free)
                            tv[half] = p.comp("tensor", lambda t, half=half, rc=rc, rcg=rcg: t.transpose(
                                ps_pt[half][:, 128 * rc:128 * rc + 128],
                                s_p[:, 128 * rcg:128 * rcg + 128], consts.ident_bf[:, :]),
                                waits=w, inc=(s_mm if rc == 7 else None))
                    mmv = s_mm.count
                    ev1 = p.comp("vector", lambda v: v.tensor_copy(s_pT[:, 0:1024], ps_pt[0][:, :]),
                                 waits=[(s_mm, mmv)], inc=s_dv)
                    ev2 = p.comp("vector", lambda v: v.tensor_copy(s_pT[:, 1024:2048], ps_pt[1][:, :]),
                                 inc=s_dv)
                    pT_free = (s_dv, ev2)
                    r1 = p.comp("vector", lambda v: v.reciprocal(s_rcp[:, :], s_sum[:, :]),
                                waits=[(s_ev, expv)], inc=s_dv)
                    for rc in range(16):
                        w = []
                        if rc == 0:
                            w = [(s_dv, ev2), (s_ld, vv)]
                            if ctx_free is not None:
                                w.append(ctx_free)
                        p.comp("tensor", lambda t, rc=rc: t.matmul(
                            ps_ctx[:, :], s_pT[:, 128 * rc:128 * rc + 128], s_vh[:, rc, :],
                            start=(rc == 0), stop=(rc == 15), skip_group_check=True),
                            waits=w, inc=(s_mm if rc == 15 else None))
                    mmv = s_mm.count
                    c1 = p.comp("vector", lambda v: v.tensor_scalar(
                        s_ctx[:, :], ps_ctx[:, :], s_rcp[:, :], None, OP.mult),
                        waits=[(s_mm, mmv), (s_dv, r1)], inc=s_dv)
                    ctv = p.comp("tensor", lambda t: t.transpose(
                        ps_ctxT[:, :], s_ctx[:, :], consts.ident_bf[:, :]),
                        waits=[(s_dv, c1), (s_ev, s_ev.count)], inc=s_mm)
                    bvslice = s_bcols["bv"].ap()[(h % 2) * 64:(h % 2) * 64 + 64, h // 2:h // 2 + 1]
                    ce = p.comp("scalar", lambda s, qb=qb, bvslice=bvslice: s.activation(
                        s_cstrip[0:64, 128 * qb:128 * qb + 128], ps_ctxT[0:64, :], AF.Identity,
                        bias=bvslice), waits=[(s_mm, ctv)], inc=s_ev)
                    ctx_free = (s_ev, ce)
                sv = p.dma("sync", lambda sy, b=b, h=h: sy.dma_start(
                    out=bass.AP(scr["cf"], (64 * h) * NTOK + 2048 * b, [[NTOK, 64], [1, 2048]]),
                    in_=s_cstrip[:, :]), waits=[(s_ev, s_ev.count)], inc=s_st)
                bh_prev_done = (s_st, sv)
        phaseB_st = s_st.count
    p.barrier(f"la_bar_b{tag}")

    with contextlib.ExitStack() as esC:
        s_cfm = esC.enter_context(nc.sbuf_tensor(f"la_cfm{tag}", [128, 8, Tg], BF16))
        s_x2 = esC.enter_context(nc.sbuf_tensor(f"la_x2{tag}", [128, 8, Tg], F32))
        s_xs2 = esC.enter_context(nc.sbuf_tensor(f"la_xs2{tag}", [128, 8, Tg], F32))
        s_h2 = esC.enter_context(nc.sbuf_tensor(f"la_h2{tag}", [128, 8, Tg], F32))
        s_y = esC.enter_context(nc.sbuf_tensor(f"la_y{tag}", [128, 8, Tg], F32))
        s_t = esC.enter_context(nc.sbuf_tensor(f"la_t{tag}", [128, Tg], F32))
        ln2 = make_ln_ctx(p, nc, esC, consts, Tg, "lc" + tag)
        ps_p = [esC.enter_context(nc.psum_tensor(f"la_pc{tag}{i}", [128, Tg], F32)) for i in range(2)]
        bankC_free = {}
        ln_last = None
        st_prev = None
        h2_free = None
        for g in range(NG):
            lw = [(s_st, phaseB_st)]
            if ln_last is not None:
                lw.append((ln2["sem_dv"], ln_last))
            cv = p.dma("sync", lambda sy, g=g: sy.dma_start(
                out=s_cfm[:, :, :],
                in_=bass.AP(scr["cf"], g * Tg, [[NTOK, 128], [128 * NTOK, 8], [1, Tg]])),
                waits=lw, inc=s_ld)
            xv = p.dma("sync", lambda sy, g=g: sy.dma_start(
                out=s_x2[:, :, :],
                in_=bass.AP(x_in, g * Tg, [[NTOK, 128], [128 * NTOK, 8], [1, Tg]])),
                inc=s_ld)
            xsv = p.dma("sync", lambda sy, g=g: sy.dma_start(
                out=s_xs2[:, :, :],
                in_=bass.AP(scr["xs"], g * Tg, [[NTOK, 128], [128 * NTOK, 8], [1, Tg]])),
                inc=s_ld)
            for ot in range(8):
                bi = ot % 2
                for ic in range(8):
                    w = []
                    if ic == 0:
                        w = [(s_ld, cv), (s_ld, s_ld.count)]
                        if bi in bankC_free:
                            w.append(bankC_free[bi])
                    p.comp("tensor", lambda t, ic=ic, ot=ot, bi=bi: t.matmul(
                        ps_p[bi][:, :], s_wo[:, ic, ot * 128:(ot + 1) * 128],
                        s_cfm.ap()[:, ic, :], start=(ic == 0), stop=(ic == 7),
                        skip_group_check=True), waits=w, inc=(s_mm if ic == 7 else None))
                mmv = s_mm.count
                e = p.comp("scalar", lambda s, ot=ot, bi=bi: s.activation(
                    s_t[:, :], ps_p[bi][:, :], AF.Identity, bias=s_bcols["bo"][:, ot:ot + 1]),
                    waits=[(s_mm, mmv)] + ([(s_dv, h2_free)] if (h2_free and ot == 0) else []), inc=s_ev)
                hv = p.comp("vector", lambda v, ot=ot: v.tensor_tensor(
                    s_h2.ap()[:, ot, :], s_t[:, :], s_xs2.ap()[:, ot, :], OP.add),
                    waits=[(s_ev, e), (s_ld, xsv)], inc=s_dv)
                bankC_free[bi] = (s_dv, hv)
            h2_done = s_dv.count
            wr = [(s_st, st_prev)] if st_prev is not None else []
            lnsem, lnval = emit_layernorm(
                p, ln2, s_h2.ap(), s_y.ap(), [(s_dv, h2_done), (s_ld, xv)],
                write_waits=wr, eps=1e-12,
                affine=(s_bcols["gam"], s_bcols["bet"]), resid3=s_x2.ap())
            ln_last = lnval
            h2_free = lnval
            sv = p.dma("sync", lambda sy, g=g: sy.dma_start(
                out=bass.AP(y_out, g * Tg, [[NTOK, 128], [128 * NTOK, 8], [1, Tg]]),
                in_=s_y[:, :, :]), waits=[(lnsem, lnval)], inc=s_st)
            st_prev = sv
    p.wait("sync", [(s_st, s_st.count)])
    return s_st


def emit_l0attn(p, nc, es, consts, x_in, y_out, W, scr, tag):
    Tg = 512
    NG = NTOK // Tg
    CJ = 128
    NCH = T // CJ

    s_ld = p.sem(f"da_ld{tag}")
    s_mm = p.sem(f"da_mm{tag}")
    s_ev = p.sem(f"da_ev{tag}")
    s_dv = p.sem(f"da_dv{tag}")
    s_gp = p.sem(f"da_gp{tag}")
    s_st = p.sem(f"da_st{tag}")

    cols = {}
    for key, ncol in (("ln1g", 8), ("ln1b", 8), ("ln2g", 8), ("ln2b", 8),
                      ("rmsw", 8), ("b2", 8), ("onw", 4), ("cq", 32), ("ck", 32),
                      ("cv", 64), ("dtb", 1), ("alog", 1)):
        cols[key] = es.enter_context(nc.sbuf_tensor(f"da_c{key}{tag}", [128, ncol], F32))
    maskLI = es.enter_context(nc.sbuf_tensor(f"da_mLI{tag}", [128, 128], F32))
    maskSU = es.enter_context(nc.sbuf_tensor(f"da_mSU{tag}", [128, 128], F32))
    maskLS = es.enter_context(nc.sbuf_tensor(f"da_mLS{tag}", [128, 128], F32))

    for key in ("ln1g", "ln1b", "ln2g", "ln2b", "rmsw", "b2", "onw", "cq", "ck", "cv", "dtb", "alog"):
        nr = W[key].shape[0]
        p.dma("gpsimd", lambda g, key=key, nr=nr: g.dma_start(
            out=cols[key][0:nr, :], in_=W[key].ap()), inc=s_ld)
    wld = s_ld.count
    msem = p.sem(f"da_msk{tag}")

    with contextlib.ExitStack() as esM:
        s_iota = esM.enter_context(nc.sbuf_tensor(f"da_iot{tag}", [1, 128], F32))
        s_ic = esM.enter_context(nc.sbuf_tensor(f"da_ioc{tag}", [128, 1], F32))
        s_dd = esM.enter_context(nc.sbuf_tensor(f"da_dd{tag}", [128, 128], F32))
        s_m01 = esM.enter_context(nc.sbuf_tensor(f"da_m01{tag}", [128, 128], F32))
        ps_m = esM.enter_context(nc.psum_tensor(f"da_psm{tag}", [128, 128], F32))
        v0 = p.comp("vector", lambda v: v.tensor_tensor_scan(
            s_iota[0:1, :], consts.ones_row[0:1, :], consts.ones_row[0:1, :],
            0.0, OP.mult, OP.add), waits=[(consts.sem, consts.ready)], inc=msem)
        v0 = p.comp("vector", lambda v: v.tensor_scalar(
            s_iota[0:1, :], s_iota[0:1, :], -1.0, None, OP.add),
            waits=[(msem, v0)], inc=msem)
        m1 = p.comp("tensor", lambda t: t.matmul(
            ps_m[:, 0:1], s_iota[0:1, :], consts.ones_row[0:1, 0:1],
            start=True, stop=True, skip_group_check=True),
            waits=[(msem, v0)], inc=s_mm)
        v0 = p.comp("vector", lambda v: v.tensor_copy(s_ic[:, :], ps_m[:, 0:1]),
                    waits=[(s_mm, m1)], inc=msem)
        m1 = p.comp("tensor", lambda t: t.matmul(
            ps_m[:, :], consts.ones_row[0:1, :], s_iota[0:1, :],
            start=True, stop=True, skip_group_check=True),
            waits=[(msem, v0)], inc=s_mm)
        v0 = p.comp("vector", lambda v: v.tensor_scalar(
            s_dd[:, :], ps_m[:, :], s_ic.ap()[:, 0:1], None, OP.subtract),
            waits=[(s_mm, m1)], inc=msem)
        v0 = p.comp("vector", lambda v: v.tensor_scalar(
            s_m01[:, :], s_dd[:, :], 0.0, None, OP.is_gt), waits=[(msem, v0)], inc=msem)
        v0 = p.comp("vector", lambda v: v.tensor_scalar(
            maskLI[:, :], s_m01[:, :], -1.0, 1e30, OP.add, OP.mult),
            waits=[(msem, v0)], inc=msem)
        v0 = p.comp("vector", lambda v: v.tensor_scalar(
            s_m01[:, :], s_dd[:, :], 0.0, None, OP.is_ge), waits=[(msem, v0)], inc=msem)
        v0 = p.comp("vector", lambda v: v.tensor_scalar(
            maskLS[:, :], s_m01[:, :], -1.0, 1e30, OP.add, OP.mult),
            waits=[(msem, v0)], inc=msem)
        v0 = p.comp("vector", lambda v: v.tensor_scalar(
            maskSU[:, :], s_m01[:, :], -1e30, None, OP.mult),
            waits=[(msem, v0)], inc=msem)
    mval = msem.count

    with contextlib.ExitStack() as esP:
        s_wq = esP.enter_context(nc.sbuf_tensor(f"da_wq{tag}", [128, 8, D], BF16))
        s_wk = esP.enter_context(nc.sbuf_tensor(f"da_wk{tag}", [128, 8, D], BF16))
        s_wv = esP.enter_context(nc.sbuf_tensor(f"da_wv{tag}", [128, 8, 2048], BF16))
        s_wg = esP.enter_context(nc.sbuf_tensor(f"da_wg{tag}", [128, 8, 2048], BF16))
        s_wab = esP.enter_context(nc.sbuf_tensor(f"da_wab{tag}", [128, 8, 8], BF16))
        for dst, key, nch in ((s_wq, "Wq", 8), (s_wk, "Wk", 8), (s_wv, "Wv", 8),
                              (s_wg, "Wg", 8), (s_wab, "Wab", 8)):
            ocols = W[key].shape[1]
            p.dma("gpsimd", lambda g, dst=dst, key=key, nch=nch, ocols=ocols: g.dma_start(
                out=dst[:, 0:nch, 0:ocols],
                in_=bass.AP(W[key], 0, [[ocols, 128], [128 * ocols, nch], [1, ocols]])), inc=s_ld)
        wld = s_ld.count
        s_x = esP.enter_context(nc.sbuf_tensor(f"da_x{tag}", [128, 8, Tg], F32))
        s_xs = esP.enter_context(nc.sbuf_tensor(f"da_xs{tag}", [128, 8, Tg], F32))
        s_h1 = s_x
        s_hn = esP.enter_context(nc.sbuf_tensor(f"da_hn{tag}", [128, 8, Tg], BF16))
        s_pr = esP.enter_context(nc.sbuf_tensor(f"da_pr{tag}", [128, 16, Tg], BF16))
        s_sq = esP.enter_context(nc.sbuf_tensor(f"da_sq{tag}", [128, Tg], F32))
        s_abuf = esP.enter_context(nc.sbuf_tensor(f"da_abuf{tag}", [8, Tg], F32))
        s_row = [esP.enter_context(nc.sbuf_tensor(f"da_row{i}{tag}", [1, Tg], F32)) for i in range(2)]
        ln = make_ln_ctx(p, nc, esP, consts, Tg, "dp" + tag)
        lnA = make_ln_ctx(p, nc, esP, consts, Tg, "dA" + tag, share=ln)
        ps_p = [esP.enter_context(nc.psum_tensor(f"da_pp{tag}{i}", [128, Tg], F32)) for i in range(2)]

        bank_free = {}
        ln_last = None
        pr_free = {}
        cast_prev = None
        h1_free = None
        for g in range(NG):
            xw = [] if ln_last is None else [(ln["sem_dv"], ln_last)]
            if h1_free is not None:
                xw.append(h1_free)
            xv = p.dma("sync", lambda sy, g=g: sy.dma_start(
                out=s_x[:, :, :],
                in_=bass.AP(x_in, g * Tg, [[NTOK, 128], [128 * NTOK, 8], [1, Tg]])),
                waits=xw, inc=s_ld)
            wr = [(s_st, s_st.count)] if g > 0 else []
            lnsem, lnval = emit_layernorm(p, ln, s_x.ap(), s_xs.ap(), [(s_ld, xv)],
                                          write_waits=wr)
            ln_last = lnval
            sxv = p.dma("sync", lambda sy, g=g: sy.dma_start(
                out=bass.AP(scr["xsub"], g * Tg, [[NTOK, 128], [128 * NTOK, 8], [1, Tg]]),
                in_=s_xs[:, :, :]), waits=[(lnsem, lnval)], inc=s_st)
            wr2 = [(lnA["sem_mm"], lnA["sem_mm"].count)] if g > 0 else []
            ln2sem, ln2val = emit_layernorm(p, lnA, s_xs.ap(), s_h1.ap(),
                                            [(lnsem, lnval)],
                                            write_waits=wr2 + [(lnsem, lnval)],
                                            affine=(cols["ln1g"], cols["ln1b"]))
            aq = None
            for c in range(8):
                w = [(ln2sem, ln2val)] if c == 0 else [(s_mm, aq)]
                av = p.comp("scalar", lambda s, c=c: s.activation(s_sq[:, :], s_h1.ap()[:, c, :], AF.Square),
                            waits=w, inc=s_ev)
                aq = p.comp("tensor", lambda t, c=c: t.matmul(
                    ps_p[0][0:1, :], consts.ones_col[:, :], s_sq[:, :],
                    start=(c == 0), stop=(c == 7), skip_group_check=True),
                    waits=[(s_ev, av)] + ([(s_dv, s_dv.count)] if c == 0 else []),
                    inc=s_mm)
            v1 = p.comp("vector", lambda v: v.tensor_scalar(s_row[0][:1, :], ps_p[0][0:1, :],
                                                            1.0 / 1024.0, 1e-6, OP.mult, OP.add),
                        waits=[(s_mm, aq)], inc=s_dv)
            a1 = p.comp("scalar", lambda s: s.activation(s_row[0][:1, :], s_row[0][:1, :], AF.Ln),
                        waits=[(s_dv, v1)], inc=s_ev)
            a2 = p.comp("scalar", lambda s: s.activation(s_row[0][:1, :], s_row[0][:1, :], AF.Exp,
                                                         scale=-0.5), waits=[(s_ev, a1)], inc=s_ev)
            bv_ = p.comp("tensor", lambda t: t.matmul(ps_p[0][:, :], consts.ones_row[:1, :],
                                                      s_row[0][:1, :], start=True, stop=True,
                                                      skip_group_check=True),
                         waits=[(s_ev, a2)], inc=s_mm)
            cast_last = None
            for c in range(8):
                w = [(s_mm, bv_)] if c == 0 else []
                if c == 0 and cast_prev is not None:
                    w.append(cast_prev)
                p.comp("vector", lambda v, c=c: v.tensor_tensor(s_sq[:, :], s_h1.ap()[:, c, :],
                                                                ps_p[0][:, :], OP.mult), waits=w)
                cast_last = p.comp("vector", lambda v, c=c: v.tensor_scalar(
                    s_hn.ap()[:, c, :], s_sq[:, :], cols["rmsw"][:, c:c + 1], None, OP.mult),
                    inc=s_dv)
            cast_prev = None

            def proj(wsb, n_ot, dstd, extra_w=()):
                stv = None
                for ot in range(n_ot):
                    bi = ot % 2
                    for ic in range(8):
                        w = []
                        if ic == 0:
                            w = [(s_dv, cast_last), (s_ld, wld)] + list(extra_w)
                            if bi in bank_free:
                                w.append(bank_free[bi])
                        p.comp("tensor", lambda t, ic=ic, ot=ot, wsb=wsb, bi=bi: t.matmul(
                            ps_p[bi][:, :], wsb[:, ic, ot * 128:(ot + 1) * 128],
                            s_hn.ap()[:, ic, :], start=(ic == 0), stop=(ic == 7),
                            skip_group_check=True),
                            waits=w, inc=(s_mm if ic == 7 else None))
                    mmv = s_mm.count
                    ew = [(s_mm, mmv)]
                    if (ot % 16) in pr_free:
                        ew.append(pr_free[ot % 16])
                    e = p.comp("scalar", lambda s, ot=ot, bi=bi: s.activation(
                        s_pr.ap()[:, ot % 16, :], ps_p[bi][:, :], AF.Copy),
                        waits=ew, inc=s_ev)
                    bank_free[bi] = (s_ev, e)
                    if ot % 16 == n_ot - 1 or ot % 16 == 15:
                        nch = min(n_ot, 16)
                        stv = p.dma("sync", lambda sy, g=g, dstd=dstd, nch=nch: sy.dma_start(
                            out=bass.AP(dstd, g * Tg, [[NTOK, 128], [128 * NTOK, nch], [1, Tg]]),
                            in_=s_pr[:, 0:nch, :]), waits=[(s_ev, s_ev.count)], inc=s_st)
                        for j in range(nch):
                            pr_free[j] = (s_st, stv)
                return stv
            proj(s_wq, 8, scr["qraw"])
            proj(s_wk, 8, scr["kraw"])
            proj(s_wv, 16, scr["vraw"])
            proj(s_wg, 16, scr["gate"])
            for ic in range(8):
                w = []
                if ic == 0:
                    w = [(s_dv, cast_last)]
                    if 0 in bank_free:
                        w.append(bank_free[0])
                p.comp("tensor", lambda t, ic=ic: t.matmul(
                    ps_p[0][0:8, :], s_wab[:, ic, :], s_hn.ap()[:, ic, :],
                    start=(ic == 0), stop=(ic == 7), skip_group_check=True),
                    waits=w, inc=(s_mm if ic == 7 else None))
            mmv = s_mm.count
            ew = [(s_mm, mmv)]
            if g > 0:
                ew.append((s_st, s_st.count))
            e = p.comp("scalar", lambda s: s.activation(s_abuf[0:8, :], ps_p[0][0:8, :], AF.Copy),
                       waits=ew, inc=s_ev)
            bank_free[0] = (s_ev, e)
            p.dma("sync", lambda sy, g=g: sy.dma_start(
                out=bass.AP(scr["ab"], g * Tg, [[NTOK, 8], [1, Tg]]),
                in_=s_abuf[0:8, :]),
                waits=[(s_ev, e)], inc=s_st)
            cast_prev = (s_mm, s_mm.count)
            h1_free = (s_dv, cast_last)
            if "hdump" in scr:
                p.dma("sync", lambda sy, g=g: sy.dma_start(
                    out=bass.AP(scr["hdump"], g * Tg, [[NTOK, 128], [128 * NTOK, 8], [1, Tg]]),
                    in_=s_hn[:, :, :]), waits=[(s_dv, cast_last)], inc=s_st)
        phaseP_st = s_st.count
    p.barrier(f"da_barP{tag}")

    with contextlib.ExitStack() as esG:
        s_ab = esG.enter_context(nc.sbuf_tensor(f"da_ab{tag}", [4, NTOK], F32))
        s_abb = esG.enter_context(nc.sbuf_tensor(f"da_abb{tag}", [4, NTOK], F32))
        s_g1 = esG.enter_context(nc.sbuf_tensor(f"da_g1{tag}", [4, NTOK], F32))
        s_g2 = esG.enter_context(nc.sbuf_tensor(f"da_g2{tag}", [4, NTOK], F32))
        s_B = esG.enter_context(nc.sbuf_tensor(f"da_B{tag}", [4, 2 * (T + 1)], F32))
        s_on = esG.enter_context(nc.sbuf_tensor(f"da_on{tag}", [4, T], F32))
        s_ea = esG.enter_context(nc.sbuf_tensor(f"da_ea{tag}", [4, 1], F32))
        lv = p.dma("gpsimd", lambda g: g.dma_start(
            out=s_ab[:, :], in_=bass.AP(scr["ab"], 0, [[NTOK, 4], [1, NTOK]])), inc=s_ld)
        lv = p.dma("gpsimd", lambda g: g.dma_start(
            out=s_abb[:, :], in_=bass.AP(scr["ab"], 4 * NTOK, [[NTOK, 4], [1, NTOK]])), inc=s_ld)
        ze = p.comp("gpsimd", lambda g: g.memset(s_on[:, :], 1.0))
        ze = p.comp("gpsimd", lambda g: g.memset(s_B[:, :], 0.0))
        zv = p.comp("gpsimd", lambda g: g.tensor_copy(s_B[0:4, 0:1], s_B[0:4, 0:1]), inc=s_gp)
        a0 = p.comp("scalar", lambda s: s.activation(s_ea[0:4, :], cols["alog"][0:4, 0:1], AF.Exp),
                    waits=[(s_ld, wld)], inc=s_ev)
        v0 = p.comp("vector", lambda v: v.tensor_scalar(s_ea[0:4, :], s_ea[0:4, :], -1.0, None, OP.mult),
                    waits=[(s_ev, a0)], inc=s_dv)
        v1 = p.comp("vector", lambda v: v.tensor_scalar(s_ab[0:4, :], s_ab[0:4, :],
                                                        cols["dtb"][0:4, 0:1], None, OP.add),
                    waits=[(s_ld, lv), (s_dv, v0)], inc=s_dv)
        a1 = p.comp("scalar", lambda s: s.activation(s_g1[0:4, :], s_ab[0:4, :], AF.Abs),
                    waits=[(s_dv, v1)], inc=s_ev)
        a2 = p.comp("scalar", lambda s: s.activation(s_g1[0:4, :], s_g1[0:4, :], AF.Exp, scale=-1.0),
                    waits=[(s_ev, a1)], inc=s_ev)
        v2 = p.comp("vector", lambda v: v.tensor_scalar(s_g1[0:4, :], s_g1[0:4, :], 1.0, None, OP.add),
                    waits=[(s_ev, a2)], inc=s_dv)
        a3 = p.comp("scalar", lambda s: s.activation(s_g1[0:4, :], s_g1[0:4, :], AF.Ln),
                    waits=[(s_dv, v2)], inc=s_ev)
        v3 = p.comp("vector", lambda v: v.tensor_scalar(s_g2[0:4, :], s_ab[0:4, :], 0.0, None, OP.max),
                    waits=[(s_ev, a3)], inc=s_dv)
        v4 = p.comp("vector", lambda v: v.tensor_tensor(s_g1[0:4, :], s_g1[0:4, :], s_g2[0:4, :], OP.add),
                    waits=[(s_dv, v3)], inc=s_dv)
        v5 = p.comp("vector", lambda v: v.tensor_scalar(s_g1[0:4, :], s_g1[0:4, :], s_ea[0:4, 0:1],
                                                        None, OP.mult),
                    waits=[(s_dv, v4)], inc=s_dv)
        sc1 = p.comp("vector", lambda v: v.tensor_tensor_scan(
            s_B[0:4, 1:T + 1], s_on[0:4, 0:T], s_g1[0:4, 0:T], 0.0, OP.mult, OP.add),
            waits=[(s_dv, v5), (s_gp, zv)], inc=s_dv)
        sc2 = p.comp("vector", lambda v: v.tensor_tensor_scan(
            s_B[0:4, T + 2:2 * T + 2], s_on[0:4, 0:T], s_g1[0:4, T:2 * T], 0.0, OP.mult, OP.add),
            waits=[(s_dv, sc1)], inc=s_dv)
        a4 = p.comp("scalar", lambda s: s.activation(s_g2[0:4, :], s_abb[0:4, :], AF.Sigmoid),
                    waits=[(s_dv, sc2)], inc=s_ev)
        p.dma("sync", lambda sy: sy.dma_start(
            out=bass.AP(scr["Bsc"], 0, [[2 * (T + 1), 4], [1, 2 * (T + 1)]]), in_=s_B[0:4, :]),
            waits=[(s_dv, sc2)], inc=s_st)
        p.dma("sync", lambda sy: sy.dma_start(
            out=bass.AP(scr["beta"], 0, [[NTOK, 4], [1, NTOK]]), in_=s_g2[0:4, :]),
            waits=[(s_ev, a4)], inc=s_st)
    p.barrier(f"da_barG{tag}")

    with contextlib.ExitStack() as esV:
        PADW = T + 3
        s_cb = esV.enter_context(nc.sbuf_tensor(f"da_cb{tag}", [128, 2, 2 * PADW], BF16))
        s_cv = esV.enter_context(nc.sbuf_tensor(f"da_cvb{tag}", [128, 2, NTOK], F32))
        s_bsq = esV.enter_context(nc.sbuf_tensor(f"da_bsq{tag}", [128, 512], F32))
        s_out = esV.enter_context(nc.sbuf_tensor(f"da_vout{tag}", [128, 2, NTOK], BF16))
        s_vrow = esV.enter_context(nc.sbuf_tensor(f"da_vrow{tag}", [1, 512], F32))
        ps_r = esV.enter_context(nc.psum_tensor(f"da_psr{tag}", [1, 512], F32))
        ps_bc = esV.enter_context(nc.psum_tensor(f"da_psbc{tag}", [128, 512], F32))
        zr = p.sem(f"da_vz{tag}")
        p.comp("gpsimd", lambda g: g.memset(s_cb[:, :, :], 0.0))
        zv = p.comp("gpsimd", lambda g: g.tensor_copy(s_cb[:, 0, 0:4], s_cb[:, 0, 0:4]), inc=zr)

        def conv_unit(src_dram, dst_dram, cwcol, chunks, do_norm, inv_scale, gate_done):
            nonlocal_busy = []
            lvs = []
            for ci, cg in enumerate(chunks):
                for sq_ in range(2):
                    lvs.append(p.dma("gpsimd", lambda g, ci=ci, cg=cg, sq_=sq_: g.dma_start(
                        out=s_cb[:, ci, sq_ * PADW + 3:sq_ * PADW + 3 + T],
                        in_=bass.AP(src_dram, cg * 128 * NTOK + sq_ * T,
                                    [[NTOK, 128], [1, T]])),
                        waits=[(zr, zv)] + list(gate_done), inc=s_ld))
            lv = lvs[-1]
            dvv = None
            for ci, cg in enumerate(chunks):
                for sq_ in range(2):
                    base = sq_ * PADW
                    w = [(s_ld, lv)] if (ci == 0 and sq_ == 0) else []
                    p.comp("vector", lambda v, ci=ci, cg=cg, base=base, sq_=sq_: v.tensor_scalar(
                        s_cv[:, ci, sq_ * T:(sq_ + 1) * T], s_cb[:, ci, base:base + T],
                        cwcol[:, cg * 4:cg * 4 + 1], None, OP.mult), waits=w)
                    for j in range(1, 4):
                        dvv = p.comp("vector", lambda v, ci=ci, cg=cg, base=base, j=j, sq_=sq_:
                                     v.scalar_tensor_tensor(
                                         s_cv[:, ci, sq_ * T:(sq_ + 1) * T],
                                         s_cb[:, ci, base + j:base + j + T],
                                         cwcol[:, cg * 4 + j:cg * 4 + j + 1],
                                         s_cv[:, ci, sq_ * T:(sq_ + 1) * T],
                                         OP.mult, OP.add),
                                     inc=(s_dv if (j == 3) else None))
            avs = []
            for ci in range(len(chunks)):
                avs.append(p.comp("scalar", lambda s, ci=ci: s.activation(
                    s_cv[:, ci, :], s_cv[:, ci, :], AF.Silu),
                    waits=[(s_dv, dvv)] if ci == 0 else [], inc=s_ev))
            a_last = avs[-1]
            if not do_norm:
                ov = p.comp("vector", lambda v: v.tensor_copy(s_out[:, 0:len(chunks), :],
                                                              s_cv[:, 0:len(chunks), :]),
                            waits=[(s_ev, a_last)], inc=s_dv)
                norm_done = (s_dv, ov)
            else:
                last = None
                for blk in range(8):
                    t0 = blk * 512
                    mv = None
                    for ci in range(2):
                        aq = p.comp("scalar", lambda s, ci=ci, t0=t0: s.activation(
                            s_bsq[:, :], s_cv[:, ci, t0:t0 + 512], AF.Square),
                            waits=([(s_ev, a_last)] if (blk == 0 and ci == 0) else []) +
                                  [(s_mm, s_mm.count)], inc=s_ev)
                        mv = p.comp("tensor", lambda t, ci=ci: t.matmul(
                            ps_r[0:1, :], consts.ones_col[:, :], s_bsq[:, :],
                            start=(ci == 0), stop=(ci == 1), skip_group_check=True),
                            waits=[(s_ev, aq)] + ([(s_dv, s_dv.count)] if ci == 0 else []),
                            inc=s_mm)
                    v1 = p.comp("vector", lambda v: v.tensor_scalar(
                        s_vrow[0:1, :], ps_r[0:1, :], 1.0, 1e-6, OP.mult, OP.add),
                        waits=[(s_mm, mv)], inc=s_dv)
                    a1 = p.comp("scalar", lambda s: s.activation(s_vrow[0:1, :], s_vrow[0:1, :], AF.Ln),
                                waits=[(s_dv, v1)], inc=s_ev)
                    a2 = p.comp("scalar", lambda s: s.activation(s_vrow[0:1, :], s_vrow[0:1, :],
                                                                 AF.Exp, scale=-0.5),
                                waits=[(s_ev, a1)], inc=s_ev)
                    v2 = p.comp("vector", lambda v, isc=inv_scale: v.tensor_scalar(
                        s_vrow[0:1, :], s_vrow[0:1, :], isc, None, OP.mult),
                        waits=[(s_ev, a2)], inc=s_dv)
                    bv_ = p.comp("tensor", lambda t: t.matmul(
                        ps_bc[:, :], consts.ones_row[:1, :], s_vrow[0:1, :],
                        start=True, stop=True, skip_group_check=True),
                        waits=[(s_dv, v2)], inc=s_mm)
                    for ci in range(2):
                        last = p.comp("vector", lambda v, ci=ci, t0=t0: v.tensor_tensor(
                            s_out[:, ci, t0:t0 + 512], s_cv[:, ci, t0:t0 + 512],
                            ps_bc[:, :], OP.mult),
                            waits=[(s_mm, bv_)] if ci == 0 else [], inc=s_dv)
                norm_done = (s_dv, last)
            stv = p.dma("sync", lambda sy, dst_dram=dst_dram, chunks=chunks: sy.dma_start(
                out=bass.AP(dst_dram, chunks[0] * 128 * NTOK,
                            [[NTOK, 128], [128 * NTOK, len(chunks)], [1, NTOK]]),
                in_=s_out[:, 0:len(chunks), :]), waits=[norm_done], inc=s_st)
            return (s_st, stv)

        gd = []
        for h in range(4):
            gd = [conv_unit(scr["qraw"], scr["qn"], cols["cq"], [2 * h, 2 * h + 1], True, 1.0 / 16.0, gd)]
        for h in range(4):
            gd = [conv_unit(scr["kraw"], scr["kn"], cols["ck"], [2 * h, 2 * h + 1], True, 1.0, gd)]
        for c2 in range(8):
            gd = [conv_unit(scr["vraw"], scr["vv"], cols["cv"], [2 * c2, 2 * c2 + 1], False, 1.0, gd)]
    p.barrier(f"da_barV{tag}")

    with contextlib.ExitStack() as esS:
        s_S = esS.enter_context(nc.sbuf_tensor(f"da_S{tag}", [128, 2, 512], F32))
        s_Sb = esS.enter_context(nc.sbuf_tensor(f"da_Sbf{tag}", [128, 2, 512], BF16))
        s_q = esS.enter_context(nc.sbuf_tensor(f"da_sq2{tag}", [128, 2, 128], BF16))
        s_k = esS.enter_context(nc.sbuf_tensor(f"da_sk2{tag}", [128, 2, 128], BF16))
        s_v = esS.enter_context(nc.sbuf_tensor(f"da_sv2{tag}", [128, 4, 128], BF16))
        s_gt = esS.enter_context(nc.sbuf_tensor(f"da_sgt{tag}", [128, 4, 128], BF16))
        s_gs = esS.enter_context(nc.sbuf_tensor(f"da_sgs{tag}", [128, 4, 128], BF16))
        s_qh2 = esS.enter_context(nc.sbuf_tensor(f"da_qh2{tag}", [128, 2, 128], BF16))
        s_kh2 = esS.enter_context(nc.sbuf_tensor(f"da_kh2{tag}", [128, 2, 128], BF16))
        s_kc2 = esS.enter_context(nc.sbuf_tensor(f"da_kc2{tag}", [128, 2, 128], BF16))
        s_kvT = esS.enter_context(nc.sbuf_tensor(f"da_kvT{tag}", [128, 2, 128], BF16))
        s_vtm = esS.enter_context(nc.sbuf_tensor(f"da_vtm{tag}", [128, 512], BF16))
        s_Brow = esS.enter_context(nc.sbuf_tensor(f"da_Brow{tag}", [1, T + 1], F32))
        s_brow = esS.enter_context(nc.sbuf_tensor(f"da_brow{tag}", [1, T], F32))
        s_blr = esS.enter_context(nc.sbuf_tensor(f"da_blr{tag}", [1, 128], F32))
        s_lamr = esS.enter_context(nc.sbuf_tensor(f"da_lamr{tag}", [1, 128], F32))
        s_ecr = esS.enter_context(nc.sbuf_tensor(f"da_ecr{tag}", [1, 128], F32))
        s_blm = esS.enter_context(nc.sbuf_tensor(f"da_blm{tag}", [1, 128], F32))
        s_LAM = esS.enter_context(nc.sbuf_tensor(f"da_LAM{tag}", [128, 128], F32))
        s_EC = esS.enter_context(nc.sbuf_tensor(f"da_EC{tag}", [128, 128], F32))
        s_BL = esS.enter_context(nc.sbuf_tensor(f"da_BL{tag}", [128, 128], F32))
        s_BB = esS.enter_context(nc.sbuf_tensor(f"da_BB{tag}", [128, 128], F32))
        s_blc = esS.enter_context(nc.sbuf_tensor(f"da_blc{tag}", [128, 1], F32))
        s_bbc = esS.enter_context(nc.sbuf_tensor(f"da_bbc{tag}", [128, 1], F32))
        s_D = esS.enter_context(nc.sbuf_tensor(f"da_D{tag}", [128, 128], F32))
        s_Eu = esS.enter_context(nc.sbuf_tensor(f"da_Eu{tag}", [128, 128], F32))
        s_El = esS.enter_context(nc.sbuf_tensor(f"da_El{tag}", [128, 128], F32))
        s_Ei = esS.enter_context(nc.sbuf_tensor(f"da_Ei{tag}", [128, 128], F32))
        s_N = [esS.enter_context(nc.sbuf_tensor(f"da_N{tag}{i}", [128, 128], BF16)) for i in range(2)]
        s_Nt = [esS.enter_context(nc.sbuf_tensor(f"da_Nt{tag}{i}", [128, 128], BF16)) for i in range(2)]
        s_Wt = esS.enter_context(nc.sbuf_tensor(f"da_Wt{tag}", [128, 128], BF16))
        s_U = esS.enter_context(nc.sbuf_tensor(f"da_U{tag}", [128, 512], BF16))
        s_Uf = esS.enter_context(nc.sbuf_tensor(f"da_Uf{tag}", [128, 512], F32))
        s_osq = esS.enter_context(nc.sbuf_tensor(f"da_osq{tag}", [128, 512], F32))
        s_ORS = esS.enter_context(nc.sbuf_tensor(f"da_ORS{tag}", [128, 128], F32))
        s_of = esS.enter_context(nc.sbuf_tensor(f"da_of{tag}", [128, 4, 128], BF16))
        s_tt = esS.enter_context(nc.sbuf_tensor(f"da_tt{tag}", [128, 128], F32))
        s_orow = esS.enter_context(nc.sbuf_tensor(f"da_orow{tag}", [1, 128], F32))
        ps_bc2 = esS.enter_context(nc.psum_tensor(f"da_pbc2{tag}", [128, 128], F32))
        ps_K = esS.enter_context(nc.psum_tensor(f"da_pK{tag}", [128, 128], F32))
        ps_U = esS.enter_context(nc.psum_tensor(f"da_pU{tag}", [128, 512], F32))
        ps_X = esS.enter_context(nc.psum_tensor(f"da_pX{tag}", [128, 128], F32))
        ps_O = esS.enter_context(nc.psum_tensor(f"da_pO{tag}", [128, 512], F32))
        ps_S2 = esS.enter_context(nc.psum_tensor(f"da_pS2{tag}", [128, 512], F32))
        ps_ro = esS.enter_context(nc.psum_tensor(f"da_pro{tag}", [1, 128], F32))
        ps_T = esS.enter_context(nc.psum_tensor(f"da_pT{tag}", [128, 128], BF16))

        zs = p.sem(f"da_zs{tag}")

        last_dv = [0]
        last_ev = [0]

        def dve(fn, waits=()):
            v = p.comp("vector", fn, waits=[(s_dv, last_dv[0])] + list(waits), inc=s_dv)
            last_dv[0] = v
            return v

        def act(fn, waits=()):
            v = p.comp("scalar", fn, waits=[(s_ev, last_ev[0])] + list(waits), inc=s_ev)
            last_ev[0] = v
            return v

        for b in range(SEQ_PER_CORE):
            for h in range(4):
                zz = p.comp("gpsimd", lambda g: g.memset(s_S[:, :, :], 0.0),
                            waits=[(s_dv, last_dv[0]), (s_mm, s_mm.count)])
                zz = p.comp("gpsimd", lambda g: g.memset(s_Sb[:, :, :], 0.0))
                zv = p.comp("gpsimd", lambda g: g.tensor_copy(s_S[:, 0, 0:4], s_S[:, 0, 0:4]),
                            inc=zs)
                zval = zs.count
                lB = p.dma("gpsimd", lambda g, b=b, h=h: g.dma_start(
                    out=s_Brow[0:1, :],
                    in_=bass.AP(scr["Bsc"], h * 2 * (T + 1) + b * (T + 1), [[T + 1, 1], [1, T + 1]])),
                    inc=s_ld)
                lb2 = p.dma("gpsimd", lambda g, b=b, h=h: g.dma_start(
                    out=s_brow[0:1, :],
                    in_=bass.AP(scr["beta"], h * NTOK + b * T, [[T, 1], [1, T]])),
                    inc=s_ld)
                for j in range(NCH):
                    c0 = j * CJ
                    tok0 = b * T + c0
                    lq = p.dma("gpsimd", lambda g, h=h, tok0=tok0: g.dma_start(
                        out=s_q[:, :, :],
                        in_=bass.AP(scr["qn"], (256 * h) * NTOK + tok0,
                                    [[NTOK, 128], [128 * NTOK, 2], [1, 128]])),
                        waits=[(s_mm, s_mm.count)], inc=s_ld)
                    lk = p.dma("gpsimd", lambda g, h=h, tok0=tok0: g.dma_start(
                        out=s_k[:, :, :],
                        in_=bass.AP(scr["kn"], (256 * h) * NTOK + tok0,
                                    [[NTOK, 128], [128 * NTOK, 2], [1, 128]])),
                        inc=s_ld)
                    lv_ = p.dma("gpsimd", lambda g, h=h, tok0=tok0: g.dma_start(
                        out=s_v[:, :, :],
                        in_=bass.AP(scr["vv"], (512 * h) * NTOK + tok0,
                                    [[NTOK, 128], [128 * NTOK, 4], [1, 128]])),
                        inc=s_ld)
                    lg = p.dma("gpsimd", lambda g, h=h, tok0=tok0: g.dma_start(
                        out=s_gt[:, :, :],
                        in_=bass.AP(scr["gate"], (512 * h) * NTOK + tok0,
                                    [[NTOK, 128], [128 * NTOK, 4], [1, 128]])),
                        inc=s_ld)
                    dve(lambda v, c0=c0: v.tensor_scalar(
                        s_blr[0:1, :], s_Brow[0:1, 1 + c0:1 + c0 + CJ],
                        s_Brow.ap()[0:1, c0:c0 + 1], None, OP.subtract),
                        waits=[(s_ld, lB)])
                    act(lambda s: s.activation(s_lamr[0:1, :], s_blr[0:1, :], AF.Exp),
                        waits=[(s_dv, last_dv[0])])
                    dve(lambda v: v.tensor_scalar(
                        s_ecr[0:1, :], s_blr[0:1, :], -1.0, s_blr.ap()[0:1, 127:128],
                        OP.mult, OP.add))
                    act(lambda s: s.activation(s_ecr[0:1, :], s_ecr[0:1, :], AF.Exp),
                        waits=[(s_dv, last_dv[0])])
                    dve(lambda v, b=b, c0=c0: v.tensor_tensor(
                        s_blm[0:1, :], s_brow[0:1, c0:c0 + CJ], s_lamr[0:1, :], OP.mult),
                        waits=[(s_ld, lb2), (s_ev, last_ev[0])])
                    rows_done = last_dv[0]
                    bc_jobs = [(s_lamr, s_LAM), (s_ecr, s_EC), (s_blm, s_BL)]
                    mvp = None
                    for rsrc, rdst in bc_jobs:
                        mv = p.comp("tensor", lambda t, rsrc=rsrc: t.matmul(
                            ps_bc2[:, :], consts.ones_row[:1, :], rsrc[0:1, :],
                            start=True, stop=True, skip_group_check=True),
                            waits=[(s_dv, rows_done), (s_ev, last_ev[0])] +
                                  ([(s_dv, last_dv[0])] if mvp else []), inc=s_mm)
                        dve(lambda v, rdst=rdst: v.tensor_copy(rdst[:, :], ps_bc2[:, :]),
                            waits=[(s_mm, mv)])
                        mvp = mv
                    mv = p.comp("tensor", lambda t, b=b, c0=c0: t.matmul(
                        ps_bc2[:, :], consts.ones_row[:1, :], s_brow[0:1, c0:c0 + CJ],
                        start=True, stop=True, skip_group_check=True),
                        waits=[(s_dv, last_dv[0])], inc=s_mm)
                    dve(lambda v: v.tensor_copy(s_BB[:, :], ps_bc2[:, :]), waits=[(s_mm, mv)])
                    mv = p.comp("tensor", lambda t: t.matmul(
                        ps_bc2[:, 0:1], s_blr[0:1, :], consts.ones_row[0:1, 0:1],
                        start=True, stop=True, skip_group_check=True),
                        waits=[(s_dv, last_dv[0])], inc=s_mm)
                    dve(lambda v: v.tensor_copy(s_blc[:, :], ps_bc2[:, 0:1]), waits=[(s_mm, mv)])
                    mv = p.comp("tensor", lambda t, b=b, c0=c0: t.matmul(
                        ps_bc2[:, 0:1], s_brow[0:1, c0:c0 + CJ], consts.ones_row[0:1, 0:1],
                        start=True, stop=True, skip_group_check=True),
                        waits=[(s_dv, last_dv[0])], inc=s_mm)
                    dve(lambda v: v.tensor_copy(s_bbc[:, :], ps_bc2[:, 0:1]), waits=[(s_mm, mv)])
                    mv = p.comp("tensor", lambda t: t.matmul(
                        ps_bc2[:, :], consts.ones_row[:1, :], s_blr[0:1, :],
                        start=True, stop=True, skip_group_check=True),
                        waits=[(s_dv, last_dv[0])], inc=s_mm)
                    dve(lambda v: v.tensor_scalar(s_D[:, :], ps_bc2[:, :], -1.0,
                                                  s_blc.ap()[:, 0:1], OP.mult, OP.add),
                        waits=[(s_mm, mv)])
                    dve(lambda v: v.scalar_tensor_tensor(s_Eu[:, :], s_D[:, :], -1.0,
                                                         maskLI[:, :], OP.mult, OP.add),
                        waits=[(msem, mval), (s_ev, last_ev[0])])
                    dve(lambda v: v.tensor_tensor(s_El[:, :], s_D[:, :], maskSU[:, :], OP.add))
                    dve(lambda v: v.scalar_tensor_tensor(s_Ei[:, :], s_D[:, :], -1.0,
                                                         maskLS[:, :], OP.mult, OP.add))
                    act(lambda s: s.activation(s_Eu[:, :], s_Eu[:, :], AF.Exp),
                        waits=[(s_dv, last_dv[0])])
                    act(lambda s: s.activation(s_El[:, :], s_El[:, :], AF.Exp))
                    act(lambda s: s.activation(s_Ei[:, :], s_Ei[:, :], AF.Exp))
                    etiles = last_ev[0]
                    for ci in range(2):
                        p.comp("tensor", lambda t, ci=ci: t.matmul(
                            ps_K[:, :], s_k[:, ci, :], s_k[:, ci, :],
                            start=(ci == 0), stop=(ci == 1), skip_group_check=True),
                            waits=[(s_ld, lk), (s_dv, last_dv[0])] if ci == 0 else [],
                            inc=(s_mm if ci == 1 else None))
                    kkv = s_mm.count
                    dve(lambda v: v.tensor_tensor(s_tt[:, :], ps_K[:, :], s_El[:, :], OP.mult),
                        waits=[(s_mm, kkv), (s_ev, etiles)])
                    dve(lambda v: v.tensor_scalar(s_N[0][:, :], s_tt[:, :],
                                                  s_bbc.ap()[:, 0:1], None, OP.mult))
                    dve(lambda v: v.tensor_tensor(s_tt[:, :], ps_K[:, :], s_Eu[:, :], OP.mult))
                    ndone = dve(lambda v: v.tensor_tensor(s_Nt[0][:, :], s_tt[:, :], s_BB[:, :],
                                                          OP.mult))
                    for ci in range(2):
                        p.comp("tensor", lambda t, ci=ci: t.matmul(
                            ps_K[:, :], s_k[:, ci, :], s_q[:, ci, :],
                            start=(ci == 0), stop=(ci == 1), skip_group_check=True),
                            waits=[(s_ld, lq), (s_dv, ndone)] if ci == 0 else [],
                            inc=(s_mm if ci == 1 else None))
                    kqv = s_mm.count
                    wtv = dve(lambda v: v.tensor_tensor(s_Wt[:, :], ps_K[:, :], s_Ei[:, :], OP.mult),
                              waits=[(s_mm, kqv)])
                    for ci in range(2):
                        dve(lambda v, ci=ci: v.tensor_tensor(s_qh2[:, ci, :], s_q[:, ci, :],
                                                             s_LAM[:, :], OP.mult))
                        dve(lambda v, ci=ci: v.tensor_tensor(s_kh2[:, ci, :], s_k[:, ci, :],
                                                             s_BL[:, :], OP.mult))
                        dve(lambda v, ci=ci: v.tensor_tensor(s_kc2[:, ci, :], s_k[:, ci, :],
                                                             s_EC[:, :], OP.mult))
                    hats = last_dv[0]
                    tv = None
                    for ci in range(2):
                        tv = p.comp("tensor", lambda t, ci=ci: t.transpose(
                            ps_T[:, :], s_kc2[:, ci, :], consts.ident_bf[:, :]),
                            waits=[(s_dv, last_dv[0])], inc=s_mm)
                        dve(lambda v, ci=ci: v.tensor_copy(s_kvT[:, ci, :], ps_T[:, :]),
                            waits=[(s_mm, tv)])
                    for ci in range(4):
                        tv = p.comp("tensor", lambda t, ci=ci: t.transpose(
                            ps_T[:, :], s_v[:, ci, :], consts.ident_bf[:, :]),
                            waits=[(s_ld, lv_), (s_dv, last_dv[0])], inc=s_mm)
                        dve(lambda v, ci=ci: v.tensor_copy(s_vtm[:, ci * 128:(ci + 1) * 128],
                                                           ps_T[:, :]),
                            waits=[(s_mm, tv)])
                    vt_done = last_dv[0]
                    for ci in range(2):
                        p.comp("tensor", lambda t, ci=ci: t.matmul(
                            ps_U[:, :], s_kh2[:, ci, :], s_Sb[:, ci, :],
                            start=(ci == 0), stop=(ci == 1), skip_group_check=True),
                            waits=[(s_dv, vt_done), (zs, zval)] if ci == 0 else [],
                            inc=(s_mm if ci == 1 else None))
                    o2v = s_mm.count
                    dve(lambda v: v.scalar_tensor_tensor(
                        s_Uf[:, :], s_vtm[:, :], s_bbc.ap()[:, 0:1], ps_U[:, :],
                        OP.mult, OP.subtract),
                        waits=[(s_mm, o2v)])
                    dve(lambda v: v.tensor_copy(s_U[:, :], s_Uf[:, :]))
                    cpar = 0
                    for it in range(7):
                        mv = p.comp("tensor", lambda t, cpar=cpar: t.matmul(
                            ps_U[:, :], s_Nt[cpar][:, :], s_U[:, :],
                            start=True, stop=True, skip_group_check=True),
                            waits=[(s_dv, last_dv[0])], inc=s_mm)
                        dve(lambda v, it=it: v.tensor_tensor(
                            s_Uf[:, :], s_Uf[:, :], ps_U[:, :],
                            OP.subtract if it == 0 else OP.add),
                            waits=[(s_mm, mv)])
                        dve(lambda v: v.tensor_copy(s_U[:, :], s_Uf[:, :]))
                        if it < 6:
                            mv = p.comp("tensor", lambda t, cpar=cpar: t.matmul(
                                ps_X[:, :], s_Nt[cpar][:, :], s_N[cpar][:, :],
                                start=True, stop=True, skip_group_check=True),
                                waits=[(s_dv, last_dv[0])], inc=s_mm)
                            dve(lambda v, cpar=cpar: v.tensor_copy(s_N[1 - cpar][:, :], ps_X[:, :]),
                                waits=[(s_mm, mv)])
                            mv = p.comp("tensor", lambda t, cpar=cpar: t.transpose(
                                ps_T[:, :], s_N[1 - cpar][:, :], consts.ident_bf[:, :]),
                                waits=[(s_dv, last_dv[0])], inc=s_mm)
                            dve(lambda v, cpar=cpar: v.tensor_copy(s_Nt[1 - cpar][:, :], ps_T[:, :]),
                                waits=[(s_mm, mv)])
                            cpar = 1 - cpar
                    u_done = last_dv[0]
                    for vt in range(4):
                        for ci in range(2):
                            p.comp("tensor", lambda t, vt=vt, ci=ci: t.matmul(
                                ps_O[:, vt * 128:(vt + 1) * 128],
                                s_Sb[:, ci, vt * 128:(vt + 1) * 128], s_qh2[:, ci, :],
                                start=(ci == 0), stop=False, skip_group_check=True),
                                waits=[(s_dv, u_done)] if (vt == 0 and ci == 0) else [])
                        p.comp("tensor", lambda t, vt=vt: t.matmul(
                            ps_O[:, vt * 128:(vt + 1) * 128],
                            s_U[:, vt * 128:(vt + 1) * 128], s_Wt[:, :],
                            start=False, stop=True, skip_group_check=True),
                            inc=(s_mm if vt == 3 else None))
                    ov = s_mm.count
                    if "dbg" in scr and b == 1 and h == 0 and j == 3:
                        for off, t_sb in ((0, s_Wt), (1, s_U), (2, s_Ei), (3, s_LAM), (5, maskLS)):
                            width = 512 if t_sb is s_U else 128
                            p.dma("gpsimd", lambda g, off=off, t_sb=t_sb, width=width: g.dma_start(
                                out=bass.AP(scr["dbg"], off * 128 * 512, [[width, 128], [1, width]]),
                                in_=t_sb[:, 0:width] if t_sb is not s_U else t_sb[:, :]),
                                waits=[(s_mm, ov), (s_dv, last_dv[0]), (s_ev, last_ev[0])], inc=s_st)
                        p.dma("gpsimd", lambda g: g.dma_start(
                            out=bass.AP(scr["dbg"], 4 * 128 * 512, [[128, 1], [1, 128]]),
                            in_=s_blr[0:1, :]), inc=s_st)
                    for ci in range(2):
                        mv = p.comp("tensor", lambda t, ci=ci: t.matmul(
                            ps_S2[:, :], s_kvT[:, ci, :], s_U[:, :],
                            start=True, stop=True, skip_group_check=True),
                            waits=[(s_dv, last_dv[0])], inc=s_mm)
                        dve(lambda v, ci=ci: v.scalar_tensor_tensor(
                            s_S[:, ci, :], s_S[:, ci, :], s_LAM.ap()[:, 127:128],
                            ps_S2[:, :], OP.mult, OP.add),
                            waits=[(s_mm, mv)])
                        dve(lambda v, ci=ci: v.tensor_copy(s_Sb[:, ci, :], s_S[:, ci, :]))
                    dump_wait = []
                    if "oraw" in scr:
                        aqd = act(lambda s: s.activation(s_osq[:, :], ps_O[:, :], AF.Copy),
                                  waits=[(s_mm, ov)])
                        odma = p.dma("sync", lambda sy, h=h, tok0=tok0: sy.dma_start(
                            out=bass.AP(scr["oraw"], (512 * h) * NTOK + tok0,
                                        [[NTOK, 128], [128 * NTOK, 4], [1, 128]]),
                            in_=s_osq.ap()[:, 0:512].rearrange("p (a b) -> p a b", b=128)),
                            waits=[(s_ev, aqd)], inc=s_st)
                        dump_wait = [(s_st, odma)]
                    aq = act(lambda s: s.activation(s_osq[:, :], ps_O[:, :], AF.Square),
                             waits=[(s_mm, ov)] + dump_wait)
                    mv = None
                    for vt in range(4):
                        mv = p.comp("tensor", lambda t, vt=vt: t.matmul(
                            ps_ro[0:1, :], consts.ones_col[:, :],
                            s_osq[:, vt * 128:(vt + 1) * 128],
                            start=(vt == 0), stop=(vt == 3), skip_group_check=True),
                            waits=[(s_ev, aq), (s_dv, last_dv[0])] if vt == 0 else [],
                            inc=(s_mm if vt == 3 else None))
                    dve(lambda v: v.tensor_scalar(s_orow[0:1, :], ps_ro[0:1, :],
                                                  1.0 / 512.0, 1e-6, OP.mult, OP.add),
                        waits=[(s_mm, mv)])
                    act(lambda s: s.activation(s_orow[0:1, :], s_orow[0:1, :], AF.Ln),
                        waits=[(s_dv, last_dv[0])])
                    act(lambda s: s.activation(s_orow[0:1, :], s_orow[0:1, :], AF.Exp, scale=-0.5))
                    mv = p.comp("tensor", lambda t: t.matmul(
                        ps_bc2[:, :], consts.ones_row[:1, :], s_orow[0:1, :],
                        start=True, stop=True, skip_group_check=True),
                        waits=[(s_ev, last_ev[0]), (s_dv, last_dv[0])], inc=s_mm)
                    dve(lambda v: v.tensor_copy(s_ORS[:, :], ps_bc2[:, :]), waits=[(s_mm, mv)])
                    av = act(lambda s: s.activation(s_gs[:, :, :], s_gt[:, :, :], AF.Silu),
                             waits=[(s_ld, lg)])
                    for vt in range(4):
                        dve(lambda v, vt=vt: v.tensor_tensor(
                            s_tt[:, :], ps_O[:, vt * 128:(vt + 1) * 128], s_ORS[:, :], OP.mult))
                        dve(lambda v, vt=vt: v.scalar_tensor_tensor(
                            s_of[:, vt, :], s_tt[:, :], cols["onw"][:, vt:vt + 1],
                            s_gs[:, vt, :], OP.mult, OP.mult),
                            waits=[(s_ev, av)])
                    stv = p.dma("sync", lambda sy, h=h, tok0=tok0: sy.dma_start(
                        out=bass.AP(scr["osc"], (512 * h) * NTOK + tok0,
                                    [[NTOK, 128], [128 * NTOK, 4], [1, 128]]),
                        in_=s_of[:, :, :]), waits=[(s_dv, last_dv[0])], inc=s_st)
    p.barrier(f"da_barS{tag}")

    with contextlib.ExitStack() as esO:
        s_wo = esO.enter_context(nc.sbuf_tensor(f"da_wo{tag}", [128, 16, D], BF16))
        s_w2 = esO.enter_context(nc.sbuf_tensor(f"da_w2{tag}", [128, 8, D], BF16))
        for dst, key, nch in ((s_wo, "Wo", 16), (s_w2, "W2", 8)):
            p.dma("gpsimd", lambda g, dst=dst, key=key, nch=nch: g.dma_start(
                out=dst[:, :, :],
                in_=bass.AP(W[key], 0, [[D, 128], [128 * D, nch], [1, D]])), inc=s_ld)
        wld2 = s_ld.count
        s_o = esO.enter_context(nc.sbuf_tensor(f"da_og{tag}", [128, 16, Tg], BF16))
        s_xso = esO.enter_context(nc.sbuf_tensor(f"da_xso{tag}", [128, 8, Tg], F32))
        s_xo = esO.enter_context(nc.sbuf_tensor(f"da_xo{tag}", [128, 8, Tg], F32))
        s_x2 = esO.enter_context(nc.sbuf_tensor(f"da_x2b{tag}", [128, 8, Tg], F32))
        s_h2n = esO.enter_context(nc.sbuf_tensor(f"da_h2n{tag}", [128, 8, Tg], BF16))
        s_t2 = esO.enter_context(nc.sbuf_tensor(f"da_t2{tag}", [128, Tg], F32))
        s_y2 = esO.enter_context(nc.sbuf_tensor(f"da_y2{tag}", [128, 8, Tg], F32))
        lnO = make_ln_ctx(p, nc, esO, consts, Tg, "dO" + tag)
        ps_p2 = [esO.enter_context(nc.psum_tensor(f"da_po{tag}{i}", [128, Tg], F32)) for i in range(2)]
        bank_free = {}
        ln_last = None
        st_prev = None
        for g in range(NG):
            lw = [(s_ld, wld2)]
            if ln_last is not None:
                lw.append((s_dv, s_dv.count))
            lo = p.dma("sync", lambda sy, g=g: sy.dma_start(
                out=s_o[:, :, :],
                in_=bass.AP(scr["osc"], g * Tg, [[NTOK, 128], [128 * NTOK, 16], [1, Tg]])),
                waits=lw, inc=s_ld)
            lxs = p.dma("sync", lambda sy, g=g: sy.dma_start(
                out=s_xso[:, :, :],
                in_=bass.AP(scr["xsub"], g * Tg, [[NTOK, 128], [128 * NTOK, 8], [1, Tg]])),
                inc=s_ld)
            lx = p.dma("sync", lambda sy, g=g: sy.dma_start(
                out=s_xo[:, :, :],
                in_=bass.AP(x_in, g * Tg, [[NTOK, 128], [128 * NTOK, 8], [1, Tg]])),
                inc=s_ld)
            x2_done = None
            for ot in range(8):
                bi = ot % 2
                for ic in range(16):
                    w = []
                    if ic == 0:
                        w = [(s_ld, lo)]
                        if bi in bank_free:
                            w.append(bank_free[bi])
                    p.comp("tensor", lambda t, ic=ic, ot=ot, bi=bi: t.matmul(
                        ps_p2[bi][:, :], s_wo[:, ic, ot * 128:(ot + 1) * 128],
                        s_o.ap()[:, ic, :], start=(ic == 0), stop=(ic == 15),
                        skip_group_check=True), waits=w, inc=(s_mm if ic == 15 else None))
                mmv = s_mm.count
                hv = p.comp("vector", lambda v, ot=ot, bi=bi: v.tensor_tensor(
                    s_x2.ap()[:, ot, :], ps_p2[bi][:, :], s_xso.ap()[:, ot, :], OP.add),
                    waits=[(s_mm, mmv), (s_ld, lxs)] +
                          ([(lnO["sem_dv"], x2_done)] if (x2_done and ot == 0) else []),
                    inc=s_dv)
                bank_free[bi] = (s_dv, hv)
            x2v = s_dv.count
            wr = []
            wr = [(s_mm, s_mm.count)] if g > 0 else []
            lnsem, lnval = emit_layernorm(p, lnO, s_x2.ap(), s_h2n.ap(),
                                          [(s_dv, x2v)], write_waits=wr,
                                          affine=(cols["ln2g"], cols["ln2b"]))
            ln_last = lnval
            xxv = None
            for c in range(8):
                xxv = p.comp("vector", lambda v, c=c: v.tensor_tensor(
                    s_xo.ap()[:, c, :], s_x2.ap()[:, c, :], s_xo.ap()[:, c, :], OP.add),
                    waits=[(s_dv, x2v), (s_ld, lx)] if c == 0 else [], inc=s_dv)
            yv = None
            for ot in range(8):
                bi = ot % 2
                for ic in range(8):
                    w = []
                    if ic == 0:
                        w = [(lnsem, lnval)]
                        if bi in bank_free:
                            w.append(bank_free[bi])
                    p.comp("tensor", lambda t, ic=ic, ot=ot, bi=bi: t.matmul(
                        ps_p2[bi][:, :], s_w2[:, ic, ot * 128:(ot + 1) * 128],
                        s_h2n.ap()[:, ic, :], start=(ic == 0), stop=(ic == 7),
                        skip_group_check=True), waits=w, inc=(s_mm if ic == 7 else None))
                mmv = s_mm.count
                gw = [(s_mm, mmv)]
                if yv is not None:
                    gw.append((s_dv, yv))
                ge = p.comp("scalar", lambda s, ot=ot, bi=bi: s.activation(
                    s_t2[:, :], ps_p2[bi][:, :], AF.Gelu, bias=cols["b2"][:, ot:ot + 1]),
                    waits=gw, inc=s_ev)
                yv = p.comp("vector", lambda v, ot=ot: v.tensor_tensor(
                    s_y2.ap()[:, ot, :], s_t2[:, :], s_xo.ap()[:, ot, :], OP.add),
                    waits=[(s_ev, ge), (s_dv, xxv)] +
                          ([(s_st, st_prev)] if (st_prev and ot == 0) else []), inc=s_dv)
                bank_free[bi] = (s_dv, yv)
            st_prev = p.dma("sync", lambda sy, g=g: sy.dma_start(
                out=bass.AP(y_out, g * Tg, [[NTOK, 128], [128 * NTOK, 8], [1, Tg]]),
                in_=s_y2[:, :, :]), waits=[(s_dv, s_dv.count)], inc=s_st)
            x2_done = lnval
    p.wait("sync", [(s_st, s_st.count)])
    return s_st
